# version 01
# baseline (speedup 1.0000x reference)
import contextlib
import numpy as np
import ml_dtypes
import concourse.bass as bass
import concourse.mybir as mybir
from concourse.bass_utils import run_bass_kernel_spmd

F32 = mybir.dt.float32
BF16 = mybir.dt.bfloat16
AF = mybir.ActivationFunctionType
ALU = mybir.AluOpType
AX = mybir.AxisListType.X

D = 1024
KC = 8
DFF = 4096
FC = 32
E = 8
EPS = 1e-5
NEG = -1e9

BF = ml_dtypes.bfloat16


class Sched:
    ENGS = ("sync", "scalar", "vector", "tensor", "gpsimd")

    def __init__(self, nc, es):
        self.nc = nc
        self.es = es
        self.seq = {e: 0 for e in self.ENGS}
        self.cseq = 0
        self.sem = {e: es.enter_context(nc.semaphore(f"is_{e}")) for e in self.ENGS}
        self.csem = es.enter_context(nc.semaphore("cs"))
        self.dgrp = {}
        self.last_w = {}
        self.readers = {}
        self.watermark = {e: {} for e in self.ENGS}
        self.eng_ops = {e: [] for e in self.ENGS}

    def _group(self, name, queue):
        g = self.dgrp.get(name)
        if g is None:
            g = [self.es.enter_context(self.nc.semaphore(f"dg_{name}")), 0, queue]
            self.dgrp[name] = g
        assert g[2] == queue, f"dma group {name} used from {queue} and {g[2]}"
        return g

    def _need(self, eng, tok, waits, force=False, dma_queue=None):
        if tok is None:
            return
        kind = tok[0]
        if kind == "i":
            _, peng, val = tok
            if peng == eng and eng == "tensor" and not force:
                return
            sem, target, wkey = self.sem[peng], val, ("i", peng)
        elif kind == "d":
            _, grp, val = tok
            g = self.dgrp[grp]
            if dma_queue is not None and g[2] == dma_queue:
                return
            sem, target, wkey = g[0], 16 * val, ("d", grp)
        else:
            _, val = tok
            sem, target, wkey = self.csem, val, ("c",)
        wm = self.watermark[eng]
        if wm.get(wkey, 0) >= target:
            return
        wm[wkey] = target
        waits.append((sem, target))

    def op(self, eng, fn, reads=(), writes=(), kind="i", sreads=(), dgrp=None):
        waits = []
        dq = eng if kind == "d" else None
        for r in sreads:
            self._need(eng, self.last_w.get(r), waits, force=True, dma_queue=dq)
        for r in reads:
            self._need(eng, self.last_w.get(r), waits, dma_queue=dq)
        for w in writes:
            self._need(eng, self.last_w.get(w), waits, dma_queue=dq)
            for t in self.readers.get(w, ()):
                self._need(eng, t, waits, dma_queue=dq)
        if kind == "d":
            g = self._group(dgrp, eng)
            g[1] += 1
            tok = ("d", dgrp, g[1])
            inc = (g[0], 16)
        elif kind == "c":
            self.cseq += 1
            tok = ("c", self.cseq)
            inc = (self.csem, 1)
        else:
            self.seq[eng] += 1
            tok = ("i", eng, self.seq[eng])
            inc = (self.sem[eng], 1)
        self.eng_ops[eng].append((fn, waits, inc))
        for r in list(reads) + list(sreads):
            self.readers.setdefault(r, []).append(tok)
        for w in writes:
            self.last_w[w] = tok
            self.readers[w] = []
        return tok

    def barrier_wait(self, eng, keys):
        waits = []
        for k in keys:
            self._need(eng, self.last_w.get(k), waits)
            for t in self.readers.get(k, ()):
                self._need(eng, t, waits)
        if waits:
            self.eng_ops[eng].append((None, waits, None))

    def replay(self, block):
        for ename in self.ENGS:
            ops = self.eng_ops[ename]

            def make(op_list):
                def run(eng):
                    for fn, waits, inc in op_list:
                        for (s, tgt) in waits:
                            eng.wait_ge(s, tgt)
                        if fn is not None:
                            fn(eng).then_inc(*inc)
                return run

            getattr(block, ename)(make(ops))


def build_program(S, upto=9):
    assert S % 256 == 0
    TW = min(512, S)
    TAU = S // TW
    NT = S // 128
    GPT = TW // 128
    CAPTH = float(S) - 0.5

    nc = bass.Bass("TRN2", target_bir_lowering=False)

    h_xtm = nc.dram_tensor("x_tm", [S, D], F32, kind="ExternalInput")
    h_xfm = nc.dram_tensor("x_fm", [D, S], F32, kind="ExternalInput")
    h_wq = nc.dram_tensor("wq_b", [KC, D, 128], F32, kind="ExternalInput")
    h_wk = nc.dram_tensor("wk_b", [KC, D, 128], F32, kind="ExternalInput")
    h_wv = nc.dram_tensor("wv_b", [KC, D, 128], F32, kind="ExternalInput")
    h_wo = nc.dram_tensor("wo_b", [2, D, 512], F32, kind="ExternalInput")
    h_w1 = nc.dram_tensor("w1_b", [FC, D, 128], F32, kind="ExternalInput")
    h_w3 = nc.dram_tensor("w3_b", [FC, D, 128], F32, kind="ExternalInput")
    h_w2 = nc.dram_tensor("w2_b", [FC, 128, D], F32, kind="ExternalInput")
    h_we = nc.dram_tensor("we_b", [E, 2, D, 512], F32, kind="ExternalInput")
    h_wg = nc.dram_tensor("wg", [D, E], F32, kind="ExternalInput")
    h_be = nc.dram_tensor("be", [E, D], F32, kind="ExternalInput")
    h_n1 = nc.dram_tensor("n1c", [128, KC], F32, kind="ExternalInput")
    h_n2 = nc.dram_tensor("n2c", [128, KC], F32, kind="ExternalInput")
    h_cos = nc.dram_tensor("cosT", [128, S], BF16, kind="ExternalInput")
    h_sin = nc.dram_tensor("sinT", [128, S], BF16, kind="ExternalInput")
    h_idn = nc.dram_tensor("idn", [128, 128], F32, kind="ExternalInput")
    h_prm = nc.dram_tensor("prm", [128, 128], BF16, kind="ExternalInput")
    h_mskT = nc.dram_tensor("mskT", [128, 128], BF16, kind="ExternalInput")
    h_ltri = nc.dram_tensor("ltri", [128, 128], F32, kind="ExternalInput")
    h_sel = nc.dram_tensor("selm", [E, E * 128], BF16, kind="ExternalInput")
    h_selc = nc.dram_tensor("selc", [E, 1], F32, kind="ExternalInput")
    h_out = nc.dram_tensor("out", [D, S], F32, kind="ExternalOutput")
    d_ccin = nc.dram_tensor("cc_in", [1, 512], F32, kind="Internal")
    d_ccout = nc.dram_tensor("cc_out", [8, 512], F32, kind="Internal")

    es = contextlib.ExitStack()

    def sb(name, p, f, dt):
        return es.enter_context(nc.sbuf_tensor(name, [p, f], dt))

    t_xn = sb("xn", 128, KC * S, BF16)
    t_A = sb("uA", 128, KC * S, BF16)
    t_B = sb("uB", 128, KC * S, BF16)
    t_C = sb("uC", 128, NT * 130, BF16)
    t_q = sb("qbuf", 128, S, BF16)
    t_k = sb("kbuf", 128, S, BF16)
    t_xne = sb("xne", 128, max(KC * TW, S), BF16)
    t_wr = sb("wring", 128, 2 * 4096, BF16)
    t_cos = sb("cosb", 128, S, BF16)
    t_sin = sb("sinb", 128, S, BF16)
    t_f32 = sb("f32a", 128, max(KC * TW + 2 * TW, 3 * 1024), F32)
    t_f32b = sb("f32b", 128, 3 * TW, F32)
    t_pg = sb("pgr", 128, 4 * TW, BF16)
    t_w2g = sb("w2g", 128, 4 * D, BF16)
    t_rope = sb("ropes", 128, 2 * TW, BF16)
    t_pt = sb("ptb", 128, 1024, BF16)
    t_sil = sb("silb", 128, 2 * TW, BF16)
    t_ev = sb("evac", 128, 2 * TW, F32)
    t_gb = sb("gbrow", 64, 128, BF16)
    t_idn = sb("idnb", 128, 128, F32)
    t_prm = sb("prmb", 128, 128, BF16)
    t_mskT = sb("mskTb", 128, 128, BF16)
    t_ltri = sb("ltrib", 128, 128, F32)
    t_sel = sb("selb", E, E * 128, BF16)
    t_n1 = sb("n1b", 128, KC, F32)
    t_n2 = sb("n2b", 128, KC, F32)
    t_wg = sb("wgb", 128, KC * E, F32)
    t_be = sb("beb", E, D, BF16)
    t_stats = sb("statc", 128, 5 * NT, F32)
    t_statT = sb("statT", 3 * NT, 128, F32)
    t_rows = sb("rows", 1, TW, F32)
    t_mask = sb("maskal", 128, NT * E, F32)
    t_gate = sb("gateal", 128, NT, F32)
    t_kept = sb("keptal", 128, NT * E, F32)
    t_gall = sb("gall", 128, NT * E, F32)
    t_gt = sb("gtb", E, S, BF16)
    t_offs = sb("offs", 1, NT * E + 512 + E, F32)
    t_allc = sb("allc", E, E, F32)
    t_selc = sb("selcb", E, 1, F32)
    t_ones = sb("onesf", 128, 1, F32)
    t_onesb = sb("onesbf", 128, 1, BF16)
    t_oner = sb("onerf", 1, 128, F32)
    t_onerb = sb("onerbf", 1, 128, BF16)
    t_nB = sb("negB", 128, KC, F32)
    t_sc = sb("smallc", 128, max(2 * NT, 20), F32)
    t_eps = sb("epsc", 128, 1, F32)
    nc.const_aps.register(F32, EPS, t_eps[:])
    t_sr = sb("smallr", 1, 32, F32)
    t_rec = sb("recrow", 1, 128, BF16)

    ps_st = es.enter_context(nc.psum_tensor("ps_st", [128, 1024], F32))
    ps_ctx = es.enter_context(nc.psum_tensor("ps_ctx", [128, 2048], F32))
    ps_a = es.enter_context(nc.psum_tensor("ps_a", [128, 512], F32))
    ps_b = es.enter_context(nc.psum_tensor("ps_b", [128, 512], F32))

    sch = Sched(nc, es)
    OP = sch.op

    def finish():
        with nc.allow_low_precision(reason="bf16 by design"):
            with nc.Block("main") as block:
                sch.replay(block)
        es.close()
        return nc

    def dbg_finish(stores):
        keys = []
        for idx, (o, i_, rd) in enumerate(stores):
            k = f"dbgout{idx}"
            OP("gpsimd", lambda e, o=o, i_=i_: e.dma_start(out=o, in_=i_),
               reads=rd, writes=[k], kind="d", dgrp="dbg")
            keys.append(k)
        sch.barrier_wait("gpsimd", keys)
        return finish()

    for hh, tt, key in ((h_idn, t_idn, "idn"), (h_prm, t_prm, "prm"),
                        (h_mskT, t_mskT, "mskT"), (h_ltri, t_ltri, "ltri"),
                        (h_sel, t_sel, "sel"), (h_n1, t_n1, "n1"),
                        (h_n2, t_n2, "n2"), (h_cos, t_cos, "cos"),
                        (h_sin, t_sin, "sin"), (h_selc, t_selc, "selc")):
        OP("sync", lambda e, hh=hh, tt=tt: e.dma_start(out=tt[:], in_=hh[:]),
           writes=[key], kind="d", dgrp=key)
    OP("sync", lambda e: e.dma_start(
        out=t_wg.ap().rearrange("p (kc e) -> p kc e", e=E),
        in_=h_wg.rearrange("(kc p) e -> p kc e", p=128)), writes=["wg"], kind="d", dgrp="wg")
    OP("gpsimd", lambda e: e.dma_start(out=t_be[:], in_=h_be[:]), writes=["be"], kind="d", dgrp="be")
    OP("vector", lambda e: e.memset(t_ones[:], 1.0), writes=["ones"])
    OP("vector", lambda e: e.memset(t_onesb[:], 1.0), writes=["onesb"])
    OP("vector", lambda e: e.memset(t_oner[:], 1.0), writes=["oner"])
    OP("vector", lambda e: e.memset(t_onerb[:], 1.0), writes=["onerb"])
    OP("vector", lambda e: e.memset(t_offs[0:1, NT * E:NT * E + 512], 0.0), writes=["cpad"])
    OP("vector", lambda e: e.memset(t_eps[:], EPS), writes=["epsc"])

    xtm = (t_f32[:, 0:1024], t_f32[:, 1024:2048])
    sqdump = t_f32[:, 2048:3072]
    C_S1, C_S2, C_MEAN, C_RSTD, C_RSTD1 = 0, NT, 2 * NT, 3 * NT, 4 * NT
    for i in range(NT):
        xb, key = xtm[i % 2], f"xtm{i % 2}"
        OP("sync", lambda e, xb=xb, i=i: e.dma_start(
            out=xb[:], in_=h_xtm[i * 128:(i + 1) * 128, :]), writes=[key, "f32a"],
            kind="d", dgrp=key)
        OP("vector", lambda e, xb=xb, i=i: e.reduce_sum(
            t_stats[:, C_S1 + i:C_S1 + i + 1], xb[:], axis=AX),
            reads=[key], writes=["stats"])
        OP("scalar", lambda e, xb=xb, i=i: e.activation(
            sqdump[:], xb[:], AF.Square, accum_out=t_stats[:, C_S2 + i:C_S2 + i + 1]),
            reads=[key], writes=["stats", "f32a"])
    S1 = t_stats[:, C_S1:C_S1 + NT]
    S2 = t_stats[:, C_S2:C_S2 + NT]
    MEAN = t_stats[:, C_MEAN:C_MEAN + NT]
    RSTD = t_stats[:, C_RSTD:C_RSTD + NT]
    RSTD1 = t_stats[:, C_RSTD1:C_RSTD1 + NT]
    TV = t_sc[:, 0:NT]
    SD = t_sc[:, NT:2 * NT]
    OP("vector", lambda e: e.tensor_scalar_mul(MEAN, S1, 1.0 / D), reads=["stats"], writes=["stats"])
    OP("scalar", lambda e: e.activation(TV, MEAN, AF.Square), reads=["stats"], writes=["tsc"])
    OP("vector", lambda e: e.scalar_tensor_tensor(TV, S2, 1.0 / D, TV, ALU.mult, ALU.subtract),
       reads=["stats", "tsc"], writes=["tsc"])
    OP("scalar", lambda e: e.activation(SD, TV, AF.Sqrt, bias=EPS),
       reads=["tsc", "epsc"], writes=["tsc"])
    OP("vector", lambda e: e.reciprocal(RSTD, SD), reads=["tsc"], writes=["stats"])
    OP("scalar", lambda e: e.activation(RSTD1, RSTD, AF.Square), reads=["stats"], writes=["stats"])
    OP("vector", lambda e: e.tensor_mul(RSTD1, RSTD1, TV), reads=["stats", "tsc"], writes=["stats"])
    OP("scalar", lambda e: e.activation(RSTD1, RSTD1, AF.Sqrt, bias=EPS),
       reads=["stats", "epsc"], writes=["stats"])
    OP("vector", lambda e: e.reciprocal(RSTD1, RSTD1), reads=["stats"], writes=["stats"])
    OP("tensor", lambda e: e.transpose(ps_a[0:3 * NT, 0:128],
                                       t_stats[:, C_MEAN:C_MEAN + 3 * NT], t_idn[:]),
       reads=["stats", "idn"], writes=["ps:A"])
    OP("scalar", lambda e: e.copy(t_statT[:], ps_a[0:3 * NT, 0:128]),
       reads=["ps:A"], writes=["statT"])

    xn32 = t_f32[:, 0:KC * TW]
    xping = (t_f32[:, KC * TW:KC * TW + TW], t_f32[:, KC * TW + TW:KC * TW + 2 * TW])
    mean_b = t_f32b[:, 0:TW]
    rstd_b = t_f32b[:, TW:2 * TW]
    rstd1_b = t_f32b[:, 2 * TW:3 * TW]
    rstd2_b = t_f32b[:, 0:TW]

    for tau in range(TAU):
        t0 = tau * TW
        for si, dst, bk in ((0, mean_b, "bc0"), (1, rstd_b, "bc1"), (2, rstd1_b, "bc2")):
            g0 = tau * GPT
            OP("sync", lambda e, si=si, g0=g0: e.dma_start(
                out=t_rows.ap().rearrange("o (a b) -> o a b", b=128),
                in_=t_statT[si * NT + g0:si * NT + g0 + GPT, :]),
                reads=["statT"], writes=["rows"], kind="d", dgrp="rows")
            OP("tensor", lambda e, si=si: e.matmul(
                ps_b[:, 0:TW], t_oner[:], t_rows[0:1, 0:TW],
                start=True, stop=True),
                reads=["rows", "oner"], writes=["ps:B"])
            OP("scalar", lambda e, dst=dst: e.copy(dst, ps_b[:, 0:TW]),
               reads=["ps:B"], writes=[bk])
        for kc in range(KC):
            xp, xk = xping[kc % 2], f"xp{kc % 2}"
            OP("sync", lambda e, xp=xp, kc=kc, t0=t0: e.dma_start(
                out=xp[:], in_=h_xfm[kc * 128:(kc + 1) * 128, t0:t0 + TW]),
                writes=[xk, "f32a"], kind="d", dgrp=xk)
            xnk = xn32[:, kc * TW:(kc + 1) * TW]
            OP("vector", lambda e, xp=xp, xnk=xnk: e.tensor_sub(xnk, xp[:], mean_b),
               reads=[xk, "bc0"], writes=[f"xn32:{kc}", "f32a"])
            OP("vector", lambda e, xnk=xnk: e.tensor_mul(xnk, xnk, rstd_b),
               reads=[f"xn32:{kc}", "bc1"], writes=[f"xn32:{kc}"])
            OP("scalar", lambda e, xnk=xnk, kc=kc, t0=t0: e.copy(
                t_xn[:, kc * S + t0:kc * S + t0 + TW], xnk),
                reads=[f"xn32:{kc}"], writes=[f"xn:{kc}:{tau}"])
            OP("vector", lambda e, xnk=xnk, kc=kc, t0=t0: e.scalar_tensor_tensor(
                t_A[:, kc * S + t0:kc * S + t0 + TW], xnk, t_n1[:, kc:kc + 1],
                rstd1_b, ALU.mult, ALU.mult),
                reads=[f"xn32:{kc}", "bc2"], sreads=["n1"], writes=[f"A:{kc}:{tau}"])
        for g in range(GPT):
            gg = tau * GPT + g
            for kc in range(KC):
                OP("tensor", lambda e, kc=kc, g=g: e.matmul(
                    ps_a[:, 0:E],
                    xn32[:, kc * TW + g * 128:kc * TW + g * 128 + 128],
                    t_wg[:, kc * E:(kc + 1) * E],
                    start=(kc == 0), stop=(kc == KC - 1)),
                    reads=[f"xn32:{kc}", "wg"], writes=["ps:A"])
            m8, nm8, sume = t_sc[:, 0:1], t_sc[:, 1:2], t_sc[:, 2:3]
            lgs = t_sc[:, 4:12]
            dmp = t_sc[:, 12:20]
            OP("vector", lambda e, lgs=lgs: e.tensor_copy(lgs, ps_a[:, 0:E]),
               reads=["ps:A"], writes=["lgs", "tsc"])
            OP("vector", lambda e, lgs=lgs: e.reduce_max(m8, lgs, axis=AX),
               reads=["lgs"], writes=["m8", "tsc"])
            OP("vector", lambda e, gg=gg, lgs=lgs: e.tensor_single_scalar(
                t_mask[:, gg * E:(gg + 1) * E], lgs, m8, ALU.is_ge),
                reads=["lgs"], sreads=["m8"], writes=["maskal"])
            OP("vector", lambda e: e.tensor_scalar_mul(nm8, m8, -1.0),
               reads=["m8"], writes=["nm8", "tsc"])
            OP("scalar", lambda e, dmp=dmp, lgs=lgs: e.activation(
                dmp, lgs, AF.Exp, bias=nm8, accum_out=sume),
                reads=["lgs"], sreads=["nm8"], writes=["sume", "tsc"])
            OP("vector", lambda e, gg=gg: e.reciprocal(t_gate[:, gg:gg + 1], sume),
               reads=["sume"], writes=["gateal"])
            if upto == 2:
                OP("vector", lambda e, gg=gg, lgs=lgs: e.tensor_copy(
                    t_kept[:, gg * E:(gg + 1) * E], lgs),
                    reads=["lgs"], writes=["keptal"])
                OP("vector", lambda e, gg=gg: e.tensor_copy(
                    t_gall[:, gg * E:gg * E + 1], m8),
                    sreads=["m8"], writes=["gall"])

    if upto == 1:
        return dbg_finish(
            [(h_out[kc * 128:(kc + 1) * 128, :], t_xn[:, kc * S:(kc + 1) * S],
              [f"xn:{kc}:{tt}" for tt in range(TAU)]) for kc in range(KC)])

    OP("vector", lambda e: e.memset(t_offs[0:1, 0:E], 0.0), writes=["offr"])
    for g in range(NT):
        if g > 0:
            OP("vector", lambda e, g=g: e.tensor_copy(
                t_offs[0:1, g * E:(g + 1) * E], ps_b[0:1, 0:E]),
                reads=["ps:B"], writes=["offr"])
        OP("tensor", lambda e, g=g: e.matmul(
            ps_b[0:1, 0:E], t_ones[:], t_mask[:, g * E:(g + 1) * E],
            start=(g == 0), stop=(g == NT - 1)),
            reads=["maskal", "ones", "offr"], writes=["ps:B"])
    OP("vector", lambda e: e.tensor_copy(t_offs[0:1, NT * E:NT * E + E], ps_b[0:1, 0:E]),
       reads=["ps:B", "cpad"], writes=["cpad"])
    OP("sync", lambda e: e.dma_start(out=d_ccin[:], in_=t_offs[0:1, NT * E:NT * E + 512]),
       reads=["cpad"], writes=["ccin"], kind="d", dgrp="ccin")
    OP("gpsimd", lambda e: e.collective_compute(
        "AllGather", ALU.bypass, replica_groups=[list(range(8))],
        ins=[d_ccin[:]], outs=[d_ccout[:]]),
        reads=["ccin"], writes=["ccout"], kind="c")

    def emit_offsets():
        OP("gpsimd", lambda e: e.dma_start(out=t_allc[:], in_=d_ccout[:, 0:E]),
           reads=["ccout"], writes=["allc"], kind="d", dgrp="allc")
        OP("tensor", lambda e: e.matmul(ps_b[0:1, 0:E], t_selc[:], t_allc[:],
                                        start=True, stop=True),
           reads=["allc", "selc"], writes=["ps:B"])
        OP("scalar", lambda e: e.copy(t_offs[0:1, NT * E + 512:NT * E + 512 + E],
                                      ps_b[0:1, 0:E]),
           reads=["ps:B"], writes=["coff"])
        for g in range(NT):
            OP("vector", lambda e, g=g: e.tensor_add(
                t_offs[0:1, g * E:(g + 1) * E], t_offs[0:1, g * E:(g + 1) * E],
                t_offs[0:1, NT * E + 512:NT * E + 512 + E]),
                reads=["offr", "coff"], writes=["offr"])

    if upto == 2:
        emit_offsets()
        return dbg_finish([
            (h_out[0:128, 0:NT * E], t_mask[:], ["maskal"]),
            (h_out[128:256, 0:NT], t_gate[:], ["gateal"]),
            (h_out[256:257, 0:NT * E], t_offs[0:1, 0:NT * E], ["offr"]),
            (h_out[384:512, 0:NT * E], t_kept[:], ["keptal"]),
            (h_out[512:640, 0:NT * E], t_gall[:], ["gall"]),
        ])

    def wr(slot, off, n):
        return t_wr[:, slot * 4096 + off:slot * 4096 + off + n]

    WRALL = lambda s: [f"wr{s}a", f"wr{s}b", f"wr{s}c", f"wr{s}d"]

    def proj_rope(c, reg, woff, hw, dst, dkey, scale):
        slot = c % 2
        wsb = wr(slot, woff, KC * 128)
        wkey = f"wr{slot}{reg}"
        OP("gpsimd", lambda e, wsb=wsb: e.dma_start(
            out=wsb.rearrange("p (kc f) -> p kc f", f=128),
            in_=hw[c].rearrange("(kc p) f -> p kc f", p=128)),
            writes=[wkey], kind="d", dgrp=wkey)
        for tau in range(TAU):
            t0 = tau * TW
            for kc in range(KC):
                OP("tensor", lambda e, kc=kc, t0=t0, wsb=wsb: e.matmul(
                    ps_a[:, 0:TW], wsb[:, kc * 128:(kc + 1) * 128],
                    t_A[:, kc * S + t0:kc * S + t0 + TW],
                    start=(kc == 0), stop=(kc == KC - 1)),
                    reads=[wkey, f"A:{kc}:{tau}"], writes=["ps:A"])
            qr, qk = t_pg[:, (tau % 2) * TW:(tau % 2 + 1) * TW], f"qraw{tau % 2}"
            OP("scalar", lambda e, qr=qr: e.copy(qr, ps_a[:, 0:TW]),
               reads=["ps:A"], writes=[qk])
            OP("tensor", lambda e, qr=qr: e.matmul(
                ps_b[:, 0:TW], t_prm[:], qr, start=True, stop=True),
                reads=[qk, "prm"], writes=["ps:B"])
            tt1, tt2 = t_rope[:, 0:TW], t_rope[:, TW:2 * TW]
            OP("vector", lambda e, qr=qr, t0=t0: e.scalar_tensor_tensor(
                tt1, qr, scale, t_cos[:, t0:t0 + TW], ALU.mult, ALU.mult),
                reads=[qk, "cos"], writes=["tt1"])
            OP("vector", lambda e, t0=t0: e.scalar_tensor_tensor(
                tt2, ps_b[:, 0:TW], scale, t_sin[:, t0:t0 + TW], ALU.mult, ALU.mult),
                reads=["ps:B", "sin"], writes=["tt2"])
            OP("vector", lambda e, t0=t0, dst=dst: e.tensor_add(
                dst[:, t0:t0 + TW], tt1, tt2),
                reads=["tt1", "tt2"], writes=[dkey])

    def build_V(c):
        slot = c % 2
        wsb = wr(slot, 2048, KC * 128)
        wkey = f"wr{slot}c"
        OP("gpsimd", lambda e, c=c, wsb=wsb: e.dma_start(
            out=wsb.rearrange("p (kc f) -> p kc f", f=128),
            in_=h_wv[c].rearrange("(kc p) f -> p kc f", p=128)),
            writes=[wkey], kind="d", dgrp=wkey)
        if c == 0:
            OP("vector", lambda e: e.memset(
                t_C[:, 0:NT * 130].rearrange("p (kt h z) -> p kt h z", h=2, z=65)[:, :, :, 64],
                1.0), writes=["C"])
        for g4 in range((NT + 3) // 4):
            tts = list(range(4 * g4, min(4 * g4 + 4, NT)))
            for i, tt in enumerate(tts):
                for kc in range(KC):
                    OP("tensor", lambda e, tt=tt, i=i, kc=kc, wsb=wsb: e.matmul(
                        ps_a[:, i * 128:(i + 1) * 128],
                        t_A[:, kc * S + tt * 128:kc * S + (tt + 1) * 128],
                        wsb[:, kc * 128:(kc + 1) * 128],
                        start=(kc == 0 and i == 0), stop=(kc == KC - 1)),
                        reads=[wkey, f"A:{kc}:{(tt * 128) // TW}"], writes=["ps:A"])
            n = len(tts)
            OP("vector", lambda e, tts=tts, n=n, c=c: e.tensor_copy(
                t_C[:, 0:NT * 130].rearrange("p (kt h z) -> p kt h z", h=2, z=65)[
                    :, tts[0]:tts[0] + n, :, 0:64],
                ps_a.ap().rearrange("p (i h z) -> p i h z", h=2, z=64)[:, 0:n, :, :]),
                reads=["ps:A"], writes=["C"])

    def bound_B(c):
        sqb = t_xne[:, 0:S]
        for wi, (which, buf, bk) in enumerate((("q", t_q, "qrope"), ("k", t_k, "krope"))):
            OP("scalar", lambda e, buf=buf: e.activation(sqb, buf[:], AF.Square),
               reads=[bk], writes=["sqb", "xnereg"])
            for tau in range(TAU):
                OP("tensor", lambda e, tau=tau: e.matmul(
                    ps_b[0:1, 0:TW], t_onesb[:], sqb[:, tau * TW:(tau + 1) * TW],
                    start=True, stop=True),
                    reads=["sqb", "xnereg", "onesb"], writes=["ps:B"])
                OP("vector", lambda e, tau=tau, wi=wi: e.reduce_max(
                    t_sr[0:1, wi * 8 + tau:wi * 8 + tau + 1], ps_b[0:1, 0:TW], axis=AX),
                    reads=["ps:B"], writes=[f"nm{wi}", "tsr"])
        OP("vector", lambda e: e.reduce_max(t_sr[0:1, 16:17], t_sr[0:1, 0:TAU], axis=AX),
           sreads=["nm0"], writes=["tsr"])
        OP("vector", lambda e: e.reduce_max(t_sr[0:1, 17:18], t_sr[0:1, 8:8 + TAU], axis=AX),
           sreads=["nm1"], writes=["tsr"])
        OP("vector", lambda e: e.tensor_mul(t_sr[0:1, 18:19], t_sr[0:1, 16:17], t_sr[0:1, 17:18]),
           sreads=["tsr"], writes=["tsr"])
        OP("scalar", lambda e: e.activation(t_sr[0:1, 19:20], t_sr[0:1, 18:19], AF.Sqrt,
                                            scale=1.05),
           reads=["tsr"], writes=["bb"])
        OP("tensor", lambda e: e.matmul(ps_b[:, 0:1], t_oner[:], t_sr[0:1, 19:20],
                                        start=True, stop=True),
           reads=["bb", "oner"], writes=["ps:B"])
        OP("scalar", lambda e, c=c: e.activation(t_nB[:, c:c + 1], ps_b[:, 0:1],
                                                 AF.Copy, scale=-1.0),
           reads=["ps:B"], writes=["negB"])

    def attention_head(c, half):
        r0 = 64 * half
        h = 2 * c + half
        for kt in range(NT):
            qis = list(range(kt, NT))
            for bi in range(0, len(qis), 4):
                batch = qis[bi:bi + 4]
                bank = (bi // 4) % 2
                stk = f"ps:st{bank}"
                for i, qi in enumerate(batch):
                    OP("tensor", lambda e, kt=kt, qi=qi, i=i, bank=bank, r0=r0: e.matmul(
                        ps_st[:, bank * 512 + i * 128:bank * 512 + (i + 1) * 128],
                        t_k[r0:r0 + 64, kt * 128:(kt + 1) * 128],
                        t_q[r0:r0 + 64, qi * 128:(qi + 1) * 128],
                        start=True, stop=True),
                        reads=["krope", "qrope"], writes=[stk])
                    if qi == kt:
                        OP("vector", lambda e, i=i, bank=bank: e.tensor_add(
                            ps_st[:, bank * 512 + i * 128:bank * 512 + (i + 1) * 128],
                            ps_st[:, bank * 512 + i * 128:bank * 512 + (i + 1) * 128],
                            t_mskT[:]),
                            reads=[stk, "mskT"], writes=[stk])
                nb = len(batch)
                OP("scalar", lambda e, nb=nb, bank=bank, c=c: e.activation(
                    t_pt[:, bank * 512:bank * 512 + nb * 128],
                    ps_st[:, bank * 512:bank * 512 + nb * 128],
                    AF.Exp, bias=t_nB[:, c:c + 1]),
                    reads=[stk], sreads=["negB"], writes=[f"pt{bank}"])
                for i, qi in enumerate(batch):
                    OP("tensor", lambda e, kt=kt, qi=qi, i=i, bank=bank, h=h: e.matmul(
                        ps_ctx[0:65, qi * 128:(qi + 1) * 128],
                        t_C[:, kt * 130 + 65 * (h % 2):kt * 130 + 65 * (h % 2) + 65],
                        t_pt[:, bank * 512 + i * 128:bank * 512 + (i + 1) * 128],
                        start=(kt == 0 and qi % 4 == 0), stop=(kt == qi)),
                        reads=[f"pt{bank}", "C"], writes=[f"ps:ctx{qi // 4}"])
        kc = c
        for qi in range(NT):
            tau = (qi * 128) // TW
            OP("vector", lambda e, qi=qi: e.reciprocal(
                t_rec[0:1, :], ps_ctx[64:65, qi * 128:(qi + 1) * 128]),
                reads=[f"ps:ctx{qi // 4}"], writes=["rec"])
            OP("tensor", lambda e: e.matmul(ps_st[0:64, 0:128], t_onerb[0:1, 0:64],
                                            t_rec[0:1, :], start=True, stop=True),
               reads=["rec", "onerb"], writes=["ps:st0"])
            OP("scalar", lambda e: e.copy(t_gb[:], ps_st[0:64, 0:128]),
               reads=["ps:st0"], writes=["gbs"])
            OP("vector", lambda e, qi=qi, half=half, kc=kc, tau=tau: e.tensor_mul(
                t_B[:, kc * S + qi * 128:kc * S + (qi + 1) * 128][64 * half:64 * half + 64, :],
                ps_ctx[0:64, qi * 128:(qi + 1) * 128], t_gb[:]),
                reads=[f"ps:ctx{qi // 4}", "gbs"], writes=[f"B:{kc}:{tau}"])

    for c in range(KC):
        proj_rope(c, "a", 0, h_wq, t_q, "qrope", 0.125)
        proj_rope(c, "b", 1024, h_wk, t_k, "krope", 1.0)
        build_V(c)
        bound_B(c)
        attention_head(c, 0)
        attention_head(c, 1)

    if upto == 3:
        return dbg_finish(
            [(h_out[kc * 128:(kc + 1) * 128, :], t_B[:, kc * S:(kc + 1) * S],
              [f"B:{kc}:{tt}" for tt in range(TAU)]) for kc in range(KC)])

    y_sl = t_f32[:, 0:KC * TW]
    for half in range(2):
        OP("gpsimd", lambda e, half=half: e.dma_start(
            out=wr(half, 0, 4096).rearrange("p (kc f) -> p kc f", f=512),
            in_=h_wo[half].rearrange("(kc p) f -> p kc f", p=128)),
            writes=[f"wr{half}a", f"wr{half}b", f"wr{half}c"], kind="d",
            dgrp=f"wr{half}a")
    for tau in range(TAU):
        t0 = tau * TW
        for ic in range(KC):
            half, iw = ic // 4, ic % 4
            pse, pk = (ps_a, "ps:A") if ic % 2 == 0 else (ps_b, "ps:B")
            for kc in range(KC):
                OP("tensor", lambda e, half=half, iw=iw, kc=kc, t0=t0, pse=pse: e.matmul(
                    pse[:, 0:TW],
                    wr(half, kc * 512 + iw * 128, 128),
                    t_B[:, kc * S + t0:kc * S + t0 + TW],
                    start=(kc == 0), stop=(kc == KC - 1)),
                    reads=[f"wr{half}a", f"B:{kc}:{tau}"], writes=[pk])
            OP("vector", lambda e, ic=ic, t0=t0, pse=pse, pk=pk: e.tensor_add(
                y_sl[:, ic * TW:(ic + 1) * TW], pse[:, 0:TW],
                t_xn[:, ic * S + t0:ic * S + t0 + TW]),
                reads=[pk, f"xn:{ic}:{tau}"], writes=[f"y:{ic}", "f32a"])
        for ic in range(KC):
            si = t_sil[:, 0:TW]
            OP("scalar", lambda e, ic=ic, si=si: e.activation(
                si, y_sl[:, ic * TW:(ic + 1) * TW], AF.Square),
                reads=[f"y:{ic}"], writes=["ysq", "silreg"])
            OP("tensor", lambda e, ic=ic, si=si: e.matmul(
                ps_st[0:1, 0:TW], t_onesb[:], si,
                start=(ic == 0), stop=(ic == KC - 1)),
                reads=["ysq", "silreg", "onesb"], writes=["ps:st0"])
        rrow = t_rows[0:1, 0:TW]
        OP("scalar", lambda e: e.activation(rrow, ps_st[0:1, 0:TW], AF.Sqrt,
                                            scale=1.0 / D, bias=EPS),
           reads=["ps:st0", "epsc"], writes=["r2row", "rows"])
        OP("vector", lambda e: e.reciprocal(rrow, rrow), reads=["r2row"], writes=["r2row"])
        OP("tensor", lambda e: e.matmul(ps_st[:, 0:TW], t_oner[:], rrow,
                                        start=True, stop=True),
           reads=["r2row", "oner"], writes=["ps:st0"])
        OP("scalar", lambda e: e.copy(rstd2_b, ps_st[:, 0:TW]),
           reads=["ps:st0"], writes=["bc3", "bc0"])
        for ic in range(KC):
            OP("vector", lambda e, ic=ic, t0=t0: e.scalar_tensor_tensor(
                t_A[:, ic * S + t0:ic * S + t0 + TW],
                y_sl[:, ic * TW:(ic + 1) * TW], t_n2[:, ic:ic + 1], rstd2_b,
                ALU.mult, ALU.mult),
                reads=[f"y:{ic}", "bc3"], sreads=["n2"], writes=[f"A:{ic}:{tau}"])
            xp, xk = xping[ic % 2], f"xp{ic % 2}"
            OP("sync", lambda e, ic=ic, t0=t0, xp=xp: e.dma_start(
                out=xp[:], in_=h_xfm[ic * 128:(ic + 1) * 128, t0:t0 + TW]),
                writes=[xk, "f32a"], kind="d", dgrp=xk)
            ev, ek = t_ev[:, (ic % 2) * TW:(ic % 2 + 1) * TW], f"ev{ic % 2}"
            OP("vector", lambda e, ic=ic, xp=xp, ev=ev: e.tensor_add(
                ev, y_sl[:, ic * TW:(ic + 1) * TW], xp[:]),
                reads=[f"y:{ic}", xk], writes=[ek])
            OP("gpsimd", lambda e, ic=ic, t0=t0, ev=ev: e.dma_start(
                out=h_out[ic * 128:(ic + 1) * 128, t0:t0 + TW], in_=ev),
                reads=[ek], writes=[f"out:{ic}:{tau}"], kind="d",
                dgrp=f"oute{ic % 2}")

    if upto == 4:
        outkeys = [f"out:{ic}:{tt}" for ic in range(KC) for tt in range(TAU)]
        sch.barrier_wait("sync", outkeys)
        return finish()

    emit_offsets()

    for g in range(NT):
        OP("tensor", lambda e, g=g: e.matmul(
            ps_a[:, g * E:(g + 1) * E], t_ltri[:], t_mask[:, g * E:(g + 1) * E],
            start=(g == 0), stop=False),
            reads=["ltri", "maskal"], writes=["ps:A"])
    OP("tensor", lambda e: e.matmul(
        ps_a[:, 0:NT * E], t_oner[:], t_offs[0:1, 0:NT * E], start=False, stop=True),
        reads=["oner", "offr"], writes=["ps:A"])
    OP("vector", lambda e: e.tensor_single_scalar(
        t_kept[:], ps_a[:, 0:NT * E], CAPTH, ALU.is_lt),
        reads=["ps:A"], writes=["keptal"])
    for g in range(NT):
        OP("vector", lambda e, g=g: e.scalar_tensor_tensor(
            t_gall[:, g * E:(g + 1) * E], t_mask[:, g * E:(g + 1) * E],
            t_gate[:, g:g + 1], t_kept[:, g * E:(g + 1) * E], ALU.mult, ALU.mult),
            reads=["maskal", "keptal"], sreads=["gateal"], writes=["gall"])
    for g in range(NT):
        OP("tensor", lambda e, g=g: e.transpose(
            ps_b[0:E, 0:128], t_gall[:, g * E:(g + 1) * E], t_idn[:]),
            reads=["gall", "idn"], writes=["ps:B"])
        OP("scalar", lambda e, g=g: e.copy(t_gt[:, g * 128:(g + 1) * 128],
                                           ps_b[0:E, 0:128]),
           reads=["ps:B"], writes=["gt"])
    if upto == 41:
        return dbg_finish([
            (h_out[0:128, 0:NT * E], t_kept[:], ["keptal"]),
            (h_out[128:256, 0:NT * E], t_gall[:], ["gall"]),
            (h_out[256:256 + E, 0:S], t_gt[:], ["gt"]),
        ])
    for ee in range(E):
        for tau in range(TAU):
            OP("tensor", lambda e, ee=ee, tau=tau: e.matmul(
                ps_b[:, 0:TW], t_sel[:, ee * 128:ee * 128 + 128],
                t_gt[:, tau * TW:(tau + 1) * TW], start=True, stop=True),
                reads=["sel", "gt"], writes=["ps:B"])
            OP("scalar", lambda e, ee=ee, tau=tau: e.copy(
                t_B[:, ee * S + tau * TW:ee * S + (tau + 1) * TW], ps_b[:, 0:TW]),
                reads=["ps:B"], writes=[f"B:{ee}:{tau}"])
    if upto == 43:
        return dbg_finish(
            [(h_out[ee * 128:ee * 128 + 128, 0:S], t_B[:, ee * S:(ee + 1) * S],
              [f"B:{ee}:{tt}" for tt in range(TAU)]) for ee in range(E)])

    def moe_psum(ic):
        if ic < 4:
            return ps_ctx[:, ic * 512:(ic + 1) * 512], f"ps:ctx{ic}"
        if ic < 6:
            return ps_st[:, (ic - 4) * 512:(ic - 3) * 512], f"ps:st{ic - 4}"
        return (ps_a, "ps:A") if ic == 6 else (ps_b, "ps:B")

    EMM = {45: 0, 46: 0, 47: 1, 48: 2, 49: 4}.get(upto, E)
    for tau in range(TAU):
        t0 = tau * TW
        for ee in range(E):
            for kc in range(KC):
                OP("vector", lambda e, ee=ee, kc=kc, t0=t0, tau=tau: e.tensor_mul(
                    t_xne[:, kc * TW:(kc + 1) * TW],
                    t_xn[:, kc * S + t0:kc * S + t0 + TW],
                    t_B[:, ee * S + t0:ee * S + t0 + TW]),
                    reads=[f"xn:{kc}:{tau}", f"B:{ee}:{tau}"],
                    writes=[f"xne:{kc}", "xnereg"])
            for hh in range(2):
                slot = (ee * 2 + hh) % 2
                if upto != 45:
                    OP("gpsimd", lambda e, ee=ee, hh=hh, slot=slot: e.dma_start(
                        out=wr(slot, 0, 4096).rearrange("p (kc f) -> p kc f", f=512),
                        in_=h_we[ee, hh].rearrange("(kc p) f -> p kc f", p=128)),
                        writes=[f"wr{slot}a", f"wr{slot}b", f"wr{slot}c"],
                        kind="d", dgrp=f"wr{slot}a")
                if ee >= EMM:
                    continue
                for kc in range(KC):
                    for iw in range(4):
                        ic = hh * 4 + iw
                        pse, pk = moe_psum(ic)
                        OP("tensor", lambda e, slot=slot, kc=kc, iw=iw, pse=pse,
                           ee=ee, hh=hh: e.matmul(
                            pse[:, 0:TW],
                            wr(slot, kc * 512 + iw * 128, 128),
                            t_xne[:, kc * TW:(kc + 1) * TW],
                            start=(ee == 0 and kc == 0), stop=False),
                            reads=[f"wr{slot}a", f"xne:{kc}"], writes=[pk])
        if upto in (45, 46):
            continue
        for ic in range(KC):
            pse, pk = moe_psum(ic)
            OP("tensor", lambda e, ic=ic, tau=tau, pse=pse: e.matmul(
                pse[:, 0:TW], t_be[:, ic * 128:(ic + 1) * 128],
                t_gt[:, tau * TW:(tau + 1) * TW], start=False, stop=True),
                reads=["be", "gt"], writes=[pk])
            ev, ek = t_ev[:, (ic % 2) * TW:(ic % 2 + 1) * TW], f"ev{ic % 2}"
            OP("scalar", lambda e, pse=pse, ev=ev: e.copy(ev, pse[:, 0:TW]),
               reads=[pk], writes=[ek])
            if upto == 44:
                OP("gpsimd", lambda e, ic=ic, t0=t0, ev=ev: e.dma_start(
                    out=h_out[ic * 128:(ic + 1) * 128, t0:t0 + TW], in_=ev),
                    reads=[ek, f"out:{ic}:{tau}"], writes=[f"out:{ic}:{tau}"],
                    kind="d", dgrp=f"oute{ic % 2}")
            else:
                OP("gpsimd", lambda e, ic=ic, t0=t0, ev=ev: e.dma_start(
                    out=h_out[ic * 128:(ic + 1) * 128, t0:t0 + TW], in_=ev,
                    accum_op=ALU.add),
                    reads=[ek, f"out:{ic}:{tau}"], writes=[f"out:{ic}:{tau}"],
                    kind="d", dgrp=f"oute{ic % 2}")

    if upto in (45, 46):
        return dbg_finish([(h_out[0:128, 0:TW], t_xne[:, 0:TW], ["xne:0"])])
    if upto in (5, 44, 47, 48, 49):
        outkeys = [f"out:{ic}:{tt}" for ic in range(KC) for tt in range(TAU)]
        sch.barrier_wait("sync", outkeys)
        sch.barrier_wait("gpsimd", outkeys)
        return finish()

    FG = 4
    ff_acc = t_f32[:, 0:KC * TW]
    for tau in range(TAU):
        t0 = tau * TW
        for fc in range(FC):
            slot = fc % 2
            OP("gpsimd", lambda e, fc=fc, slot=slot: e.dma_start(
                out=wr(slot, 0, KC * 128).rearrange("p (kc f) -> p kc f", f=128),
                in_=h_w1[fc].rearrange("(kc p) f -> p kc f", p=128)),
                writes=[f"wr{slot}a"], kind="d", dgrp=f"wr{slot}a")
            OP("gpsimd", lambda e, fc=fc, slot=slot: e.dma_start(
                out=wr(slot, 1024, KC * 128).rearrange("p (kc f) -> p kc f", f=128),
                in_=h_w3[fc].rearrange("(kc p) f -> p kc f", p=128)),
                writes=[f"wr{slot}b"], kind="d", dgrp=f"wr{slot}b")
            p1, p1k = (ps_a, "ps:A") if slot == 0 else (ps_st[:, 0:512], "ps:st0")
            p3, p3k = (ps_b, "ps:B") if slot == 0 else (ps_st[:, 512:1024], "ps:st1")
            for kc in range(KC):
                OP("tensor", lambda e, kc=kc, t0=t0, slot=slot, p1=p1: e.matmul(
                    p1[:, 0:TW], wr(slot, kc * 128, 128),
                    t_A[:, kc * S + t0:kc * S + t0 + TW],
                    start=(kc == 0), stop=(kc == KC - 1)),
                    reads=[f"wr{slot}a", f"A:{kc}:{tau}"], writes=[p1k])
            for kc in range(KC):
                OP("tensor", lambda e, kc=kc, t0=t0, slot=slot, p3=p3: e.matmul(
                    p3[:, 0:TW], wr(slot, 1024 + kc * 128, 128),
                    t_A[:, kc * S + t0:kc * S + t0 + TW],
                    start=(kc == 0), stop=(kc == KC - 1)),
                    reads=[f"wr{slot}b", f"A:{kc}:{tau}"], writes=[p3k])
            si, sk = t_sil[:, slot * TW:(slot + 1) * TW], f"sil{slot}"
            OP("scalar", lambda e, si=si, p1=p1: e.activation(si, p1[:, 0:TW], AF.Silu),
               reads=[p1k], writes=[sk, "silreg"])
            j = fc % FG
            pgt, pgk = t_pg[:, j * TW:(j + 1) * TW], f"pg{j}"
            OP("vector", lambda e, si=si, p3=p3, pgt=pgt: e.tensor_mul(
                pgt, si, p3[:, 0:TW]),
                reads=[sk, p3k], writes=[pgk])
            OP("gpsimd", lambda e, fc=fc, j=j: e.dma_start(
                out=t_w2g[:, j * D:(j + 1) * D], in_=h_w2[fc]),
                writes=[f"w2g{j}"], kind="d", dgrp=f"w2g{j}")
            if j == FG - 1:
                fg0 = fc - FG + 1
                for oc in range(KC):
                    fps = ps_ctx[:, (oc % 2) * 512:(oc % 2) * 512 + TW]
                    fpk = f"ps:ctx{oc % 2}"
                    for jj in range(FG):
                        OP("tensor", lambda e, jj=jj, oc=oc, fps=fps: e.matmul(
                            fps, t_w2g[:, jj * D + oc * 128:jj * D + (oc + 1) * 128],
                            t_pg[:, jj * TW:(jj + 1) * TW],
                            start=(jj == 0), stop=(jj == FG - 1)),
                            reads=[f"w2g{jj}", f"pg{jj}"], writes=[fpk])
                    if fg0 == 0:
                        OP("vector", lambda e, oc=oc, fps=fps: e.tensor_copy(
                            ff_acc[:, oc * TW:(oc + 1) * TW], fps),
                            reads=[fpk], writes=[f"y:{oc}"])
                    else:
                        OP("vector", lambda e, oc=oc, fps=fps: e.tensor_add(
                            ff_acc[:, oc * TW:(oc + 1) * TW],
                            ff_acc[:, oc * TW:(oc + 1) * TW], fps),
                            reads=[fpk, f"y:{oc}"], writes=[f"y:{oc}"])
        for oc in range(KC):
            acc_op = {61: ALU.bypass}.get(upto, ALU.add)
            OP("gpsimd", lambda e, oc=oc, t0=t0, acc_op=acc_op: e.dma_start(
                out=h_out[oc * 128:(oc + 1) * 128, t0:t0 + TW],
                in_=ff_acc[:, oc * TW:(oc + 1) * TW], accum_op=acc_op),
                reads=[f"y:{oc}", f"out:{oc}:{tau}"],
                writes=[f"out:{oc}:{tau}", f"ffdma{oc % 2}"], kind="d",
                dgrp=f"oute{oc % 2}")

    outkeys = [f"out:{ic}:{tau}" for ic in range(KC) for tau in range(TAU)]
    sch.barrier_wait("gpsimd", outkeys)
    sch.barrier_wait("sync", outkeys)
    return finish()


_CACHE = {}


def _consts(S):
    DKH = 64
    pos = np.arange(S, dtype=np.float64)
    k = np.arange(DKH) // 2
    ang = pos[:, None] / np.power(10000.0, 2.0 * k / DKH)[None, :]
    cos = np.cos(ang)
    sin = np.sin(ang) * np.where(np.arange(DKH) % 2 == 0, -1.0, 1.0)[None, :]
    cosT = np.ascontiguousarray(np.tile(cos.T, (2, 1)).astype(BF))
    sinT = np.ascontiguousarray(np.tile(sin.T, (2, 1)).astype(BF))
    idn = np.eye(128, dtype=np.float32)
    prm = np.zeros((128, 128), dtype=np.float32)
    prm[np.arange(128), np.arange(128) ^ 1] = 1.0
    mskT = np.ascontiguousarray(
        np.where(np.arange(128)[:, None] > np.arange(128)[None, :], NEG, 0.0).astype(BF))
    ltri = np.ascontiguousarray(np.triu(np.ones((128, 128), dtype=np.float32), k=1))
    selm = np.zeros((E, E * 128), dtype=np.float32)
    for ee in range(E):
        selm[ee, ee * 128:(ee + 1) * 128] = 1.0
    return dict(cosT=cosT, sinT=sinT, idn=idn, prm=np.ascontiguousarray(prm.astype(BF)),
                mskT=mskT, ltri=ltri, selm=np.ascontiguousarray(selm.astype(BF)))


def prep_inputs(x, wq, wk, wv, wo, n1, n2, w1, w2, w3, wg, We, be, S):
    cst = _consts(S)
    f32 = np.float32

    def T(w):
        return np.ascontiguousarray(np.asarray(w, dtype=f32).T)

    def blocks(wT, n):
        return np.ascontiguousarray(wT.reshape(D, n, wT.shape[1] // n).transpose(1, 0, 2))

    wq_b = blocks(T(wq), KC)
    wk_b = blocks(T(wk), KC)
    wv_b = blocks(T(wv), KC)
    wo_b = blocks(T(wo), 2)
    w1_b = blocks(T(w1), FC)
    w3_b = blocks(T(w3), FC)
    w2_b = np.ascontiguousarray(T(w2).reshape(FC, 128, D))
    WeT = np.asarray(We, dtype=f32).transpose(0, 2, 1)
    we_b = np.ascontiguousarray(WeT.reshape(E, D, 2, 512).transpose(0, 2, 1, 3))
    n1c = np.ascontiguousarray(np.asarray(n1, dtype=f32).reshape(KC, 128).T)
    n2c = np.ascontiguousarray(np.asarray(n2, dtype=f32).reshape(KC, 128).T)
    base = dict(wq_b=wq_b, wk_b=wk_b, wv_b=wv_b, wo_b=wo_b, w1_b=w1_b,
                w3_b=w3_b, w2_b=w2_b, we_b=we_b,
                wg=np.ascontiguousarray(np.asarray(wg, dtype=f32)),
                be=np.ascontiguousarray(np.asarray(be, dtype=f32)),
                n1c=n1c, n2c=n2c, **cst)
    xa = np.asarray(x, dtype=f32)
    in_maps = []
    for b in range(xa.shape[0]):
        m = dict(base)
        m["x_tm"] = np.ascontiguousarray(xa[b])
        m["x_fm"] = np.ascontiguousarray(xa[b].T)
        m["selc"] = (np.arange(E) < b).astype(f32).reshape(E, 1)
        in_maps.append(m)
    return in_maps


def kernel(L0_attn_x, L0_attn_wq, L0_attn_wk, L0_attn_wv, L0_attn_wo,
           L0_attn_norm1_w, L0_attn_norm2_w, L0_attn_w1, L0_attn_w2,
           L0_attn_w3, L0_ffn_wg, L0_ffn_W, L0_ffn_b):
    import os
    upto = int(os.environ.get("K_UPTO", "9"))
    S = np.asarray(L0_attn_x).shape[1]
    key = (S, upto)
    if key not in _CACHE:
        _CACHE[key] = build_program(S, upto)
    nc = _CACHE[key]
    in_maps = prep_inputs(L0_attn_x, L0_attn_wq, L0_attn_wk, L0_attn_wv,
                          L0_attn_wo, L0_attn_norm1_w, L0_attn_norm2_w,
                          L0_attn_w1, L0_attn_w2, L0_attn_w3, L0_ffn_wg,
                          L0_ffn_W, L0_ffn_b, S)
    res = run_bass_kernel_spmd(nc, in_maps, core_ids=list(range(8)))
    out = np.stack([np.asarray(res.results[b]["out"]).T for b in range(8)])
    return np.ascontiguousarray(out.astype(np.float32))


# version 04
# speedup vs baseline: 1.0838x; 1.0838x over previous
import contextlib
import numpy as np
import ml_dtypes
import concourse.bass as bass
import concourse.mybir as mybir
from concourse.bass_utils import run_bass_kernel_spmd

F32 = mybir.dt.float32
BF16 = mybir.dt.bfloat16
AF = mybir.ActivationFunctionType
ALU = mybir.AluOpType
AX = mybir.AxisListType.X

D = 1024
KC = 8
DFF = 4096
FC = 32
E = 8
EPS = 1e-5
NEG = -1e9

BF = ml_dtypes.bfloat16


class Sched:
    ENGS = ("sync", "scalar", "vector", "tensor", "gpsimd")

    def __init__(self, nc, es):
        self.nc = nc
        self.es = es
        self.seq = {e: 0 for e in self.ENGS}
        self.cseq = 0
        self.sem = {e: es.enter_context(nc.semaphore(f"is_{e}")) for e in self.ENGS}
        self.csem = es.enter_context(nc.semaphore("cs"))
        self.dgrp = {}
        self.last_w = {}
        self.readers = {}
        self.watermark = {e: {} for e in self.ENGS}
        self.eng_ops = {e: [] for e in self.ENGS}

    def _group(self, name, queue):
        g = self.dgrp.get(name)
        if g is None:
            g = [self.es.enter_context(self.nc.semaphore(f"dg_{name}")), 0, queue]
            self.dgrp[name] = g
        assert g[2] == queue, f"dma group {name} used from {queue} and {g[2]}"
        return g

    def _need(self, eng, tok, waits, force=False, dma_queue=None):
        if tok is None:
            return
        kind = tok[0]
        if kind == "i":
            _, peng, val = tok
            if peng == eng and eng == "tensor" and not force:
                return
            sem, target, wkey = self.sem[peng], val, ("i", peng)
        elif kind == "d":
            _, grp, val = tok
            g = self.dgrp[grp]
            if dma_queue is not None and g[2] == dma_queue:
                return
            sem, target, wkey = g[0], 16 * val, ("d", grp)
        else:
            _, val = tok
            sem, target, wkey = self.csem, val, ("c",)
        wm = self.watermark[eng]
        if wm.get(wkey, 0) >= target:
            return
        wm[wkey] = target
        waits.append((sem, target))

    def op(self, eng, fn, reads=(), writes=(), kind="i", sreads=(), dgrp=None):
        waits = []
        dq = eng if kind == "d" else None
        for r in sreads:
            self._need(eng, self.last_w.get(r), waits, force=True, dma_queue=dq)
        for r in reads:
            self._need(eng, self.last_w.get(r), waits, dma_queue=dq)
        for w in writes:
            self._need(eng, self.last_w.get(w), waits, dma_queue=dq)
            for t in self.readers.get(w, ()):
                self._need(eng, t, waits, dma_queue=dq)
        if kind == "d":
            g = self._group(dgrp, eng)
            g[1] += 1
            tok = ("d", dgrp, g[1])
            inc = (g[0], 16)
        elif kind == "c":
            self.cseq += 1
            tok = ("c", self.cseq)
            inc = (self.csem, 1)
        else:
            self.seq[eng] += 1
            tok = ("i", eng, self.seq[eng])
            inc = (self.sem[eng], 1)
        self.eng_ops[eng].append((fn, waits, inc))
        for r in list(reads) + list(sreads):
            self.readers.setdefault(r, []).append(tok)
        for w in writes:
            self.last_w[w] = tok
            self.readers[w] = []
        return tok

    def barrier_wait(self, eng, keys):
        waits = []
        for k in keys:
            self._need(eng, self.last_w.get(k), waits)
            for t in self.readers.get(k, ()):
                self._need(eng, t, waits)
        if waits:
            self.eng_ops[eng].append((None, waits, None))

    def replay(self, block):
        for ename in self.ENGS:
            ops = self.eng_ops[ename]

            def make(op_list):
                def run(eng):
                    for fn, waits, inc in op_list:
                        for (s, tgt) in waits:
                            eng.wait_ge(s, tgt)
                        if fn is not None:
                            fn(eng).then_inc(*inc)
                return run

            getattr(block, ename)(make(ops))


def build_program(S, upto=9):
    assert S % 256 == 0
    TW = min(512, S)
    TAU = S // TW
    NT = S // 128
    GPT = TW // 128
    CAPTH = float(S) - 0.5

    nc = bass.Bass("TRN2", target_bir_lowering=False)

    h_xtm = nc.dram_tensor("x_tm", [S, D], F32, kind="ExternalInput")
    h_xfm = nc.dram_tensor("x_fm", [D, S], F32, kind="ExternalInput")
    h_wq = nc.dram_tensor("wq_b", [KC, D, 128], F32, kind="ExternalInput")
    h_wk = nc.dram_tensor("wk_b", [KC, D, 128], F32, kind="ExternalInput")
    h_wv = nc.dram_tensor("wv_b", [KC, D, 128], F32, kind="ExternalInput")
    h_wo = nc.dram_tensor("wo_b", [2, D, 512], F32, kind="ExternalInput")
    h_w1 = nc.dram_tensor("w1_b", [FC, D, 128], F32, kind="ExternalInput")
    h_w3 = nc.dram_tensor("w3_b", [FC, D, 128], F32, kind="ExternalInput")
    h_w2 = nc.dram_tensor("w2_b", [FC, 128, D], F32, kind="ExternalInput")
    h_we = nc.dram_tensor("we_b", [E, 2, D, 512], F32, kind="ExternalInput")
    h_wg = nc.dram_tensor("wg", [D, E], F32, kind="ExternalInput")
    h_be = nc.dram_tensor("be", [E, D], F32, kind="ExternalInput")
    h_n1 = nc.dram_tensor("n1c", [128, KC], F32, kind="ExternalInput")
    h_n2 = nc.dram_tensor("n2c", [128, KC], F32, kind="ExternalInput")
    h_cos = nc.dram_tensor("cosT", [128, S], BF16, kind="ExternalInput")
    h_sin = nc.dram_tensor("sinT", [128, S], BF16, kind="ExternalInput")
    h_idn = nc.dram_tensor("idn", [128, 128], F32, kind="ExternalInput")
    h_prm = nc.dram_tensor("prm", [128, 128], BF16, kind="ExternalInput")
    h_mskT = nc.dram_tensor("mskT", [128, 128], BF16, kind="ExternalInput")
    h_ltri = nc.dram_tensor("ltri", [128, 128], F32, kind="ExternalInput")
    h_sel = nc.dram_tensor("selm", [E, E * 128], BF16, kind="ExternalInput")
    h_selc = nc.dram_tensor("selc", [E, 1], F32, kind="ExternalInput")
    h_out = nc.dram_tensor("out", [D, S], F32, kind="ExternalOutput")
    d_ccin = nc.dram_tensor("cc_in", [1, 512], F32, kind="Internal")
    d_ccout = nc.dram_tensor("cc_out", [8, 512], F32, kind="Internal")

    es = contextlib.ExitStack()

    def sb(name, p, f, dt):
        return es.enter_context(nc.sbuf_tensor(name, [p, f], dt))

    t_xn = sb("xn", 128, KC * S, BF16)
    t_A = sb("uA", 128, KC * S, BF16)
    t_B = sb("uB", 128, KC * S, BF16)
    t_C = sb("uC", 128, NT * 130, BF16)
    t_q = sb("qbuf", 128, S, BF16)
    t_k = sb("kbuf", 128, S, BF16)
    t_xne = sb("xne", 128, max(KC * TW, S), BF16)
    t_wr = sb("wring", 128, 2 * 4096, BF16)
    t_cos = sb("cosb", 128, S, BF16)
    t_sin = sb("sinb", 128, S, BF16)
    t_f32 = sb("f32a", 128, max(KC * TW + 2 * TW, 3 * 1024), F32)
    t_f32b = sb("f32b", 128, 3 * TW, F32)
    t_pg = sb("pgr", 128, 4 * TW, BF16)
    t_w2g = sb("w2g", 128, 4 * D, BF16)
    t_rope = sb("ropes", 128, 2 * TW, BF16)
    t_pt = sb("ptb", 128, 1024, BF16)
    t_sil = sb("silb", 128, 2 * TW, BF16)
    t_ev = sb("evac", 128, 2 * TW, F32)
    t_gb = sb("gbrow", 64, 512, BF16)
    t_idn = sb("idnb", 128, 128, F32)
    t_prm = sb("prmb", 128, 128, BF16)
    t_mskT = sb("mskTb", 128, 128, BF16)
    t_ltri = sb("ltrib", 128, 128, F32)
    t_sel = sb("selb", E, E * 128, BF16)
    t_n1 = sb("n1b", 128, KC, F32)
    t_n2 = sb("n2b", 128, KC, F32)
    t_wg = sb("wgb", 128, KC * E, F32)
    t_be = sb("beb", E, D, BF16)
    t_stats = sb("statc", 128, 5 * NT, F32)
    t_statT = sb("statT", 3 * NT, 128, F32)
    t_rows = sb("rows", 1, max(TW, 512), F32)
    t_mask = sb("maskal", 128, NT * E, F32)
    t_gate = sb("gateal", 128, NT, F32)
    t_kept = sb("keptal", 128, NT * E, F32)
    t_gall = sb("gall", 128, NT * E, F32)
    t_gt = sb("gtb", E, S, BF16)
    t_offs = sb("offs", 1, NT * E + E, F32)
    t_allc = sb("allc", E, E, F32)
    t_selc = sb("selcb", E, 1, F32)
    t_ones = sb("onesf", 128, 1, F32)
    t_onesb = sb("onesbf", 128, 1, BF16)
    t_oner = sb("onerf", 1, 128, F32)
    t_onerb = sb("onerbf", 1, 128, BF16)
    t_nB = sb("negB", 128, KC, F32)
    t_sc = sb("smallc", 128, max(2 * NT, 20), F32)
    t_eps = sb("epsc", 128, 1, F32)
    nc.const_aps.register(F32, EPS, t_eps[:])
    t_sr = sb("smallr", 1, 32, F32)
    t_rec = sb("recrow", 1, 512, BF16)

    ps_st = es.enter_context(nc.psum_tensor("ps_st", [128, 1024], F32))
    ps_ctx = es.enter_context(nc.psum_tensor("ps_ctx", [128, 2048], F32))
    ps_a = es.enter_context(nc.psum_tensor("ps_a", [128, 512], F32))
    ps_b = es.enter_context(nc.psum_tensor("ps_b", [128, 512], F32))

    sch = Sched(nc, es)
    OP = sch.op

    def finish():
        with nc.allow_low_precision(reason="bf16 by design"):
            with nc.Block("main") as block:
                sch.replay(block)
        es.close()
        return nc

    def dbg_finish(stores):
        keys = []
        for idx, (o, i_, rd) in enumerate(stores):
            k = f"dbgout{idx}"
            OP("gpsimd", lambda e, o=o, i_=i_: e.dma_start(out=o, in_=i_),
               reads=rd, writes=[k], kind="d", dgrp="dbg")
            keys.append(k)
        sch.barrier_wait("gpsimd", keys)
        return finish()

    for hh, tt, key in ((h_idn, t_idn, "idn"), (h_prm, t_prm, "prm"),
                        (h_mskT, t_mskT, "mskT"), (h_ltri, t_ltri, "ltri"),
                        (h_sel, t_sel, "sel"), (h_n1, t_n1, "n1"),
                        (h_n2, t_n2, "n2"), (h_cos, t_cos, "cos"),
                        (h_sin, t_sin, "sin"), (h_selc, t_selc, "selc")):
        OP("sync", lambda e, hh=hh, tt=tt: e.dma_start(out=tt[:], in_=hh[:]),
           writes=[key], kind="d", dgrp=key)
    OP("sync", lambda e: e.dma_start(
        out=t_wg.ap().rearrange("p (kc e) -> p kc e", e=E),
        in_=h_wg.rearrange("(kc p) e -> p kc e", p=128)), writes=["wg"], kind="d", dgrp="wg")
    OP("gpsimd", lambda e: e.dma_start(out=t_be[:], in_=h_be[:]), writes=["be"], kind="d", dgrp="be")
    OP("vector", lambda e: e.memset(t_ones[:], 1.0), writes=["ones"])
    OP("vector", lambda e: e.memset(t_onesb[:], 1.0), writes=["onesb"])
    OP("vector", lambda e: e.memset(t_oner[:], 1.0), writes=["oner"])
    OP("vector", lambda e: e.memset(t_onerb[:], 1.0), writes=["onerb"])
    OP("vector", lambda e: e.memset(t_rows[0:1, 0:512], 0.0), writes=["cpad", "rows"])
    OP("vector", lambda e: e.memset(t_eps[:], EPS), writes=["epsc"])

    xtm = (t_f32[:, 0:1024], t_f32[:, 1024:2048])
    sqdump = t_f32[:, 2048:3072]
    C_S1, C_S2, C_MEAN, C_RSTD, C_RSTD1 = 0, NT, 2 * NT, 3 * NT, 4 * NT
    for i in range(NT):
        xb, key = xtm[i % 2], f"xtm{i % 2}"
        OP("sync", lambda e, xb=xb, i=i: e.dma_start(
            out=xb[:], in_=h_xtm[i * 128:(i + 1) * 128, :]), writes=[key, "f32a"],
            kind="d", dgrp=key)
        OP("vector", lambda e, xb=xb, i=i: e.reduce_sum(
            t_stats[:, C_S1 + i:C_S1 + i + 1], xb[:], axis=AX),
            reads=[key], writes=["stats"])
        OP("scalar", lambda e, xb=xb, i=i: e.activation(
            sqdump[:], xb[:], AF.Square, accum_out=t_stats[:, C_S2 + i:C_S2 + i + 1]),
            reads=[key], writes=["stats", "f32a"])
    S1 = t_stats[:, C_S1:C_S1 + NT]
    S2 = t_stats[:, C_S2:C_S2 + NT]
    MEAN = t_stats[:, C_MEAN:C_MEAN + NT]
    RSTD = t_stats[:, C_RSTD:C_RSTD + NT]
    RSTD1 = t_stats[:, C_RSTD1:C_RSTD1 + NT]
    TV = t_sc[:, 0:NT]
    SD = t_sc[:, NT:2 * NT]
    OP("vector", lambda e: e.tensor_scalar_mul(MEAN, S1, 1.0 / D), reads=["stats"], writes=["stats"])
    OP("scalar", lambda e: e.activation(TV, MEAN, AF.Square), reads=["stats"], writes=["tsc"])
    OP("vector", lambda e: e.scalar_tensor_tensor(TV, S2, 1.0 / D, TV, ALU.mult, ALU.subtract),
       reads=["stats", "tsc"], writes=["tsc"])
    OP("scalar", lambda e: e.activation(SD, TV, AF.Sqrt, bias=EPS),
       reads=["tsc", "epsc"], writes=["tsc"])
    OP("vector", lambda e: e.reciprocal(RSTD, SD), reads=["tsc"], writes=["stats"])
    OP("scalar", lambda e: e.activation(RSTD1, RSTD, AF.Square), reads=["stats"], writes=["stats"])
    OP("vector", lambda e: e.tensor_mul(RSTD1, RSTD1, TV), reads=["stats", "tsc"], writes=["stats"])
    OP("scalar", lambda e: e.activation(RSTD1, RSTD1, AF.Sqrt, bias=EPS),
       reads=["stats", "epsc"], writes=["stats"])
    OP("vector", lambda e: e.reciprocal(RSTD1, RSTD1), reads=["stats"], writes=["stats"])
    OP("tensor", lambda e: e.transpose(ps_a[0:3 * NT, 0:128],
                                       t_stats[:, C_MEAN:C_MEAN + 3 * NT], t_idn[:]),
       reads=["stats", "idn"], writes=["ps:A"])
    OP("scalar", lambda e: e.copy(t_statT[:], ps_a[0:3 * NT, 0:128]),
       reads=["ps:A"], writes=["statT"])

    xn32 = t_f32[:, 0:KC * TW]
    xping = (t_f32[:, KC * TW:KC * TW + TW], t_f32[:, KC * TW + TW:KC * TW + 2 * TW])
    mean_b = t_f32b[:, 0:TW]
    rstd_b = t_f32b[:, TW:2 * TW]
    rstd1_b = t_f32b[:, 2 * TW:3 * TW]
    rstd2_b = t_f32b[:, 0:TW]

    for tau in range(TAU):
        t0 = tau * TW
        for si, dst, bk in ((0, mean_b, "bc0"), (1, rstd_b, "bc1"), (2, rstd1_b, "bc2")):
            g0 = tau * GPT
            OP("sync", lambda e, si=si, g0=g0: e.dma_start(
                out=t_rows[0:1, 0:TW].rearrange("o (a b) -> o a b", b=128),
                in_=t_statT[si * NT + g0:si * NT + g0 + GPT, :]),
                reads=["statT"], writes=["rows"], kind="d", dgrp="rows")
            OP("tensor", lambda e, si=si: e.matmul(
                ps_b[:, 0:TW], t_oner[:], t_rows[0:1, 0:TW],
                start=True, stop=True),
                reads=["rows", "oner"], writes=["ps:B"])
            OP("scalar", lambda e, dst=dst: e.copy(dst, ps_b[:, 0:TW]),
               reads=["ps:B"], writes=[bk])
        for kc in range(KC):
            xp, xk = xping[kc % 2], f"xp{kc % 2}"
            OP("sync", lambda e, xp=xp, kc=kc, t0=t0: e.dma_start(
                out=xp[:], in_=h_xfm[kc * 128:(kc + 1) * 128, t0:t0 + TW]),
                writes=[xk, "f32a"], kind="d", dgrp=xk)
            xnk = xn32[:, kc * TW:(kc + 1) * TW]
            OP("vector", lambda e, xp=xp, xnk=xnk: e.tensor_sub(xnk, xp[:], mean_b),
               reads=[xk, "bc0"], writes=[f"xn32:{kc}", "f32a"])
            OP("vector", lambda e, xnk=xnk: e.tensor_mul(xnk, xnk, rstd_b),
               reads=[f"xn32:{kc}", "bc1"], writes=[f"xn32:{kc}"])
            OP("scalar", lambda e, xnk=xnk, kc=kc, t0=t0: e.copy(
                t_xn[:, kc * S + t0:kc * S + t0 + TW], xnk),
                reads=[f"xn32:{kc}"], writes=[f"xn:{kc}:{tau}"])
            OP("vector", lambda e, xnk=xnk, kc=kc, t0=t0: e.scalar_tensor_tensor(
                t_A[:, kc * S + t0:kc * S + t0 + TW], xnk, t_n1[:, kc:kc + 1],
                rstd1_b, ALU.mult, ALU.mult),
                reads=[f"xn32:{kc}", "bc2"], sreads=["n1"], writes=[f"A:{kc}:{tau}"])
        for g in range(GPT):
            gg = tau * GPT + g
            for kc in range(KC):
                OP("tensor", lambda e, kc=kc, g=g: e.matmul(
                    ps_a[:, 0:E],
                    xn32[:, kc * TW + g * 128:kc * TW + g * 128 + 128],
                    t_wg[:, kc * E:(kc + 1) * E],
                    start=(kc == 0), stop=(kc == KC - 1)),
                    reads=[f"xn32:{kc}", "wg"], writes=["ps:A"])
            m8, nm8, sume = t_sc[:, 0:1], t_sc[:, 1:2], t_sc[:, 2:3]
            lgs = t_sc[:, 4:12]
            dmp = t_sc[:, 12:20]
            OP("vector", lambda e, lgs=lgs: e.tensor_copy(lgs, ps_a[:, 0:E]),
               reads=["ps:A"], writes=["lgs", "tsc"])
            OP("vector", lambda e, lgs=lgs: e.reduce_max(m8, lgs, axis=AX),
               reads=["lgs"], writes=["m8", "tsc"])
            OP("vector", lambda e, gg=gg, lgs=lgs: e.tensor_single_scalar(
                t_mask[:, gg * E:(gg + 1) * E], lgs, m8, ALU.is_ge),
                reads=["lgs"], sreads=["m8"], writes=["maskal"])
            OP("vector", lambda e: e.tensor_scalar_mul(nm8, m8, -1.0),
               reads=["m8"], writes=["nm8", "tsc"])
            OP("scalar", lambda e, dmp=dmp, lgs=lgs: e.activation(
                dmp, lgs, AF.Exp, bias=nm8, accum_out=sume),
                reads=["lgs"], sreads=["nm8"], writes=["sume", "tsc"])
            OP("vector", lambda e, gg=gg: e.reciprocal(t_gate[:, gg:gg + 1], sume),
               reads=["sume"], writes=["gateal"])
            if upto == 2:
                OP("vector", lambda e, gg=gg, lgs=lgs: e.tensor_copy(
                    t_kept[:, gg * E:(gg + 1) * E], lgs),
                    reads=["lgs"], writes=["keptal"])
                OP("vector", lambda e, gg=gg: e.tensor_copy(
                    t_gall[:, gg * E:gg * E + 1], m8),
                    sreads=["m8"], writes=["gall"])

    if upto == 1:
        return dbg_finish(
            [(h_out[kc * 128:(kc + 1) * 128, :], t_xn[:, kc * S:(kc + 1) * S],
              [f"xn:{kc}:{tt}" for tt in range(TAU)]) for kc in range(KC)])

    OP("vector", lambda e: e.memset(t_offs[0:1, 0:E], 0.0), writes=["offr"])
    for g in range(NT):
        if g > 0:
            OP("vector", lambda e, g=g: e.tensor_copy(
                t_offs[0:1, g * E:(g + 1) * E], ps_b[0:1, 0:E]),
                reads=["ps:B"], writes=["offr"])
        OP("tensor", lambda e, g=g: e.matmul(
            ps_b[0:1, 0:E], t_ones[:], t_mask[:, g * E:(g + 1) * E],
            start=(g == 0), stop=(g == NT - 1)),
            reads=["maskal", "ones", "offr"], writes=["ps:B"])
    OP("vector", lambda e: e.tensor_copy(t_rows[0:1, 0:E], ps_b[0:1, 0:E]),
       reads=["ps:B", "cpad"], writes=["cpad", "rows"])
    OP("sync", lambda e: e.dma_start(out=d_ccin[:], in_=t_rows[0:1, 0:512]),
       reads=["cpad", "rows"], writes=["ccin"], kind="d", dgrp="ccin")
    OP("gpsimd", lambda e: e.collective_compute(
        "AllGather", ALU.bypass, replica_groups=[list(range(8))],
        ins=[d_ccin[:]], outs=[d_ccout[:]]),
        reads=["ccin"], writes=["ccout"], kind="c")

    def emit_offsets():
        OP("gpsimd", lambda e: e.dma_start(out=t_allc[:], in_=d_ccout[:, 0:E]),
           reads=["ccout"], writes=["allc"], kind="d", dgrp="allc")
        OP("tensor", lambda e: e.matmul(ps_b[0:1, 0:E], t_selc[:], t_allc[:],
                                        start=True, stop=True),
           reads=["allc", "selc"], writes=["ps:B"])
        OP("scalar", lambda e: e.copy(t_offs[0:1, NT * E:NT * E + E],
                                      ps_b[0:1, 0:E]),
           reads=["ps:B"], writes=["coff"])
        for g in range(NT):
            OP("vector", lambda e, g=g: e.tensor_add(
                t_offs[0:1, g * E:(g + 1) * E], t_offs[0:1, g * E:(g + 1) * E],
                t_offs[0:1, NT * E:NT * E + E]),
                reads=["offr", "coff"], writes=["offr"])

    if upto == 2:
        emit_offsets()
        return dbg_finish([
            (h_out[0:128, 0:NT * E], t_mask[:], ["maskal"]),
            (h_out[128:256, 0:NT], t_gate[:], ["gateal"]),
            (h_out[256:257, 0:NT * E], t_offs[0:1, 0:NT * E], ["offr"]),
            (h_out[384:512, 0:NT * E], t_kept[:], ["keptal"]),
            (h_out[512:640, 0:NT * E], t_gall[:], ["gall"]),
        ])

    def wr(slot, off, n):
        return t_wr[:, slot * 4096 + off:slot * 4096 + off + n]

    WRALL = lambda s: [f"wr{s}a", f"wr{s}b", f"wr{s}c", f"wr{s}d"]

    pctr = {"ab": 0, "st": 0}

    def nextab():
        pctr["ab"] += 1
        return (ps_a, "ps:A") if pctr["ab"] % 2 == 0 else (ps_b, "ps:B")

    def nextst():
        pctr["st"] += 1
        b = pctr["st"] % 2
        return b, ps_st[:, b * 512:(b + 1) * 512], f"ps:st{b}"

    def proj_rope(c, reg, woff, hw, dst, dkey, scale):
        slot = c % 2
        wsb = wr(slot, woff, KC * 128)
        wkey = f"wr{slot}{reg}"
        OP("gpsimd", lambda e, wsb=wsb: e.dma_start(
            out=wsb.rearrange("p (kc f) -> p kc f", f=128),
            in_=hw[c].rearrange("(kc p) f -> p kc f", p=128)),
            writes=[wkey], kind="d", dgrp=wkey)
        for tau in range(TAU):
            t0 = tau * TW
            pse, pk = nextab()
            for kc in range(KC):
                OP("tensor", lambda e, kc=kc, t0=t0, wsb=wsb, pse=pse: e.matmul(
                    pse[:, 0:TW], wsb[:, kc * 128:(kc + 1) * 128],
                    t_A[:, kc * S + t0:kc * S + t0 + TW],
                    start=(kc == 0), stop=(kc == KC - 1)),
                    reads=[wkey, f"A:{kc}:{tau}"], writes=[pk])
            qr, qk = t_pg[:, (tau % 2) * TW:(tau % 2 + 1) * TW], f"qraw{tau % 2}"
            OP("scalar", lambda e, qr=qr, pse=pse: e.copy(qr, pse[:, 0:TW]),
               reads=[pk], writes=[qk])
            sb_, sps, spk = nextst()
            OP("tensor", lambda e, qr=qr, sps=sps: e.matmul(
                sps[:, 0:TW], t_prm[:], qr, start=True, stop=True),
                reads=[qk, "prm"], writes=[spk])
            tt1 = t_rope[:, (tau % 2) * TW:(tau % 2 + 1) * TW]
            tk1 = f"tt1{tau % 2}"
            OP("vector", lambda e, qr=qr, t0=t0, tt1=tt1: e.scalar_tensor_tensor(
                tt1, qr, scale, t_cos[:, t0:t0 + TW], ALU.mult, ALU.mult),
                reads=[qk, "cos"], writes=[tk1])
            OP("vector", lambda e, t0=t0, dst=dst, sps=sps, tt1=tt1: e.scalar_tensor_tensor(
                dst[:, t0:t0 + TW], sps[:, 0:TW], scale, t_sin[:, t0:t0 + TW],
                ALU.mult, ALU.mult),
                reads=[spk, "sin"], writes=["tt2s"])
            OP("vector", lambda e, t0=t0, dst=dst, tt1=tt1: e.tensor_add(
                dst[:, t0:t0 + TW], dst[:, t0:t0 + TW], tt1),
                reads=["tt2s", tk1], writes=[dkey])

    def build_V(c):
        slot = c % 2
        wsb = wr(slot, 2048, KC * 128)
        wkey = f"wr{slot}c"
        OP("gpsimd", lambda e, c=c, wsb=wsb: e.dma_start(
            out=wsb.rearrange("p (kc f) -> p kc f", f=128),
            in_=h_wv[c].rearrange("(kc p) f -> p kc f", p=128)),
            writes=[wkey], kind="d", dgrp=wkey)
        if c == 0:
            OP("vector", lambda e: e.memset(
                t_C[:, 0:NT * 130].rearrange("p (kt h z) -> p kt h z", h=2, z=65)[:, :, :, 64],
                1.0), writes=["C"])
        for g4 in range((NT + 3) // 4):
            tts = list(range(4 * g4, min(4 * g4 + 4, NT)))
            pse, pk = nextab()
            for i, tt in enumerate(tts):
                for kc in range(KC):
                    OP("tensor", lambda e, tt=tt, i=i, kc=kc, wsb=wsb, pse=pse: e.matmul(
                        pse[:, i * 128:(i + 1) * 128],
                        t_A[:, kc * S + tt * 128:kc * S + (tt + 1) * 128],
                        wsb[:, kc * 128:(kc + 1) * 128],
                        start=(kc == 0 and i == 0), stop=(kc == KC - 1)),
                        reads=[wkey, f"A:{kc}:{(tt * 128) // TW}"], writes=[pk])
            n = len(tts)
            OP("vector", lambda e, tts=tts, n=n, c=c, pse=pse: e.tensor_copy(
                t_C[:, 0:NT * 130].rearrange("p (kt h z) -> p kt h z", h=2, z=65)[
                    :, tts[0]:tts[0] + n, :, 0:64],
                pse.ap().rearrange("p (i h z) -> p i h z", h=2, z=64)[:, 0:n, :, :]),
                reads=[pk], writes=["C"])

    def bound_B(c):
        sqb = t_xne[:, 0:S]
        for wi, (which, buf, bk) in enumerate((("q", t_q, "qrope"), ("k", t_k, "krope"))):
            OP("scalar", lambda e, buf=buf: e.activation(sqb, buf[:], AF.Square),
               reads=[bk], writes=["sqb", "xnereg"])
            for tau in range(TAU):
                OP("tensor", lambda e, tau=tau: e.matmul(
                    ps_b[0:1, 0:TW], t_onesb[:], sqb[:, tau * TW:(tau + 1) * TW],
                    start=True, stop=True),
                    reads=["sqb", "xnereg", "onesb"], writes=["ps:B"])
                OP("vector", lambda e, tau=tau, wi=wi: e.reduce_max(
                    t_sr[0:1, wi * 8 + tau:wi * 8 + tau + 1], ps_b[0:1, 0:TW], axis=AX),
                    reads=["ps:B"], writes=[f"nm{wi}", "tsr"])
        OP("vector", lambda e: e.reduce_max(t_sr[0:1, 16:17], t_sr[0:1, 0:TAU], axis=AX),
           sreads=["nm0"], writes=["tsr"])
        OP("vector", lambda e: e.reduce_max(t_sr[0:1, 17:18], t_sr[0:1, 8:8 + TAU], axis=AX),
           sreads=["nm1"], writes=["tsr"])
        OP("vector", lambda e: e.tensor_mul(t_sr[0:1, 18:19], t_sr[0:1, 16:17], t_sr[0:1, 17:18]),
           sreads=["tsr"], writes=["tsr"])
        OP("scalar", lambda e: e.activation(t_sr[0:1, 19:20], t_sr[0:1, 18:19], AF.Sqrt,
                                            scale=1.05),
           reads=["tsr"], writes=["bb"])
        OP("tensor", lambda e: e.matmul(ps_b[:, 0:1], t_oner[:], t_sr[0:1, 19:20],
                                        start=True, stop=True),
           reads=["bb", "oner"], writes=["ps:B"])
        OP("scalar", lambda e, c=c: e.activation(t_nB[:, c:c + 1], ps_b[:, 0:1],
                                                 AF.Copy, scale=-1.0),
           reads=["ps:B"], writes=["negB"])

    def attention_head(c, half):
        r0 = 64 * half
        h = 2 * c + half
        for kt in range(NT):
            qis = list(range(kt, NT))
            for bi in range(0, len(qis), 4):
                batch = qis[bi:bi + 4]
                bank, sps, stk = nextst()
                for i, qi in enumerate(batch):
                    OP("tensor", lambda e, kt=kt, qi=qi, i=i, sps=sps, r0=r0: e.matmul(
                        sps[:, i * 128:(i + 1) * 128],
                        t_k[r0:r0 + 64, kt * 128:(kt + 1) * 128],
                        t_q[r0:r0 + 64, qi * 128:(qi + 1) * 128],
                        start=True, stop=True),
                        reads=["krope", "qrope"], writes=[stk])
                    if qi == kt:
                        OP("vector", lambda e, i=i, sps=sps: e.tensor_add(
                            sps[:, i * 128:(i + 1) * 128],
                            sps[:, i * 128:(i + 1) * 128],
                            t_mskT[:]),
                            reads=[stk, "mskT"], writes=[stk])
                nb = len(batch)
                OP("scalar", lambda e, nb=nb, bank=bank, sps=sps, c=c: e.activation(
                    t_pt[:, bank * 512:bank * 512 + nb * 128],
                    sps[:, 0:nb * 128],
                    AF.Exp, bias=t_nB[:, c:c + 1]),
                    reads=[stk], sreads=["negB"], writes=[f"pt{bank}"])
                for i, qi in enumerate(batch):
                    OP("tensor", lambda e, kt=kt, qi=qi, i=i, bank=bank, h=h: e.matmul(
                        ps_ctx[0:65, qi * 128:(qi + 1) * 128],
                        t_C[:, kt * 130 + 65 * (h % 2):kt * 130 + 65 * (h % 2) + 65],
                        t_pt[:, bank * 512 + i * 128:bank * 512 + (i + 1) * 128],
                        start=(kt == 0 and qi % 4 == 0), stop=(kt == qi)),
                        reads=[f"pt{bank}", "C"], writes=[f"ps:ctx{qi // 4}"])

    def attn_evac(c, half):
        kc = c
        for q0 in range(0, NT, 4):
            n = min(4, NT - q0)
            for i in range(n):
                OP("vector", lambda e, q0=q0, i=i: e.reciprocal(
                    t_rec[0:1, i * 128:(i + 1) * 128],
                    ps_ctx[64:65, (q0 + i) * 128:(q0 + i + 1) * 128]),
                    reads=[f"ps:ctx{(q0 + i) // 4}"], writes=["rec"])
            bank, sps, stk = nextst()
            OP("tensor", lambda e, n=n, sps=sps: e.matmul(
                sps[0:64, 0:n * 128], t_onerb[0:1, 0:64],
                t_rec[0:1, 0:n * 128], start=True, stop=True),
                reads=["rec", "onerb"], writes=[stk])
            OP("scalar", lambda e, n=n, sps=sps: e.copy(t_gb[:, 0:n * 128],
                                                        sps[0:64, 0:n * 128]),
               reads=[stk], writes=["gbs"])
            for i in range(n):
                qi = q0 + i
                tau = (qi * 128) // TW
                OP("vector", lambda e, qi=qi, i=i, half=half, kc=kc, tau=tau: e.tensor_mul(
                    t_B[:, kc * S + qi * 128:kc * S + (qi + 1) * 128][64 * half:64 * half + 64, :],
                    ps_ctx[0:64, qi * 128:(qi + 1) * 128],
                    t_gb[:, i * 128:(i + 1) * 128]),
                    reads=[f"ps:ctx{qi // 4}", "gbs"], writes=[f"B:{kc}:{tau}"])

    for c in range(KC):
        proj_rope(c, "a", 0, h_wq, t_q, "qrope", 0.125)
        proj_rope(c, "b", 1024, h_wk, t_k, "krope", 1.0)
        build_V(c)
        bound_B(c)
        if c > 0:
            attn_evac(c - 1, 1)
        attention_head(c, 0)
        attn_evac(c, 0)
        attention_head(c, 1)
    attn_evac(KC - 1, 1)

    if upto == 3:
        return dbg_finish(
            [(h_out[kc * 128:(kc + 1) * 128, :], t_B[:, kc * S:(kc + 1) * S],
              [f"B:{kc}:{tt}" for tt in range(TAU)]) for kc in range(KC)])

    y_sl = t_f32[:, 0:KC * TW]
    for half in range(2):
        OP("gpsimd", lambda e, half=half: e.dma_start(
            out=wr(half, 0, 4096).rearrange("p (kc f) -> p kc f", f=512),
            in_=h_wo[half].rearrange("(kc p) f -> p kc f", p=128)),
            writes=[f"wr{half}a", f"wr{half}b", f"wr{half}c"], kind="d",
            dgrp=f"wr{half}a")
    for tau in range(TAU):
        t0 = tau * TW
        for ic in range(KC):
            half, iw = ic // 4, ic % 4
            pse, pk = (ps_a, "ps:A") if ic % 2 == 0 else (ps_b, "ps:B")
            for kc in range(KC):
                OP("tensor", lambda e, half=half, iw=iw, kc=kc, t0=t0, pse=pse: e.matmul(
                    pse[:, 0:TW],
                    wr(half, kc * 512 + iw * 128, 128),
                    t_B[:, kc * S + t0:kc * S + t0 + TW],
                    start=(kc == 0), stop=(kc == KC - 1)),
                    reads=[f"wr{half}a", f"B:{kc}:{tau}"], writes=[pk])
            OP("vector", lambda e, ic=ic, t0=t0, pse=pse, pk=pk: e.tensor_add(
                y_sl[:, ic * TW:(ic + 1) * TW], pse[:, 0:TW],
                t_xn[:, ic * S + t0:ic * S + t0 + TW]),
                reads=[pk, f"xn:{ic}:{tau}"], writes=[f"y:{ic}", "f32a"])
        for ic in range(KC):
            si = t_sil[:, 0:TW]
            OP("scalar", lambda e, ic=ic, si=si: e.activation(
                si, y_sl[:, ic * TW:(ic + 1) * TW], AF.Square),
                reads=[f"y:{ic}"], writes=["ysq", "silreg"])
            OP("tensor", lambda e, ic=ic, si=si: e.matmul(
                ps_st[0:1, 0:TW], t_onesb[:], si,
                start=(ic == 0), stop=(ic == KC - 1)),
                reads=["ysq", "silreg", "onesb"], writes=["ps:st0"])
        rrow = t_rows[0:1, 0:TW]
        OP("scalar", lambda e: e.activation(rrow, ps_st[0:1, 0:TW], AF.Sqrt,
                                            scale=1.0 / D, bias=EPS),
           reads=["ps:st0", "epsc"], writes=["r2row", "rows"])
        OP("vector", lambda e: e.reciprocal(rrow, rrow), reads=["r2row"], writes=["r2row"])
        OP("tensor", lambda e: e.matmul(ps_st[:, 0:TW], t_oner[:], rrow,
                                        start=True, stop=True),
           reads=["r2row", "oner"], writes=["ps:st0"])
        OP("scalar", lambda e: e.copy(rstd2_b, ps_st[:, 0:TW]),
           reads=["ps:st0"], writes=["bc3", "bc0"])
        for ic in range(KC):
            OP("vector", lambda e, ic=ic, t0=t0: e.scalar_tensor_tensor(
                t_A[:, ic * S + t0:ic * S + t0 + TW],
                y_sl[:, ic * TW:(ic + 1) * TW], t_n2[:, ic:ic + 1], rstd2_b,
                ALU.mult, ALU.mult),
                reads=[f"y:{ic}", "bc3"], sreads=["n2"], writes=[f"A:{ic}:{tau}"])
            xp, xk = xping[ic % 2], f"xp{ic % 2}"
            OP("sync", lambda e, ic=ic, t0=t0, xp=xp: e.dma_start(
                out=xp[:], in_=h_xfm[ic * 128:(ic + 1) * 128, t0:t0 + TW]),
                writes=[xk, "f32a"], kind="d", dgrp=xk)
            ev, ek = t_ev[:, (ic % 2) * TW:(ic % 2 + 1) * TW], f"ev{ic % 2}"
            OP("vector", lambda e, ic=ic, xp=xp, ev=ev: e.tensor_add(
                ev, y_sl[:, ic * TW:(ic + 1) * TW], xp[:]),
                reads=[f"y:{ic}", xk], writes=[ek])
            OP("gpsimd", lambda e, ic=ic, t0=t0, ev=ev: e.dma_start(
                out=h_out[ic * 128:(ic + 1) * 128, t0:t0 + TW], in_=ev),
                reads=[ek], writes=[f"out:{ic}:{tau}"], kind="d",
                dgrp=f"oute{ic % 2}")

    if upto == 4:
        outkeys = [f"out:{ic}:{tt}" for ic in range(KC) for tt in range(TAU)]
        sch.barrier_wait("sync", outkeys)
        return finish()

    emit_offsets()

    for g in range(NT):
        OP("tensor", lambda e, g=g: e.matmul(
            ps_a[:, g * E:(g + 1) * E], t_ltri[:], t_mask[:, g * E:(g + 1) * E],
            start=(g == 0), stop=False),
            reads=["ltri", "maskal"], writes=["ps:A"])
    OP("tensor", lambda e: e.matmul(
        ps_a[:, 0:NT * E], t_oner[:], t_offs[0:1, 0:NT * E], start=False, stop=True),
        reads=["oner", "offr"], writes=["ps:A"])
    OP("vector", lambda e: e.tensor_single_scalar(
        t_kept[:], ps_a[:, 0:NT * E], CAPTH, ALU.is_lt),
        reads=["ps:A"], writes=["keptal"])
    for g in range(NT):
        OP("vector", lambda e, g=g: e.scalar_tensor_tensor(
            t_gall[:, g * E:(g + 1) * E], t_mask[:, g * E:(g + 1) * E],
            t_gate[:, g:g + 1], t_kept[:, g * E:(g + 1) * E], ALU.mult, ALU.mult),
            reads=["maskal", "keptal"], sreads=["gateal"], writes=["gall"])
    for g in range(NT):
        OP("tensor", lambda e, g=g: e.transpose(
            ps_b[0:E, 0:128], t_gall[:, g * E:(g + 1) * E], t_idn[:]),
            reads=["gall", "idn"], writes=["ps:B"])
        OP("scalar", lambda e, g=g: e.copy(t_gt[:, g * 128:(g + 1) * 128],
                                           ps_b[0:E, 0:128]),
           reads=["ps:B"], writes=["gt"])
    if upto == 41:
        return dbg_finish([
            (h_out[0:128, 0:NT * E], t_kept[:], ["keptal"]),
            (h_out[128:256, 0:NT * E], t_gall[:], ["gall"]),
            (h_out[256:256 + E, 0:S], t_gt[:], ["gt"]),
        ])
    for ee in range(E):
        for tau in range(TAU):
            OP("tensor", lambda e, ee=ee, tau=tau: e.matmul(
                ps_b[:, 0:TW], t_sel[:, ee * 128:ee * 128 + 128],
                t_gt[:, tau * TW:(tau + 1) * TW], start=True, stop=True),
                reads=["sel", "gt"], writes=["ps:B"])
            OP("scalar", lambda e, ee=ee, tau=tau: e.copy(
                t_B[:, ee * S + tau * TW:ee * S + (tau + 1) * TW], ps_b[:, 0:TW]),
                reads=["ps:B"], writes=[f"B:{ee}:{tau}"])
    if upto == 43:
        return dbg_finish(
            [(h_out[ee * 128:ee * 128 + 128, 0:S], t_B[:, ee * S:(ee + 1) * S],
              [f"B:{ee}:{tt}" for tt in range(TAU)]) for ee in range(E)])

    def moe_psum(ic):
        if ic < 4:
            return ps_ctx[:, ic * 512:(ic + 1) * 512], f"ps:ctx{ic}"
        if ic < 6:
            return ps_st[:, (ic - 4) * 512:(ic - 3) * 512], f"ps:st{ic - 4}"
        return (ps_a, "ps:A") if ic == 6 else (ps_b, "ps:B")

    EMM = {45: 0, 46: 0, 47: 1, 48: 2, 49: 4}.get(upto, E)
    for tau in range(TAU):
        t0 = tau * TW
        for ee in range(E):
            for kc in range(KC):
                OP("vector", lambda e, ee=ee, kc=kc, t0=t0, tau=tau: e.tensor_mul(
                    t_xne[:, kc * TW:(kc + 1) * TW],
                    t_xn[:, kc * S + t0:kc * S + t0 + TW],
                    t_B[:, ee * S + t0:ee * S + t0 + TW]),
                    reads=[f"xn:{kc}:{tau}", f"B:{ee}:{tau}"],
                    writes=[f"xne:{kc}", "xnereg"])
            for hh in range(2):
                slot3 = (tau * E * 2 + ee * 2 + hh) % 3
                if slot3 < 2:
                    wbuf = wr(slot3, 0, 4096)
                    wks = [f"wr{slot3}a", f"wr{slot3}b", f"wr{slot3}c"]
                else:
                    wbuf = t_w2g[:, 0:4096]
                    wks = ["w2g0", "w2g1", "w2g2", "w2g3"]
                if upto != 45:
                    OP("gpsimd", lambda e, ee=ee, hh=hh, wbuf=wbuf: e.dma_start(
                        out=wbuf.rearrange("p (kc f) -> p kc f", f=512),
                        in_=h_we[ee, hh].rearrange("(kc p) f -> p kc f", p=128)),
                        writes=wks, kind="d", dgrp=wks[0])
                if ee >= EMM:
                    continue
                for kc in range(KC):
                    for iw in range(4):
                        ic = hh * 4 + iw
                        pse, pk = moe_psum(ic)
                        OP("tensor", lambda e, kc=kc, iw=iw, pse=pse, wbuf=wbuf,
                           ee=ee, hh=hh: e.matmul(
                            pse[:, 0:TW],
                            wbuf[:, kc * 512 + iw * 128:kc * 512 + (iw + 1) * 128],
                            t_xne[:, kc * TW:(kc + 1) * TW],
                            start=(ee == 0 and kc == 0), stop=False),
                            reads=[wks[0], f"xne:{kc}"], writes=[pk])
        if upto in (45, 46):
            continue
        for ic in range(KC):
            pse, pk = moe_psum(ic)
            OP("tensor", lambda e, ic=ic, tau=tau, pse=pse: e.matmul(
                pse[:, 0:TW], t_be[:, ic * 128:(ic + 1) * 128],
                t_gt[:, tau * TW:(tau + 1) * TW], start=False, stop=True),
                reads=["be", "gt"], writes=[pk])
            ev, ek = t_ev[:, (ic % 2) * TW:(ic % 2 + 1) * TW], f"ev{ic % 2}"
            OP("scalar", lambda e, pse=pse, ev=ev: e.copy(ev, pse[:, 0:TW]),
               reads=[pk], writes=[ek])
            if upto == 44:
                OP("gpsimd", lambda e, ic=ic, t0=t0, ev=ev: e.dma_start(
                    out=h_out[ic * 128:(ic + 1) * 128, t0:t0 + TW], in_=ev),
                    reads=[ek, f"out:{ic}:{tau}"], writes=[f"out:{ic}:{tau}"],
                    kind="d", dgrp=f"oute{ic % 2}")
            else:
                OP("gpsimd", lambda e, ic=ic, t0=t0, ev=ev: e.dma_start(
                    out=h_out[ic * 128:(ic + 1) * 128, t0:t0 + TW], in_=ev,
                    accum_op=ALU.add),
                    reads=[ek, f"out:{ic}:{tau}"], writes=[f"out:{ic}:{tau}"],
                    kind="d", dgrp=f"oute{ic % 2}")

    if upto in (45, 46):
        return dbg_finish([(h_out[0:128, 0:TW], t_xne[:, 0:TW], ["xne:0"])])
    if upto in (5, 44, 47, 48, 49):
        outkeys = [f"out:{ic}:{tt}" for ic in range(KC) for tt in range(TAU)]
        sch.barrier_wait("sync", outkeys)
        sch.barrier_wait("gpsimd", outkeys)
        return finish()

    FG = 4
    ff_acc = t_f32[:, 0:KC * TW]
    for tau in range(TAU):
        t0 = tau * TW
        for fc in range(FC):
            slot = fc % 2
            OP("gpsimd", lambda e, fc=fc, slot=slot: e.dma_start(
                out=wr(slot, 0, KC * 128).rearrange("p (kc f) -> p kc f", f=128),
                in_=h_w1[fc].rearrange("(kc p) f -> p kc f", p=128)),
                writes=[f"wr{slot}a"], kind="d", dgrp=f"wr{slot}a")
            OP("gpsimd", lambda e, fc=fc, slot=slot: e.dma_start(
                out=wr(slot, 1024, KC * 128).rearrange("p (kc f) -> p kc f", f=128),
                in_=h_w3[fc].rearrange("(kc p) f -> p kc f", p=128)),
                writes=[f"wr{slot}b"], kind="d", dgrp=f"wr{slot}b")
            p1, p1k = (ps_a, "ps:A") if slot == 0 else (ps_st[:, 0:512], "ps:st0")
            p3, p3k = (ps_b, "ps:B") if slot == 0 else (ps_st[:, 512:1024], "ps:st1")
            for kc in range(KC):
                OP("tensor", lambda e, kc=kc, t0=t0, slot=slot, p1=p1: e.matmul(
                    p1[:, 0:TW], wr(slot, kc * 128, 128),
                    t_A[:, kc * S + t0:kc * S + t0 + TW],
                    start=(kc == 0), stop=(kc == KC - 1)),
                    reads=[f"wr{slot}a", f"A:{kc}:{tau}"], writes=[p1k])
            for kc in range(KC):
                OP("tensor", lambda e, kc=kc, t0=t0, slot=slot, p3=p3: e.matmul(
                    p3[:, 0:TW], wr(slot, 1024 + kc * 128, 128),
                    t_A[:, kc * S + t0:kc * S + t0 + TW],
                    start=(kc == 0), stop=(kc == KC - 1)),
                    reads=[f"wr{slot}b", f"A:{kc}:{tau}"], writes=[p3k])
            si, sk = t_sil[:, slot * TW:(slot + 1) * TW], f"sil{slot}"
            OP("scalar", lambda e, si=si, p1=p1: e.activation(si, p1[:, 0:TW], AF.Silu),
               reads=[p1k], writes=[sk, "silreg"])
            j = fc % FG
            pgt, pgk = t_pg[:, j * TW:(j + 1) * TW], f"pg{j}"
            OP("vector", lambda e, si=si, p3=p3, pgt=pgt: e.tensor_mul(
                pgt, si, p3[:, 0:TW]),
                reads=[sk, p3k], writes=[pgk])
            OP("gpsimd", lambda e, fc=fc, j=j: e.dma_start(
                out=t_w2g[:, j * D:(j + 1) * D], in_=h_w2[fc]),
                writes=[f"w2g{j}"], kind="d", dgrp=f"w2g{j}")
            if j == FG - 1:
                fg0 = fc - FG + 1
                for oc in range(KC):
                    fps = ps_ctx[:, (oc % 2) * 512:(oc % 2) * 512 + TW]
                    fpk = f"ps:ctx{oc % 2}"
                    for jj in range(FG):
                        OP("tensor", lambda e, jj=jj, oc=oc, fps=fps: e.matmul(
                            fps, t_w2g[:, jj * D + oc * 128:jj * D + (oc + 1) * 128],
                            t_pg[:, jj * TW:(jj + 1) * TW],
                            start=(jj == 0), stop=(jj == FG - 1)),
                            reads=[f"w2g{jj}", f"pg{jj}"], writes=[fpk])
                    if fg0 == 0:
                        OP("vector", lambda e, oc=oc, fps=fps: e.tensor_copy(
                            ff_acc[:, oc * TW:(oc + 1) * TW], fps),
                            reads=[fpk], writes=[f"y:{oc}"])
                    else:
                        OP("vector", lambda e, oc=oc, fps=fps: e.tensor_add(
                            ff_acc[:, oc * TW:(oc + 1) * TW],
                            ff_acc[:, oc * TW:(oc + 1) * TW], fps),
                            reads=[fpk, f"y:{oc}"], writes=[f"y:{oc}"])
        for oc in range(KC):
            acc_op = {61: ALU.bypass}.get(upto, ALU.add)
            OP("gpsimd", lambda e, oc=oc, t0=t0, acc_op=acc_op: e.dma_start(
                out=h_out[oc * 128:(oc + 1) * 128, t0:t0 + TW],
                in_=ff_acc[:, oc * TW:(oc + 1) * TW], accum_op=acc_op),
                reads=[f"y:{oc}", f"out:{oc}:{tau}"],
                writes=[f"out:{oc}:{tau}", f"ffdma{oc % 2}"], kind="d",
                dgrp=f"oute{oc % 2}")

    outkeys = [f"out:{ic}:{tau}" for ic in range(KC) for tau in range(TAU)]
    sch.barrier_wait("gpsimd", outkeys)
    sch.barrier_wait("sync", outkeys)
    return finish()


_CACHE = {}


def _consts(S):
    DKH = 64
    pos = np.arange(S, dtype=np.float64)
    k = np.arange(DKH) // 2
    ang = pos[:, None] / np.power(10000.0, 2.0 * k / DKH)[None, :]
    cos = np.cos(ang)
    sin = np.sin(ang) * np.where(np.arange(DKH) % 2 == 0, -1.0, 1.0)[None, :]
    cosT = np.ascontiguousarray(np.tile(cos.T, (2, 1)).astype(BF))
    sinT = np.ascontiguousarray(np.tile(sin.T, (2, 1)).astype(BF))
    idn = np.eye(128, dtype=np.float32)
    prm = np.zeros((128, 128), dtype=np.float32)
    prm[np.arange(128), np.arange(128) ^ 1] = 1.0
    mskT = np.ascontiguousarray(
        np.where(np.arange(128)[:, None] > np.arange(128)[None, :], NEG, 0.0).astype(BF))
    ltri = np.ascontiguousarray(np.triu(np.ones((128, 128), dtype=np.float32), k=1))
    selm = np.zeros((E, E * 128), dtype=np.float32)
    for ee in range(E):
        selm[ee, ee * 128:(ee + 1) * 128] = 1.0
    return dict(cosT=cosT, sinT=sinT, idn=idn, prm=np.ascontiguousarray(prm.astype(BF)),
                mskT=mskT, ltri=ltri, selm=np.ascontiguousarray(selm.astype(BF)))


def prep_inputs(x, wq, wk, wv, wo, n1, n2, w1, w2, w3, wg, We, be, S):
    cst = _consts(S)
    f32 = np.float32

    def T(w):
        return np.ascontiguousarray(np.asarray(w, dtype=f32).T)

    def blocks(wT, n):
        return np.ascontiguousarray(wT.reshape(D, n, wT.shape[1] // n).transpose(1, 0, 2))

    wq_b = blocks(T(wq), KC)
    wk_b = blocks(T(wk), KC)
    wv_b = blocks(T(wv), KC)
    wo_b = blocks(T(wo), 2)
    w1_b = blocks(T(w1), FC)
    w3_b = blocks(T(w3), FC)
    w2_b = np.ascontiguousarray(T(w2).reshape(FC, 128, D))
    WeT = np.asarray(We, dtype=f32).transpose(0, 2, 1)
    we_b = np.ascontiguousarray(WeT.reshape(E, D, 2, 512).transpose(0, 2, 1, 3))
    n1c = np.ascontiguousarray(np.asarray(n1, dtype=f32).reshape(KC, 128).T)
    n2c = np.ascontiguousarray(np.asarray(n2, dtype=f32).reshape(KC, 128).T)
    base = dict(wq_b=wq_b, wk_b=wk_b, wv_b=wv_b, wo_b=wo_b, w1_b=w1_b,
                w3_b=w3_b, w2_b=w2_b, we_b=we_b,
                wg=np.ascontiguousarray(np.asarray(wg, dtype=f32)),
                be=np.ascontiguousarray(np.asarray(be, dtype=f32)),
                n1c=n1c, n2c=n2c, **cst)
    xa = np.asarray(x, dtype=f32)
    in_maps = []
    for b in range(xa.shape[0]):
        m = dict(base)
        m["x_tm"] = np.ascontiguousarray(xa[b])
        m["x_fm"] = np.ascontiguousarray(xa[b].T)
        m["selc"] = (np.arange(E) < b).astype(f32).reshape(E, 1)
        in_maps.append(m)
    return in_maps


def kernel(L0_attn_x, L0_attn_wq, L0_attn_wk, L0_attn_wv, L0_attn_wo,
           L0_attn_norm1_w, L0_attn_norm2_w, L0_attn_w1, L0_attn_w2,
           L0_attn_w3, L0_ffn_wg, L0_ffn_W, L0_ffn_b):
    import os
    upto = int(os.environ.get("K_UPTO", "9"))
    S = np.asarray(L0_attn_x).shape[1]
    key = (S, upto)
    if key not in _CACHE:
        _CACHE[key] = build_program(S, upto)
    nc = _CACHE[key]
    in_maps = prep_inputs(L0_attn_x, L0_attn_wq, L0_attn_wk, L0_attn_wv,
                          L0_attn_wo, L0_attn_norm1_w, L0_attn_norm2_w,
                          L0_attn_w1, L0_attn_w2, L0_attn_w3, L0_ffn_wg,
                          L0_ffn_W, L0_ffn_b, S)
    res = run_bass_kernel_spmd(nc, in_maps, core_ids=list(range(8)))
    out = np.stack([np.asarray(res.results[b]["out"]).T for b in range(8)])
    return np.ascontiguousarray(out.astype(np.float32))


# version 05
# speedup vs baseline: 1.2197x; 1.1254x over previous
import contextlib
import numpy as np
import ml_dtypes
import concourse.bass as bass
import concourse.mybir as mybir
from concourse.bass_utils import run_bass_kernel_spmd

F32 = mybir.dt.float32
BF16 = mybir.dt.bfloat16
AF = mybir.ActivationFunctionType
ALU = mybir.AluOpType
AX = mybir.AxisListType.X

D = 1024
KC = 8
DFF = 4096
FC = 32
E = 8
EPS = 1e-5
NEG = -1e9

BF = ml_dtypes.bfloat16


class Sched:
    ENGS = ("sync", "scalar", "vector", "tensor", "gpsimd")

    def __init__(self, nc, es):
        self.nc = nc
        self.es = es
        self.seq = {e: 0 for e in self.ENGS}
        self.cseq = 0
        self.sem = {e: es.enter_context(nc.semaphore(f"is_{e}")) for e in self.ENGS}
        self.csem = es.enter_context(nc.semaphore("cs"))
        self.dgrp = {}
        self.last_w = {}
        self.readers = {}
        self.watermark = {e: {} for e in self.ENGS}
        self.eng_ops = {e: [] for e in self.ENGS}

    def _group(self, name, queue):
        g = self.dgrp.get(name)
        if g is None:
            g = [self.es.enter_context(self.nc.semaphore(f"dg_{name}")), 0, queue]
            self.dgrp[name] = g
        assert g[2] == queue, f"dma group {name} used from {queue} and {g[2]}"
        return g

    def _need(self, eng, tok, waits, force=False, dma_queue=None):
        if tok is None:
            return
        kind = tok[0]
        if kind == "i":
            _, peng, val = tok
            if peng == eng and eng == "tensor" and not force:
                return
            sem, target, wkey = self.sem[peng], val, ("i", peng)
        elif kind == "d":
            _, grp, val = tok
            g = self.dgrp[grp]
            if dma_queue is not None and g[2] == dma_queue:
                return
            sem, target, wkey = g[0], 16 * val, ("d", grp)
        else:
            _, val = tok
            sem, target, wkey = self.csem, val, ("c",)
        wm = self.watermark[eng]
        if wm.get(wkey, 0) >= target:
            return
        wm[wkey] = target
        waits.append((sem, target))

    def op(self, eng, fn, reads=(), writes=(), kind="i", sreads=(), dgrp=None):
        waits = []
        dq = eng if kind == "d" else None
        for r in sreads:
            self._need(eng, self.last_w.get(r), waits, force=True, dma_queue=dq)
        for r in reads:
            self._need(eng, self.last_w.get(r), waits, dma_queue=dq)
        for w in writes:
            self._need(eng, self.last_w.get(w), waits, dma_queue=dq)
            for t in self.readers.get(w, ()):
                self._need(eng, t, waits, dma_queue=dq)
        if kind == "d":
            g = self._group(dgrp, eng)
            g[1] += 1
            tok = ("d", dgrp, g[1])
            inc = (g[0], 16)
        elif kind == "c":
            self.cseq += 1
            tok = ("c", self.cseq)
            inc = (self.csem, 1)
        else:
            self.seq[eng] += 1
            tok = ("i", eng, self.seq[eng])
            inc = (self.sem[eng], 1)
        self.eng_ops[eng].append((fn, waits, inc))
        for r in list(reads) + list(sreads):
            self.readers.setdefault(r, []).append(tok)
        for w in writes:
            self.last_w[w] = tok
            self.readers[w] = []
        return tok

    def barrier_wait(self, eng, keys):
        waits = []
        for k in keys:
            self._need(eng, self.last_w.get(k), waits)
            for t in self.readers.get(k, ()):
                self._need(eng, t, waits)
        if waits:
            self.eng_ops[eng].append((None, waits, None))

    def replay(self, block):
        for ename in self.ENGS:
            ops = self.eng_ops[ename]

            def make(op_list):
                def run(eng):
                    for fn, waits, inc in op_list:
                        for (s, tgt) in waits:
                            eng.wait_ge(s, tgt)
                        if fn is not None:
                            fn(eng).then_inc(*inc)
                return run

            getattr(block, ename)(make(ops))


def build_program(S, upto=9):
    assert S % 256 == 0
    TW = min(512, S)
    TAU = S // TW
    NT = S // 128
    GPT = TW // 128
    CAPTH = float(S) - 0.5

    nc = bass.Bass("TRN2", target_bir_lowering=False)

    h_xtm = nc.dram_tensor("x_tm", [S, D], F32, kind="ExternalInput")
    h_xfm = nc.dram_tensor("x_fm", [D, S], F32, kind="ExternalInput")
    h_wq = nc.dram_tensor("wq_b", [KC, D, 128], F32, kind="ExternalInput")
    h_wk = nc.dram_tensor("wk_b", [KC, D, 128], F32, kind="ExternalInput")
    h_wv = nc.dram_tensor("wv_b", [KC, D, 128], F32, kind="ExternalInput")
    h_wo = nc.dram_tensor("wo_b", [2, D, 512], F32, kind="ExternalInput")
    h_w1 = nc.dram_tensor("w1_b", [FC, D, 128], F32, kind="ExternalInput")
    h_w3 = nc.dram_tensor("w3_b", [FC, D, 128], F32, kind="ExternalInput")
    h_w2 = nc.dram_tensor("w2_b", [FC, 128, D], F32, kind="ExternalInput")
    h_we = nc.dram_tensor("we_b", [E, 2, D, 512], F32, kind="ExternalInput")
    h_wg = nc.dram_tensor("wg", [D, E], F32, kind="ExternalInput")
    h_be = nc.dram_tensor("be", [E, D], F32, kind="ExternalInput")
    h_n1 = nc.dram_tensor("n1c", [128, KC], F32, kind="ExternalInput")
    h_n2 = nc.dram_tensor("n2c", [128, KC], F32, kind="ExternalInput")
    h_cos = nc.dram_tensor("cosT", [128, S], BF16, kind="ExternalInput")
    h_sin = nc.dram_tensor("sinT", [128, S], BF16, kind="ExternalInput")
    h_idn = nc.dram_tensor("idn", [128, 128], F32, kind="ExternalInput")
    h_prm = nc.dram_tensor("prm", [128, 128], BF16, kind="ExternalInput")
    h_mskT = nc.dram_tensor("mskT", [128, 128], BF16, kind="ExternalInput")
    h_ltri = nc.dram_tensor("ltri", [128, 128], F32, kind="ExternalInput")
    h_sel = nc.dram_tensor("selm", [E, E * 128], BF16, kind="ExternalInput")
    h_selc = nc.dram_tensor("selc", [E, 1], F32, kind="ExternalInput")
    h_out = nc.dram_tensor("out", [D, S], F32, kind="ExternalOutput")
    d_ccin = nc.dram_tensor("cc_in", [1, 512], F32, kind="Internal")
    d_ccout = nc.dram_tensor("cc_out", [8, 512], F32, kind="Internal")

    es = contextlib.ExitStack()

    def sb(name, p, f, dt):
        return es.enter_context(nc.sbuf_tensor(name, [p, f], dt))

    t_xn = sb("xn", 128, KC * S, BF16)
    t_A = sb("uA", 128, KC * S, BF16)
    t_B = sb("uB", 128, KC * S, BF16)
    t_C = sb("uC", 128, NT * 130, BF16)
    t_q = sb("qbuf", 128, S, BF16)
    t_k = sb("kbuf", 128, S, BF16)
    t_xne = sb("xne", 128, max(KC * TW, S), BF16)
    t_wr = sb("wring", 128, 2 * 4096, BF16)
    t_cos = sb("cosb", 128, S, BF16)
    t_sin = sb("sinb", 128, S, BF16)
    t_f32 = sb("f32a", 128, max(KC * TW + 2 * TW, 3 * 1024), F32)
    t_f32b = sb("f32b", 128, 3 * TW, F32)
    t_pg = sb("pgr", 128, 4 * TW, BF16)
    t_w2g = sb("w2g", 128, 4 * D, BF16)
    t_rope = sb("ropes", 128, 2 * TW, BF16)
    t_pt = sb("ptb", 128, 1024, BF16)
    t_sil = sb("silb", 128, 2 * TW, BF16)
    t_ev = sb("evac", 128, 2 * TW, F32)
    t_gb = sb("gbrow", 64, 512, BF16)
    t_idn = sb("idnb", 128, 128, F32)
    t_prm = sb("prmb", 128, 128, BF16)
    t_mskT = sb("mskTb", 128, 128, BF16)
    t_ltri = sb("ltrib", 128, 128, F32)
    t_sel = sb("selb", E, E * 128, BF16)
    t_n1 = sb("n1b", 128, KC, F32)
    t_n2 = sb("n2b", 128, KC, F32)
    t_wg = sb("wgb", 128, KC * E, F32)
    t_be = sb("beb", E, D, BF16)
    t_stats = sb("statc", 128, 5 * NT, F32)
    t_statT = sb("statT", 3 * NT, 128, F32)
    t_rows = sb("rows", 1, max(TW, 512), F32)
    t_mask = sb("maskal", 128, NT * E, F32)
    t_gate = sb("gateal", 128, NT, F32)
    t_kept = sb("keptal", 128, NT * E, F32)
    t_gall = sb("gall", 128, NT * E, F32)
    t_gt = sb("gtb", E, S, BF16)
    t_offs = sb("offs", 1, NT * E + E, F32)
    t_allc = sb("allc", E, E, F32)
    t_selc = sb("selcb", E, 1, F32)
    t_ones = sb("onesf", 128, 1, F32)
    t_onesb = sb("onesbf", 128, 1, BF16)
    t_oner = sb("onerf", 1, 128, F32)
    t_onerb = sb("onerbf", 1, 128, BF16)
    t_nB = sb("negB", 128, KC, F32)
    t_sc = sb("smallc", 128, max(2 * NT, 20), F32)
    t_eps = sb("epsc", 128, 1, F32)
    nc.const_aps.register(F32, EPS, t_eps[:])
    t_sr = sb("smallr", 1, 32, F32)
    t_rec = sb("recrow", 1, 512, BF16)

    ps_st = es.enter_context(nc.psum_tensor("ps_st", [128, 1024], F32))
    ps_ctx = es.enter_context(nc.psum_tensor("ps_ctx", [128, 2048], F32))
    ps_a = es.enter_context(nc.psum_tensor("ps_a", [128, 512], F32))
    ps_b = es.enter_context(nc.psum_tensor("ps_b", [128, 512], F32))

    sch = Sched(nc, es)
    OP = sch.op

    def finish():
        with nc.allow_low_precision(reason="bf16 by design"):
            with nc.Block("main") as block:
                sch.replay(block)
        es.close()
        return nc

    def dbg_finish(stores):
        keys = []
        for idx, (o, i_, rd) in enumerate(stores):
            k = f"dbgout{idx}"
            OP("gpsimd", lambda e, o=o, i_=i_: e.dma_start(out=o, in_=i_),
               reads=rd, writes=[k], kind="d", dgrp="dbg")
            keys.append(k)
        sch.barrier_wait("gpsimd", keys)
        return finish()

    for hh, tt, key in ((h_idn, t_idn, "idn"), (h_prm, t_prm, "prm"),
                        (h_mskT, t_mskT, "mskT"), (h_ltri, t_ltri, "ltri"),
                        (h_sel, t_sel, "sel"), (h_n1, t_n1, "n1"),
                        (h_n2, t_n2, "n2"), (h_cos, t_cos, "cos"),
                        (h_sin, t_sin, "sin"), (h_selc, t_selc, "selc")):
        OP("sync", lambda e, hh=hh, tt=tt: e.dma_start(out=tt[:], in_=hh[:]),
           writes=[key], kind="d", dgrp=key)
    OP("sync", lambda e: e.dma_start(
        out=t_wg.ap().rearrange("p (kc e) -> p kc e", e=E),
        in_=h_wg.rearrange("(kc p) e -> p kc e", p=128)), writes=["wg"], kind="d", dgrp="wg")
    OP("gpsimd", lambda e: e.dma_start(out=t_be[:], in_=h_be[:]), writes=["be"], kind="d", dgrp="be")
    OP("vector", lambda e: e.memset(t_ones[:], 1.0), writes=["ones"])
    OP("vector", lambda e: e.memset(t_onesb[:], 1.0), writes=["onesb"])
    OP("vector", lambda e: e.memset(t_oner[:], 1.0), writes=["oner"])
    OP("vector", lambda e: e.memset(t_onerb[:], 1.0), writes=["onerb"])
    OP("vector", lambda e: e.memset(t_rows[0:1, 0:512], 0.0), writes=["cpad", "rows"])
    OP("vector", lambda e: e.memset(t_eps[:], EPS), writes=["epsc"])

    xtm = (t_f32[:, 0:1024], t_f32[:, 1024:2048])
    sqdump = t_f32[:, 2048:3072]
    C_S1, C_S2, C_MEAN, C_RSTD, C_RSTD1 = 0, NT, 2 * NT, 3 * NT, 4 * NT
    for i in range(NT):
        xb, key = xtm[i % 2], f"xtm{i % 2}"
        OP("sync", lambda e, xb=xb, i=i: e.dma_start(
            out=xb[:], in_=h_xtm[i * 128:(i + 1) * 128, :]), writes=[key, "f32a"],
            kind="d", dgrp=key)
        OP("vector", lambda e, xb=xb, i=i: e.reduce_sum(
            t_stats[:, C_S1 + i:C_S1 + i + 1], xb[:], axis=AX),
            reads=[key], writes=["stats"])
        OP("scalar", lambda e, xb=xb, i=i: e.activation(
            sqdump[:], xb[:], AF.Square, accum_out=t_stats[:, C_S2 + i:C_S2 + i + 1]),
            reads=[key], writes=["stats", "f32a"])
    S1 = t_stats[:, C_S1:C_S1 + NT]
    S2 = t_stats[:, C_S2:C_S2 + NT]
    MEAN = t_stats[:, C_MEAN:C_MEAN + NT]
    RSTD = t_stats[:, C_RSTD:C_RSTD + NT]
    RSTD1 = t_stats[:, C_RSTD1:C_RSTD1 + NT]
    TV = t_sc[:, 0:NT]
    SD = t_sc[:, NT:2 * NT]
    OP("vector", lambda e: e.tensor_scalar_mul(MEAN, S1, 1.0 / D), reads=["stats"], writes=["stats"])
    OP("scalar", lambda e: e.activation(TV, MEAN, AF.Square), reads=["stats"], writes=["tsc"])
    OP("vector", lambda e: e.scalar_tensor_tensor(TV, S2, 1.0 / D, TV, ALU.mult, ALU.subtract),
       reads=["stats", "tsc"], writes=["tsc"])
    OP("scalar", lambda e: e.activation(SD, TV, AF.Sqrt, bias=EPS),
       reads=["tsc", "epsc"], writes=["tsc"])
    OP("vector", lambda e: e.reciprocal(RSTD, SD), reads=["tsc"], writes=["stats"])
    OP("scalar", lambda e: e.activation(RSTD1, RSTD, AF.Square), reads=["stats"], writes=["stats"])
    OP("vector", lambda e: e.tensor_mul(RSTD1, RSTD1, TV), reads=["stats", "tsc"], writes=["stats"])
    OP("scalar", lambda e: e.activation(RSTD1, RSTD1, AF.Sqrt, bias=EPS),
       reads=["stats", "epsc"], writes=["stats"])
    OP("vector", lambda e: e.reciprocal(RSTD1, RSTD1), reads=["stats"], writes=["stats"])
    OP("tensor", lambda e: e.transpose(ps_a[0:3 * NT, 0:128],
                                       t_stats[:, C_MEAN:C_MEAN + 3 * NT], t_idn[:]),
       reads=["stats", "idn"], writes=["ps:A"])
    OP("scalar", lambda e: e.copy(t_statT[:], ps_a[0:3 * NT, 0:128]),
       reads=["ps:A"], writes=["statT"])

    xn32 = t_f32[:, 0:KC * TW]
    xping = (t_f32[:, KC * TW:KC * TW + TW], t_f32[:, KC * TW + TW:KC * TW + 2 * TW])
    mean_b = t_f32b[:, 0:TW]
    rstd_b = t_f32b[:, TW:2 * TW]
    rstd1_b = t_f32b[:, 2 * TW:3 * TW]
    rstd2_b = t_f32b[:, 0:TW]

    for tau in range(TAU):
        t0 = tau * TW
        for si, dst, bk in ((0, mean_b, "bc0"), (1, rstd_b, "bc1"), (2, rstd1_b, "bc2")):
            g0 = tau * GPT
            OP("sync", lambda e, si=si, g0=g0: e.dma_start(
                out=t_rows[0:1, 0:TW].rearrange("o (a b) -> o a b", b=128),
                in_=t_statT[si * NT + g0:si * NT + g0 + GPT, :]),
                reads=["statT"], writes=["rows"], kind="d", dgrp="rows")
            OP("tensor", lambda e, si=si: e.matmul(
                ps_b[:, 0:TW], t_oner[:], t_rows[0:1, 0:TW],
                start=True, stop=True),
                reads=["rows", "oner"], writes=["ps:B"])
            OP("scalar", lambda e, dst=dst: e.copy(dst, ps_b[:, 0:TW]),
               reads=["ps:B"], writes=[bk])
        for kc in range(KC):
            xp, xk = xping[kc % 2], f"xp{kc % 2}"
            OP("sync", lambda e, xp=xp, kc=kc, t0=t0: e.dma_start(
                out=xp[:], in_=h_xfm[kc * 128:(kc + 1) * 128, t0:t0 + TW]),
                writes=[xk, "f32a"], kind="d", dgrp=xk)
            xnk = xn32[:, kc * TW:(kc + 1) * TW]
            OP("vector", lambda e, xp=xp, xnk=xnk: e.tensor_sub(xnk, xp[:], mean_b),
               reads=[xk, "bc0"], writes=[f"xn32:{kc}", "f32a"])
            OP("vector", lambda e, xnk=xnk: e.tensor_mul(xnk, xnk, rstd_b),
               reads=[f"xn32:{kc}", "bc1"], writes=[f"xn32:{kc}"])
            OP("scalar", lambda e, xnk=xnk, kc=kc, t0=t0: e.copy(
                t_xn[:, kc * S + t0:kc * S + t0 + TW], xnk),
                reads=[f"xn32:{kc}"], writes=[f"xn:{kc}:{tau}"])
            OP("vector", lambda e, xnk=xnk, kc=kc, t0=t0: e.scalar_tensor_tensor(
                t_A[:, kc * S + t0:kc * S + t0 + TW], xnk, t_n1[:, kc:kc + 1],
                rstd1_b, ALU.mult, ALU.mult),
                reads=[f"xn32:{kc}", "bc2"], sreads=["n1"], writes=[f"A:{kc}:{tau}"])
        for g in range(GPT):
            gg = tau * GPT + g
            for kc in range(KC):
                OP("tensor", lambda e, kc=kc, g=g: e.matmul(
                    ps_a[:, 0:E],
                    xn32[:, kc * TW + g * 128:kc * TW + g * 128 + 128],
                    t_wg[:, kc * E:(kc + 1) * E],
                    start=(kc == 0), stop=(kc == KC - 1)),
                    reads=[f"xn32:{kc}", "wg"], writes=["ps:A"])
            m8, nm8, sume = t_sc[:, 0:1], t_sc[:, 1:2], t_sc[:, 2:3]
            lgs = t_sc[:, 4:12]
            dmp = t_sc[:, 12:20]
            OP("vector", lambda e, lgs=lgs: e.tensor_copy(lgs, ps_a[:, 0:E]),
               reads=["ps:A"], writes=["lgs", "tsc"])
            OP("vector", lambda e, lgs=lgs: e.reduce_max(m8, lgs, axis=AX),
               reads=["lgs"], writes=["m8", "tsc"])
            OP("vector", lambda e, gg=gg, lgs=lgs: e.tensor_single_scalar(
                t_mask[:, gg * E:(gg + 1) * E], lgs, m8, ALU.is_ge),
                reads=["lgs"], sreads=["m8"], writes=["maskal"])
            OP("vector", lambda e: e.tensor_scalar_mul(nm8, m8, -1.0),
               reads=["m8"], writes=["nm8", "tsc"])
            OP("scalar", lambda e, dmp=dmp, lgs=lgs: e.activation(
                dmp, lgs, AF.Exp, bias=nm8, accum_out=sume),
                reads=["lgs"], sreads=["nm8"], writes=["sume", "tsc"])
            OP("vector", lambda e, gg=gg: e.reciprocal(t_gate[:, gg:gg + 1], sume),
               reads=["sume"], writes=["gateal"])
            if upto == 2:
                OP("vector", lambda e, gg=gg, lgs=lgs: e.tensor_copy(
                    t_kept[:, gg * E:(gg + 1) * E], lgs),
                    reads=["lgs"], writes=["keptal"])
                OP("vector", lambda e, gg=gg: e.tensor_copy(
                    t_gall[:, gg * E:gg * E + 1], m8),
                    sreads=["m8"], writes=["gall"])

    if upto == 1:
        return dbg_finish(
            [(h_out[kc * 128:(kc + 1) * 128, :], t_xn[:, kc * S:(kc + 1) * S],
              [f"xn:{kc}:{tt}" for tt in range(TAU)]) for kc in range(KC)])

    OP("vector", lambda e: e.memset(t_offs[0:1, 0:E], 0.0), writes=["offr"])
    for g in range(NT):
        if g > 0:
            OP("vector", lambda e, g=g: e.tensor_copy(
                t_offs[0:1, g * E:(g + 1) * E], ps_b[0:1, 0:E]),
                reads=["ps:B"], writes=["offr"])
        OP("tensor", lambda e, g=g: e.matmul(
            ps_b[0:1, 0:E], t_ones[:], t_mask[:, g * E:(g + 1) * E],
            start=(g == 0), stop=(g == NT - 1)),
            reads=["maskal", "ones", "offr"], writes=["ps:B"])
    OP("vector", lambda e: e.tensor_copy(t_rows[0:1, 0:E], ps_b[0:1, 0:E]),
       reads=["ps:B", "cpad"], writes=["cpad", "rows"])
    OP("sync", lambda e: e.dma_start(out=d_ccin[:], in_=t_rows[0:1, 0:512]),
       reads=["cpad", "rows"], writes=["ccin"], kind="d", dgrp="ccin")
    OP("gpsimd", lambda e: e.collective_compute(
        "AllGather", ALU.bypass, replica_groups=[list(range(8))],
        ins=[d_ccin[:]], outs=[d_ccout[:]]),
        reads=["ccin"], writes=["ccout"], kind="c")

    def emit_offsets():
        OP("gpsimd", lambda e: e.dma_start(out=t_allc[:], in_=d_ccout[:, 0:E]),
           reads=["ccout"], writes=["allc"], kind="d", dgrp="allc")
        OP("tensor", lambda e: e.matmul(ps_b[0:1, 0:E], t_selc[:], t_allc[:],
                                        start=True, stop=True),
           reads=["allc", "selc"], writes=["ps:B"])
        OP("scalar", lambda e: e.copy(t_offs[0:1, NT * E:NT * E + E],
                                      ps_b[0:1, 0:E]),
           reads=["ps:B"], writes=["coff"])
        for g in range(NT):
            OP("vector", lambda e, g=g: e.tensor_add(
                t_offs[0:1, g * E:(g + 1) * E], t_offs[0:1, g * E:(g + 1) * E],
                t_offs[0:1, NT * E:NT * E + E]),
                reads=["offr", "coff"], writes=["offr"])

    if upto == 2:
        emit_offsets()
        return dbg_finish([
            (h_out[0:128, 0:NT * E], t_mask[:], ["maskal"]),
            (h_out[128:256, 0:NT], t_gate[:], ["gateal"]),
            (h_out[256:257, 0:NT * E], t_offs[0:1, 0:NT * E], ["offr"]),
            (h_out[384:512, 0:NT * E], t_kept[:], ["keptal"]),
            (h_out[512:640, 0:NT * E], t_gall[:], ["gall"]),
        ])

    def wr(slot, off, n):
        return t_wr[:, slot * 4096 + off:slot * 4096 + off + n]

    WRALL = lambda s: [f"wr{s}a", f"wr{s}b", f"wr{s}c", f"wr{s}d"]

    pctr = {"ab": 0, "st": 0}

    def nextab():
        pctr["ab"] += 1
        return (ps_a, "ps:A") if pctr["ab"] % 2 == 0 else (ps_b, "ps:B")

    def nextst():
        pctr["st"] += 1
        b = pctr["st"] % 2
        return b, ps_st[:, b * 512:(b + 1) * 512], f"ps:st{b}"

    def proj_rope(c, reg, woff, hw, dst, dkey, scale):
        slot = c % 2
        wsb = wr(slot, woff, KC * 128)
        wkey = f"wr{slot}{reg}"
        OP("gpsimd", lambda e, wsb=wsb: e.dma_start(
            out=wsb.rearrange("p (kc f) -> p kc f", f=128),
            in_=hw[c].rearrange("(kc p) f -> p kc f", p=128)),
            writes=[wkey], kind="d", dgrp=wkey)
        for tau in range(TAU):
            t0 = tau * TW
            pse, pk = nextab()
            for kc in range(KC):
                OP("tensor", lambda e, kc=kc, t0=t0, wsb=wsb, pse=pse: e.matmul(
                    pse[:, 0:TW], wsb[:, kc * 128:(kc + 1) * 128],
                    t_A[:, kc * S + t0:kc * S + t0 + TW],
                    start=(kc == 0), stop=(kc == KC - 1)),
                    reads=[wkey, f"A:{kc}:{tau}"], writes=[pk])
            qr, qk = t_pg[:, (tau % 2) * TW:(tau % 2 + 1) * TW], f"qraw{tau % 2}"
            OP("scalar", lambda e, qr=qr, pse=pse: e.copy(qr, pse[:, 0:TW]),
               reads=[pk], writes=[qk])
            sb_, sps, spk = nextst()
            OP("tensor", lambda e, qr=qr, sps=sps: e.matmul(
                sps[:, 0:TW], t_prm[:], qr, start=True, stop=True),
                reads=[qk, "prm"], writes=[spk])
            tt1 = t_rope[:, (tau % 2) * TW:(tau % 2 + 1) * TW]
            tk1 = f"tt1{tau % 2}"
            OP("vector", lambda e, qr=qr, t0=t0, tt1=tt1: e.scalar_tensor_tensor(
                tt1, qr, scale, t_cos[:, t0:t0 + TW], ALU.mult, ALU.mult),
                reads=[qk, "cos"], writes=[tk1])
            OP("vector", lambda e, t0=t0, dst=dst, sps=sps, tt1=tt1: e.scalar_tensor_tensor(
                dst[:, t0:t0 + TW], sps[:, 0:TW], scale, t_sin[:, t0:t0 + TW],
                ALU.mult, ALU.mult),
                reads=[spk, "sin"], writes=["tt2s"])
            OP("vector", lambda e, t0=t0, dst=dst, tt1=tt1: e.tensor_add(
                dst[:, t0:t0 + TW], dst[:, t0:t0 + TW], tt1),
                reads=["tt2s", tk1], writes=[dkey])

    def build_V(c):
        slot = c % 2
        wsb = wr(slot, 2048, KC * 128)
        wkey = f"wr{slot}c"
        OP("gpsimd", lambda e, c=c, wsb=wsb: e.dma_start(
            out=wsb.rearrange("p (kc f) -> p kc f", f=128),
            in_=h_wv[c].rearrange("(kc p) f -> p kc f", p=128)),
            writes=[wkey], kind="d", dgrp=wkey)
        if c == 0:
            OP("vector", lambda e: e.memset(
                t_C[:, 0:NT * 130].rearrange("p (kt h z) -> p kt h z", h=2, z=65)[:, :, :, 64],
                1.0), writes=["C"])
        for g4 in range((NT + 3) // 4):
            tts = list(range(4 * g4, min(4 * g4 + 4, NT)))
            pse, pk = nextab()
            for i, tt in enumerate(tts):
                for kc in range(KC):
                    OP("tensor", lambda e, tt=tt, i=i, kc=kc, wsb=wsb, pse=pse: e.matmul(
                        pse[:, i * 128:(i + 1) * 128],
                        t_A[:, kc * S + tt * 128:kc * S + (tt + 1) * 128],
                        wsb[:, kc * 128:(kc + 1) * 128],
                        start=(kc == 0 and i == 0), stop=(kc == KC - 1)),
                        reads=[wkey, f"A:{kc}:{(tt * 128) // TW}"], writes=[pk])
            n = len(tts)
            OP("vector", lambda e, tts=tts, n=n, c=c, pse=pse: e.tensor_copy(
                t_C[:, 0:NT * 130].rearrange("p (kt h z) -> p kt h z", h=2, z=65)[
                    :, tts[0]:tts[0] + n, :, 0:64],
                pse.ap().rearrange("p (i h z) -> p i h z", h=2, z=64)[:, 0:n, :, :]),
                reads=[pk], writes=["C"])

    def bound_B(c):
        sqb = t_xne[:, 0:S]
        for wi, (which, buf, bk) in enumerate((("q", t_q, "qrope"), ("k", t_k, "krope"))):
            OP("scalar", lambda e, buf=buf: e.activation(sqb, buf[:], AF.Square),
               reads=[bk], writes=["sqb", "xnereg"])
            for tau in range(TAU):
                OP("tensor", lambda e, tau=tau: e.matmul(
                    ps_b[0:1, 0:TW], t_onesb[:], sqb[:, tau * TW:(tau + 1) * TW],
                    start=True, stop=True),
                    reads=["sqb", "xnereg", "onesb"], writes=["ps:B"])
                OP("vector", lambda e, tau=tau, wi=wi: e.reduce_max(
                    t_sr[0:1, wi * 8 + tau:wi * 8 + tau + 1], ps_b[0:1, 0:TW], axis=AX),
                    reads=["ps:B"], writes=[f"nm{wi}", "tsr"])
        OP("vector", lambda e: e.reduce_max(t_sr[0:1, 16:17], t_sr[0:1, 0:TAU], axis=AX),
           sreads=["nm0"], writes=["tsr"])
        OP("vector", lambda e: e.reduce_max(t_sr[0:1, 17:18], t_sr[0:1, 8:8 + TAU], axis=AX),
           sreads=["nm1"], writes=["tsr"])
        OP("vector", lambda e: e.tensor_mul(t_sr[0:1, 18:19], t_sr[0:1, 16:17], t_sr[0:1, 17:18]),
           sreads=["tsr"], writes=["tsr"])
        OP("scalar", lambda e: e.activation(t_sr[0:1, 19:20], t_sr[0:1, 18:19], AF.Sqrt,
                                            scale=1.05),
           reads=["tsr"], writes=["bb"])
        OP("tensor", lambda e: e.matmul(ps_b[:, 0:1], t_oner[:], t_sr[0:1, 19:20],
                                        start=True, stop=True),
           reads=["bb", "oner"], writes=["ps:B"])
        OP("scalar", lambda e, c=c: e.activation(t_nB[:, c:c + 1], ps_b[:, 0:1],
                                                 AF.Copy, scale=-1.0),
           reads=["ps:B"], writes=["negB"])

    def attention_head(c, half):
        r0 = 64 * half
        h = 2 * c + half

        def emit_ctx(kt, batch, bank):
            for i, qi in enumerate(batch):
                OP("tensor", lambda e, kt=kt, qi=qi, i=i, bank=bank, h=h: e.matmul(
                    ps_ctx[0:65, qi * 128:(qi + 1) * 128],
                    t_C[:, kt * 130 + 65 * (h % 2):kt * 130 + 65 * (h % 2) + 65],
                    t_pt[:, bank * 512 + i * 128:bank * 512 + (i + 1) * 128],
                    start=(kt == 0 and qi % 4 == 0), stop=(kt == qi)),
                    reads=[f"pt{bank}", "C"], writes=[f"ps:ctx{qi // 4}"])

        pend = None
        for kt in range(NT):
            qis = list(range(kt, NT))
            for bi in range(0, len(qis), 4):
                batch = qis[bi:bi + 4]
                bank, sps, stk = nextst()
                for i, qi in enumerate(batch):
                    OP("tensor", lambda e, kt=kt, qi=qi, i=i, sps=sps, r0=r0: e.matmul(
                        sps[:, i * 128:(i + 1) * 128],
                        t_k[r0:r0 + 64, kt * 128:(kt + 1) * 128],
                        t_q[r0:r0 + 64, qi * 128:(qi + 1) * 128],
                        start=True, stop=True),
                        reads=["krope", "qrope"], writes=[stk])
                    if qi == kt:
                        OP("vector", lambda e, i=i, sps=sps: e.tensor_add(
                            sps[:, i * 128:(i + 1) * 128],
                            sps[:, i * 128:(i + 1) * 128],
                            t_mskT[:]),
                            reads=[stk, "mskT"], writes=[stk])
                nb = len(batch)
                OP("scalar", lambda e, nb=nb, bank=bank, sps=sps, c=c: e.activation(
                    t_pt[:, bank * 512:bank * 512 + nb * 128],
                    sps[:, 0:nb * 128],
                    AF.Exp, bias=t_nB[:, c:c + 1]),
                    reads=[stk], sreads=["negB"], writes=[f"pt{bank}"])
                if pend is not None:
                    emit_ctx(*pend)
                pend = (kt, batch, bank)
        emit_ctx(*pend)

    def attn_evac(c, half):
        kc = c
        for q0 in range(0, NT, 4):
            n = min(4, NT - q0)
            for i in range(n):
                OP("vector", lambda e, q0=q0, i=i: e.reciprocal(
                    t_rec[0:1, i * 128:(i + 1) * 128],
                    ps_ctx[64:65, (q0 + i) * 128:(q0 + i + 1) * 128]),
                    reads=[f"ps:ctx{(q0 + i) // 4}"], writes=["rec"])
            bank, sps, stk = nextst()
            OP("tensor", lambda e, n=n, sps=sps: e.matmul(
                sps[0:64, 0:n * 128], t_onerb[0:1, 0:64],
                t_rec[0:1, 0:n * 128], start=True, stop=True),
                reads=["rec", "onerb"], writes=[stk])
            OP("scalar", lambda e, n=n, sps=sps: e.copy(t_gb[:, 0:n * 128],
                                                        sps[0:64, 0:n * 128]),
               reads=[stk], writes=["gbs"])
            for i in range(n):
                qi = q0 + i
                tau = (qi * 128) // TW
                OP("vector", lambda e, qi=qi, i=i, half=half, kc=kc, tau=tau: e.tensor_mul(
                    t_B[:, kc * S + qi * 128:kc * S + (qi + 1) * 128][64 * half:64 * half + 64, :],
                    ps_ctx[0:64, qi * 128:(qi + 1) * 128],
                    t_gb[:, i * 128:(i + 1) * 128]),
                    reads=[f"ps:ctx{qi // 4}", "gbs"], writes=[f"B:{kc}:{tau}"])

    for c in range(KC):
        proj_rope(c, "a", 0, h_wq, t_q, "qrope", 0.125)
        proj_rope(c, "b", 1024, h_wk, t_k, "krope", 1.0)
        build_V(c)
        bound_B(c)
        if c > 0:
            attn_evac(c - 1, 1)
        attention_head(c, 0)
        attn_evac(c, 0)
        attention_head(c, 1)
    attn_evac(KC - 1, 1)

    if upto == 3:
        return dbg_finish(
            [(h_out[kc * 128:(kc + 1) * 128, :], t_B[:, kc * S:(kc + 1) * S],
              [f"B:{kc}:{tt}" for tt in range(TAU)]) for kc in range(KC)])

    y_sl = t_f32[:, 0:KC * TW]
    for half in range(2):
        OP("gpsimd", lambda e, half=half: e.dma_start(
            out=wr(half, 0, 4096).rearrange("p (kc f) -> p kc f", f=512),
            in_=h_wo[half].rearrange("(kc p) f -> p kc f", p=128)),
            writes=[f"wr{half}a", f"wr{half}b", f"wr{half}c"], kind="d",
            dgrp=f"wr{half}a")
    for tau in range(TAU):
        t0 = tau * TW
        for ic in range(KC):
            half, iw = ic // 4, ic % 4
            pse, pk = (ps_a, "ps:A") if ic % 2 == 0 else (ps_b, "ps:B")
            for kc in range(KC):
                OP("tensor", lambda e, half=half, iw=iw, kc=kc, t0=t0, pse=pse: e.matmul(
                    pse[:, 0:TW],
                    wr(half, kc * 512 + iw * 128, 128),
                    t_B[:, kc * S + t0:kc * S + t0 + TW],
                    start=(kc == 0), stop=(kc == KC - 1)),
                    reads=[f"wr{half}a", f"B:{kc}:{tau}"], writes=[pk])
            OP("vector", lambda e, ic=ic, t0=t0, pse=pse, pk=pk: e.tensor_add(
                y_sl[:, ic * TW:(ic + 1) * TW], pse[:, 0:TW],
                t_xn[:, ic * S + t0:ic * S + t0 + TW]),
                reads=[pk, f"xn:{ic}:{tau}"], writes=[f"y:{ic}", "f32a"])
        for ic in range(KC):
            si = t_sil[:, 0:TW]
            OP("scalar", lambda e, ic=ic, si=si: e.activation(
                si, y_sl[:, ic * TW:(ic + 1) * TW], AF.Square),
                reads=[f"y:{ic}"], writes=["ysq", "silreg"])
            OP("tensor", lambda e, ic=ic, si=si: e.matmul(
                ps_st[0:1, 0:TW], t_onesb[:], si,
                start=(ic == 0), stop=(ic == KC - 1)),
                reads=["ysq", "silreg", "onesb"], writes=["ps:st0"])
        rrow = t_rows[0:1, 0:TW]
        OP("scalar", lambda e: e.activation(rrow, ps_st[0:1, 0:TW], AF.Sqrt,
                                            scale=1.0 / D, bias=EPS),
           reads=["ps:st0", "epsc"], writes=["r2row", "rows"])
        OP("vector", lambda e: e.reciprocal(rrow, rrow), reads=["r2row"], writes=["r2row"])
        OP("tensor", lambda e: e.matmul(ps_st[:, 0:TW], t_oner[:], rrow,
                                        start=True, stop=True),
           reads=["r2row", "oner"], writes=["ps:st0"])
        OP("scalar", lambda e: e.copy(rstd2_b, ps_st[:, 0:TW]),
           reads=["ps:st0"], writes=["bc3", "bc0"])
        for ic in range(KC):
            OP("vector", lambda e, ic=ic, t0=t0: e.scalar_tensor_tensor(
                t_A[:, ic * S + t0:ic * S + t0 + TW],
                y_sl[:, ic * TW:(ic + 1) * TW], t_n2[:, ic:ic + 1], rstd2_b,
                ALU.mult, ALU.mult),
                reads=[f"y:{ic}", "bc3"], sreads=["n2"], writes=[f"A:{ic}:{tau}"])
            xp, xk = xping[ic % 2], f"xp{ic % 2}"
            OP("sync", lambda e, ic=ic, t0=t0, xp=xp: e.dma_start(
                out=xp[:], in_=h_xfm[ic * 128:(ic + 1) * 128, t0:t0 + TW]),
                writes=[xk, "f32a"], kind="d", dgrp=xk)
            ev, ek = t_ev[:, (ic % 2) * TW:(ic % 2 + 1) * TW], f"ev{ic % 2}"
            OP("vector", lambda e, ic=ic, xp=xp, ev=ev: e.tensor_add(
                ev, y_sl[:, ic * TW:(ic + 1) * TW], xp[:]),
                reads=[f"y:{ic}", xk], writes=[ek])
            OP("gpsimd", lambda e, ic=ic, t0=t0, ev=ev: e.dma_start(
                out=h_out[ic * 128:(ic + 1) * 128, t0:t0 + TW], in_=ev),
                reads=[ek], writes=[f"out:{ic}:{tau}"], kind="d",
                dgrp=f"oute{ic % 2}")

    if upto == 4:
        outkeys = [f"out:{ic}:{tt}" for ic in range(KC) for tt in range(TAU)]
        sch.barrier_wait("sync", outkeys)
        return finish()

    emit_offsets()

    for g in range(NT):
        OP("tensor", lambda e, g=g: e.matmul(
            ps_a[:, g * E:(g + 1) * E], t_ltri[:], t_mask[:, g * E:(g + 1) * E],
            start=(g == 0), stop=False),
            reads=["ltri", "maskal"], writes=["ps:A"])
    OP("tensor", lambda e: e.matmul(
        ps_a[:, 0:NT * E], t_oner[:], t_offs[0:1, 0:NT * E], start=False, stop=True),
        reads=["oner", "offr"], writes=["ps:A"])
    OP("vector", lambda e: e.tensor_single_scalar(
        t_kept[:], ps_a[:, 0:NT * E], CAPTH, ALU.is_lt),
        reads=["ps:A"], writes=["keptal"])
    for g in range(NT):
        OP("vector", lambda e, g=g: e.scalar_tensor_tensor(
            t_gall[:, g * E:(g + 1) * E], t_mask[:, g * E:(g + 1) * E],
            t_gate[:, g:g + 1], t_kept[:, g * E:(g + 1) * E], ALU.mult, ALU.mult),
            reads=["maskal", "keptal"], sreads=["gateal"], writes=["gall"])
    for g in range(NT):
        OP("tensor", lambda e, g=g: e.transpose(
            ps_b[0:E, 0:128], t_gall[:, g * E:(g + 1) * E], t_idn[:]),
            reads=["gall", "idn"], writes=["ps:B"])
        OP("scalar", lambda e, g=g: e.copy(t_gt[:, g * 128:(g + 1) * 128],
                                           ps_b[0:E, 0:128]),
           reads=["ps:B"], writes=["gt"])
    if upto == 41:
        return dbg_finish([
            (h_out[0:128, 0:NT * E], t_kept[:], ["keptal"]),
            (h_out[128:256, 0:NT * E], t_gall[:], ["gall"]),
            (h_out[256:256 + E, 0:S], t_gt[:], ["gt"]),
        ])
    for ee in range(E):
        for tau in range(TAU):
            OP("tensor", lambda e, ee=ee, tau=tau: e.matmul(
                ps_b[:, 0:TW], t_sel[:, ee * 128:ee * 128 + 128],
                t_gt[:, tau * TW:(tau + 1) * TW], start=True, stop=True),
                reads=["sel", "gt"], writes=["ps:B"])
            OP("scalar", lambda e, ee=ee, tau=tau: e.copy(
                t_B[:, ee * S + tau * TW:ee * S + (tau + 1) * TW], ps_b[:, 0:TW]),
                reads=["ps:B"], writes=[f"B:{ee}:{tau}"])
    if upto == 43:
        return dbg_finish(
            [(h_out[ee * 128:ee * 128 + 128, 0:S], t_B[:, ee * S:(ee + 1) * S],
              [f"B:{ee}:{tt}" for tt in range(TAU)]) for ee in range(E)])

    def moe_psum(ic):
        if ic < 4:
            return ps_ctx[:, ic * 512:(ic + 1) * 512], f"ps:ctx{ic}"
        if ic < 6:
            return ps_st[:, (ic - 4) * 512:(ic - 3) * 512], f"ps:st{ic - 4}"
        return (ps_a, "ps:A") if ic == 6 else (ps_b, "ps:B")

    EMM = {45: 0, 46: 0, 47: 1, 48: 2, 49: 4}.get(upto, E)
    for tau in range(TAU):
        t0 = tau * TW
        for ee in range(E):
            for kc in range(KC):
                OP("vector", lambda e, ee=ee, kc=kc, t0=t0, tau=tau: e.tensor_mul(
                    t_xne[:, kc * TW:(kc + 1) * TW],
                    t_xn[:, kc * S + t0:kc * S + t0 + TW],
                    t_B[:, ee * S + t0:ee * S + t0 + TW]),
                    reads=[f"xn:{kc}:{tau}", f"B:{ee}:{tau}"],
                    writes=[f"xne:{kc}", "xnereg"])
            for hh in range(2):
                slot3 = (tau * E * 2 + ee * 2 + hh) % 3
                if slot3 < 2:
                    wbuf = wr(slot3, 0, 4096)
                    wks = [f"wr{slot3}a", f"wr{slot3}b", f"wr{slot3}c"]
                else:
                    wbuf = t_w2g[:, 0:4096]
                    wks = ["w2g0", "w2g1", "w2g2", "w2g3"]
                if upto != 45:
                    OP("gpsimd", lambda e, ee=ee, hh=hh, wbuf=wbuf: e.dma_start(
                        out=wbuf.rearrange("p (kc f) -> p kc f", f=512),
                        in_=h_we[ee, hh].rearrange("(kc p) f -> p kc f", p=128)),
                        writes=wks, kind="d", dgrp=wks[0])
                if ee >= EMM:
                    continue
                for kc in range(KC):
                    for iw in range(4):
                        ic = hh * 4 + iw
                        pse, pk = moe_psum(ic)
                        OP("tensor", lambda e, kc=kc, iw=iw, pse=pse, wbuf=wbuf,
                           ee=ee, hh=hh: e.matmul(
                            pse[:, 0:TW],
                            wbuf[:, kc * 512 + iw * 128:kc * 512 + (iw + 1) * 128],
                            t_xne[:, kc * TW:(kc + 1) * TW],
                            start=(ee == 0 and kc == 0), stop=False),
                            reads=[wks[0], f"xne:{kc}"], writes=[pk])
        if upto in (45, 46):
            continue
        for ic in range(KC):
            pse, pk = moe_psum(ic)
            OP("tensor", lambda e, ic=ic, tau=tau, pse=pse: e.matmul(
                pse[:, 0:TW], t_be[:, ic * 128:(ic + 1) * 128],
                t_gt[:, tau * TW:(tau + 1) * TW], start=False, stop=True),
                reads=["be", "gt"], writes=[pk])
            ev, ek = t_ev[:, (ic % 2) * TW:(ic % 2 + 1) * TW], f"ev{ic % 2}"
            OP("scalar", lambda e, pse=pse, ev=ev: e.copy(ev, pse[:, 0:TW]),
               reads=[pk], writes=[ek])
            if upto == 44:
                OP("gpsimd", lambda e, ic=ic, t0=t0, ev=ev: e.dma_start(
                    out=h_out[ic * 128:(ic + 1) * 128, t0:t0 + TW], in_=ev),
                    reads=[ek, f"out:{ic}:{tau}"], writes=[f"out:{ic}:{tau}"],
                    kind="d", dgrp=f"oute{ic % 2}")
            else:
                OP("gpsimd", lambda e, ic=ic, t0=t0, ev=ev: e.dma_start(
                    out=h_out[ic * 128:(ic + 1) * 128, t0:t0 + TW], in_=ev,
                    accum_op=ALU.add),
                    reads=[ek, f"out:{ic}:{tau}"], writes=[f"out:{ic}:{tau}"],
                    kind="d", dgrp=f"oute{ic % 2}")

    if upto in (45, 46):
        return dbg_finish([(h_out[0:128, 0:TW], t_xne[:, 0:TW], ["xne:0"])])
    if upto in (5, 44, 47, 48, 49):
        outkeys = [f"out:{ic}:{tt}" for ic in range(KC) for tt in range(TAU)]
        sch.barrier_wait("sync", outkeys)
        sch.barrier_wait("gpsimd", outkeys)
        return finish()

    FG = 4
    ff_acc = t_f32[:, 0:KC * TW]
    for tau in range(TAU):
        t0 = tau * TW
        for fc in range(FC):
            slot = fc % 2
            OP("gpsimd", lambda e, fc=fc, slot=slot: e.dma_start(
                out=wr(slot, 0, KC * 128).rearrange("p (kc f) -> p kc f", f=128),
                in_=h_w1[fc].rearrange("(kc p) f -> p kc f", p=128)),
                writes=[f"wr{slot}a"], kind="d", dgrp=f"wr{slot}a")
            OP("gpsimd", lambda e, fc=fc, slot=slot: e.dma_start(
                out=wr(slot, 1024, KC * 128).rearrange("p (kc f) -> p kc f", f=128),
                in_=h_w3[fc].rearrange("(kc p) f -> p kc f", p=128)),
                writes=[f"wr{slot}b"], kind="d", dgrp=f"wr{slot}b")
            p1, p1k = (ps_a, "ps:A") if slot == 0 else (ps_st[:, 0:512], "ps:st0")
            p3, p3k = (ps_b, "ps:B") if slot == 0 else (ps_st[:, 512:1024], "ps:st1")
            for kc in range(KC):
                OP("tensor", lambda e, kc=kc, t0=t0, slot=slot, p1=p1: e.matmul(
                    p1[:, 0:TW], wr(slot, kc * 128, 128),
                    t_A[:, kc * S + t0:kc * S + t0 + TW],
                    start=(kc == 0), stop=(kc == KC - 1)),
                    reads=[f"wr{slot}a", f"A:{kc}:{tau}"], writes=[p1k])
            for kc in range(KC):
                OP("tensor", lambda e, kc=kc, t0=t0, slot=slot, p3=p3: e.matmul(
                    p3[:, 0:TW], wr(slot, 1024 + kc * 128, 128),
                    t_A[:, kc * S + t0:kc * S + t0 + TW],
                    start=(kc == 0), stop=(kc == KC - 1)),
                    reads=[f"wr{slot}b", f"A:{kc}:{tau}"], writes=[p3k])
            si, sk = t_sil[:, slot * TW:(slot + 1) * TW], f"sil{slot}"
            OP("scalar", lambda e, si=si, p1=p1: e.activation(si, p1[:, 0:TW], AF.Silu),
               reads=[p1k], writes=[sk, "silreg"])
            j = fc % FG
            pgt, pgk = t_pg[:, j * TW:(j + 1) * TW], f"pg{j}"
            OP("vector", lambda e, si=si, p3=p3, pgt=pgt: e.tensor_mul(
                pgt, si, p3[:, 0:TW]),
                reads=[sk, p3k], writes=[pgk])
            OP("gpsimd", lambda e, fc=fc, j=j: e.dma_start(
                out=t_w2g[:, j * D:(j + 1) * D], in_=h_w2[fc]),
                writes=[f"w2g{j}"], kind="d", dgrp=f"w2g{j}")
            if j == FG - 1:
                fg0 = fc - FG + 1
                for oc in range(KC):
                    fps = ps_ctx[:, (oc % 2) * 512:(oc % 2) * 512 + TW]
                    fpk = f"ps:ctx{oc % 2}"
                    for jj in range(FG):
                        OP("tensor", lambda e, jj=jj, oc=oc, fps=fps: e.matmul(
                            fps, t_w2g[:, jj * D + oc * 128:jj * D + (oc + 1) * 128],
                            t_pg[:, jj * TW:(jj + 1) * TW],
                            start=(jj == 0), stop=(jj == FG - 1)),
                            reads=[f"w2g{jj}", f"pg{jj}"], writes=[fpk])
                    if fg0 == 0:
                        OP("vector", lambda e, oc=oc, fps=fps: e.tensor_copy(
                            ff_acc[:, oc * TW:(oc + 1) * TW], fps),
                            reads=[fpk], writes=[f"y:{oc}"])
                    else:
                        OP("vector", lambda e, oc=oc, fps=fps: e.tensor_add(
                            ff_acc[:, oc * TW:(oc + 1) * TW],
                            ff_acc[:, oc * TW:(oc + 1) * TW], fps),
                            reads=[fpk, f"y:{oc}"], writes=[f"y:{oc}"])
        for oc in range(KC):
            acc_op = {61: ALU.bypass}.get(upto, ALU.add)
            OP("gpsimd", lambda e, oc=oc, t0=t0, acc_op=acc_op: e.dma_start(
                out=h_out[oc * 128:(oc + 1) * 128, t0:t0 + TW],
                in_=ff_acc[:, oc * TW:(oc + 1) * TW], accum_op=acc_op),
                reads=[f"y:{oc}", f"out:{oc}:{tau}"],
                writes=[f"out:{oc}:{tau}", f"ffdma{oc % 2}"], kind="d",
                dgrp=f"oute{oc % 2}")

    outkeys = [f"out:{ic}:{tau}" for ic in range(KC) for tau in range(TAU)]
    sch.barrier_wait("gpsimd", outkeys)
    sch.barrier_wait("sync", outkeys)
    return finish()


_CACHE = {}


def _consts(S):
    DKH = 64
    pos = np.arange(S, dtype=np.float64)
    k = np.arange(DKH) // 2
    ang = pos[:, None] / np.power(10000.0, 2.0 * k / DKH)[None, :]
    cos = np.cos(ang)
    sin = np.sin(ang) * np.where(np.arange(DKH) % 2 == 0, -1.0, 1.0)[None, :]
    cosT = np.ascontiguousarray(np.tile(cos.T, (2, 1)).astype(BF))
    sinT = np.ascontiguousarray(np.tile(sin.T, (2, 1)).astype(BF))
    idn = np.eye(128, dtype=np.float32)
    prm = np.zeros((128, 128), dtype=np.float32)
    prm[np.arange(128), np.arange(128) ^ 1] = 1.0
    mskT = np.ascontiguousarray(
        np.where(np.arange(128)[:, None] > np.arange(128)[None, :], NEG, 0.0).astype(BF))
    ltri = np.ascontiguousarray(np.triu(np.ones((128, 128), dtype=np.float32), k=1))
    selm = np.zeros((E, E * 128), dtype=np.float32)
    for ee in range(E):
        selm[ee, ee * 128:(ee + 1) * 128] = 1.0
    return dict(cosT=cosT, sinT=sinT, idn=idn, prm=np.ascontiguousarray(prm.astype(BF)),
                mskT=mskT, ltri=ltri, selm=np.ascontiguousarray(selm.astype(BF)))


def prep_inputs(x, wq, wk, wv, wo, n1, n2, w1, w2, w3, wg, We, be, S):
    cst = _consts(S)
    f32 = np.float32

    def T(w):
        return np.ascontiguousarray(np.asarray(w, dtype=f32).T)

    def blocks(wT, n):
        return np.ascontiguousarray(wT.reshape(D, n, wT.shape[1] // n).transpose(1, 0, 2))

    wq_b = blocks(T(wq), KC)
    wk_b = blocks(T(wk), KC)
    wv_b = blocks(T(wv), KC)
    wo_b = blocks(T(wo), 2)
    w1_b = blocks(T(w1), FC)
    w3_b = blocks(T(w3), FC)
    w2_b = np.ascontiguousarray(T(w2).reshape(FC, 128, D))
    WeT = np.asarray(We, dtype=f32).transpose(0, 2, 1)
    we_b = np.ascontiguousarray(WeT.reshape(E, D, 2, 512).transpose(0, 2, 1, 3))
    n1c = np.ascontiguousarray(np.asarray(n1, dtype=f32).reshape(KC, 128).T)
    n2c = np.ascontiguousarray(np.asarray(n2, dtype=f32).reshape(KC, 128).T)
    base = dict(wq_b=wq_b, wk_b=wk_b, wv_b=wv_b, wo_b=wo_b, w1_b=w1_b,
                w3_b=w3_b, w2_b=w2_b, we_b=we_b,
                wg=np.ascontiguousarray(np.asarray(wg, dtype=f32)),
                be=np.ascontiguousarray(np.asarray(be, dtype=f32)),
                n1c=n1c, n2c=n2c, **cst)
    xa = np.asarray(x, dtype=f32)
    in_maps = []
    for b in range(xa.shape[0]):
        m = dict(base)
        m["x_tm"] = np.ascontiguousarray(xa[b])
        m["x_fm"] = np.ascontiguousarray(xa[b].T)
        m["selc"] = (np.arange(E) < b).astype(f32).reshape(E, 1)
        in_maps.append(m)
    return in_maps


def kernel(L0_attn_x, L0_attn_wq, L0_attn_wk, L0_attn_wv, L0_attn_wo,
           L0_attn_norm1_w, L0_attn_norm2_w, L0_attn_w1, L0_attn_w2,
           L0_attn_w3, L0_ffn_wg, L0_ffn_W, L0_ffn_b):
    import os
    upto = int(os.environ.get("K_UPTO", "9"))
    S = np.asarray(L0_attn_x).shape[1]
    key = (S, upto)
    if key not in _CACHE:
        _CACHE[key] = build_program(S, upto)
    nc = _CACHE[key]
    in_maps = prep_inputs(L0_attn_x, L0_attn_wq, L0_attn_wk, L0_attn_wv,
                          L0_attn_wo, L0_attn_norm1_w, L0_attn_norm2_w,
                          L0_attn_w1, L0_attn_w2, L0_attn_w3, L0_ffn_wg,
                          L0_ffn_W, L0_ffn_b, S)
    res = run_bass_kernel_spmd(nc, in_maps, core_ids=list(range(8)))
    out = np.stack([np.asarray(res.results[b]["out"]).T for b in range(8)])
    return np.ascontiguousarray(out.astype(np.float32))


# version 06
# speedup vs baseline: 1.2727x; 1.0434x over previous
import contextlib
import numpy as np
import ml_dtypes
import concourse.bass as bass
import concourse.mybir as mybir
from concourse.bass_utils import run_bass_kernel_spmd

F32 = mybir.dt.float32
BF16 = mybir.dt.bfloat16
AF = mybir.ActivationFunctionType
ALU = mybir.AluOpType
AX = mybir.AxisListType.X

D = 1024
KC = 8
DFF = 4096
FC = 32
E = 8
EPS = 1e-5
NEG = -1e9

BF = ml_dtypes.bfloat16


class Sched:
    ENGS = ("sync", "scalar", "vector", "tensor", "gpsimd")

    def __init__(self, nc, es):
        self.nc = nc
        self.es = es
        self.seq = {e: 0 for e in self.ENGS}
        self.cseq = 0
        self.sem = {e: es.enter_context(nc.semaphore(f"is_{e}")) for e in self.ENGS}
        self.csem = es.enter_context(nc.semaphore("cs"))
        self.dgrp = {}
        self.last_w = {}
        self.readers = {}
        self.watermark = {e: {} for e in self.ENGS}
        self.eng_ops = {e: [] for e in self.ENGS}

    def _group(self, name, queue):
        g = self.dgrp.get(name)
        if g is None:
            g = [self.es.enter_context(self.nc.semaphore(f"dg_{name}")), 0, queue]
            self.dgrp[name] = g
        assert g[2] == queue, f"dma group {name} used from {queue} and {g[2]}"
        return g

    def _need(self, eng, tok, waits, force=False, dma_queue=None):
        if tok is None:
            return
        kind = tok[0]
        if kind == "i":
            _, peng, val = tok
            if peng == eng and eng == "tensor" and not force:
                return
            sem, target, wkey = self.sem[peng], val, ("i", peng)
        elif kind == "d":
            _, grp, val = tok
            g = self.dgrp[grp]
            if dma_queue is not None and g[2] == dma_queue:
                return
            sem, target, wkey = g[0], 16 * val, ("d", grp)
        else:
            _, val = tok
            sem, target, wkey = self.csem, val, ("c",)
        wm = self.watermark[eng]
        if wm.get(wkey, 0) >= target:
            return
        wm[wkey] = target
        waits.append((sem, target))

    def op(self, eng, fn, reads=(), writes=(), kind="i", sreads=(), dgrp=None):
        waits = []
        dq = eng if kind == "d" else None
        for r in sreads:
            self._need(eng, self.last_w.get(r), waits, force=True, dma_queue=dq)
        for r in reads:
            self._need(eng, self.last_w.get(r), waits, dma_queue=dq)
        for w in writes:
            self._need(eng, self.last_w.get(w), waits, dma_queue=dq)
            for t in self.readers.get(w, ()):
                self._need(eng, t, waits, dma_queue=dq)
        if kind == "d":
            g = self._group(dgrp, eng)
            g[1] += 1
            tok = ("d", dgrp, g[1])
            inc = (g[0], 16)
        elif kind == "c":
            self.cseq += 1
            tok = ("c", self.cseq)
            inc = (self.csem, 1)
        else:
            self.seq[eng] += 1
            tok = ("i", eng, self.seq[eng])
            inc = (self.sem[eng], 1)
        self.eng_ops[eng].append((fn, waits, inc))
        for r in list(reads) + list(sreads):
            self.readers.setdefault(r, []).append(tok)
        for w in writes:
            self.last_w[w] = tok
            self.readers[w] = []
        return tok

    def barrier_wait(self, eng, keys):
        waits = []
        for k in keys:
            self._need(eng, self.last_w.get(k), waits)
            for t in self.readers.get(k, ()):
                self._need(eng, t, waits)
        if waits:
            self.eng_ops[eng].append((None, waits, None))

    def replay(self, block):
        for ename in self.ENGS:
            ops = self.eng_ops[ename]

            def make(op_list):
                def run(eng):
                    for fn, waits, inc in op_list:
                        for (s, tgt) in waits:
                            eng.wait_ge(s, tgt)
                        if fn is not None:
                            fn(eng).then_inc(*inc)
                return run

            getattr(block, ename)(make(ops))


def build_program(S, upto=9):
    assert S % 256 == 0
    TW = min(512, S)
    TAU = S // TW
    NT = S // 128
    GPT = TW // 128
    CAPTH = float(S) - 0.5

    nc = bass.Bass("TRN2", target_bir_lowering=False)

    h_xtm = nc.dram_tensor("x_tm", [S, D], F32, kind="ExternalInput")
    h_xfm = nc.dram_tensor("x_fm", [D, S], F32, kind="ExternalInput")
    h_wq = nc.dram_tensor("wq_b", [KC, D, 128], F32, kind="ExternalInput")
    h_wk = nc.dram_tensor("wk_b", [KC, D, 128], F32, kind="ExternalInput")
    h_wv = nc.dram_tensor("wv_b", [KC, D, 128], F32, kind="ExternalInput")
    h_wo = nc.dram_tensor("wo_b", [2, D, 512], F32, kind="ExternalInput")
    h_w1 = nc.dram_tensor("w1_b", [FC, D, 128], F32, kind="ExternalInput")
    h_w3 = nc.dram_tensor("w3_b", [FC, D, 128], F32, kind="ExternalInput")
    h_w2 = nc.dram_tensor("w2_b", [FC, 128, D], F32, kind="ExternalInput")
    h_we = nc.dram_tensor("we_b", [E, 2, D, 512], F32, kind="ExternalInput")
    h_wg = nc.dram_tensor("wg", [D, E], F32, kind="ExternalInput")
    h_be = nc.dram_tensor("be", [E, D], F32, kind="ExternalInput")
    h_n1 = nc.dram_tensor("n1c", [128, KC], F32, kind="ExternalInput")
    h_n2 = nc.dram_tensor("n2c", [128, KC], F32, kind="ExternalInput")
    h_cos = nc.dram_tensor("cosT", [128, S], BF16, kind="ExternalInput")
    h_sin = nc.dram_tensor("sinT", [128, S], BF16, kind="ExternalInput")
    h_idn = nc.dram_tensor("idn", [128, 128], F32, kind="ExternalInput")
    h_prm = nc.dram_tensor("prm", [128, 128], BF16, kind="ExternalInput")
    h_mskT = nc.dram_tensor("mskT", [128, 128], BF16, kind="ExternalInput")
    h_ltri = nc.dram_tensor("ltri", [128, 128], F32, kind="ExternalInput")
    h_sel = nc.dram_tensor("selm", [E, E * 128], BF16, kind="ExternalInput")
    h_selc = nc.dram_tensor("selc", [E, 1], F32, kind="ExternalInput")
    h_out = nc.dram_tensor("out", [D, S], F32, kind="ExternalOutput")
    d_ccin = nc.dram_tensor("cc_in", [1, 512], F32, kind="Internal")
    d_ccout = nc.dram_tensor("cc_out", [8, 512], F32, kind="Internal")

    es = contextlib.ExitStack()

    def sb(name, p, f, dt):
        return es.enter_context(nc.sbuf_tensor(name, [p, f], dt))

    t_xn = sb("xn", 128, KC * S, BF16)
    t_A = sb("uA", 128, KC * S, BF16)
    t_B = sb("uB", 128, KC * S, BF16)
    t_C = sb("uC", 128, NT * 130, BF16)
    t_q = sb("qbuf", 128, S, BF16)
    t_k = sb("kbuf", 128, S, BF16)
    t_xne = sb("xne", 128, max(KC * TW, S), BF16)
    t_wr = sb("wring", 128, 2 * 4096, BF16)
    t_cos = sb("cosb", 128, S, BF16)
    t_sin = sb("sinb", 128, S, BF16)
    t_f32 = sb("f32a", 128, max(KC * TW + 2 * TW, 3 * 1024), F32)
    t_f32b = sb("f32b", 128, 3 * TW, F32)
    t_pg = sb("pgr", 128, 4 * TW, BF16)
    t_w2g = sb("w2g", 128, 4 * D, BF16)
    t_rope = sb("ropes", 128, 2 * TW, BF16)
    t_pt = sb("ptb", 128, 1024, BF16)
    t_sil = sb("silb", 128, 2 * TW, BF16)
    t_ev = sb("evac", 128, 2 * TW, F32)
    t_gb = sb("gbrow", 64, 512, BF16)
    t_idn = sb("idnb", 128, 128, F32)
    t_prm = sb("prmb", 128, 128, BF16)
    t_mskT = sb("mskTb", 128, 128, BF16)
    t_ltri = sb("ltrib", 128, 128, F32)
    t_sel = sb("selb", E, E * 128, BF16)
    t_n1 = sb("n1b", 128, KC, F32)
    t_n2 = sb("n2b", 128, KC, F32)
    t_wg = sb("wgb", 128, KC * E, F32)
    t_be = sb("beb", E, D, BF16)
    t_stats = sb("statc", 128, 5 * NT, F32)
    t_statT = sb("statT", 3 * NT, 128, F32)
    t_rows = sb("rows", 1, max(TW, 512), F32)
    t_mask = sb("maskal", 128, NT * E, F32)
    t_gate = sb("gateal", 128, NT, F32)
    t_kept = sb("keptal", 128, NT * E, F32)
    t_gall = sb("gall", 128, NT * E, F32)
    t_gt = sb("gtb", E, S, BF16)
    t_offs = sb("offs", 1, NT * E + E, F32)
    t_allc = sb("allc", E, E, F32)
    t_selc = sb("selcb", E, 1, F32)
    t_ones = sb("onesf", 128, 1, F32)
    t_onesb = sb("onesbf", 128, 1, BF16)
    t_oner = sb("onerf", 1, 128, F32)
    t_onerb = sb("onerbf", 1, 128, BF16)
    t_nB = sb("negB", 128, KC, F32)
    t_sc = sb("smallc", 128, max(2 * NT, 20), F32)
    t_eps = sb("epsc", 128, 1, F32)
    nc.const_aps.register(F32, EPS, t_eps[:])
    t_sr = sb("smallr", 1, 32, F32)
    t_rec = sb("recrow", 1, 512, F32)

    ps_st = es.enter_context(nc.psum_tensor("ps_st", [128, 1024], F32))
    ps_ctx = es.enter_context(nc.psum_tensor("ps_ctx", [128, 2048], F32))
    ps_a = es.enter_context(nc.psum_tensor("ps_a", [128, 512], F32))
    ps_b = es.enter_context(nc.psum_tensor("ps_b", [128, 512], F32))

    sch = Sched(nc, es)
    OP = sch.op

    def finish():
        with nc.allow_low_precision(reason="bf16 by design"):
            with nc.Block("main") as block:
                sch.replay(block)
        es.close()
        return nc

    def dbg_finish(stores):
        keys = []
        for idx, (o, i_, rd) in enumerate(stores):
            k = f"dbgout{idx}"
            OP("gpsimd", lambda e, o=o, i_=i_: e.dma_start(out=o, in_=i_),
               reads=rd, writes=[k], kind="d", dgrp="dbg")
            keys.append(k)
        sch.barrier_wait("gpsimd", keys)
        return finish()

    for hh, tt, key in ((h_idn, t_idn, "idn"), (h_prm, t_prm, "prm"),
                        (h_mskT, t_mskT, "mskT"), (h_ltri, t_ltri, "ltri"),
                        (h_sel, t_sel, "sel"), (h_n1, t_n1, "n1"),
                        (h_n2, t_n2, "n2"), (h_cos, t_cos, "cos"),
                        (h_sin, t_sin, "sin"), (h_selc, t_selc, "selc")):
        OP("sync", lambda e, hh=hh, tt=tt: e.dma_start(out=tt[:], in_=hh[:]),
           writes=[key], kind="d", dgrp=key)
    OP("sync", lambda e: e.dma_start(
        out=t_wg.ap().rearrange("p (kc e) -> p kc e", e=E),
        in_=h_wg.rearrange("(kc p) e -> p kc e", p=128)), writes=["wg"], kind="d", dgrp="wg")
    OP("gpsimd", lambda e: e.dma_start(out=t_be[:], in_=h_be[:]), writes=["be"], kind="d", dgrp="be")
    OP("vector", lambda e: e.memset(t_ones[:], 1.0), writes=["ones"])
    OP("vector", lambda e: e.memset(t_onesb[:], 1.0), writes=["onesb"])
    OP("vector", lambda e: e.memset(t_oner[:], 1.0), writes=["oner"])
    OP("vector", lambda e: e.memset(t_onerb[:], 1.0), writes=["onerb"])
    OP("vector", lambda e: e.memset(t_rows[0:1, 0:512], 0.0), writes=["cpad", "rows"])
    OP("vector", lambda e: e.memset(t_eps[:], EPS), writes=["epsc"])

    xtm = (t_f32[:, 0:1024], t_f32[:, 1024:2048])
    sqdump = t_f32[:, 2048:3072]
    C_S1, C_S2, C_MEAN, C_RSTD, C_RSTD1 = 0, NT, 2 * NT, 3 * NT, 4 * NT
    for i in range(NT):
        xb, key = xtm[i % 2], f"xtm{i % 2}"
        OP("sync", lambda e, xb=xb, i=i: e.dma_start(
            out=xb[:], in_=h_xtm[i * 128:(i + 1) * 128, :]), writes=[key, "f32a"],
            kind="d", dgrp=key)
        OP("vector", lambda e, xb=xb, i=i: e.reduce_sum(
            t_stats[:, C_S1 + i:C_S1 + i + 1], xb[:], axis=AX),
            reads=[key], writes=["stats"])
        OP("scalar", lambda e, xb=xb, i=i: e.activation(
            sqdump[:], xb[:], AF.Square, accum_out=t_stats[:, C_S2 + i:C_S2 + i + 1]),
            reads=[key], writes=["stats", "f32a"])
    S1 = t_stats[:, C_S1:C_S1 + NT]
    S2 = t_stats[:, C_S2:C_S2 + NT]
    MEAN = t_stats[:, C_MEAN:C_MEAN + NT]
    RSTD = t_stats[:, C_RSTD:C_RSTD + NT]
    RSTD1 = t_stats[:, C_RSTD1:C_RSTD1 + NT]
    TV = t_sc[:, 0:NT]
    SD = t_sc[:, NT:2 * NT]
    OP("vector", lambda e: e.tensor_scalar_mul(MEAN, S1, 1.0 / D), reads=["stats"], writes=["stats"])
    OP("scalar", lambda e: e.activation(TV, MEAN, AF.Square), reads=["stats"], writes=["tsc"])
    OP("vector", lambda e: e.scalar_tensor_tensor(TV, S2, 1.0 / D, TV, ALU.mult, ALU.subtract),
       reads=["stats", "tsc"], writes=["tsc"])
    OP("scalar", lambda e: e.activation(SD, TV, AF.Sqrt, bias=EPS),
       reads=["tsc", "epsc"], writes=["tsc"])
    OP("vector", lambda e: e.reciprocal(RSTD, SD), reads=["tsc"], writes=["stats"])
    OP("scalar", lambda e: e.activation(RSTD1, RSTD, AF.Square), reads=["stats"], writes=["stats"])
    OP("vector", lambda e: e.tensor_mul(RSTD1, RSTD1, TV), reads=["stats", "tsc"], writes=["stats"])
    OP("scalar", lambda e: e.activation(RSTD1, RSTD1, AF.Sqrt, bias=EPS),
       reads=["stats", "epsc"], writes=["stats"])
    OP("vector", lambda e: e.reciprocal(RSTD1, RSTD1), reads=["stats"], writes=["stats"])
    OP("tensor", lambda e: e.transpose(ps_a[0:3 * NT, 0:128],
                                       t_stats[:, C_MEAN:C_MEAN + 3 * NT], t_idn[:]),
       reads=["stats", "idn"], writes=["ps:A"])
    OP("scalar", lambda e: e.copy(t_statT[:], ps_a[0:3 * NT, 0:128]),
       reads=["ps:A"], writes=["statT"])

    xn32 = t_f32[:, 0:KC * TW]
    xping = (t_f32[:, KC * TW:KC * TW + TW], t_f32[:, KC * TW + TW:KC * TW + 2 * TW])
    mean_b = t_f32b[:, 0:TW]
    rstd_b = t_f32b[:, TW:2 * TW]
    rstd1_b = t_f32b[:, 2 * TW:3 * TW]
    rstd2_b = t_f32b[:, 0:TW]

    for tau in range(TAU):
        t0 = tau * TW
        for si, dst, bk in ((0, mean_b, "bc0"), (1, rstd_b, "bc1"), (2, rstd1_b, "bc2")):
            g0 = tau * GPT
            OP("sync", lambda e, si=si, g0=g0: e.dma_start(
                out=t_rows[0:1, 0:TW].rearrange("o (a b) -> o a b", b=128),
                in_=t_statT[si * NT + g0:si * NT + g0 + GPT, :]),
                reads=["statT"], writes=["rows"], kind="d", dgrp="rows")
            OP("tensor", lambda e, si=si: e.matmul(
                ps_b[:, 0:TW], t_oner[:], t_rows[0:1, 0:TW],
                start=True, stop=True),
                reads=["rows", "oner"], writes=["ps:B"])
            OP("scalar", lambda e, dst=dst: e.copy(dst, ps_b[:, 0:TW]),
               reads=["ps:B"], writes=[bk])
        for kc in range(KC):
            xp, xk = xping[kc % 2], f"xp{kc % 2}"
            OP("sync", lambda e, xp=xp, kc=kc, t0=t0: e.dma_start(
                out=xp[:], in_=h_xfm[kc * 128:(kc + 1) * 128, t0:t0 + TW]),
                writes=[xk, "f32a"], kind="d", dgrp=xk)
            xnk = xn32[:, kc * TW:(kc + 1) * TW]
            OP("vector", lambda e, xp=xp, xnk=xnk: e.tensor_sub(xnk, xp[:], mean_b),
               reads=[xk, "bc0"], writes=[f"xn32:{kc}", "f32a"])
            OP("vector", lambda e, xnk=xnk: e.tensor_mul(xnk, xnk, rstd_b),
               reads=[f"xn32:{kc}", "bc1"], writes=[f"xn32:{kc}"])
            OP("scalar", lambda e, xnk=xnk, kc=kc, t0=t0: e.copy(
                t_xn[:, kc * S + t0:kc * S + t0 + TW], xnk),
                reads=[f"xn32:{kc}"], writes=[f"xn:{kc}:{tau}"])
            OP("vector", lambda e, xnk=xnk, kc=kc, t0=t0: e.scalar_tensor_tensor(
                t_A[:, kc * S + t0:kc * S + t0 + TW], xnk, t_n1[:, kc:kc + 1],
                rstd1_b, ALU.mult, ALU.mult),
                reads=[f"xn32:{kc}", "bc2"], sreads=["n1"], writes=[f"A:{kc}:{tau}"])
        for g in range(GPT):
            gg = tau * GPT + g
            for kc in range(KC):
                OP("tensor", lambda e, kc=kc, g=g: e.matmul(
                    ps_a[:, 0:E],
                    xn32[:, kc * TW + g * 128:kc * TW + g * 128 + 128],
                    t_wg[:, kc * E:(kc + 1) * E],
                    start=(kc == 0), stop=(kc == KC - 1)),
                    reads=[f"xn32:{kc}", "wg"], writes=["ps:A"])
            m8, nm8, sume = t_sc[:, 0:1], t_sc[:, 1:2], t_sc[:, 2:3]
            lgs = t_sc[:, 4:12]
            dmp = t_sc[:, 12:20]
            OP("vector", lambda e, lgs=lgs: e.tensor_copy(lgs, ps_a[:, 0:E]),
               reads=["ps:A"], writes=["lgs", "tsc"])
            OP("vector", lambda e, lgs=lgs: e.reduce_max(m8, lgs, axis=AX),
               reads=["lgs"], writes=["m8", "tsc"])
            OP("vector", lambda e, gg=gg, lgs=lgs: e.tensor_single_scalar(
                t_mask[:, gg * E:(gg + 1) * E], lgs, m8, ALU.is_ge),
                reads=["lgs"], sreads=["m8"], writes=["maskal"])
            OP("vector", lambda e: e.tensor_scalar_mul(nm8, m8, -1.0),
               reads=["m8"], writes=["nm8", "tsc"])
            OP("scalar", lambda e, dmp=dmp, lgs=lgs: e.activation(
                dmp, lgs, AF.Exp, bias=nm8, accum_out=sume),
                reads=["lgs"], sreads=["nm8"], writes=["sume", "tsc"])
            OP("vector", lambda e, gg=gg: e.reciprocal(t_gate[:, gg:gg + 1], sume),
               reads=["sume"], writes=["gateal"])
            if upto == 2:
                OP("vector", lambda e, gg=gg, lgs=lgs: e.tensor_copy(
                    t_kept[:, gg * E:(gg + 1) * E], lgs),
                    reads=["lgs"], writes=["keptal"])
                OP("vector", lambda e, gg=gg: e.tensor_copy(
                    t_gall[:, gg * E:gg * E + 1], m8),
                    sreads=["m8"], writes=["gall"])

    if upto == 1:
        return dbg_finish(
            [(h_out[kc * 128:(kc + 1) * 128, :], t_xn[:, kc * S:(kc + 1) * S],
              [f"xn:{kc}:{tt}" for tt in range(TAU)]) for kc in range(KC)])

    OP("vector", lambda e: e.memset(t_offs[0:1, 0:E], 0.0), writes=["offr"])
    for g in range(NT):
        if g > 0:
            OP("vector", lambda e, g=g: e.tensor_copy(
                t_offs[0:1, g * E:(g + 1) * E], ps_b[0:1, 0:E]),
                reads=["ps:B"], writes=["offr"])
        OP("tensor", lambda e, g=g: e.matmul(
            ps_b[0:1, 0:E], t_ones[:], t_mask[:, g * E:(g + 1) * E],
            start=(g == 0), stop=(g == NT - 1)),
            reads=["maskal", "ones", "offr"], writes=["ps:B"])
    OP("vector", lambda e: e.tensor_copy(t_rows[0:1, 0:E], ps_b[0:1, 0:E]),
       reads=["ps:B", "cpad"], writes=["cpad", "rows"])
    OP("sync", lambda e: e.dma_start(out=d_ccin[:], in_=t_rows[0:1, 0:512]),
       reads=["cpad", "rows"], writes=["ccin"], kind="d", dgrp="ccin")
    OP("gpsimd", lambda e: e.collective_compute(
        "AllGather", ALU.bypass, replica_groups=[list(range(8))],
        ins=[d_ccin[:]], outs=[d_ccout[:]]),
        reads=["ccin"], writes=["ccout"], kind="c")

    def emit_offsets():
        OP("gpsimd", lambda e: e.dma_start(out=t_allc[:], in_=d_ccout[:, 0:E]),
           reads=["ccout"], writes=["allc"], kind="d", dgrp="allc")
        OP("tensor", lambda e: e.matmul(ps_b[0:1, 0:E], t_selc[:], t_allc[:],
                                        start=True, stop=True),
           reads=["allc", "selc"], writes=["ps:B"])
        OP("scalar", lambda e: e.copy(t_offs[0:1, NT * E:NT * E + E],
                                      ps_b[0:1, 0:E]),
           reads=["ps:B"], writes=["coff"])
        for g in range(NT):
            OP("vector", lambda e, g=g: e.tensor_add(
                t_offs[0:1, g * E:(g + 1) * E], t_offs[0:1, g * E:(g + 1) * E],
                t_offs[0:1, NT * E:NT * E + E]),
                reads=["offr", "coff"], writes=["offr"])

    if upto == 2:
        emit_offsets()
        return dbg_finish([
            (h_out[0:128, 0:NT * E], t_mask[:], ["maskal"]),
            (h_out[128:256, 0:NT], t_gate[:], ["gateal"]),
            (h_out[256:257, 0:NT * E], t_offs[0:1, 0:NT * E], ["offr"]),
            (h_out[384:512, 0:NT * E], t_kept[:], ["keptal"]),
            (h_out[512:640, 0:NT * E], t_gall[:], ["gall"]),
        ])

    def wr(slot, off, n):
        return t_wr[:, slot * 4096 + off:slot * 4096 + off + n]

    WRALL = lambda s: [f"wr{s}a", f"wr{s}b", f"wr{s}c", f"wr{s}d"]

    pctr = {"ab": 0, "st": 0}

    def nextab():
        pctr["ab"] += 1
        return (ps_a, "ps:A") if pctr["ab"] % 2 == 0 else (ps_b, "ps:B")

    def nextst():
        pctr["st"] += 1
        b = pctr["st"] % 2
        return b, ps_st[:, b * 512:(b + 1) * 512], f"ps:st{b}"

    def proj_rope(c, reg, woff, hw, dst, dkey, scale):
        slot = c % 2
        wsb = wr(slot, woff, KC * 128)
        wkey = f"wr{slot}{reg}"
        OP("gpsimd", lambda e, wsb=wsb: e.dma_start(
            out=wsb.rearrange("p (kc f) -> p kc f", f=128),
            in_=hw[c].rearrange("(kc p) f -> p kc f", p=128)),
            writes=[wkey], kind="d", dgrp=wkey)
        for tau in range(TAU):
            t0 = tau * TW
            pse, pk = nextab()
            for kc in range(KC):
                OP("tensor", lambda e, kc=kc, t0=t0, wsb=wsb, pse=pse: e.matmul(
                    pse[:, 0:TW], wsb[:, kc * 128:(kc + 1) * 128],
                    t_A[:, kc * S + t0:kc * S + t0 + TW],
                    start=(kc == 0), stop=(kc == KC - 1)),
                    reads=[wkey, f"A:{kc}:{tau}"], writes=[pk])
            qr, qk = t_pg[:, (tau % 2) * TW:(tau % 2 + 1) * TW], f"qraw{tau % 2}"
            OP("scalar", lambda e, qr=qr, pse=pse: e.copy(qr, pse[:, 0:TW]),
               reads=[pk], writes=[qk])
            sb_, sps, spk = nextst()
            OP("tensor", lambda e, qr=qr, sps=sps: e.matmul(
                sps[:, 0:TW], t_prm[:], qr, start=True, stop=True),
                reads=[qk, "prm"], writes=[spk])
            tt1 = t_rope[:, (tau % 2) * TW:(tau % 2 + 1) * TW]
            tk1 = f"tt1{tau % 2}"
            OP("vector", lambda e, qr=qr, t0=t0, tt1=tt1: e.scalar_tensor_tensor(
                tt1, qr, scale, t_cos[:, t0:t0 + TW], ALU.mult, ALU.mult),
                reads=[qk, "cos"], writes=[tk1])
            OP("vector", lambda e, t0=t0, dst=dst, sps=sps, tt1=tt1: e.scalar_tensor_tensor(
                dst[:, t0:t0 + TW], sps[:, 0:TW], scale, t_sin[:, t0:t0 + TW],
                ALU.mult, ALU.mult),
                reads=[spk, "sin"], writes=["tt2s"])
            OP("vector", lambda e, t0=t0, dst=dst, tt1=tt1: e.tensor_add(
                dst[:, t0:t0 + TW], dst[:, t0:t0 + TW], tt1),
                reads=["tt2s", tk1], writes=[dkey])

    def build_V(c):
        slot = c % 2
        wsb = wr(slot, 2048, KC * 128)
        wkey = f"wr{slot}c"
        OP("gpsimd", lambda e, c=c, wsb=wsb: e.dma_start(
            out=wsb.rearrange("p (kc f) -> p kc f", f=128),
            in_=h_wv[c].rearrange("(kc p) f -> p kc f", p=128)),
            writes=[wkey], kind="d", dgrp=wkey)
        if c == 0:
            OP("vector", lambda e: e.memset(
                t_C[:, 0:NT * 130].rearrange("p (kt h z) -> p kt h z", h=2, z=65)[:, :, :, 64],
                1.0), writes=["C"])
        for g4 in range((NT + 3) // 4):
            tts = list(range(4 * g4, min(4 * g4 + 4, NT)))
            pse, pk = nextab()
            for i, tt in enumerate(tts):
                for kc in range(KC):
                    OP("tensor", lambda e, tt=tt, i=i, kc=kc, wsb=wsb, pse=pse: e.matmul(
                        pse[:, i * 128:(i + 1) * 128],
                        t_A[:, kc * S + tt * 128:kc * S + (tt + 1) * 128],
                        wsb[:, kc * 128:(kc + 1) * 128],
                        start=(kc == 0 and i == 0), stop=(kc == KC - 1)),
                        reads=[wkey, f"A:{kc}:{(tt * 128) // TW}"], writes=[pk])
            n = len(tts)
            OP("vector", lambda e, tts=tts, n=n, c=c, pse=pse: e.tensor_copy(
                t_C[:, 0:NT * 130].rearrange("p (kt h z) -> p kt h z", h=2, z=65)[
                    :, tts[0]:tts[0] + n, :, 0:64],
                pse.ap().rearrange("p (i h z) -> p i h z", h=2, z=64)[:, 0:n, :, :]),
                reads=[pk], writes=["C"])

    def bound_B(c):
        sqb = t_xne[:, 0:S]
        for wi, (which, buf, bk) in enumerate((("q", t_q, "qrope"), ("k", t_k, "krope"))):
            OP("scalar", lambda e, buf=buf: e.activation(sqb, buf[:], AF.Square),
               reads=[bk], writes=["sqb", "xnereg"])
            for tau in range(TAU):
                OP("tensor", lambda e, tau=tau: e.matmul(
                    ps_b[0:1, 0:TW], t_onesb[:], sqb[:, tau * TW:(tau + 1) * TW],
                    start=True, stop=True),
                    reads=["sqb", "xnereg", "onesb"], writes=["ps:B"])
                OP("vector", lambda e, tau=tau, wi=wi: e.reduce_max(
                    t_sr[0:1, wi * 8 + tau:wi * 8 + tau + 1], ps_b[0:1, 0:TW], axis=AX),
                    reads=["ps:B"], writes=[f"nm{wi}", "tsr"])
        OP("vector", lambda e: e.reduce_max(t_sr[0:1, 16:17], t_sr[0:1, 0:TAU], axis=AX),
           sreads=["nm0"], writes=["tsr"])
        OP("vector", lambda e: e.reduce_max(t_sr[0:1, 17:18], t_sr[0:1, 8:8 + TAU], axis=AX),
           sreads=["nm1"], writes=["tsr"])
        OP("vector", lambda e: e.tensor_mul(t_sr[0:1, 18:19], t_sr[0:1, 16:17], t_sr[0:1, 17:18]),
           sreads=["tsr"], writes=["tsr"])
        OP("scalar", lambda e: e.activation(t_sr[0:1, 19:20], t_sr[0:1, 18:19], AF.Sqrt,
                                            scale=1.05),
           reads=["tsr"], writes=["bb"])
        OP("tensor", lambda e: e.matmul(ps_b[:, 0:1], t_oner[:], t_sr[0:1, 19:20],
                                        start=True, stop=True),
           reads=["bb", "oner"], writes=["ps:B"])
        OP("scalar", lambda e, c=c: e.activation(t_nB[:, c:c + 1], ps_b[:, 0:1],
                                                 AF.Copy, scale=-1.0),
           reads=["ps:B"], writes=["negB"])

    def attention_head(c, half):
        r0 = 64 * half
        h = 2 * c + half

        def emit_ctx(kt, batch, bank):
            for i, qi in enumerate(batch):
                OP("tensor", lambda e, kt=kt, qi=qi, i=i, bank=bank, h=h: e.matmul(
                    ps_ctx[0:65, qi * 128:(qi + 1) * 128],
                    t_C[:, kt * 130 + 65 * (h % 2):kt * 130 + 65 * (h % 2) + 65],
                    t_pt[:, bank * 512 + i * 128:bank * 512 + (i + 1) * 128],
                    start=(kt == 0 and qi % 4 == 0), stop=(kt == qi)),
                    reads=[f"pt{bank}", "C"], writes=[f"ps:ctx{qi // 4}"])

        pend = None
        for kt in range(NT):
            qis = list(range(kt, NT))
            for bi in range(0, len(qis), 4):
                batch = qis[bi:bi + 4]
                bank, sps, stk = nextst()
                for i, qi in enumerate(batch):
                    OP("tensor", lambda e, kt=kt, qi=qi, i=i, sps=sps, r0=r0: e.matmul(
                        sps[:, i * 128:(i + 1) * 128],
                        t_k[r0:r0 + 64, kt * 128:(kt + 1) * 128],
                        t_q[r0:r0 + 64, qi * 128:(qi + 1) * 128],
                        start=True, stop=True),
                        reads=["krope", "qrope"], writes=[stk])
                    if qi == kt:
                        OP("vector", lambda e, i=i, sps=sps: e.tensor_add(
                            sps[:, i * 128:(i + 1) * 128],
                            sps[:, i * 128:(i + 1) * 128],
                            t_mskT[:]),
                            reads=[stk, "mskT"], writes=[stk])
                nb = len(batch)
                OP("scalar", lambda e, nb=nb, bank=bank, sps=sps, c=c: e.activation(
                    t_pt[:, bank * 512:bank * 512 + nb * 128],
                    sps[:, 0:nb * 128],
                    AF.Exp, bias=t_nB[:, c:c + 1]),
                    reads=[stk], sreads=["negB"], writes=[f"pt{bank}"])
                if pend is not None:
                    emit_ctx(*pend)
                pend = (kt, batch, bank)
        emit_ctx(*pend)

    def attn_evac(c, half):
        kc = c
        for q0 in range(0, NT, 4):
            n = min(4, NT - q0)
            OP("scalar", lambda e, q0=q0, n=n: e.activation(
                t_rows[0:1, 0:n * 128], ps_ctx[64:65, q0 * 128:(q0 + n) * 128],
                AF.Ln),
                reads=[f"ps:ctx{q0 // 4}"] + (
                    [f"ps:ctx{(q0 + n - 1) // 4}"] if (q0 + n - 1) // 4 != q0 // 4 else []),
                writes=["rows"])
            OP("scalar", lambda e, n=n: e.activation(
                t_rec[0:1, 0:n * 128], t_rows[0:1, 0:n * 128], AF.Exp, scale=-1.0),
                reads=["rows"], writes=["rec"])
            bank, sps, stk = nextst()
            OP("tensor", lambda e, n=n, sps=sps: e.matmul(
                sps[0:64, 0:n * 128], t_oner[0:1, 0:64],
                t_rec[0:1, 0:n * 128], start=True, stop=True),
                reads=["rec", "oner"], writes=[stk])
            OP("scalar", lambda e, n=n, sps=sps: e.copy(t_gb[:, 0:n * 128],
                                                        sps[0:64, 0:n * 128]),
               reads=[stk], writes=["gbs"])
            tau = (q0 * 128) // TW
            bkeys = sorted({f"B:{kc}:{((q0 + i) * 128) // TW}" for i in range(n)})
            OP("vector", lambda e, q0=q0, n=n, half=half, kc=kc: e.tensor_mul(
                t_B[:, kc * S + q0 * 128:kc * S + (q0 + n) * 128][64 * half:64 * half + 64, :],
                ps_ctx[0:64, q0 * 128:(q0 + n) * 128],
                t_gb[:, 0:n * 128]),
                reads=[f"ps:ctx{q0 // 4}", "gbs"], writes=bkeys)

    for c in range(KC):
        proj_rope(c, "a", 0, h_wq, t_q, "qrope", 0.125)
        proj_rope(c, "b", 1024, h_wk, t_k, "krope", 1.0)
        build_V(c)
        bound_B(c)
        if c > 0:
            attn_evac(c - 1, 1)
        attention_head(c, 0)
        attn_evac(c, 0)
        attention_head(c, 1)
    attn_evac(KC - 1, 1)

    if upto == 3:
        return dbg_finish(
            [(h_out[kc * 128:(kc + 1) * 128, :], t_B[:, kc * S:(kc + 1) * S],
              [f"B:{kc}:{tt}" for tt in range(TAU)]) for kc in range(KC)])

    y_sl = t_f32[:, 0:KC * TW]
    for half in range(2):
        OP("gpsimd", lambda e, half=half: e.dma_start(
            out=wr(half, 0, 4096).rearrange("p (kc f) -> p kc f", f=512),
            in_=h_wo[half].rearrange("(kc p) f -> p kc f", p=128)),
            writes=[f"wr{half}a", f"wr{half}b", f"wr{half}c"], kind="d",
            dgrp=f"wr{half}a")
    for tau in range(TAU):
        t0 = tau * TW
        for ic in range(KC):
            half, iw = ic // 4, ic % 4
            pse, pk = (ps_a, "ps:A") if ic % 2 == 0 else (ps_b, "ps:B")
            for kc in range(KC):
                OP("tensor", lambda e, half=half, iw=iw, kc=kc, t0=t0, pse=pse: e.matmul(
                    pse[:, 0:TW],
                    wr(half, kc * 512 + iw * 128, 128),
                    t_B[:, kc * S + t0:kc * S + t0 + TW],
                    start=(kc == 0), stop=(kc == KC - 1)),
                    reads=[f"wr{half}a", f"B:{kc}:{tau}"], writes=[pk])
            OP("vector", lambda e, ic=ic, t0=t0, pse=pse, pk=pk: e.tensor_add(
                y_sl[:, ic * TW:(ic + 1) * TW], pse[:, 0:TW],
                t_xn[:, ic * S + t0:ic * S + t0 + TW]),
                reads=[pk, f"xn:{ic}:{tau}"], writes=[f"y:{ic}", "f32a"])
        for ic in range(KC):
            si = t_sil[:, 0:TW]
            OP("scalar", lambda e, ic=ic, si=si: e.activation(
                si, y_sl[:, ic * TW:(ic + 1) * TW], AF.Square),
                reads=[f"y:{ic}"], writes=["ysq", "silreg"])
            OP("tensor", lambda e, ic=ic, si=si: e.matmul(
                ps_st[0:1, 0:TW], t_onesb[:], si,
                start=(ic == 0), stop=(ic == KC - 1)),
                reads=["ysq", "silreg", "onesb"], writes=["ps:st0"])
        rrow = t_rows[0:1, 0:TW]
        OP("scalar", lambda e: e.activation(rrow, ps_st[0:1, 0:TW], AF.Sqrt,
                                            scale=1.0 / D, bias=EPS),
           reads=["ps:st0", "epsc"], writes=["r2row", "rows"])
        OP("vector", lambda e: e.reciprocal(rrow, rrow), reads=["r2row"], writes=["r2row"])
        OP("tensor", lambda e: e.matmul(ps_st[:, 0:TW], t_oner[:], rrow,
                                        start=True, stop=True),
           reads=["r2row", "oner"], writes=["ps:st0"])
        OP("scalar", lambda e: e.copy(rstd2_b, ps_st[:, 0:TW]),
           reads=["ps:st0"], writes=["bc3", "bc0"])
        for ic in range(KC):
            OP("vector", lambda e, ic=ic, t0=t0: e.scalar_tensor_tensor(
                t_A[:, ic * S + t0:ic * S + t0 + TW],
                y_sl[:, ic * TW:(ic + 1) * TW], t_n2[:, ic:ic + 1], rstd2_b,
                ALU.mult, ALU.mult),
                reads=[f"y:{ic}", "bc3"], sreads=["n2"], writes=[f"A:{ic}:{tau}"])
            xp, xk = xping[ic % 2], f"xp{ic % 2}"
            OP("sync", lambda e, ic=ic, t0=t0, xp=xp: e.dma_start(
                out=xp[:], in_=h_xfm[ic * 128:(ic + 1) * 128, t0:t0 + TW]),
                writes=[xk, "f32a"], kind="d", dgrp=xk)
            ev, ek = t_ev[:, (ic % 2) * TW:(ic % 2 + 1) * TW], f"ev{ic % 2}"
            OP("vector", lambda e, ic=ic, xp=xp, ev=ev: e.tensor_add(
                ev, y_sl[:, ic * TW:(ic + 1) * TW], xp[:]),
                reads=[f"y:{ic}", xk], writes=[ek])
            OP("gpsimd", lambda e, ic=ic, t0=t0, ev=ev: e.dma_start(
                out=h_out[ic * 128:(ic + 1) * 128, t0:t0 + TW], in_=ev),
                reads=[ek], writes=[f"out:{ic}:{tau}"], kind="d",
                dgrp=f"oute{ic % 2}")

    if upto == 4:
        outkeys = [f"out:{ic}:{tt}" for ic in range(KC) for tt in range(TAU)]
        sch.barrier_wait("sync", outkeys)
        return finish()

    emit_offsets()

    for g in range(NT):
        OP("tensor", lambda e, g=g: e.matmul(
            ps_a[:, g * E:(g + 1) * E], t_ltri[:], t_mask[:, g * E:(g + 1) * E],
            start=(g == 0), stop=False),
            reads=["ltri", "maskal"], writes=["ps:A"])
    OP("tensor", lambda e: e.matmul(
        ps_a[:, 0:NT * E], t_oner[:], t_offs[0:1, 0:NT * E], start=False, stop=True),
        reads=["oner", "offr"], writes=["ps:A"])
    OP("vector", lambda e: e.tensor_single_scalar(
        t_kept[:], ps_a[:, 0:NT * E], CAPTH, ALU.is_lt),
        reads=["ps:A"], writes=["keptal"])
    for g in range(NT):
        OP("vector", lambda e, g=g: e.scalar_tensor_tensor(
            t_gall[:, g * E:(g + 1) * E], t_mask[:, g * E:(g + 1) * E],
            t_gate[:, g:g + 1], t_kept[:, g * E:(g + 1) * E], ALU.mult, ALU.mult),
            reads=["maskal", "keptal"], sreads=["gateal"], writes=["gall"])
    for g in range(NT):
        OP("tensor", lambda e, g=g: e.transpose(
            ps_b[0:E, 0:128], t_gall[:, g * E:(g + 1) * E], t_idn[:]),
            reads=["gall", "idn"], writes=["ps:B"])
        OP("scalar", lambda e, g=g: e.copy(t_gt[:, g * 128:(g + 1) * 128],
                                           ps_b[0:E, 0:128]),
           reads=["ps:B"], writes=["gt"])
    if upto == 41:
        return dbg_finish([
            (h_out[0:128, 0:NT * E], t_kept[:], ["keptal"]),
            (h_out[128:256, 0:NT * E], t_gall[:], ["gall"]),
            (h_out[256:256 + E, 0:S], t_gt[:], ["gt"]),
        ])
    for ee in range(E):
        for tau in range(TAU):
            OP("tensor", lambda e, ee=ee, tau=tau: e.matmul(
                ps_b[:, 0:TW], t_sel[:, ee * 128:ee * 128 + 128],
                t_gt[:, tau * TW:(tau + 1) * TW], start=True, stop=True),
                reads=["sel", "gt"], writes=["ps:B"])
            OP("scalar", lambda e, ee=ee, tau=tau: e.copy(
                t_B[:, ee * S + tau * TW:ee * S + (tau + 1) * TW], ps_b[:, 0:TW]),
                reads=["ps:B"], writes=[f"B:{ee}:{tau}"])
    if upto == 43:
        return dbg_finish(
            [(h_out[ee * 128:ee * 128 + 128, 0:S], t_B[:, ee * S:(ee + 1) * S],
              [f"B:{ee}:{tt}" for tt in range(TAU)]) for ee in range(E)])

    def moe_psum(ic):
        if ic < 4:
            return ps_ctx[:, ic * 512:(ic + 1) * 512], f"ps:ctx{ic}"
        if ic < 6:
            return ps_st[:, (ic - 4) * 512:(ic - 3) * 512], f"ps:st{ic - 4}"
        return (ps_a, "ps:A") if ic == 6 else (ps_b, "ps:B")

    EMM = {45: 0, 46: 0, 47: 1, 48: 2, 49: 4}.get(upto, E)
    for tau in range(TAU):
        t0 = tau * TW
        for ee in range(E):
            for kc in range(KC):
                OP("vector", lambda e, ee=ee, kc=kc, t0=t0, tau=tau: e.tensor_mul(
                    t_xne[:, kc * TW:(kc + 1) * TW],
                    t_xn[:, kc * S + t0:kc * S + t0 + TW],
                    t_B[:, ee * S + t0:ee * S + t0 + TW]),
                    reads=[f"xn:{kc}:{tau}", f"B:{ee}:{tau}"],
                    writes=[f"xne:{kc}", "xnereg"])
            for hh in range(2):
                slot3 = (tau * E * 2 + ee * 2 + hh) % 3
                if slot3 < 2:
                    wbuf = wr(slot3, 0, 4096)
                    wks = [f"wr{slot3}a", f"wr{slot3}b", f"wr{slot3}c"]
                else:
                    wbuf = t_w2g[:, 0:4096]
                    wks = ["w2g0", "w2g1", "w2g2", "w2g3"]
                if upto != 45:
                    OP("gpsimd", lambda e, ee=ee, hh=hh, wbuf=wbuf: e.dma_start(
                        out=wbuf.rearrange("p (kc f) -> p kc f", f=512),
                        in_=h_we[ee, hh].rearrange("(kc p) f -> p kc f", p=128)),
                        writes=wks, kind="d", dgrp=wks[0])
                if ee >= EMM:
                    continue
                for kc in range(KC):
                    for iw in range(4):
                        ic = hh * 4 + iw
                        pse, pk = moe_psum(ic)
                        OP("tensor", lambda e, kc=kc, iw=iw, pse=pse, wbuf=wbuf,
                           ee=ee, hh=hh: e.matmul(
                            pse[:, 0:TW],
                            wbuf[:, kc * 512 + iw * 128:kc * 512 + (iw + 1) * 128],
                            t_xne[:, kc * TW:(kc + 1) * TW],
                            start=(ee == 0 and kc == 0), stop=False),
                            reads=[wks[0], f"xne:{kc}"], writes=[pk])
        if upto in (45, 46):
            continue
        for ic in range(KC):
            pse, pk = moe_psum(ic)
            OP("tensor", lambda e, ic=ic, tau=tau, pse=pse: e.matmul(
                pse[:, 0:TW], t_be[:, ic * 128:(ic + 1) * 128],
                t_gt[:, tau * TW:(tau + 1) * TW], start=False, stop=True),
                reads=["be", "gt"], writes=[pk])
            ev, ek = t_ev[:, (ic % 2) * TW:(ic % 2 + 1) * TW], f"ev{ic % 2}"
            OP("scalar", lambda e, pse=pse, ev=ev: e.copy(ev, pse[:, 0:TW]),
               reads=[pk], writes=[ek])
            if upto == 44:
                OP("gpsimd", lambda e, ic=ic, t0=t0, ev=ev: e.dma_start(
                    out=h_out[ic * 128:(ic + 1) * 128, t0:t0 + TW], in_=ev),
                    reads=[ek, f"out:{ic}:{tau}"], writes=[f"out:{ic}:{tau}"],
                    kind="d", dgrp=f"oute{ic % 2}")
            else:
                OP("gpsimd", lambda e, ic=ic, t0=t0, ev=ev: e.dma_start(
                    out=h_out[ic * 128:(ic + 1) * 128, t0:t0 + TW], in_=ev,
                    accum_op=ALU.add),
                    reads=[ek, f"out:{ic}:{tau}"], writes=[f"out:{ic}:{tau}"],
                    kind="d", dgrp=f"oute{ic % 2}")

    if upto in (45, 46):
        return dbg_finish([(h_out[0:128, 0:TW], t_xne[:, 0:TW], ["xne:0"])])
    if upto in (5, 44, 47, 48, 49):
        outkeys = [f"out:{ic}:{tt}" for ic in range(KC) for tt in range(TAU)]
        sch.barrier_wait("sync", outkeys)
        sch.barrier_wait("gpsimd", outkeys)
        return finish()

    FG = 4
    ff_acc = t_f32[:, 0:KC * TW]
    for tau in range(TAU):
        t0 = tau * TW
        for fc in range(FC):
            slot = fc % 2
            OP("gpsimd", lambda e, fc=fc, slot=slot: e.dma_start(
                out=wr(slot, 0, KC * 128).rearrange("p (kc f) -> p kc f", f=128),
                in_=h_w1[fc].rearrange("(kc p) f -> p kc f", p=128)),
                writes=[f"wr{slot}a"], kind="d", dgrp=f"wr{slot}a")
            OP("gpsimd", lambda e, fc=fc, slot=slot: e.dma_start(
                out=wr(slot, 1024, KC * 128).rearrange("p (kc f) -> p kc f", f=128),
                in_=h_w3[fc].rearrange("(kc p) f -> p kc f", p=128)),
                writes=[f"wr{slot}b"], kind="d", dgrp=f"wr{slot}b")
            p1, p1k = (ps_a, "ps:A") if slot == 0 else (ps_st[:, 0:512], "ps:st0")
            p3, p3k = (ps_b, "ps:B") if slot == 0 else (ps_st[:, 512:1024], "ps:st1")
            for kc in range(KC):
                OP("tensor", lambda e, kc=kc, t0=t0, slot=slot, p1=p1: e.matmul(
                    p1[:, 0:TW], wr(slot, kc * 128, 128),
                    t_A[:, kc * S + t0:kc * S + t0 + TW],
                    start=(kc == 0), stop=(kc == KC - 1)),
                    reads=[f"wr{slot}a", f"A:{kc}:{tau}"], writes=[p1k])
            for kc in range(KC):
                OP("tensor", lambda e, kc=kc, t0=t0, slot=slot, p3=p3: e.matmul(
                    p3[:, 0:TW], wr(slot, 1024 + kc * 128, 128),
                    t_A[:, kc * S + t0:kc * S + t0 + TW],
                    start=(kc == 0), stop=(kc == KC - 1)),
                    reads=[f"wr{slot}b", f"A:{kc}:{tau}"], writes=[p3k])
            si, sk = t_sil[:, slot * TW:(slot + 1) * TW], f"sil{slot}"
            OP("scalar", lambda e, si=si, p1=p1: e.activation(si, p1[:, 0:TW], AF.Silu),
               reads=[p1k], writes=[sk, "silreg"])
            j = fc % FG
            pgt, pgk = t_pg[:, j * TW:(j + 1) * TW], f"pg{j}"
            OP("vector", lambda e, si=si, p3=p3, pgt=pgt: e.tensor_mul(
                pgt, si, p3[:, 0:TW]),
                reads=[sk, p3k], writes=[pgk])
            OP("gpsimd", lambda e, fc=fc, j=j: e.dma_start(
                out=t_w2g[:, j * D:(j + 1) * D], in_=h_w2[fc]),
                writes=[f"w2g{j}"], kind="d", dgrp=f"w2g{j}")
            if j == FG - 1:
                fg0 = fc - FG + 1
                for oc in range(KC):
                    fps = ps_ctx[:, (oc % 2) * 512:(oc % 2) * 512 + TW]
                    fpk = f"ps:ctx{oc % 2}"
                    for jj in range(FG):
                        OP("tensor", lambda e, jj=jj, oc=oc, fps=fps: e.matmul(
                            fps, t_w2g[:, jj * D + oc * 128:jj * D + (oc + 1) * 128],
                            t_pg[:, jj * TW:(jj + 1) * TW],
                            start=(jj == 0), stop=(jj == FG - 1)),
                            reads=[f"w2g{jj}", f"pg{jj}"], writes=[fpk])
                    if fg0 == 0:
                        OP("vector", lambda e, oc=oc, fps=fps: e.tensor_copy(
                            ff_acc[:, oc * TW:(oc + 1) * TW], fps),
                            reads=[fpk], writes=[f"y:{oc}"])
                    else:
                        OP("vector", lambda e, oc=oc, fps=fps: e.tensor_add(
                            ff_acc[:, oc * TW:(oc + 1) * TW],
                            ff_acc[:, oc * TW:(oc + 1) * TW], fps),
                            reads=[fpk, f"y:{oc}"], writes=[f"y:{oc}"])
        for oc in range(KC):
            acc_op = {61: ALU.bypass}.get(upto, ALU.add)
            OP("gpsimd", lambda e, oc=oc, t0=t0, acc_op=acc_op: e.dma_start(
                out=h_out[oc * 128:(oc + 1) * 128, t0:t0 + TW],
                in_=ff_acc[:, oc * TW:(oc + 1) * TW], accum_op=acc_op),
                reads=[f"y:{oc}", f"out:{oc}:{tau}"],
                writes=[f"out:{oc}:{tau}", f"ffdma{oc % 2}"], kind="d",
                dgrp=f"oute{oc % 2}")

    outkeys = [f"out:{ic}:{tau}" for ic in range(KC) for tau in range(TAU)]
    sch.barrier_wait("gpsimd", outkeys)
    sch.barrier_wait("sync", outkeys)
    return finish()


_CACHE = {}


def _consts(S):
    DKH = 64
    pos = np.arange(S, dtype=np.float64)
    k = np.arange(DKH) // 2
    ang = pos[:, None] / np.power(10000.0, 2.0 * k / DKH)[None, :]
    cos = np.cos(ang)
    sin = np.sin(ang) * np.where(np.arange(DKH) % 2 == 0, -1.0, 1.0)[None, :]
    cosT = np.ascontiguousarray(np.tile(cos.T, (2, 1)).astype(BF))
    sinT = np.ascontiguousarray(np.tile(sin.T, (2, 1)).astype(BF))
    idn = np.eye(128, dtype=np.float32)
    prm = np.zeros((128, 128), dtype=np.float32)
    prm[np.arange(128), np.arange(128) ^ 1] = 1.0
    mskT = np.ascontiguousarray(
        np.where(np.arange(128)[:, None] > np.arange(128)[None, :], NEG, 0.0).astype(BF))
    ltri = np.ascontiguousarray(np.triu(np.ones((128, 128), dtype=np.float32), k=1))
    selm = np.zeros((E, E * 128), dtype=np.float32)
    for ee in range(E):
        selm[ee, ee * 128:(ee + 1) * 128] = 1.0
    return dict(cosT=cosT, sinT=sinT, idn=idn, prm=np.ascontiguousarray(prm.astype(BF)),
                mskT=mskT, ltri=ltri, selm=np.ascontiguousarray(selm.astype(BF)))


def prep_inputs(x, wq, wk, wv, wo, n1, n2, w1, w2, w3, wg, We, be, S):
    cst = _consts(S)
    f32 = np.float32

    def T(w):
        return np.ascontiguousarray(np.asarray(w, dtype=f32).T)

    def blocks(wT, n):
        return np.ascontiguousarray(wT.reshape(D, n, wT.shape[1] // n).transpose(1, 0, 2))

    wq_b = blocks(T(wq), KC)
    wk_b = blocks(T(wk), KC)
    wv_b = blocks(T(wv), KC)
    wo_b = blocks(T(wo), 2)
    w1_b = blocks(T(w1), FC)
    w3_b = blocks(T(w3), FC)
    w2_b = np.ascontiguousarray(T(w2).reshape(FC, 128, D))
    WeT = np.asarray(We, dtype=f32).transpose(0, 2, 1)
    we_b = np.ascontiguousarray(WeT.reshape(E, D, 2, 512).transpose(0, 2, 1, 3))
    n1c = np.ascontiguousarray(np.asarray(n1, dtype=f32).reshape(KC, 128).T)
    n2c = np.ascontiguousarray(np.asarray(n2, dtype=f32).reshape(KC, 128).T)
    base = dict(wq_b=wq_b, wk_b=wk_b, wv_b=wv_b, wo_b=wo_b, w1_b=w1_b,
                w3_b=w3_b, w2_b=w2_b, we_b=we_b,
                wg=np.ascontiguousarray(np.asarray(wg, dtype=f32)),
                be=np.ascontiguousarray(np.asarray(be, dtype=f32)),
                n1c=n1c, n2c=n2c, **cst)
    xa = np.asarray(x, dtype=f32)
    in_maps = []
    for b in range(xa.shape[0]):
        m = dict(base)
        m["x_tm"] = np.ascontiguousarray(xa[b])
        m["x_fm"] = np.ascontiguousarray(xa[b].T)
        m["selc"] = (np.arange(E) < b).astype(f32).reshape(E, 1)
        in_maps.append(m)
    return in_maps


def kernel(L0_attn_x, L0_attn_wq, L0_attn_wk, L0_attn_wv, L0_attn_wo,
           L0_attn_norm1_w, L0_attn_norm2_w, L0_attn_w1, L0_attn_w2,
           L0_attn_w3, L0_ffn_wg, L0_ffn_W, L0_ffn_b):
    import os
    upto = int(os.environ.get("K_UPTO", "9"))
    S = np.asarray(L0_attn_x).shape[1]
    key = (S, upto)
    if key not in _CACHE:
        _CACHE[key] = build_program(S, upto)
    nc = _CACHE[key]
    in_maps = prep_inputs(L0_attn_x, L0_attn_wq, L0_attn_wk, L0_attn_wv,
                          L0_attn_wo, L0_attn_norm1_w, L0_attn_norm2_w,
                          L0_attn_w1, L0_attn_w2, L0_attn_w3, L0_ffn_wg,
                          L0_ffn_W, L0_ffn_b, S)
    res = run_bass_kernel_spmd(nc, in_maps, core_ids=list(range(8)))
    out = np.stack([np.asarray(res.results[b]["out"]).T for b in range(8)])
    return np.ascontiguousarray(out.astype(np.float32))


# version 07
# speedup vs baseline: 1.3034x; 1.0241x over previous
import contextlib
import numpy as np
import ml_dtypes
import concourse.bass as bass
import concourse.mybir as mybir
from concourse.bass_utils import run_bass_kernel_spmd

F32 = mybir.dt.float32
BF16 = mybir.dt.bfloat16
AF = mybir.ActivationFunctionType
ALU = mybir.AluOpType
AX = mybir.AxisListType.X

D = 1024
KC = 8
DFF = 4096
FC = 32
E = 8
EPS = 1e-5
NEG = -1e9

BF = ml_dtypes.bfloat16


class Sched:
    ENGS = ("sync", "scalar", "vector", "tensor", "gpsimd")

    def __init__(self, nc, es):
        self.nc = nc
        self.es = es
        self.seq = {e: 0 for e in self.ENGS}
        self.cseq = 0
        self.sem = {e: es.enter_context(nc.semaphore(f"is_{e}")) for e in self.ENGS}
        self.csem = es.enter_context(nc.semaphore("cs"))
        self.dgrp = {}
        self.last_w = {}
        self.readers = {}
        self.watermark = {e: {} for e in self.ENGS}
        self.eng_ops = {e: [] for e in self.ENGS}

    def _group(self, name, queue):
        g = self.dgrp.get(name)
        if g is None:
            g = [self.es.enter_context(self.nc.semaphore(f"dg_{name}")), 0, queue]
            self.dgrp[name] = g
        assert g[2] == queue, f"dma group {name} used from {queue} and {g[2]}"
        return g

    def _need(self, eng, tok, waits, force=False, dma_queue=None):
        if tok is None:
            return
        kind = tok[0]
        if kind == "i":
            _, peng, val = tok
            if peng == eng and eng == "tensor" and not force:
                return
            sem, target, wkey = self.sem[peng], val, ("i", peng)
        elif kind == "d":
            _, grp, val = tok
            g = self.dgrp[grp]
            if dma_queue is not None and g[2] == dma_queue:
                return
            sem, target, wkey = g[0], 16 * val, ("d", grp)
        else:
            _, val = tok
            sem, target, wkey = self.csem, val, ("c",)
        wm = self.watermark[eng]
        if wm.get(wkey, 0) >= target:
            return
        wm[wkey] = target
        waits.append((sem, target))

    def op(self, eng, fn, reads=(), writes=(), kind="i", sreads=(), dgrp=None):
        waits = []
        dq = eng if kind == "d" else None
        for r in sreads:
            self._need(eng, self.last_w.get(r), waits, force=True, dma_queue=dq)
        for r in reads:
            self._need(eng, self.last_w.get(r), waits, dma_queue=dq)
        for w in writes:
            self._need(eng, self.last_w.get(w), waits, dma_queue=dq)
            for t in self.readers.get(w, ()):
                self._need(eng, t, waits, dma_queue=dq)
        if kind == "d":
            g = self._group(dgrp, eng)
            g[1] += 1
            tok = ("d", dgrp, g[1])
            inc = (g[0], 16)
        elif kind == "c":
            self.cseq += 1
            tok = ("c", self.cseq)
            inc = (self.csem, 1)
        else:
            self.seq[eng] += 1
            tok = ("i", eng, self.seq[eng])
            inc = (self.sem[eng], 1)
        self.eng_ops[eng].append((fn, waits, inc))
        for r in list(reads) + list(sreads):
            self.readers.setdefault(r, []).append(tok)
        for w in writes:
            self.last_w[w] = tok
            self.readers[w] = []
        return tok

    def barrier_wait(self, eng, keys):
        waits = []
        for k in keys:
            self._need(eng, self.last_w.get(k), waits)
            for t in self.readers.get(k, ()):
                self._need(eng, t, waits)
        if waits:
            self.eng_ops[eng].append((None, waits, None))

    def replay(self, block):
        for ename in self.ENGS:
            ops = self.eng_ops[ename]

            def make(op_list):
                def run(eng):
                    for fn, waits, inc in op_list:
                        for (s, tgt) in waits:
                            eng.wait_ge(s, tgt)
                        if fn is not None:
                            fn(eng).then_inc(*inc)
                return run

            getattr(block, ename)(make(ops))


def build_program(S, upto=9):
    assert S % 256 == 0
    TW = min(512, S)
    TAU = S // TW
    NT = S // 128
    GPT = TW // 128
    CAPTH = float(S) - 0.5

    nc = bass.Bass("TRN2", target_bir_lowering=False)

    h_xtm = nc.dram_tensor("x_tm", [S, D], F32, kind="ExternalInput")
    h_xfm = nc.dram_tensor("x_fm", [D, S], F32, kind="ExternalInput")
    h_wq = nc.dram_tensor("wq_b", [KC, D, 128], F32, kind="ExternalInput")
    h_wk = nc.dram_tensor("wk_b", [KC, D, 128], F32, kind="ExternalInput")
    h_wv = nc.dram_tensor("wv_b", [KC, D, 128], F32, kind="ExternalInput")
    h_wo = nc.dram_tensor("wo_b", [2, D, 512], F32, kind="ExternalInput")
    h_w1 = nc.dram_tensor("w1_b", [FC, D, 128], F32, kind="ExternalInput")
    h_w3 = nc.dram_tensor("w3_b", [FC, D, 128], F32, kind="ExternalInput")
    h_w2 = nc.dram_tensor("w2_b", [FC, 128, D], F32, kind="ExternalInput")
    h_we = nc.dram_tensor("we_b", [E, 2, D, 512], F32, kind="ExternalInput")
    h_wg = nc.dram_tensor("wg", [D, E], F32, kind="ExternalInput")
    h_be = nc.dram_tensor("be", [E, D], F32, kind="ExternalInput")
    h_n1 = nc.dram_tensor("n1c", [128, KC], F32, kind="ExternalInput")
    h_n2 = nc.dram_tensor("n2c", [128, KC], F32, kind="ExternalInput")
    h_cos = nc.dram_tensor("cosT", [128, S], BF16, kind="ExternalInput")
    h_sin = nc.dram_tensor("sinT", [128, S], BF16, kind="ExternalInput")
    h_idn = nc.dram_tensor("idn", [128, 128], F32, kind="ExternalInput")
    h_prm = nc.dram_tensor("prm", [128, 128], BF16, kind="ExternalInput")
    h_mskT = nc.dram_tensor("mskT", [128, 128], BF16, kind="ExternalInput")
    h_ltri = nc.dram_tensor("ltri", [128, 128], F32, kind="ExternalInput")
    h_sel = nc.dram_tensor("selm", [E, E * 128], BF16, kind="ExternalInput")
    h_selc = nc.dram_tensor("selc", [E, 1], F32, kind="ExternalInput")
    h_out = nc.dram_tensor("out", [D, S], F32, kind="ExternalOutput")
    d_ccin = nc.dram_tensor("cc_in", [1, 512], F32, kind="Internal")
    d_ccout = nc.dram_tensor("cc_out", [8, 512], F32, kind="Internal")

    es = contextlib.ExitStack()

    def sb(name, p, f, dt):
        return es.enter_context(nc.sbuf_tensor(name, [p, f], dt))

    t_xn = sb("xn", 128, KC * S, BF16)
    t_A = sb("uA", 128, KC * S, BF16)
    t_B = sb("uB", 128, KC * S, BF16)
    t_C = sb("uC", 128, NT * 130, BF16)
    t_q = sb("qbuf", 128, S, BF16)
    t_k = sb("kbuf", 128, S, BF16)
    t_xne = sb("xne", 128, max(KC * TW, S), BF16)
    t_wr = sb("wring", 128, 2 * 4096, BF16)
    t_cos = sb("cosb", 128, S, BF16)
    t_sin = sb("sinb", 128, S, BF16)
    t_f32 = sb("f32a", 128, max(KC * TW + 2 * TW, 3 * 1024), F32)
    t_f32b = sb("f32b", 128, 3 * TW, F32)
    t_pg = sb("pgr", 128, 4 * TW, BF16)
    t_w2g = sb("w2g", 128, 4 * D, BF16)
    t_rope = sb("ropes", 128, 2 * TW, BF16)
    t_pt = sb("ptb", 128, 1024, BF16)
    t_sil = sb("silb", 128, 2 * TW, BF16)
    t_ev = sb("evac", 128, 2 * TW, F32)
    t_gb = sb("gbrow", 64, 512, BF16)
    t_idn = sb("idnb", 128, 128, F32)
    t_prm = sb("prmb", 128, 128, BF16)
    t_mskT = sb("mskTb", 128, 128, BF16)
    t_ltri = sb("ltrib", 128, 128, F32)
    t_sel = sb("selb", E, E * 128, BF16)
    t_n1 = sb("n1b", 128, KC, F32)
    t_n2 = sb("n2b", 128, KC, F32)
    t_wg = sb("wgb", 128, KC * E, F32)
    t_be = sb("beb", E, D, BF16)
    t_stats = sb("statc", 128, 5 * NT, F32)
    t_statT = sb("statT", 3 * NT, 128, F32)
    t_rows = sb("rows", 1, max(TW, 512), F32)
    t_mask = sb("maskal", 128, NT * E, F32)
    t_gate = sb("gateal", 128, NT, F32)
    t_kept = sb("keptal", 128, NT * E, F32)
    t_gall = sb("gall", 128, NT * E, F32)
    t_gt = sb("gtb", E, S, BF16)
    t_offs = sb("offs", 1, NT * E + E, F32)
    t_allc = sb("allc", E, E, F32)
    t_selc = sb("selcb", E, 1, F32)
    t_ones = sb("onesf", 128, 1, F32)
    t_onesb = sb("onesbf", 128, 1, BF16)
    t_oner = sb("onerf", 1, 128, F32)
    t_onerb = sb("onerbf", 1, 128, BF16)
    t_nB = sb("negB", 128, KC, F32)
    t_sc = sb("smallc", 128, max(2 * NT, 20), F32)
    t_eps = sb("epsc", 128, 1, F32)
    nc.const_aps.register(F32, EPS, t_eps[:])
    t_sr = sb("smallr", 1, 32, F32)
    t_rec = sb("recrow", 1, 512, F32)

    ps_st = es.enter_context(nc.psum_tensor("ps_st", [128, 1024], F32))
    ps_ctx = es.enter_context(nc.psum_tensor("ps_ctx", [128, 2048], F32))
    ps_a = es.enter_context(nc.psum_tensor("ps_a", [128, 512], F32))
    ps_b = es.enter_context(nc.psum_tensor("ps_b", [128, 512], F32))

    sch = Sched(nc, es)
    OP = sch.op

    def finish():
        with nc.allow_low_precision(reason="bf16 by design"):
            with nc.Block("main") as block:
                sch.replay(block)
        es.close()
        return nc

    def dbg_finish(stores):
        keys = []
        for idx, (o, i_, rd) in enumerate(stores):
            k = f"dbgout{idx}"
            OP("gpsimd", lambda e, o=o, i_=i_: e.dma_start(out=o, in_=i_),
               reads=rd, writes=[k], kind="d", dgrp="dbg")
            keys.append(k)
        sch.barrier_wait("gpsimd", keys)
        return finish()

    for hh, tt, key in ((h_idn, t_idn, "idn"), (h_prm, t_prm, "prm"),
                        (h_mskT, t_mskT, "mskT"), (h_ltri, t_ltri, "ltri"),
                        (h_sel, t_sel, "sel"), (h_n1, t_n1, "n1"),
                        (h_n2, t_n2, "n2"), (h_cos, t_cos, "cos"),
                        (h_sin, t_sin, "sin"), (h_selc, t_selc, "selc")):
        OP("sync", lambda e, hh=hh, tt=tt: e.dma_start(out=tt[:], in_=hh[:]),
           writes=[key], kind="d", dgrp=key)
    OP("sync", lambda e: e.dma_start(
        out=t_wg.ap().rearrange("p (kc e) -> p kc e", e=E),
        in_=h_wg.rearrange("(kc p) e -> p kc e", p=128)), writes=["wg"], kind="d", dgrp="wg")
    OP("gpsimd", lambda e: e.dma_start(out=t_be[:], in_=h_be[:]), writes=["be"], kind="d", dgrp="be")
    OP("vector", lambda e: e.memset(t_ones[:], 1.0), writes=["ones"])
    OP("vector", lambda e: e.memset(t_onesb[:], 1.0), writes=["onesb"])
    OP("vector", lambda e: e.memset(t_oner[:], 1.0), writes=["oner"])
    OP("vector", lambda e: e.memset(t_onerb[:], 1.0), writes=["onerb"])
    OP("vector", lambda e: e.memset(t_rows[0:1, 0:512], 0.0), writes=["cpad", "rows"])
    OP("vector", lambda e: e.memset(t_eps[:], EPS), writes=["epsc"])

    xtm = (t_f32[:, 0:1024], t_f32[:, 1024:2048])
    sqdump = t_f32[:, 2048:3072]
    C_S1, C_S2, C_MEAN, C_RSTD, C_RSTD1 = 0, NT, 2 * NT, 3 * NT, 4 * NT
    for i in range(NT):
        xb, key = xtm[i % 2], f"xtm{i % 2}"
        OP("sync", lambda e, xb=xb, i=i: e.dma_start(
            out=xb[:], in_=h_xtm[i * 128:(i + 1) * 128, :]), writes=[key, "f32a"],
            kind="d", dgrp=key)
        OP("vector", lambda e, xb=xb, i=i: e.reduce_sum(
            t_stats[:, C_S1 + i:C_S1 + i + 1], xb[:], axis=AX),
            reads=[key], writes=["stats"])
        OP("scalar", lambda e, xb=xb, i=i: e.activation(
            sqdump[:], xb[:], AF.Square, accum_out=t_stats[:, C_S2 + i:C_S2 + i + 1]),
            reads=[key], writes=["stats", "f32a"])
    S1 = t_stats[:, C_S1:C_S1 + NT]
    S2 = t_stats[:, C_S2:C_S2 + NT]
    MEAN = t_stats[:, C_MEAN:C_MEAN + NT]
    RSTD = t_stats[:, C_RSTD:C_RSTD + NT]
    RSTD1 = t_stats[:, C_RSTD1:C_RSTD1 + NT]
    TV = t_sc[:, 0:NT]
    SD = t_sc[:, NT:2 * NT]
    OP("vector", lambda e: e.tensor_scalar_mul(MEAN, S1, 1.0 / D), reads=["stats"], writes=["stats"])
    OP("scalar", lambda e: e.activation(TV, MEAN, AF.Square), reads=["stats"], writes=["tsc"])
    OP("vector", lambda e: e.scalar_tensor_tensor(TV, S2, 1.0 / D, TV, ALU.mult, ALU.subtract),
       reads=["stats", "tsc"], writes=["tsc"])
    OP("scalar", lambda e: e.activation(SD, TV, AF.Sqrt, bias=EPS),
       reads=["tsc", "epsc"], writes=["tsc"])
    OP("vector", lambda e: e.reciprocal(RSTD, SD), reads=["tsc"], writes=["stats"])
    OP("scalar", lambda e: e.activation(RSTD1, RSTD, AF.Square), reads=["stats"], writes=["stats"])
    OP("vector", lambda e: e.tensor_mul(RSTD1, RSTD1, TV), reads=["stats", "tsc"], writes=["stats"])
    OP("scalar", lambda e: e.activation(RSTD1, RSTD1, AF.Sqrt, bias=EPS),
       reads=["stats", "epsc"], writes=["stats"])
    OP("vector", lambda e: e.reciprocal(RSTD1, RSTD1), reads=["stats"], writes=["stats"])
    OP("tensor", lambda e: e.transpose(ps_a[0:3 * NT, 0:128],
                                       t_stats[:, C_MEAN:C_MEAN + 3 * NT], t_idn[:]),
       reads=["stats", "idn"], writes=["ps:A"])
    OP("scalar", lambda e: e.copy(t_statT[:], ps_a[0:3 * NT, 0:128]),
       reads=["ps:A"], writes=["statT"])

    xn32 = t_f32[:, 0:KC * TW]
    xping = (t_f32[:, KC * TW:KC * TW + TW], t_f32[:, KC * TW + TW:KC * TW + 2 * TW])
    mean_b = t_f32b[:, 0:TW]
    rstd_b = t_f32b[:, TW:2 * TW]
    rstd1_b = t_f32b[:, 2 * TW:3 * TW]
    rstd2_b = t_f32b[:, 0:TW]

    for tau in range(TAU):
        t0 = tau * TW
        for si, dst, bk in ((0, mean_b, "bc0"), (1, rstd_b, "bc1"), (2, rstd1_b, "bc2")):
            g0 = tau * GPT
            OP("sync", lambda e, si=si, g0=g0: e.dma_start(
                out=t_rows[0:1, 0:TW].rearrange("o (a b) -> o a b", b=128),
                in_=t_statT[si * NT + g0:si * NT + g0 + GPT, :]),
                reads=["statT"], writes=["rows"], kind="d", dgrp="rows")
            OP("tensor", lambda e, si=si: e.matmul(
                ps_b[:, 0:TW], t_oner[:], t_rows[0:1, 0:TW],
                start=True, stop=True),
                reads=["rows", "oner"], writes=["ps:B"])
            OP("scalar", lambda e, dst=dst: e.copy(dst, ps_b[:, 0:TW]),
               reads=["ps:B"], writes=[bk])
        for kc in range(KC):
            xp, xk = xping[kc % 2], f"xp{kc % 2}"
            OP("sync", lambda e, xp=xp, kc=kc, t0=t0: e.dma_start(
                out=xp[:], in_=h_xfm[kc * 128:(kc + 1) * 128, t0:t0 + TW]),
                writes=[xk, "f32a"], kind="d", dgrp=xk)
            xnk = xn32[:, kc * TW:(kc + 1) * TW]
            OP("vector", lambda e, xp=xp, xnk=xnk: e.tensor_sub(xnk, xp[:], mean_b),
               reads=[xk, "bc0"], writes=[f"xn32:{kc}", "f32a"])
            OP("vector", lambda e, xnk=xnk: e.tensor_mul(xnk, xnk, rstd_b),
               reads=[f"xn32:{kc}", "bc1"], writes=[f"xn32:{kc}"])
            OP("scalar", lambda e, xnk=xnk, kc=kc, t0=t0: e.copy(
                t_xn[:, kc * S + t0:kc * S + t0 + TW], xnk),
                reads=[f"xn32:{kc}"], writes=[f"xn:{kc}:{tau}"])
            OP("vector", lambda e, xnk=xnk, kc=kc, t0=t0: e.scalar_tensor_tensor(
                t_A[:, kc * S + t0:kc * S + t0 + TW], xnk, t_n1[:, kc:kc + 1],
                rstd1_b, ALU.mult, ALU.mult),
                reads=[f"xn32:{kc}", "bc2"], sreads=["n1"], writes=[f"A:{kc}:{tau}"])
        for g in range(GPT):
            gg = tau * GPT + g
            for kc in range(KC):
                OP("tensor", lambda e, kc=kc, g=g: e.matmul(
                    ps_a[:, 0:E],
                    xn32[:, kc * TW + g * 128:kc * TW + g * 128 + 128],
                    t_wg[:, kc * E:(kc + 1) * E],
                    start=(kc == 0), stop=(kc == KC - 1)),
                    reads=[f"xn32:{kc}", "wg"], writes=["ps:A"])
            m8, nm8, sume = t_sc[:, 0:1], t_sc[:, 1:2], t_sc[:, 2:3]
            lgs = t_sc[:, 4:12]
            dmp = t_sc[:, 12:20]
            OP("vector", lambda e, lgs=lgs: e.tensor_copy(lgs, ps_a[:, 0:E]),
               reads=["ps:A"], writes=["lgs", "tsc"])
            OP("vector", lambda e, lgs=lgs: e.reduce_max(m8, lgs, axis=AX),
               reads=["lgs"], writes=["m8", "tsc"])
            OP("vector", lambda e, gg=gg, lgs=lgs: e.tensor_single_scalar(
                t_mask[:, gg * E:(gg + 1) * E], lgs, m8, ALU.is_ge),
                reads=["lgs"], sreads=["m8"], writes=["maskal"])
            OP("vector", lambda e: e.tensor_scalar_mul(nm8, m8, -1.0),
               reads=["m8"], writes=["nm8", "tsc"])
            OP("scalar", lambda e, dmp=dmp, lgs=lgs: e.activation(
                dmp, lgs, AF.Exp, bias=nm8, accum_out=sume),
                reads=["lgs"], sreads=["nm8"], writes=["sume", "tsc"])
            OP("vector", lambda e, gg=gg: e.reciprocal(t_gate[:, gg:gg + 1], sume),
               reads=["sume"], writes=["gateal"])
            if upto == 2:
                OP("vector", lambda e, gg=gg, lgs=lgs: e.tensor_copy(
                    t_kept[:, gg * E:(gg + 1) * E], lgs),
                    reads=["lgs"], writes=["keptal"])
                OP("vector", lambda e, gg=gg: e.tensor_copy(
                    t_gall[:, gg * E:gg * E + 1], m8),
                    sreads=["m8"], writes=["gall"])

    if upto == 1:
        return dbg_finish(
            [(h_out[kc * 128:(kc + 1) * 128, :], t_xn[:, kc * S:(kc + 1) * S],
              [f"xn:{kc}:{tt}" for tt in range(TAU)]) for kc in range(KC)])

    OP("vector", lambda e: e.memset(t_offs[0:1, 0:E], 0.0), writes=["offr"])
    for g in range(NT):
        if g > 0:
            OP("vector", lambda e, g=g: e.tensor_copy(
                t_offs[0:1, g * E:(g + 1) * E], ps_b[0:1, 0:E]),
                reads=["ps:B"], writes=["offr"])
        OP("tensor", lambda e, g=g: e.matmul(
            ps_b[0:1, 0:E], t_ones[:], t_mask[:, g * E:(g + 1) * E],
            start=(g == 0), stop=(g == NT - 1)),
            reads=["maskal", "ones", "offr"], writes=["ps:B"])
    OP("vector", lambda e: e.tensor_copy(t_rows[0:1, 0:E], ps_b[0:1, 0:E]),
       reads=["ps:B", "cpad"], writes=["cpad", "rows"])
    OP("sync", lambda e: e.dma_start(out=d_ccin[:], in_=t_rows[0:1, 0:512]),
       reads=["cpad", "rows"], writes=["ccin"], kind="d", dgrp="ccin")
    OP("gpsimd", lambda e: e.collective_compute(
        "AllGather", ALU.bypass, replica_groups=[list(range(8))],
        ins=[d_ccin[:]], outs=[d_ccout[:]]),
        reads=["ccin"], writes=["ccout"], kind="c")

    def emit_offsets():
        OP("gpsimd", lambda e: e.dma_start(out=t_allc[:], in_=d_ccout[:, 0:E]),
           reads=["ccout"], writes=["allc"], kind="d", dgrp="allc")
        OP("tensor", lambda e: e.matmul(ps_b[0:1, 0:E], t_selc[:], t_allc[:],
                                        start=True, stop=True),
           reads=["allc", "selc"], writes=["ps:B"])
        OP("scalar", lambda e: e.copy(t_offs[0:1, NT * E:NT * E + E],
                                      ps_b[0:1, 0:E]),
           reads=["ps:B"], writes=["coff"])
        for g in range(NT):
            OP("vector", lambda e, g=g: e.tensor_add(
                t_offs[0:1, g * E:(g + 1) * E], t_offs[0:1, g * E:(g + 1) * E],
                t_offs[0:1, NT * E:NT * E + E]),
                reads=["offr", "coff"], writes=["offr"])

    if upto == 2:
        emit_offsets()
        return dbg_finish([
            (h_out[0:128, 0:NT * E], t_mask[:], ["maskal"]),
            (h_out[128:256, 0:NT], t_gate[:], ["gateal"]),
            (h_out[256:257, 0:NT * E], t_offs[0:1, 0:NT * E], ["offr"]),
            (h_out[384:512, 0:NT * E], t_kept[:], ["keptal"]),
            (h_out[512:640, 0:NT * E], t_gall[:], ["gall"]),
        ])

    def wr(slot, off, n):
        return t_wr[:, slot * 4096 + off:slot * 4096 + off + n]

    WRALL = lambda s: [f"wr{s}a", f"wr{s}b", f"wr{s}c", f"wr{s}d"]

    pctr = {"ab": 0, "st": 0}

    def nextab():
        pctr["ab"] += 1
        return (ps_a, "ps:A") if pctr["ab"] % 2 == 0 else (ps_b, "ps:B")

    def nextst():
        pctr["st"] += 1
        b = pctr["st"] % 2
        return b, ps_st[:, b * 512:(b + 1) * 512], f"ps:st{b}"

    def proj_rope(c, reg, woff, hw, dst, dkey, scale):
        slot = c % 2
        wsb = wr(slot, woff, KC * 128)
        wkey = f"wr{slot}{reg}"
        OP("gpsimd", lambda e, wsb=wsb: e.dma_start(
            out=wsb.rearrange("p (kc f) -> p kc f", f=128),
            in_=hw[c].rearrange("(kc p) f -> p kc f", p=128)),
            writes=[wkey], kind="d", dgrp=wkey)
        for tau in range(TAU):
            t0 = tau * TW
            pse, pk = nextab()
            for kc in range(KC):
                OP("tensor", lambda e, kc=kc, t0=t0, wsb=wsb, pse=pse: e.matmul(
                    pse[:, 0:TW], wsb[:, kc * 128:(kc + 1) * 128],
                    t_A[:, kc * S + t0:kc * S + t0 + TW],
                    start=(kc == 0), stop=(kc == KC - 1)),
                    reads=[wkey, f"A:{kc}:{tau}"], writes=[pk])
            qr, qk = t_pg[:, (tau % 2) * TW:(tau % 2 + 1) * TW], f"qraw{tau % 2}"
            OP("scalar", lambda e, qr=qr, pse=pse: e.copy(qr, pse[:, 0:TW]),
               reads=[pk], writes=[qk])
            sb_, sps, spk = nextst()
            OP("tensor", lambda e, qr=qr, sps=sps: e.matmul(
                sps[:, 0:TW], t_prm[:], qr, start=True, stop=True),
                reads=[qk, "prm"], writes=[spk])
            tt1 = t_rope[:, (tau % 2) * TW:(tau % 2 + 1) * TW]
            tk1 = f"tt1{tau % 2}"
            OP("vector", lambda e, qr=qr, t0=t0, tt1=tt1: e.scalar_tensor_tensor(
                tt1, qr, scale, t_cos[:, t0:t0 + TW], ALU.mult, ALU.mult),
                reads=[qk, "cos"], writes=[tk1])
            OP("vector", lambda e, t0=t0, dst=dst, sps=sps, tt1=tt1: e.scalar_tensor_tensor(
                dst[:, t0:t0 + TW], sps[:, 0:TW], scale, t_sin[:, t0:t0 + TW],
                ALU.mult, ALU.mult),
                reads=[spk, "sin"], writes=["tt2s"])
            OP("vector", lambda e, t0=t0, dst=dst, tt1=tt1: e.tensor_add(
                dst[:, t0:t0 + TW], dst[:, t0:t0 + TW], tt1),
                reads=["tt2s", tk1], writes=[dkey])

    def build_V(c):
        slot = c % 2
        wsb = wr(slot, 2048, KC * 128)
        wkey = f"wr{slot}c"
        OP("gpsimd", lambda e, c=c, wsb=wsb: e.dma_start(
            out=wsb.rearrange("p (kc f) -> p kc f", f=128),
            in_=h_wv[c].rearrange("(kc p) f -> p kc f", p=128)),
            writes=[wkey], kind="d", dgrp=wkey)
        if c == 0:
            OP("vector", lambda e: e.memset(
                t_C[:, 0:NT * 130].rearrange("p (kt h z) -> p kt h z", h=2, z=65)[:, :, :, 64],
                1.0), writes=["C"])
        for g4 in range((NT + 3) // 4):
            tts = list(range(4 * g4, min(4 * g4 + 4, NT)))
            pse, pk = nextab()
            for i, tt in enumerate(tts):
                for kc in range(KC):
                    OP("tensor", lambda e, tt=tt, i=i, kc=kc, wsb=wsb, pse=pse: e.matmul(
                        pse[:, i * 128:(i + 1) * 128],
                        t_A[:, kc * S + tt * 128:kc * S + (tt + 1) * 128],
                        wsb[:, kc * 128:(kc + 1) * 128],
                        start=(kc == 0 and i == 0), stop=(kc == KC - 1)),
                        reads=[wkey, f"A:{kc}:{(tt * 128) // TW}"], writes=[pk])
            n = len(tts)
            OP("vector", lambda e, tts=tts, n=n, c=c, pse=pse: e.tensor_copy(
                t_C[:, 0:NT * 130].rearrange("p (kt h z) -> p kt h z", h=2, z=65)[
                    :, tts[0]:tts[0] + n, :, 0:64],
                pse.ap().rearrange("p (i h z) -> p i h z", h=2, z=64)[:, 0:n, :, :]),
                reads=[pk], writes=["C"])

    def bound_B(c):
        sqb = t_xne[:, 0:S]
        for wi, (which, buf, bk) in enumerate((("q", t_q, "qrope"), ("k", t_k, "krope"))):
            OP("scalar", lambda e, buf=buf: e.activation(sqb, buf[:], AF.Square),
               reads=[bk], writes=["sqb", "xnereg"])
            for tau in range(TAU):
                OP("tensor", lambda e, tau=tau: e.matmul(
                    ps_b[0:1, 0:TW], t_onesb[:], sqb[:, tau * TW:(tau + 1) * TW],
                    start=True, stop=True),
                    reads=["sqb", "xnereg", "onesb"], writes=["ps:B"])
                OP("vector", lambda e, tau=tau, wi=wi: e.reduce_max(
                    t_sr[0:1, wi * 8 + tau:wi * 8 + tau + 1], ps_b[0:1, 0:TW], axis=AX),
                    reads=["ps:B"], writes=[f"nm{wi}", "tsr"])
        OP("vector", lambda e: e.reduce_max(t_sr[0:1, 16:17], t_sr[0:1, 0:TAU], axis=AX),
           sreads=["nm0"], writes=["tsr"])
        OP("vector", lambda e: e.reduce_max(t_sr[0:1, 17:18], t_sr[0:1, 8:8 + TAU], axis=AX),
           sreads=["nm1"], writes=["tsr"])
        OP("vector", lambda e: e.tensor_mul(t_sr[0:1, 18:19], t_sr[0:1, 16:17], t_sr[0:1, 17:18]),
           sreads=["tsr"], writes=["tsr"])
        OP("scalar", lambda e: e.activation(t_sr[0:1, 19:20], t_sr[0:1, 18:19], AF.Sqrt,
                                            scale=1.05),
           reads=["tsr"], writes=["bb"])
        OP("tensor", lambda e: e.matmul(ps_b[:, 0:1], t_oner[:], t_sr[0:1, 19:20],
                                        start=True, stop=True),
           reads=["bb", "oner"], writes=["ps:B"])
        OP("scalar", lambda e, c=c: e.activation(t_nB[:, c:c + 1], ps_b[:, 0:1],
                                                 AF.Copy, scale=-1.0),
           reads=["ps:B"], writes=["negB"])

    def attention_head(c, half):
        r0 = 64 * half
        h = 2 * c + half

        def emit_ctx(kt, q0, off, batch, bank):
            for qi in batch:
                j = qi - q0
                OP("tensor", lambda e, kt=kt, qi=qi, j=j, bank=bank, h=h: e.matmul(
                    ps_ctx[0:65, qi * 128:(qi + 1) * 128],
                    t_C[:, kt * 130 + 65 * (h % 2):kt * 130 + 65 * (h % 2) + 65],
                    t_pt[:, bank * 512 + j * 128:bank * 512 + (j + 1) * 128],
                    start=(kt == 0 and qi % 4 == 0), stop=(kt == qi)),
                    reads=[f"pt{bank}", "C"], writes=[f"ps:ctx{qi // 4}"])

        pend = None
        for q0 in range(0, NT, 4):
            ng = min(4, NT - q0)
            for kt in range(q0 + ng):
                off = max(0, kt - q0)
                batch = list(range(q0 + off, q0 + ng))
                bank, sps, stk = nextst()
                OP("tensor", lambda e, kt=kt, q0=q0, off=off, ng=ng, sps=sps,
                   r0=r0: e.matmul(
                    sps[:, off * 128:ng * 128],
                    t_k[r0:r0 + 64, kt * 128:(kt + 1) * 128],
                    t_q[r0:r0 + 64, (q0 + off) * 128:(q0 + ng) * 128],
                    start=True, stop=True),
                    reads=["krope", "qrope"], writes=[stk])
                if kt >= q0:
                    OP("vector", lambda e, off=off, sps=sps: e.tensor_add(
                        sps[:, off * 128:(off + 1) * 128],
                        sps[:, off * 128:(off + 1) * 128],
                        t_mskT[:]),
                        reads=[stk, "mskT"], writes=[stk])
                OP("scalar", lambda e, off=off, ng=ng, bank=bank, sps=sps,
                   c=c: e.activation(
                    t_pt[:, bank * 512 + off * 128:bank * 512 + ng * 128],
                    sps[:, off * 128:ng * 128],
                    AF.Exp, bias=t_nB[:, c:c + 1]),
                    reads=[stk], sreads=["negB"], writes=[f"pt{bank}"])
                if pend is not None:
                    emit_ctx(*pend)
                pend = (kt, q0, off, batch, bank)
            emit_ctx(*pend)
            pend = None

    def attn_evac(c, half):
        kc = c
        for q0 in range(0, NT, 4):
            n = min(4, NT - q0)
            OP("scalar", lambda e, q0=q0, n=n: e.activation(
                t_rows[0:1, 0:n * 128], ps_ctx[64:65, q0 * 128:(q0 + n) * 128],
                AF.Ln),
                reads=[f"ps:ctx{q0 // 4}"] + (
                    [f"ps:ctx{(q0 + n - 1) // 4}"] if (q0 + n - 1) // 4 != q0 // 4 else []),
                writes=["rows"])
            OP("scalar", lambda e, n=n: e.activation(
                t_rec[0:1, 0:n * 128], t_rows[0:1, 0:n * 128], AF.Exp, scale=-1.0),
                reads=["rows"], writes=["rec"])
            bank, sps, stk = nextst()
            OP("tensor", lambda e, n=n, sps=sps: e.matmul(
                sps[0:64, 0:n * 128], t_oner[0:1, 0:64],
                t_rec[0:1, 0:n * 128], start=True, stop=True),
                reads=["rec", "oner"], writes=[stk])
            OP("scalar", lambda e, n=n, sps=sps: e.copy(t_gb[:, 0:n * 128],
                                                        sps[0:64, 0:n * 128]),
               reads=[stk], writes=["gbs"])
            tau = (q0 * 128) // TW
            bkeys = sorted({f"B:{kc}:{((q0 + i) * 128) // TW}" for i in range(n)})
            OP("vector", lambda e, q0=q0, n=n, half=half, kc=kc: e.tensor_mul(
                t_B[:, kc * S + q0 * 128:kc * S + (q0 + n) * 128][64 * half:64 * half + 64, :],
                ps_ctx[0:64, q0 * 128:(q0 + n) * 128],
                t_gb[:, 0:n * 128]),
                reads=[f"ps:ctx{q0 // 4}", "gbs"], writes=bkeys)

    for c in range(KC):
        proj_rope(c, "a", 0, h_wq, t_q, "qrope", 0.125)
        proj_rope(c, "b", 1024, h_wk, t_k, "krope", 1.0)
        build_V(c)
        bound_B(c)
        if c > 0:
            attn_evac(c - 1, 1)
        attention_head(c, 0)
        attn_evac(c, 0)
        attention_head(c, 1)
    attn_evac(KC - 1, 1)

    if upto == 3:
        return dbg_finish(
            [(h_out[kc * 128:(kc + 1) * 128, :], t_B[:, kc * S:(kc + 1) * S],
              [f"B:{kc}:{tt}" for tt in range(TAU)]) for kc in range(KC)])

    y_sl = t_f32[:, 0:KC * TW]
    for half in range(2):
        OP("gpsimd", lambda e, half=half: e.dma_start(
            out=wr(half, 0, 4096).rearrange("p (kc f) -> p kc f", f=512),
            in_=h_wo[half].rearrange("(kc p) f -> p kc f", p=128)),
            writes=[f"wr{half}a", f"wr{half}b", f"wr{half}c"], kind="d",
            dgrp=f"wr{half}a")
    for tau in range(TAU):
        t0 = tau * TW
        for ic in range(KC):
            half, iw = ic // 4, ic % 4
            pse, pk = (ps_a, "ps:A") if ic % 2 == 0 else (ps_b, "ps:B")
            for kc in range(KC):
                OP("tensor", lambda e, half=half, iw=iw, kc=kc, t0=t0, pse=pse: e.matmul(
                    pse[:, 0:TW],
                    wr(half, kc * 512 + iw * 128, 128),
                    t_B[:, kc * S + t0:kc * S + t0 + TW],
                    start=(kc == 0), stop=(kc == KC - 1)),
                    reads=[f"wr{half}a", f"B:{kc}:{tau}"], writes=[pk])
            OP("vector", lambda e, ic=ic, t0=t0, pse=pse, pk=pk: e.tensor_add(
                y_sl[:, ic * TW:(ic + 1) * TW], pse[:, 0:TW],
                t_xn[:, ic * S + t0:ic * S + t0 + TW]),
                reads=[pk, f"xn:{ic}:{tau}"], writes=[f"y:{ic}", "f32a"])
        for ic in range(KC):
            si = t_sil[:, 0:TW]
            OP("scalar", lambda e, ic=ic, si=si: e.activation(
                si, y_sl[:, ic * TW:(ic + 1) * TW], AF.Square),
                reads=[f"y:{ic}"], writes=["ysq", "silreg"])
            OP("tensor", lambda e, ic=ic, si=si: e.matmul(
                ps_st[0:1, 0:TW], t_onesb[:], si,
                start=(ic == 0), stop=(ic == KC - 1)),
                reads=["ysq", "silreg", "onesb"], writes=["ps:st0"])
        rrow = t_rows[0:1, 0:TW]
        OP("scalar", lambda e: e.activation(rrow, ps_st[0:1, 0:TW], AF.Sqrt,
                                            scale=1.0 / D, bias=EPS),
           reads=["ps:st0", "epsc"], writes=["r2row", "rows"])
        OP("vector", lambda e: e.reciprocal(rrow, rrow), reads=["r2row"], writes=["r2row"])
        OP("tensor", lambda e: e.matmul(ps_st[:, 0:TW], t_oner[:], rrow,
                                        start=True, stop=True),
           reads=["r2row", "oner"], writes=["ps:st0"])
        OP("scalar", lambda e: e.copy(rstd2_b, ps_st[:, 0:TW]),
           reads=["ps:st0"], writes=["bc3", "bc0"])
        for ic in range(KC):
            OP("vector", lambda e, ic=ic, t0=t0: e.scalar_tensor_tensor(
                t_A[:, ic * S + t0:ic * S + t0 + TW],
                y_sl[:, ic * TW:(ic + 1) * TW], t_n2[:, ic:ic + 1], rstd2_b,
                ALU.mult, ALU.mult),
                reads=[f"y:{ic}", "bc3"], sreads=["n2"], writes=[f"A:{ic}:{tau}"])
            xp, xk = xping[ic % 2], f"xp{ic % 2}"
            OP("sync", lambda e, ic=ic, t0=t0, xp=xp: e.dma_start(
                out=xp[:], in_=h_xfm[ic * 128:(ic + 1) * 128, t0:t0 + TW]),
                writes=[xk, "f32a"], kind="d", dgrp=xk)
            ev, ek = t_ev[:, (ic % 2) * TW:(ic % 2 + 1) * TW], f"ev{ic % 2}"
            OP("vector", lambda e, ic=ic, xp=xp, ev=ev: e.tensor_add(
                ev, y_sl[:, ic * TW:(ic + 1) * TW], xp[:]),
                reads=[f"y:{ic}", xk], writes=[ek])
            OP("gpsimd", lambda e, ic=ic, t0=t0, ev=ev: e.dma_start(
                out=h_out[ic * 128:(ic + 1) * 128, t0:t0 + TW], in_=ev),
                reads=[ek], writes=[f"out:{ic}:{tau}"], kind="d",
                dgrp=f"oute{ic % 2}")

    if upto == 4:
        outkeys = [f"out:{ic}:{tt}" for ic in range(KC) for tt in range(TAU)]
        sch.barrier_wait("sync", outkeys)
        return finish()

    emit_offsets()

    for g in range(NT):
        OP("tensor", lambda e, g=g: e.matmul(
            ps_a[:, g * E:(g + 1) * E], t_ltri[:], t_mask[:, g * E:(g + 1) * E],
            start=(g == 0), stop=False),
            reads=["ltri", "maskal"], writes=["ps:A"])
    OP("tensor", lambda e: e.matmul(
        ps_a[:, 0:NT * E], t_oner[:], t_offs[0:1, 0:NT * E], start=False, stop=True),
        reads=["oner", "offr"], writes=["ps:A"])
    OP("vector", lambda e: e.tensor_single_scalar(
        t_kept[:], ps_a[:, 0:NT * E], CAPTH, ALU.is_lt),
        reads=["ps:A"], writes=["keptal"])
    for g in range(NT):
        OP("vector", lambda e, g=g: e.scalar_tensor_tensor(
            t_gall[:, g * E:(g + 1) * E], t_mask[:, g * E:(g + 1) * E],
            t_gate[:, g:g + 1], t_kept[:, g * E:(g + 1) * E], ALU.mult, ALU.mult),
            reads=["maskal", "keptal"], sreads=["gateal"], writes=["gall"])
    for g in range(NT):
        OP("tensor", lambda e, g=g: e.transpose(
            ps_b[0:E, 0:128], t_gall[:, g * E:(g + 1) * E], t_idn[:]),
            reads=["gall", "idn"], writes=["ps:B"])
        OP("scalar", lambda e, g=g: e.copy(t_gt[:, g * 128:(g + 1) * 128],
                                           ps_b[0:E, 0:128]),
           reads=["ps:B"], writes=["gt"])
    if upto == 41:
        return dbg_finish([
            (h_out[0:128, 0:NT * E], t_kept[:], ["keptal"]),
            (h_out[128:256, 0:NT * E], t_gall[:], ["gall"]),
            (h_out[256:256 + E, 0:S], t_gt[:], ["gt"]),
        ])
    for ee in range(E):
        for tau in range(TAU):
            OP("tensor", lambda e, ee=ee, tau=tau: e.matmul(
                ps_b[:, 0:TW], t_sel[:, ee * 128:ee * 128 + 128],
                t_gt[:, tau * TW:(tau + 1) * TW], start=True, stop=True),
                reads=["sel", "gt"], writes=["ps:B"])
            OP("scalar", lambda e, ee=ee, tau=tau: e.copy(
                t_B[:, ee * S + tau * TW:ee * S + (tau + 1) * TW], ps_b[:, 0:TW]),
                reads=["ps:B"], writes=[f"B:{ee}:{tau}"])
    if upto == 43:
        return dbg_finish(
            [(h_out[ee * 128:ee * 128 + 128, 0:S], t_B[:, ee * S:(ee + 1) * S],
              [f"B:{ee}:{tt}" for tt in range(TAU)]) for ee in range(E)])

    def moe_psum(ic):
        if ic < 4:
            return ps_ctx[:, ic * 512:(ic + 1) * 512], f"ps:ctx{ic}"
        if ic < 6:
            return ps_st[:, (ic - 4) * 512:(ic - 3) * 512], f"ps:st{ic - 4}"
        return (ps_a, "ps:A") if ic == 6 else (ps_b, "ps:B")

    EMM = {45: 0, 46: 0, 47: 1, 48: 2, 49: 4}.get(upto, E)
    for tau in range(TAU):
        t0 = tau * TW
        for ee in range(E):
            for kc in range(KC):
                OP("vector", lambda e, ee=ee, kc=kc, t0=t0, tau=tau: e.tensor_mul(
                    t_xne[:, kc * TW:(kc + 1) * TW],
                    t_xn[:, kc * S + t0:kc * S + t0 + TW],
                    t_B[:, ee * S + t0:ee * S + t0 + TW]),
                    reads=[f"xn:{kc}:{tau}", f"B:{ee}:{tau}"],
                    writes=[f"xne:{kc}", "xnereg"])
            for hh in range(2):
                slot3 = (tau * E * 2 + ee * 2 + hh) % 3
                if slot3 < 2:
                    wbuf = wr(slot3, 0, 4096)
                    wks = [f"wr{slot3}a", f"wr{slot3}b", f"wr{slot3}c"]
                else:
                    wbuf = t_w2g[:, 0:4096]
                    wks = ["w2g0", "w2g1", "w2g2", "w2g3"]
                if upto != 45:
                    OP("gpsimd", lambda e, ee=ee, hh=hh, wbuf=wbuf: e.dma_start(
                        out=wbuf.rearrange("p (kc f) -> p kc f", f=512),
                        in_=h_we[ee, hh].rearrange("(kc p) f -> p kc f", p=128)),
                        writes=wks, kind="d", dgrp=wks[0])
                if ee >= EMM:
                    continue
                for kc in range(KC):
                    for iw in range(4):
                        ic = hh * 4 + iw
                        pse, pk = moe_psum(ic)
                        OP("tensor", lambda e, kc=kc, iw=iw, pse=pse, wbuf=wbuf,
                           ee=ee, hh=hh: e.matmul(
                            pse[:, 0:TW],
                            wbuf[:, kc * 512 + iw * 128:kc * 512 + (iw + 1) * 128],
                            t_xne[:, kc * TW:(kc + 1) * TW],
                            start=(ee == 0 and kc == 0), stop=False),
                            reads=[wks[0], f"xne:{kc}"], writes=[pk])
        if upto in (45, 46):
            continue
        for ic in range(KC):
            pse, pk = moe_psum(ic)
            OP("tensor", lambda e, ic=ic, tau=tau, pse=pse: e.matmul(
                pse[:, 0:TW], t_be[:, ic * 128:(ic + 1) * 128],
                t_gt[:, tau * TW:(tau + 1) * TW], start=False, stop=True),
                reads=["be", "gt"], writes=[pk])
            ev, ek = t_ev[:, (ic % 2) * TW:(ic % 2 + 1) * TW], f"ev{ic % 2}"
            OP("scalar", lambda e, pse=pse, ev=ev: e.copy(ev, pse[:, 0:TW]),
               reads=[pk], writes=[ek])
            if upto == 44:
                OP("gpsimd", lambda e, ic=ic, t0=t0, ev=ev: e.dma_start(
                    out=h_out[ic * 128:(ic + 1) * 128, t0:t0 + TW], in_=ev),
                    reads=[ek, f"out:{ic}:{tau}"], writes=[f"out:{ic}:{tau}"],
                    kind="d", dgrp=f"oute{ic % 2}")
            else:
                OP("gpsimd", lambda e, ic=ic, t0=t0, ev=ev: e.dma_start(
                    out=h_out[ic * 128:(ic + 1) * 128, t0:t0 + TW], in_=ev,
                    accum_op=ALU.add),
                    reads=[ek, f"out:{ic}:{tau}"], writes=[f"out:{ic}:{tau}"],
                    kind="d", dgrp=f"oute{ic % 2}")

    if upto in (45, 46):
        return dbg_finish([(h_out[0:128, 0:TW], t_xne[:, 0:TW], ["xne:0"])])
    if upto in (5, 44, 47, 48, 49):
        outkeys = [f"out:{ic}:{tt}" for ic in range(KC) for tt in range(TAU)]
        sch.barrier_wait("sync", outkeys)
        sch.barrier_wait("gpsimd", outkeys)
        return finish()

    FG = 4
    ff_acc = t_f32[:, 0:KC * TW]
    for tau in range(TAU):
        t0 = tau * TW
        for fc in range(FC):
            slot = fc % 2
            OP("gpsimd", lambda e, fc=fc, slot=slot: e.dma_start(
                out=wr(slot, 0, KC * 128).rearrange("p (kc f) -> p kc f", f=128),
                in_=h_w1[fc].rearrange("(kc p) f -> p kc f", p=128)),
                writes=[f"wr{slot}a"], kind="d", dgrp=f"wr{slot}a")
            OP("gpsimd", lambda e, fc=fc, slot=slot: e.dma_start(
                out=wr(slot, 1024, KC * 128).rearrange("p (kc f) -> p kc f", f=128),
                in_=h_w3[fc].rearrange("(kc p) f -> p kc f", p=128)),
                writes=[f"wr{slot}b"], kind="d", dgrp=f"wr{slot}b")
            p1, p1k = (ps_a, "ps:A") if slot == 0 else (ps_st[:, 0:512], "ps:st0")
            p3, p3k = (ps_b, "ps:B") if slot == 0 else (ps_st[:, 512:1024], "ps:st1")
            for kc in range(KC):
                OP("tensor", lambda e, kc=kc, t0=t0, slot=slot, p1=p1: e.matmul(
                    p1[:, 0:TW], wr(slot, kc * 128, 128),
                    t_A[:, kc * S + t0:kc * S + t0 + TW],
                    start=(kc == 0), stop=(kc == KC - 1)),
                    reads=[f"wr{slot}a", f"A:{kc}:{tau}"], writes=[p1k])
            for kc in range(KC):
                OP("tensor", lambda e, kc=kc, t0=t0, slot=slot, p3=p3: e.matmul(
                    p3[:, 0:TW], wr(slot, 1024 + kc * 128, 128),
                    t_A[:, kc * S + t0:kc * S + t0 + TW],
                    start=(kc == 0), stop=(kc == KC - 1)),
                    reads=[f"wr{slot}b", f"A:{kc}:{tau}"], writes=[p3k])
            si, sk = t_sil[:, slot * TW:(slot + 1) * TW], f"sil{slot}"
            OP("scalar", lambda e, si=si, p1=p1: e.activation(si, p1[:, 0:TW], AF.Silu),
               reads=[p1k], writes=[sk, "silreg"])
            j = fc % FG
            pgt, pgk = t_pg[:, j * TW:(j + 1) * TW], f"pg{j}"
            OP("vector", lambda e, si=si, p3=p3, pgt=pgt: e.tensor_mul(
                pgt, si, p3[:, 0:TW]),
                reads=[sk, p3k], writes=[pgk])
            OP("gpsimd", lambda e, fc=fc, j=j: e.dma_start(
                out=t_w2g[:, j * D:(j + 1) * D], in_=h_w2[fc]),
                writes=[f"w2g{j}"], kind="d", dgrp=f"w2g{j}")
            if j == FG - 1:
                fg0 = fc - FG + 1
                for oc in range(KC):
                    fps = ps_ctx[:, (oc % 2) * 512:(oc % 2) * 512 + TW]
                    fpk = f"ps:ctx{oc % 2}"
                    for jj in range(FG):
                        OP("tensor", lambda e, jj=jj, oc=oc, fps=fps: e.matmul(
                            fps, t_w2g[:, jj * D + oc * 128:jj * D + (oc + 1) * 128],
                            t_pg[:, jj * TW:(jj + 1) * TW],
                            start=(jj == 0), stop=(jj == FG - 1)),
                            reads=[f"w2g{jj}", f"pg{jj}"], writes=[fpk])
                    if fg0 == 0:
                        OP("vector", lambda e, oc=oc, fps=fps: e.tensor_copy(
                            ff_acc[:, oc * TW:(oc + 1) * TW], fps),
                            reads=[fpk], writes=[f"y:{oc}"])
                    else:
                        OP("vector", lambda e, oc=oc, fps=fps: e.tensor_add(
                            ff_acc[:, oc * TW:(oc + 1) * TW],
                            ff_acc[:, oc * TW:(oc + 1) * TW], fps),
                            reads=[fpk, f"y:{oc}"], writes=[f"y:{oc}"])
        for oc in range(KC):
            acc_op = {61: ALU.bypass}.get(upto, ALU.add)
            OP("gpsimd", lambda e, oc=oc, t0=t0, acc_op=acc_op: e.dma_start(
                out=h_out[oc * 128:(oc + 1) * 128, t0:t0 + TW],
                in_=ff_acc[:, oc * TW:(oc + 1) * TW], accum_op=acc_op),
                reads=[f"y:{oc}", f"out:{oc}:{tau}"],
                writes=[f"out:{oc}:{tau}", f"ffdma{oc % 2}"], kind="d",
                dgrp=f"oute{oc % 2}")

    outkeys = [f"out:{ic}:{tau}" for ic in range(KC) for tau in range(TAU)]
    sch.barrier_wait("gpsimd", outkeys)
    sch.barrier_wait("sync", outkeys)
    return finish()


_CACHE = {}


def _consts(S):
    DKH = 64
    pos = np.arange(S, dtype=np.float64)
    k = np.arange(DKH) // 2
    ang = pos[:, None] / np.power(10000.0, 2.0 * k / DKH)[None, :]
    cos = np.cos(ang)
    sin = np.sin(ang) * np.where(np.arange(DKH) % 2 == 0, -1.0, 1.0)[None, :]
    cosT = np.ascontiguousarray(np.tile(cos.T, (2, 1)).astype(BF))
    sinT = np.ascontiguousarray(np.tile(sin.T, (2, 1)).astype(BF))
    idn = np.eye(128, dtype=np.float32)
    prm = np.zeros((128, 128), dtype=np.float32)
    prm[np.arange(128), np.arange(128) ^ 1] = 1.0
    mskT = np.ascontiguousarray(
        np.where(np.arange(128)[:, None] > np.arange(128)[None, :], NEG, 0.0).astype(BF))
    ltri = np.ascontiguousarray(np.triu(np.ones((128, 128), dtype=np.float32), k=1))
    selm = np.zeros((E, E * 128), dtype=np.float32)
    for ee in range(E):
        selm[ee, ee * 128:(ee + 1) * 128] = 1.0
    return dict(cosT=cosT, sinT=sinT, idn=idn, prm=np.ascontiguousarray(prm.astype(BF)),
                mskT=mskT, ltri=ltri, selm=np.ascontiguousarray(selm.astype(BF)))


def prep_inputs(x, wq, wk, wv, wo, n1, n2, w1, w2, w3, wg, We, be, S):
    cst = _consts(S)
    f32 = np.float32

    def T(w):
        return np.ascontiguousarray(np.asarray(w, dtype=f32).T)

    def blocks(wT, n):
        return np.ascontiguousarray(wT.reshape(D, n, wT.shape[1] // n).transpose(1, 0, 2))

    wq_b = blocks(T(wq), KC)
    wk_b = blocks(T(wk), KC)
    wv_b = blocks(T(wv), KC)
    wo_b = blocks(T(wo), 2)
    w1_b = blocks(T(w1), FC)
    w3_b = blocks(T(w3), FC)
    w2_b = np.ascontiguousarray(T(w2).reshape(FC, 128, D))
    WeT = np.asarray(We, dtype=f32).transpose(0, 2, 1)
    we_b = np.ascontiguousarray(WeT.reshape(E, D, 2, 512).transpose(0, 2, 1, 3))
    n1c = np.ascontiguousarray(np.asarray(n1, dtype=f32).reshape(KC, 128).T)
    n2c = np.ascontiguousarray(np.asarray(n2, dtype=f32).reshape(KC, 128).T)
    base = dict(wq_b=wq_b, wk_b=wk_b, wv_b=wv_b, wo_b=wo_b, w1_b=w1_b,
                w3_b=w3_b, w2_b=w2_b, we_b=we_b,
                wg=np.ascontiguousarray(np.asarray(wg, dtype=f32)),
                be=np.ascontiguousarray(np.asarray(be, dtype=f32)),
                n1c=n1c, n2c=n2c, **cst)
    xa = np.asarray(x, dtype=f32)
    in_maps = []
    for b in range(xa.shape[0]):
        m = dict(base)
        m["x_tm"] = np.ascontiguousarray(xa[b])
        m["x_fm"] = np.ascontiguousarray(xa[b].T)
        m["selc"] = (np.arange(E) < b).astype(f32).reshape(E, 1)
        in_maps.append(m)
    return in_maps


def kernel(L0_attn_x, L0_attn_wq, L0_attn_wk, L0_attn_wv, L0_attn_wo,
           L0_attn_norm1_w, L0_attn_norm2_w, L0_attn_w1, L0_attn_w2,
           L0_attn_w3, L0_ffn_wg, L0_ffn_W, L0_ffn_b):
    import os
    upto = int(os.environ.get("K_UPTO", "9"))
    S = np.asarray(L0_attn_x).shape[1]
    key = (S, upto)
    if key not in _CACHE:
        _CACHE[key] = build_program(S, upto)
    nc = _CACHE[key]
    in_maps = prep_inputs(L0_attn_x, L0_attn_wq, L0_attn_wk, L0_attn_wv,
                          L0_attn_wo, L0_attn_norm1_w, L0_attn_norm2_w,
                          L0_attn_w1, L0_attn_w2, L0_attn_w3, L0_ffn_wg,
                          L0_ffn_W, L0_ffn_b, S)
    res = run_bass_kernel_spmd(nc, in_maps, core_ids=list(range(8)))
    out = np.stack([np.asarray(res.results[b]["out"]).T for b in range(8)])
    return np.ascontiguousarray(out.astype(np.float32))


# version 08
# speedup vs baseline: 1.3269x; 1.0180x over previous
import contextlib
import numpy as np
import ml_dtypes
import concourse.bass as bass
import concourse.mybir as mybir
from concourse.bass_utils import run_bass_kernel_spmd

F32 = mybir.dt.float32
BF16 = mybir.dt.bfloat16
AF = mybir.ActivationFunctionType
ALU = mybir.AluOpType
AX = mybir.AxisListType.X

D = 1024
KC = 8
DFF = 4096
FC = 32
E = 8
EPS = 1e-5
NEG = -1e9

BF = ml_dtypes.bfloat16


class Sched:
    ENGS = ("sync", "scalar", "vector", "tensor", "gpsimd")

    def __init__(self, nc, es):
        self.nc = nc
        self.es = es
        self.seq = {e: 0 for e in self.ENGS}
        self.cseq = 0
        self.sem = {e: es.enter_context(nc.semaphore(f"is_{e}")) for e in self.ENGS}
        self.csem = es.enter_context(nc.semaphore("cs"))
        self.dgrp = {}
        self.last_w = {}
        self.readers = {}
        self.watermark = {e: {} for e in self.ENGS}
        self.eng_ops = {e: [] for e in self.ENGS}

    def _group(self, name, queue):
        g = self.dgrp.get(name)
        if g is None:
            g = [self.es.enter_context(self.nc.semaphore(f"dg_{name}")), 0, queue]
            self.dgrp[name] = g
        assert g[2] == queue, f"dma group {name} used from {queue} and {g[2]}"
        return g

    def _need(self, eng, tok, waits, force=False, dma_queue=None):
        if tok is None:
            return
        kind = tok[0]
        if kind == "i":
            _, peng, val = tok
            if peng == eng and eng == "tensor" and not force:
                return
            sem, target, wkey = self.sem[peng], val, ("i", peng)
        elif kind == "d":
            _, grp, val = tok
            g = self.dgrp[grp]
            if dma_queue is not None and g[2] == dma_queue:
                return
            sem, target, wkey = g[0], 16 * val, ("d", grp)
        else:
            _, val = tok
            sem, target, wkey = self.csem, val, ("c",)
        wm = self.watermark[eng]
        if wm.get(wkey, 0) >= target:
            return
        wm[wkey] = target
        waits.append((sem, target))

    def op(self, eng, fn, reads=(), writes=(), kind="i", sreads=(), dgrp=None):
        waits = []
        dq = eng if kind == "d" else None
        for r in sreads:
            self._need(eng, self.last_w.get(r), waits, force=True, dma_queue=dq)
        for r in reads:
            self._need(eng, self.last_w.get(r), waits, dma_queue=dq)
        for w in writes:
            self._need(eng, self.last_w.get(w), waits, dma_queue=dq)
            for t in self.readers.get(w, ()):
                self._need(eng, t, waits, dma_queue=dq)
        if kind == "d":
            g = self._group(dgrp, eng)
            g[1] += 1
            tok = ("d", dgrp, g[1])
            inc = (g[0], 16)
        elif kind == "c":
            self.cseq += 1
            tok = ("c", self.cseq)
            inc = (self.csem, 1)
        else:
            self.seq[eng] += 1
            tok = ("i", eng, self.seq[eng])
            inc = (self.sem[eng], 1)
        self.eng_ops[eng].append((fn, waits, inc))
        for r in list(reads) + list(sreads):
            self.readers.setdefault(r, []).append(tok)
        for w in writes:
            self.last_w[w] = tok
            self.readers[w] = []
        return tok

    def barrier_wait(self, eng, keys):
        waits = []
        for k in keys:
            self._need(eng, self.last_w.get(k), waits)
            for t in self.readers.get(k, ()):
                self._need(eng, t, waits)
        if waits:
            self.eng_ops[eng].append((None, waits, None))

    def replay(self, block):
        for ename in self.ENGS:
            ops = self.eng_ops[ename]

            def make(op_list):
                def run(eng):
                    for fn, waits, inc in op_list:
                        for (s, tgt) in waits:
                            eng.wait_ge(s, tgt)
                        if fn is not None:
                            fn(eng).then_inc(*inc)
                return run

            getattr(block, ename)(make(ops))


def build_program(S, upto=9):
    assert S % 256 == 0
    TW = min(512, S)
    TAU = S // TW
    NT = S // 128
    GPT = TW // 128
    CAPTH = float(S) - 0.5

    nc = bass.Bass("TRN2", target_bir_lowering=False)

    h_xtm = nc.dram_tensor("x_tm", [S, D], F32, kind="ExternalInput")
    h_xfm = nc.dram_tensor("x_fm", [D, S], F32, kind="ExternalInput")
    h_wq = nc.dram_tensor("wq_b", [KC, D, 128], F32, kind="ExternalInput")
    h_wk = nc.dram_tensor("wk_b", [KC, D, 128], F32, kind="ExternalInput")
    h_wv = nc.dram_tensor("wv_b", [KC, D, 128], F32, kind="ExternalInput")
    h_wo = nc.dram_tensor("wo_b", [2, D, 512], F32, kind="ExternalInput")
    h_w1 = nc.dram_tensor("w1_b", [FC, D, 128], F32, kind="ExternalInput")
    h_w3 = nc.dram_tensor("w3_b", [FC, D, 128], F32, kind="ExternalInput")
    h_w2 = nc.dram_tensor("w2_b", [FC, 128, D], F32, kind="ExternalInput")
    h_we = nc.dram_tensor("we_b", [E, 2, D, 512], F32, kind="ExternalInput")
    h_wg = nc.dram_tensor("wg", [D, E], F32, kind="ExternalInput")
    h_be = nc.dram_tensor("be", [E, D], F32, kind="ExternalInput")
    h_n1 = nc.dram_tensor("n1c", [128, KC], F32, kind="ExternalInput")
    h_n2 = nc.dram_tensor("n2c", [128, KC], F32, kind="ExternalInput")
    h_cos = nc.dram_tensor("cosT", [128, S], BF16, kind="ExternalInput")
    h_sin = nc.dram_tensor("sinT", [128, S], BF16, kind="ExternalInput")
    h_idn = nc.dram_tensor("idn", [128, 128], F32, kind="ExternalInput")
    h_prm = nc.dram_tensor("prm", [128, 128], BF16, kind="ExternalInput")
    h_mskT = nc.dram_tensor("mskT", [128, 128], BF16, kind="ExternalInput")
    h_ltri = nc.dram_tensor("ltri", [128, 128], F32, kind="ExternalInput")
    h_sel = nc.dram_tensor("selm", [E, E * 128], BF16, kind="ExternalInput")
    h_selc = nc.dram_tensor("selc", [E, 1], F32, kind="ExternalInput")
    h_out = nc.dram_tensor("out", [D, S], F32, kind="ExternalOutput")
    d_ccin = nc.dram_tensor("cc_in", [1, 512], F32, kind="Internal")
    d_ccout = nc.dram_tensor("cc_out", [8, 512], F32, kind="Internal")

    es = contextlib.ExitStack()

    def sb(name, p, f, dt):
        return es.enter_context(nc.sbuf_tensor(name, [p, f], dt))

    t_xn = sb("xn", 128, KC * S, BF16)
    t_A = sb("uA", 128, KC * S, BF16)
    t_B = sb("uB", 128, KC * S, BF16)
    t_C = sb("uC", 128, NT * 130, BF16)
    t_q = sb("qbuf", 128, S, BF16)
    t_k = sb("kbuf", 128, S, BF16)
    t_xne = sb("xne", 128, max(KC * TW, S), BF16)
    t_wr = sb("wring", 128, 2 * 4096, BF16)
    t_cos = sb("cosb", 128, S, BF16)
    t_sin = sb("sinb", 128, S, BF16)
    t_f32 = sb("f32a", 128, max(KC * TW + 2 * TW, 3 * 1024), F32)
    t_f32b = sb("f32b", 128, 3 * TW, F32)
    t_pg = sb("pgr", 128, 4 * TW, BF16)
    t_w2g = sb("w2g", 128, 4 * D, BF16)
    t_rope = sb("ropes", 128, 2 * TW, BF16)
    t_pt = sb("ptb", 128, 1024, BF16)
    t_sil = sb("silb", 128, 2 * TW, BF16)
    t_ev = sb("evac", 128, 2 * TW, F32)
    t_gb = sb("gbrow", 64, 512, BF16)
    t_idn = sb("idnb", 128, 128, F32)
    t_prm = sb("prmb", 128, 128, BF16)
    t_mskT = sb("mskTb", 128, 128, BF16)
    t_ltri = sb("ltrib", 128, 128, F32)
    t_sel = sb("selb", E, E * 128, BF16)
    t_n1 = sb("n1b", 128, KC, F32)
    t_n2 = sb("n2b", 128, KC, F32)
    t_wg = sb("wgb", 128, KC * E, F32)
    t_be = sb("beb", E, D, BF16)
    t_stats = sb("statc", 128, 5 * NT, F32)
    t_statT = sb("statT", 3 * NT, 128, F32)
    t_rows = sb("rows", 1, max(TW, 512), F32)
    t_mask = sb("maskal", 128, NT * E, F32)
    t_gate = sb("gateal", 128, NT, F32)
    t_kept = sb("keptal", 128, NT * E, F32)
    t_gall = sb("gall", 128, NT * E, F32)
    t_gt = sb("gtb", E, S, BF16)
    t_offs = sb("offs", 1, NT * E + E, F32)
    t_allc = sb("allc", E, E, F32)
    t_selc = sb("selcb", E, 1, F32)
    t_ones = sb("onesf", 128, 1, F32)
    t_onesb = sb("onesbf", 128, 1, BF16)
    t_oner = sb("onerf", 1, 128, F32)
    t_onerb = sb("onerbf", 1, 128, BF16)
    t_nB = sb("negB", 128, KC, F32)
    t_sc = sb("smallc", 128, max(2 * NT, 20), F32)
    t_eps = sb("epsc", 128, 1, F32)
    nc.const_aps.register(F32, EPS, t_eps[:])
    t_sr = sb("smallr", 1, 32, F32)
    t_rec = sb("recrow", 1, 512, BF16)

    ps_st = es.enter_context(nc.psum_tensor("ps_st", [128, 1024], F32))
    ps_ctx = es.enter_context(nc.psum_tensor("ps_ctx", [128, 2048], F32))
    ps_a = es.enter_context(nc.psum_tensor("ps_a", [128, 512], F32))
    ps_b = es.enter_context(nc.psum_tensor("ps_b", [128, 512], F32))

    sch = Sched(nc, es)
    OP = sch.op

    def finish():
        with nc.allow_low_precision(reason="bf16 by design"):
            with nc.Block("main") as block:
                sch.replay(block)
        es.close()
        return nc

    def dbg_finish(stores):
        keys = []
        for idx, (o, i_, rd) in enumerate(stores):
            k = f"dbgout{idx}"
            OP("gpsimd", lambda e, o=o, i_=i_: e.dma_start(out=o, in_=i_),
               reads=rd, writes=[k], kind="d", dgrp="dbg")
            keys.append(k)
        sch.barrier_wait("gpsimd", keys)
        return finish()

    for hh, tt, key in ((h_idn, t_idn, "idn"), (h_prm, t_prm, "prm"),
                        (h_mskT, t_mskT, "mskT"), (h_ltri, t_ltri, "ltri"),
                        (h_sel, t_sel, "sel"), (h_n1, t_n1, "n1"),
                        (h_n2, t_n2, "n2"), (h_cos, t_cos, "cos"),
                        (h_sin, t_sin, "sin"), (h_selc, t_selc, "selc")):
        OP("sync", lambda e, hh=hh, tt=tt: e.dma_start(out=tt[:], in_=hh[:]),
           writes=[key], kind="d", dgrp=key)
    OP("sync", lambda e: e.dma_start(
        out=t_wg.ap().rearrange("p (kc e) -> p kc e", e=E),
        in_=h_wg.rearrange("(kc p) e -> p kc e", p=128)), writes=["wg"], kind="d", dgrp="wg")
    OP("gpsimd", lambda e: e.dma_start(out=t_be[:], in_=h_be[:]), writes=["be"], kind="d", dgrp="be")
    OP("vector", lambda e: e.memset(t_ones[:], 1.0), writes=["ones"])
    OP("vector", lambda e: e.memset(t_onesb[:], 1.0), writes=["onesb"])
    OP("vector", lambda e: e.memset(t_oner[:], 1.0), writes=["oner"])
    OP("vector", lambda e: e.memset(t_onerb[:], 1.0), writes=["onerb"])
    OP("vector", lambda e: e.memset(t_rows[0:1, 0:512], 0.0), writes=["cpad", "rows"])
    OP("vector", lambda e: e.memset(t_eps[:], EPS), writes=["epsc"])

    xtm = (t_f32[:, 0:1024], t_f32[:, 1024:2048])
    sqdump = t_f32[:, 2048:3072]
    C_S1, C_S2, C_MEAN, C_RSTD, C_RSTD1 = 0, NT, 2 * NT, 3 * NT, 4 * NT
    for i in range(NT):
        xb, key = xtm[i % 2], f"xtm{i % 2}"
        OP("sync", lambda e, xb=xb, i=i: e.dma_start(
            out=xb[:], in_=h_xtm[i * 128:(i + 1) * 128, :]), writes=[key, "f32a"],
            kind="d", dgrp=key)
        OP("vector", lambda e, xb=xb, i=i: e.reduce_sum(
            t_stats[:, C_S1 + i:C_S1 + i + 1], xb[:], axis=AX),
            reads=[key], writes=["stats"])
        OP("scalar", lambda e, xb=xb, i=i: e.activation(
            sqdump[:], xb[:], AF.Square, accum_out=t_stats[:, C_S2 + i:C_S2 + i + 1]),
            reads=[key], writes=["stats", "f32a"])
    S1 = t_stats[:, C_S1:C_S1 + NT]
    S2 = t_stats[:, C_S2:C_S2 + NT]
    MEAN = t_stats[:, C_MEAN:C_MEAN + NT]
    RSTD = t_stats[:, C_RSTD:C_RSTD + NT]
    RSTD1 = t_stats[:, C_RSTD1:C_RSTD1 + NT]
    TV = t_sc[:, 0:NT]
    SD = t_sc[:, NT:2 * NT]
    OP("vector", lambda e: e.tensor_scalar_mul(MEAN, S1, 1.0 / D), reads=["stats"], writes=["stats"])
    OP("scalar", lambda e: e.activation(TV, MEAN, AF.Square), reads=["stats"], writes=["tsc"])
    OP("vector", lambda e: e.scalar_tensor_tensor(TV, S2, 1.0 / D, TV, ALU.mult, ALU.subtract),
       reads=["stats", "tsc"], writes=["tsc"])
    OP("scalar", lambda e: e.activation(SD, TV, AF.Sqrt, bias=EPS),
       reads=["tsc", "epsc"], writes=["tsc"])
    OP("vector", lambda e: e.reciprocal(RSTD, SD), reads=["tsc"], writes=["stats"])
    OP("scalar", lambda e: e.activation(RSTD1, RSTD, AF.Square), reads=["stats"], writes=["stats"])
    OP("vector", lambda e: e.tensor_mul(RSTD1, RSTD1, TV), reads=["stats", "tsc"], writes=["stats"])
    OP("scalar", lambda e: e.activation(RSTD1, RSTD1, AF.Sqrt, bias=EPS),
       reads=["stats", "epsc"], writes=["stats"])
    OP("vector", lambda e: e.reciprocal(RSTD1, RSTD1), reads=["stats"], writes=["stats"])
    OP("tensor", lambda e: e.transpose(ps_a[0:3 * NT, 0:128],
                                       t_stats[:, C_MEAN:C_MEAN + 3 * NT], t_idn[:]),
       reads=["stats", "idn"], writes=["ps:A"])
    OP("scalar", lambda e: e.copy(t_statT[:], ps_a[0:3 * NT, 0:128]),
       reads=["ps:A"], writes=["statT"])

    xn32 = t_f32[:, 0:KC * TW]
    xping = (t_f32[:, KC * TW:KC * TW + TW], t_f32[:, KC * TW + TW:KC * TW + 2 * TW])
    mean_b = t_f32b[:, 0:TW]
    rstd_b = t_f32b[:, TW:2 * TW]
    rstd1_b = t_f32b[:, 2 * TW:3 * TW]
    rstd2_b = t_f32b[:, 0:TW]

    for tau in range(TAU):
        t0 = tau * TW
        for si, dst, bk in ((0, mean_b, "bc0"), (1, rstd_b, "bc1"), (2, rstd1_b, "bc2")):
            g0 = tau * GPT
            OP("sync", lambda e, si=si, g0=g0: e.dma_start(
                out=t_rows[0:1, 0:TW].rearrange("o (a b) -> o a b", b=128),
                in_=t_statT[si * NT + g0:si * NT + g0 + GPT, :]),
                reads=["statT"], writes=["rows"], kind="d", dgrp="rows")
            OP("tensor", lambda e, si=si: e.matmul(
                ps_b[:, 0:TW], t_oner[:], t_rows[0:1, 0:TW],
                start=True, stop=True),
                reads=["rows", "oner"], writes=["ps:B"])
            OP("scalar", lambda e, dst=dst: e.copy(dst, ps_b[:, 0:TW]),
               reads=["ps:B"], writes=[bk])
        for kc in range(KC):
            xp, xk = xping[kc % 2], f"xp{kc % 2}"
            OP("sync", lambda e, xp=xp, kc=kc, t0=t0: e.dma_start(
                out=xp[:], in_=h_xfm[kc * 128:(kc + 1) * 128, t0:t0 + TW]),
                writes=[xk, "f32a"], kind="d", dgrp=xk)
            xnk = xn32[:, kc * TW:(kc + 1) * TW]
            OP("vector", lambda e, xp=xp, xnk=xnk: e.tensor_sub(xnk, xp[:], mean_b),
               reads=[xk, "bc0"], writes=[f"xn32:{kc}", "f32a"])
            OP("vector", lambda e, xnk=xnk: e.tensor_mul(xnk, xnk, rstd_b),
               reads=[f"xn32:{kc}", "bc1"], writes=[f"xn32:{kc}"])
            OP("scalar", lambda e, xnk=xnk, kc=kc, t0=t0: e.copy(
                t_xn[:, kc * S + t0:kc * S + t0 + TW], xnk),
                reads=[f"xn32:{kc}"], writes=[f"xn:{kc}:{tau}"])
            OP("vector", lambda e, xnk=xnk, kc=kc, t0=t0: e.scalar_tensor_tensor(
                t_A[:, kc * S + t0:kc * S + t0 + TW], xnk, t_n1[:, kc:kc + 1],
                rstd1_b, ALU.mult, ALU.mult),
                reads=[f"xn32:{kc}", "bc2"], sreads=["n1"], writes=[f"A:{kc}:{tau}"])
        for g in range(GPT):
            gg = tau * GPT + g
            for kc in range(KC):
                OP("tensor", lambda e, kc=kc, g=g: e.matmul(
                    ps_a[:, 0:E],
                    xn32[:, kc * TW + g * 128:kc * TW + g * 128 + 128],
                    t_wg[:, kc * E:(kc + 1) * E],
                    start=(kc == 0), stop=(kc == KC - 1)),
                    reads=[f"xn32:{kc}", "wg"], writes=["ps:A"])
            m8, nm8, sume = t_sc[:, 0:1], t_sc[:, 1:2], t_sc[:, 2:3]
            lgs = t_sc[:, 4:12]
            dmp = t_sc[:, 12:20]
            OP("vector", lambda e, lgs=lgs: e.tensor_copy(lgs, ps_a[:, 0:E]),
               reads=["ps:A"], writes=["lgs", "tsc"])
            OP("vector", lambda e, lgs=lgs: e.reduce_max(m8, lgs, axis=AX),
               reads=["lgs"], writes=["m8", "tsc"])
            OP("vector", lambda e, gg=gg, lgs=lgs: e.tensor_single_scalar(
                t_mask[:, gg * E:(gg + 1) * E], lgs, m8, ALU.is_ge),
                reads=["lgs"], sreads=["m8"], writes=["maskal"])
            OP("vector", lambda e: e.tensor_scalar_mul(nm8, m8, -1.0),
               reads=["m8"], writes=["nm8", "tsc"])
            OP("scalar", lambda e, dmp=dmp, lgs=lgs: e.activation(
                dmp, lgs, AF.Exp, bias=nm8, accum_out=sume),
                reads=["lgs"], sreads=["nm8"], writes=["sume", "tsc"])
            OP("vector", lambda e, gg=gg: e.reciprocal(t_gate[:, gg:gg + 1], sume),
               reads=["sume"], writes=["gateal"])
            if upto == 2:
                OP("vector", lambda e, gg=gg, lgs=lgs: e.tensor_copy(
                    t_kept[:, gg * E:(gg + 1) * E], lgs),
                    reads=["lgs"], writes=["keptal"])
                OP("vector", lambda e, gg=gg: e.tensor_copy(
                    t_gall[:, gg * E:gg * E + 1], m8),
                    sreads=["m8"], writes=["gall"])

    if upto == 1:
        return dbg_finish(
            [(h_out[kc * 128:(kc + 1) * 128, :], t_xn[:, kc * S:(kc + 1) * S],
              [f"xn:{kc}:{tt}" for tt in range(TAU)]) for kc in range(KC)])

    OP("vector", lambda e: e.memset(t_offs[0:1, 0:E], 0.0), writes=["offr"])
    for g in range(NT):
        if g > 0:
            OP("vector", lambda e, g=g: e.tensor_copy(
                t_offs[0:1, g * E:(g + 1) * E], ps_b[0:1, 0:E]),
                reads=["ps:B"], writes=["offr"])
        OP("tensor", lambda e, g=g: e.matmul(
            ps_b[0:1, 0:E], t_ones[:], t_mask[:, g * E:(g + 1) * E],
            start=(g == 0), stop=(g == NT - 1)),
            reads=["maskal", "ones", "offr"], writes=["ps:B"])
    OP("vector", lambda e: e.tensor_copy(t_rows[0:1, 0:E], ps_b[0:1, 0:E]),
       reads=["ps:B", "cpad"], writes=["cpad", "rows"])
    OP("sync", lambda e: e.dma_start(out=d_ccin[:], in_=t_rows[0:1, 0:512]),
       reads=["cpad", "rows"], writes=["ccin"], kind="d", dgrp="ccin")
    OP("gpsimd", lambda e: e.collective_compute(
        "AllGather", ALU.bypass, replica_groups=[list(range(8))],
        ins=[d_ccin[:]], outs=[d_ccout[:]]),
        reads=["ccin"], writes=["ccout"], kind="c")

    def emit_offsets():
        OP("gpsimd", lambda e: e.dma_start(out=t_allc[:], in_=d_ccout[:, 0:E]),
           reads=["ccout"], writes=["allc"], kind="d", dgrp="allc")
        OP("tensor", lambda e: e.matmul(ps_b[0:1, 0:E], t_selc[:], t_allc[:],
                                        start=True, stop=True),
           reads=["allc", "selc"], writes=["ps:B"])
        OP("scalar", lambda e: e.copy(t_offs[0:1, NT * E:NT * E + E],
                                      ps_b[0:1, 0:E]),
           reads=["ps:B"], writes=["coff"])
        for g in range(NT):
            OP("vector", lambda e, g=g: e.tensor_add(
                t_offs[0:1, g * E:(g + 1) * E], t_offs[0:1, g * E:(g + 1) * E],
                t_offs[0:1, NT * E:NT * E + E]),
                reads=["offr", "coff"], writes=["offr"])

    if upto == 2:
        emit_offsets()
        return dbg_finish([
            (h_out[0:128, 0:NT * E], t_mask[:], ["maskal"]),
            (h_out[128:256, 0:NT], t_gate[:], ["gateal"]),
            (h_out[256:257, 0:NT * E], t_offs[0:1, 0:NT * E], ["offr"]),
            (h_out[384:512, 0:NT * E], t_kept[:], ["keptal"]),
            (h_out[512:640, 0:NT * E], t_gall[:], ["gall"]),
        ])

    def wr(slot, off, n):
        return t_wr[:, slot * 4096 + off:slot * 4096 + off + n]

    WRALL = lambda s: [f"wr{s}a", f"wr{s}b", f"wr{s}c", f"wr{s}d"]

    pctr = {"ab": 0, "st": 0}

    def nextab():
        pctr["ab"] += 1
        return (ps_a, "ps:A") if pctr["ab"] % 2 == 0 else (ps_b, "ps:B")

    def nextst():
        pctr["st"] += 1
        b = pctr["st"] % 2
        return b, ps_st[:, b * 512:(b + 1) * 512], f"ps:st{b}"

    def proj_rope(c, reg, woff, hw, dst, dkey, scale):
        slot = c % 2
        wsb = wr(slot, woff, KC * 128)
        wkey = f"wr{slot}{reg}"
        OP("gpsimd", lambda e, wsb=wsb: e.dma_start(
            out=wsb.rearrange("p (kc f) -> p kc f", f=128),
            in_=hw[c].rearrange("(kc p) f -> p kc f", p=128)),
            writes=[wkey], kind="d", dgrp=wkey)
        for tau in range(TAU):
            t0 = tau * TW
            pse, pk = nextab()
            for kc in range(KC):
                OP("tensor", lambda e, kc=kc, t0=t0, wsb=wsb, pse=pse: e.matmul(
                    pse[:, 0:TW], wsb[:, kc * 128:(kc + 1) * 128],
                    t_A[:, kc * S + t0:kc * S + t0 + TW],
                    start=(kc == 0), stop=(kc == KC - 1)),
                    reads=[wkey, f"A:{kc}:{tau}"], writes=[pk])
            qr, qk = t_pg[:, (tau % 2) * TW:(tau % 2 + 1) * TW], f"qraw{tau % 2}"
            OP("scalar", lambda e, qr=qr, pse=pse: e.copy(qr, pse[:, 0:TW]),
               reads=[pk], writes=[qk])
            sb_, sps, spk = nextst()
            OP("tensor", lambda e, qr=qr, sps=sps: e.matmul(
                sps[:, 0:TW], t_prm[:], qr, start=True, stop=True),
                reads=[qk, "prm"], writes=[spk])
            tt1 = t_rope[:, (tau % 2) * TW:(tau % 2 + 1) * TW]
            tk1 = f"tt1{tau % 2}"
            OP("vector", lambda e, qr=qr, t0=t0, tt1=tt1: e.scalar_tensor_tensor(
                tt1, qr, scale, t_cos[:, t0:t0 + TW], ALU.mult, ALU.mult),
                reads=[qk, "cos"], writes=[tk1])
            OP("vector", lambda e, t0=t0, dst=dst, sps=sps, tt1=tt1: e.scalar_tensor_tensor(
                dst[:, t0:t0 + TW], sps[:, 0:TW], scale, t_sin[:, t0:t0 + TW],
                ALU.mult, ALU.mult),
                reads=[spk, "sin"], writes=["tt2s"])
            OP("vector", lambda e, t0=t0, dst=dst, tt1=tt1: e.tensor_add(
                dst[:, t0:t0 + TW], dst[:, t0:t0 + TW], tt1),
                reads=["tt2s", tk1], writes=[dkey])

    def build_V(c):
        slot = c % 2
        wsb = wr(slot, 2048, KC * 128)
        wkey = f"wr{slot}c"
        OP("gpsimd", lambda e, c=c, wsb=wsb: e.dma_start(
            out=wsb.rearrange("p (kc f) -> p kc f", f=128),
            in_=h_wv[c].rearrange("(kc p) f -> p kc f", p=128)),
            writes=[wkey], kind="d", dgrp=wkey)
        if c == 0:
            OP("vector", lambda e: e.memset(
                t_C[:, 0:NT * 130].rearrange("p (kt h z) -> p kt h z", h=2, z=65)[:, :, :, 64],
                1.0), writes=["C"])
        for g4 in range((NT + 3) // 4):
            tts = list(range(4 * g4, min(4 * g4 + 4, NT)))
            pse, pk = nextab()
            for i, tt in enumerate(tts):
                for kc in range(KC):
                    OP("tensor", lambda e, tt=tt, i=i, kc=kc, wsb=wsb, pse=pse: e.matmul(
                        pse[:, i * 128:(i + 1) * 128],
                        t_A[:, kc * S + tt * 128:kc * S + (tt + 1) * 128],
                        wsb[:, kc * 128:(kc + 1) * 128],
                        start=(kc == 0 and i == 0), stop=(kc == KC - 1)),
                        reads=[wkey, f"A:{kc}:{(tt * 128) // TW}"], writes=[pk])
            n = len(tts)
            OP("vector", lambda e, tts=tts, n=n, c=c, pse=pse: e.tensor_copy(
                t_C[:, 0:NT * 130].rearrange("p (kt h z) -> p kt h z", h=2, z=65)[
                    :, tts[0]:tts[0] + n, :, 0:64],
                pse.ap().rearrange("p (i h z) -> p i h z", h=2, z=64)[:, 0:n, :, :]),
                reads=[pk], writes=["C"])

    def bound_B(c):
        sqb = t_xne[:, 0:S]
        for wi, (which, buf, bk) in enumerate((("q", t_q, "qrope"), ("k", t_k, "krope"))):
            OP("scalar", lambda e, buf=buf: e.activation(sqb, buf[:], AF.Square),
               reads=[bk], writes=["sqb", "xnereg"])
            for tau in range(TAU):
                OP("tensor", lambda e, tau=tau: e.matmul(
                    ps_b[0:1, 0:TW], t_onesb[:], sqb[:, tau * TW:(tau + 1) * TW],
                    start=True, stop=True),
                    reads=["sqb", "xnereg", "onesb"], writes=["ps:B"])
                OP("vector", lambda e, tau=tau, wi=wi: e.reduce_max(
                    t_sr[0:1, wi * 8 + tau:wi * 8 + tau + 1], ps_b[0:1, 0:TW], axis=AX),
                    reads=["ps:B"], writes=[f"nm{wi}", "tsr"])
        OP("vector", lambda e: e.reduce_max(t_sr[0:1, 16:17], t_sr[0:1, 0:TAU], axis=AX),
           sreads=["nm0"], writes=["tsr"])
        OP("vector", lambda e: e.reduce_max(t_sr[0:1, 17:18], t_sr[0:1, 8:8 + TAU], axis=AX),
           sreads=["nm1"], writes=["tsr"])
        OP("vector", lambda e: e.tensor_mul(t_sr[0:1, 18:19], t_sr[0:1, 16:17], t_sr[0:1, 17:18]),
           sreads=["tsr"], writes=["tsr"])
        OP("scalar", lambda e: e.activation(t_sr[0:1, 19:20], t_sr[0:1, 18:19], AF.Sqrt,
                                            scale=1.05),
           reads=["tsr"], writes=["bb"])
        OP("tensor", lambda e: e.matmul(ps_b[:, 0:1], t_oner[:], t_sr[0:1, 19:20],
                                        start=True, stop=True),
           reads=["bb", "oner"], writes=["ps:B"])
        OP("scalar", lambda e, c=c: e.activation(t_nB[:, c:c + 1], ps_b[:, 0:1],
                                                 AF.Copy, scale=-1.0),
           reads=["ps:B"], writes=["negB"])

    def attention_head(c, half):
        r0 = 64 * half
        h = 2 * c + half

        def emit_ctx(kt, q0, off, batch, bank):
            for qi in batch:
                j = qi - q0
                OP("tensor", lambda e, kt=kt, qi=qi, j=j, bank=bank, h=h: e.matmul(
                    ps_ctx[0:65, qi * 128:(qi + 1) * 128],
                    t_C[:, kt * 130 + 65 * (h % 2):kt * 130 + 65 * (h % 2) + 65],
                    t_pt[:, bank * 512 + j * 128:bank * 512 + (j + 1) * 128],
                    start=(kt == 0 and qi % 4 == 0), stop=(kt == qi)),
                    reads=[f"pt{bank}", "C"], writes=[f"ps:ctx{qi // 4}"])

        pend = None
        for q0 in range(0, NT, 4):
            ng = min(4, NT - q0)
            for kt in range(q0 + ng):
                off = max(0, kt - q0)
                batch = list(range(q0 + off, q0 + ng))
                bank, sps, stk = nextst()
                OP("tensor", lambda e, kt=kt, q0=q0, off=off, ng=ng, sps=sps,
                   r0=r0: e.matmul(
                    sps[:, off * 128:ng * 128],
                    t_k[r0:r0 + 64, kt * 128:(kt + 1) * 128],
                    t_q[r0:r0 + 64, (q0 + off) * 128:(q0 + ng) * 128],
                    start=True, stop=True),
                    reads=["krope", "qrope"], writes=[stk])
                if kt >= q0:
                    OP("vector", lambda e, off=off, sps=sps: e.tensor_add(
                        sps[:, off * 128:(off + 1) * 128],
                        sps[:, off * 128:(off + 1) * 128],
                        t_mskT[:]),
                        reads=[stk, "mskT"], writes=[stk])
                OP("scalar", lambda e, off=off, ng=ng, bank=bank, sps=sps,
                   c=c: e.activation(
                    t_pt[:, bank * 512 + off * 128:bank * 512 + ng * 128],
                    sps[:, off * 128:ng * 128],
                    AF.Exp, bias=t_nB[:, c:c + 1]),
                    reads=[stk], sreads=["negB"], writes=[f"pt{bank}"])
                if pend is not None:
                    emit_ctx(*pend)
                pend = (kt, q0, off, batch, bank)
            emit_ctx(*pend)
            pend = None

    def attn_evac(c, half):
        kc = c
        for q0 in range(0, NT, 4):
            n = min(4, NT - q0)
            OP("scalar", lambda e, q0=q0, n=n: e.activation(
                t_rows[0:1, 0:n * 128], ps_ctx[64:65, q0 * 128:(q0 + n) * 128],
                AF.Ln),
                reads=[f"ps:ctx{q0 // 4}"] + (
                    [f"ps:ctx{(q0 + n - 1) // 4}"] if (q0 + n - 1) // 4 != q0 // 4 else []),
                writes=["rows"])
            OP("scalar", lambda e, n=n: e.activation(
                t_rec[0:1, 0:n * 128], t_rows[0:1, 0:n * 128], AF.Exp, scale=-1.0),
                reads=["rows"], writes=["rec"])
            bank, sps, stk = nextst()
            OP("tensor", lambda e, n=n, sps=sps: e.matmul(
                sps[0:64, 0:n * 128], t_onerb[0:1, 0:64],
                t_rec[0:1, 0:n * 128], start=True, stop=True),
                reads=["rec", "onerb"], writes=[stk])
            OP("scalar", lambda e, n=n, sps=sps: e.copy(t_gb[:, 0:n * 128],
                                                        sps[0:64, 0:n * 128]),
               reads=[stk], writes=["gbs"])
            tau = (q0 * 128) // TW
            bkeys = sorted({f"B:{kc}:{((q0 + i) * 128) // TW}" for i in range(n)})
            OP("vector", lambda e, q0=q0, n=n, half=half, kc=kc: e.tensor_mul(
                t_B[:, kc * S + q0 * 128:kc * S + (q0 + n) * 128][64 * half:64 * half + 64, :],
                ps_ctx[0:64, q0 * 128:(q0 + n) * 128],
                t_gb[:, 0:n * 128]),
                reads=[f"ps:ctx{q0 // 4}", "gbs"], writes=bkeys)

    for c in range(KC):
        proj_rope(c, "a", 0, h_wq, t_q, "qrope", 0.125)
        proj_rope(c, "b", 1024, h_wk, t_k, "krope", 1.0)
        build_V(c)
        bound_B(c)
        if c > 0:
            attn_evac(c - 1, 1)
        attention_head(c, 0)
        attn_evac(c, 0)
        attention_head(c, 1)
    attn_evac(KC - 1, 1)

    if upto == 3:
        return dbg_finish(
            [(h_out[kc * 128:(kc + 1) * 128, :], t_B[:, kc * S:(kc + 1) * S],
              [f"B:{kc}:{tt}" for tt in range(TAU)]) for kc in range(KC)])

    y_sl = t_f32[:, 0:KC * TW]
    for half in range(2):
        OP("gpsimd", lambda e, half=half: e.dma_start(
            out=wr(half, 0, 4096).rearrange("p (kc f) -> p kc f", f=512),
            in_=h_wo[half].rearrange("(kc p) f -> p kc f", p=128)),
            writes=[f"wr{half}a", f"wr{half}b", f"wr{half}c"], kind="d",
            dgrp=f"wr{half}a")
    for tau in range(TAU):
        t0 = tau * TW
        for ic in range(KC):
            half, iw = ic // 4, ic % 4
            pse, pk = (ps_a, "ps:A") if ic % 2 == 0 else (ps_b, "ps:B")
            for kc in range(KC):
                OP("tensor", lambda e, half=half, iw=iw, kc=kc, t0=t0, pse=pse: e.matmul(
                    pse[:, 0:TW],
                    wr(half, kc * 512 + iw * 128, 128),
                    t_B[:, kc * S + t0:kc * S + t0 + TW],
                    start=(kc == 0), stop=(kc == KC - 1)),
                    reads=[f"wr{half}a", f"B:{kc}:{tau}"], writes=[pk])
            OP("vector", lambda e, ic=ic, t0=t0, pse=pse, pk=pk: e.tensor_add(
                y_sl[:, ic * TW:(ic + 1) * TW], pse[:, 0:TW],
                t_xn[:, ic * S + t0:ic * S + t0 + TW]),
                reads=[pk, f"xn:{ic}:{tau}"], writes=[f"y:{ic}", "f32a"])
        for ic in range(KC):
            si = t_sil[:, 0:TW]
            OP("scalar", lambda e, ic=ic, si=si: e.activation(
                si, y_sl[:, ic * TW:(ic + 1) * TW], AF.Square),
                reads=[f"y:{ic}"], writes=["ysq", "silreg"])
            OP("tensor", lambda e, ic=ic, si=si: e.matmul(
                ps_st[0:1, 0:TW], t_onesb[:], si,
                start=(ic == 0), stop=(ic == KC - 1)),
                reads=["ysq", "silreg", "onesb"], writes=["ps:st0"])
        rrow = t_rows[0:1, 0:TW]
        OP("scalar", lambda e: e.activation(rrow, ps_st[0:1, 0:TW], AF.Sqrt,
                                            scale=1.0 / D, bias=EPS),
           reads=["ps:st0", "epsc"], writes=["r2row", "rows"])
        OP("vector", lambda e: e.reciprocal(rrow, rrow), reads=["r2row"], writes=["r2row"])
        OP("tensor", lambda e: e.matmul(ps_st[:, 0:TW], t_oner[:], rrow,
                                        start=True, stop=True),
           reads=["r2row", "oner"], writes=["ps:st0"])
        OP("scalar", lambda e: e.copy(rstd2_b, ps_st[:, 0:TW]),
           reads=["ps:st0"], writes=["bc3", "bc0"])
        for ic in range(KC):
            OP("vector", lambda e, ic=ic, t0=t0: e.scalar_tensor_tensor(
                t_A[:, ic * S + t0:ic * S + t0 + TW],
                y_sl[:, ic * TW:(ic + 1) * TW], t_n2[:, ic:ic + 1], rstd2_b,
                ALU.mult, ALU.mult),
                reads=[f"y:{ic}", "bc3"], sreads=["n2"], writes=[f"A:{ic}:{tau}"])
            xp, xk = xping[ic % 2], f"xp{ic % 2}"
            OP("sync", lambda e, ic=ic, t0=t0, xp=xp: e.dma_start(
                out=xp[:], in_=h_xfm[ic * 128:(ic + 1) * 128, t0:t0 + TW]),
                writes=[xk, "f32a"], kind="d", dgrp=xk)
            ev, ek = t_ev[:, (ic % 2) * TW:(ic % 2 + 1) * TW], f"ev{ic % 2}"
            OP("vector", lambda e, ic=ic, xp=xp, ev=ev: e.tensor_add(
                ev, y_sl[:, ic * TW:(ic + 1) * TW], xp[:]),
                reads=[f"y:{ic}", xk], writes=[ek])
            OP("gpsimd", lambda e, ic=ic, t0=t0, ev=ev: e.dma_start(
                out=h_out[ic * 128:(ic + 1) * 128, t0:t0 + TW], in_=ev),
                reads=[ek], writes=[f"out:{ic}:{tau}"], kind="d",
                dgrp=f"oute{ic % 2}")

    if upto == 4:
        outkeys = [f"out:{ic}:{tt}" for ic in range(KC) for tt in range(TAU)]
        sch.barrier_wait("sync", outkeys)
        return finish()

    emit_offsets()

    for g in range(NT):
        OP("tensor", lambda e, g=g: e.matmul(
            ps_a[:, g * E:(g + 1) * E], t_ltri[:], t_mask[:, g * E:(g + 1) * E],
            start=(g == 0), stop=False),
            reads=["ltri", "maskal"], writes=["ps:A"])
    OP("tensor", lambda e: e.matmul(
        ps_a[:, 0:NT * E], t_oner[:], t_offs[0:1, 0:NT * E], start=False, stop=True),
        reads=["oner", "offr"], writes=["ps:A"])
    OP("vector", lambda e: e.tensor_single_scalar(
        t_kept[:], ps_a[:, 0:NT * E], CAPTH, ALU.is_lt),
        reads=["ps:A"], writes=["keptal"])
    for g in range(NT):
        OP("vector", lambda e, g=g: e.scalar_tensor_tensor(
            t_gall[:, g * E:(g + 1) * E], t_mask[:, g * E:(g + 1) * E],
            t_gate[:, g:g + 1], t_kept[:, g * E:(g + 1) * E], ALU.mult, ALU.mult),
            reads=["maskal", "keptal"], sreads=["gateal"], writes=["gall"])
    for g in range(NT):
        OP("tensor", lambda e, g=g: e.transpose(
            ps_b[0:E, 0:128], t_gall[:, g * E:(g + 1) * E], t_idn[:]),
            reads=["gall", "idn"], writes=["ps:B"])
        OP("scalar", lambda e, g=g: e.copy(t_gt[:, g * 128:(g + 1) * 128],
                                           ps_b[0:E, 0:128]),
           reads=["ps:B"], writes=["gt"])
    if upto == 41:
        return dbg_finish([
            (h_out[0:128, 0:NT * E], t_kept[:], ["keptal"]),
            (h_out[128:256, 0:NT * E], t_gall[:], ["gall"]),
            (h_out[256:256 + E, 0:S], t_gt[:], ["gt"]),
        ])
    for ee in range(E):
        for tau in range(TAU):
            OP("tensor", lambda e, ee=ee, tau=tau: e.matmul(
                ps_b[:, 0:TW], t_sel[:, ee * 128:ee * 128 + 128],
                t_gt[:, tau * TW:(tau + 1) * TW], start=True, stop=True),
                reads=["sel", "gt"], writes=["ps:B"])
            OP("scalar", lambda e, ee=ee, tau=tau: e.copy(
                t_B[:, ee * S + tau * TW:ee * S + (tau + 1) * TW], ps_b[:, 0:TW]),
                reads=["ps:B"], writes=[f"B:{ee}:{tau}"])
    if upto == 43:
        return dbg_finish(
            [(h_out[ee * 128:ee * 128 + 128, 0:S], t_B[:, ee * S:(ee + 1) * S],
              [f"B:{ee}:{tt}" for tt in range(TAU)]) for ee in range(E)])

    def moe_psum(ic):
        if ic < 4:
            return ps_ctx[:, ic * 512:(ic + 1) * 512], f"ps:ctx{ic}"
        if ic < 6:
            return ps_st[:, (ic - 4) * 512:(ic - 3) * 512], f"ps:st{ic - 4}"
        return (ps_a, "ps:A") if ic == 6 else (ps_b, "ps:B")

    EMM = {45: 0, 46: 0, 47: 1, 48: 2, 49: 4}.get(upto, E)
    wjobs = [(tau, ee, hh) for tau in range(TAU) for ee in range(E)
             for hh in range(2)]

    def wslot3(j):
        s = j % 3
        if s < 2:
            return wr(s, 0, 4096), [f"wr{s}a", f"wr{s}b", f"wr{s}c"]
        return t_w2g[:, 0:4096], ["w2g0", "w2g1", "w2g2", "w2g3"]

    def emit_wdma(j):
        tau_, ee_, hh_ = wjobs[j]
        wbuf, wks = wslot3(j)
        OP("gpsimd", lambda e, ee=ee_, hh=hh_, wbuf=wbuf: e.dma_start(
            out=wbuf.rearrange("p (kc f) -> p kc f", f=512),
            in_=h_we[ee, hh].rearrange("(kc p) f -> p kc f", p=128)),
            writes=wks, kind="d", dgrp=wks[0])

    if upto != 45:
        emit_wdma(0)
        emit_wdma(1)
        emit_wdma(2)
    ji = 0
    for tau in range(TAU):
        t0 = tau * TW
        for ee in range(E):
            for kc in range(KC):
                OP("vector", lambda e, ee=ee, kc=kc, t0=t0, tau=tau: e.tensor_mul(
                    t_xne[:, kc * TW:(kc + 1) * TW],
                    t_xn[:, kc * S + t0:kc * S + t0 + TW],
                    t_B[:, ee * S + t0:ee * S + t0 + TW]),
                    reads=[f"xn:{kc}:{tau}", f"B:{ee}:{tau}"],
                    writes=[f"xne:{kc}", "xnereg"])
            for hh in range(2):
                j = ji
                ji += 1
                wbuf, wks = wslot3(j)
                if ee < EMM:
                    for kc in range(KC):
                        for iw in range(4):
                            ic = hh * 4 + iw
                            pse, pk = moe_psum(ic)
                            OP("tensor", lambda e, kc=kc, iw=iw, pse=pse, wbuf=wbuf,
                               ee=ee, hh=hh: e.matmul(
                                pse[:, 0:TW],
                                wbuf[:, kc * 512 + iw * 128:kc * 512 + (iw + 1) * 128],
                                t_xne[:, kc * TW:(kc + 1) * TW],
                                start=(ee == 0 and kc == 0), stop=False),
                                reads=[wks[0], f"xne:{kc}"], writes=[pk])
                if upto != 45 and j + 3 < len(wjobs):
                    emit_wdma(j + 3)
        if upto in (45, 46):
            continue
        for ic in range(KC):
            pse, pk = moe_psum(ic)
            OP("tensor", lambda e, ic=ic, tau=tau, pse=pse: e.matmul(
                pse[:, 0:TW], t_be[:, ic * 128:(ic + 1) * 128],
                t_gt[:, tau * TW:(tau + 1) * TW], start=False, stop=True),
                reads=["be", "gt"], writes=[pk])
            ev, ek = t_ev[:, (ic % 2) * TW:(ic % 2 + 1) * TW], f"ev{ic % 2}"
            OP("scalar", lambda e, pse=pse, ev=ev: e.copy(ev, pse[:, 0:TW]),
               reads=[pk], writes=[ek])
            if upto == 44:
                OP("gpsimd", lambda e, ic=ic, t0=t0, ev=ev: e.dma_start(
                    out=h_out[ic * 128:(ic + 1) * 128, t0:t0 + TW], in_=ev),
                    reads=[ek, f"out:{ic}:{tau}"], writes=[f"out:{ic}:{tau}"],
                    kind="d", dgrp=f"oute{ic % 2}")
            else:
                OP("gpsimd", lambda e, ic=ic, t0=t0, ev=ev: e.dma_start(
                    out=h_out[ic * 128:(ic + 1) * 128, t0:t0 + TW], in_=ev,
                    accum_op=ALU.add),
                    reads=[ek, f"out:{ic}:{tau}"], writes=[f"out:{ic}:{tau}"],
                    kind="d", dgrp=f"oute{ic % 2}")

    if upto in (45, 46):
        return dbg_finish([(h_out[0:128, 0:TW], t_xne[:, 0:TW], ["xne:0"])])
    if upto in (5, 44, 47, 48, 49):
        outkeys = [f"out:{ic}:{tt}" for ic in range(KC) for tt in range(TAU)]
        sch.barrier_wait("sync", outkeys)
        sch.barrier_wait("gpsimd", outkeys)
        return finish()

    FG = 4
    ff_acc = t_f32[:, 0:KC * TW]
    for tau in range(TAU):
        t0 = tau * TW
        for fc in range(FC):
            slot = fc % 2
            OP("gpsimd", lambda e, fc=fc, slot=slot: e.dma_start(
                out=wr(slot, 0, KC * 128).rearrange("p (kc f) -> p kc f", f=128),
                in_=h_w1[fc].rearrange("(kc p) f -> p kc f", p=128)),
                writes=[f"wr{slot}a"], kind="d", dgrp=f"wr{slot}a")
            OP("gpsimd", lambda e, fc=fc, slot=slot: e.dma_start(
                out=wr(slot, 1024, KC * 128).rearrange("p (kc f) -> p kc f", f=128),
                in_=h_w3[fc].rearrange("(kc p) f -> p kc f", p=128)),
                writes=[f"wr{slot}b"], kind="d", dgrp=f"wr{slot}b")
            p1, p1k = (ps_a, "ps:A") if slot == 0 else (ps_st[:, 0:512], "ps:st0")
            p3, p3k = (ps_b, "ps:B") if slot == 0 else (ps_st[:, 512:1024], "ps:st1")
            for kc in range(KC):
                OP("tensor", lambda e, kc=kc, t0=t0, slot=slot, p1=p1: e.matmul(
                    p1[:, 0:TW], wr(slot, kc * 128, 128),
                    t_A[:, kc * S + t0:kc * S + t0 + TW],
                    start=(kc == 0), stop=(kc == KC - 1)),
                    reads=[f"wr{slot}a", f"A:{kc}:{tau}"], writes=[p1k])
            for kc in range(KC):
                OP("tensor", lambda e, kc=kc, t0=t0, slot=slot, p3=p3: e.matmul(
                    p3[:, 0:TW], wr(slot, 1024 + kc * 128, 128),
                    t_A[:, kc * S + t0:kc * S + t0 + TW],
                    start=(kc == 0), stop=(kc == KC - 1)),
                    reads=[f"wr{slot}b", f"A:{kc}:{tau}"], writes=[p3k])
            si, sk = t_sil[:, slot * TW:(slot + 1) * TW], f"sil{slot}"
            OP("scalar", lambda e, si=si, p1=p1: e.activation(si, p1[:, 0:TW], AF.Silu),
               reads=[p1k], writes=[sk, "silreg"])
            j = fc % FG
            pgt, pgk = t_pg[:, j * TW:(j + 1) * TW], f"pg{j}"
            OP("vector", lambda e, si=si, p3=p3, pgt=pgt: e.tensor_mul(
                pgt, si, p3[:, 0:TW]),
                reads=[sk, p3k], writes=[pgk])
            OP("gpsimd", lambda e, fc=fc, j=j: e.dma_start(
                out=t_w2g[:, j * D:(j + 1) * D], in_=h_w2[fc]),
                writes=[f"w2g{j}"], kind="d", dgrp=f"w2g{j}")
            if j == FG - 1:
                fg0 = fc - FG + 1
                for oc in range(KC):
                    fps = ps_ctx[:, (oc % 2) * 512:(oc % 2) * 512 + TW]
                    fpk = f"ps:ctx{oc % 2}"
                    for jj in range(FG):
                        OP("tensor", lambda e, jj=jj, oc=oc, fps=fps: e.matmul(
                            fps, t_w2g[:, jj * D + oc * 128:jj * D + (oc + 1) * 128],
                            t_pg[:, jj * TW:(jj + 1) * TW],
                            start=(jj == 0), stop=(jj == FG - 1)),
                            reads=[f"w2g{jj}", f"pg{jj}"], writes=[fpk])
                    if fg0 == 0:
                        OP("vector", lambda e, oc=oc, fps=fps: e.tensor_copy(
                            ff_acc[:, oc * TW:(oc + 1) * TW], fps),
                            reads=[fpk], writes=[f"y:{oc}"])
                    else:
                        OP("vector", lambda e, oc=oc, fps=fps: e.tensor_add(
                            ff_acc[:, oc * TW:(oc + 1) * TW],
                            ff_acc[:, oc * TW:(oc + 1) * TW], fps),
                            reads=[fpk, f"y:{oc}"], writes=[f"y:{oc}"])
        for oc in range(KC):
            acc_op = {61: ALU.bypass}.get(upto, ALU.add)
            OP("gpsimd", lambda e, oc=oc, t0=t0, acc_op=acc_op: e.dma_start(
                out=h_out[oc * 128:(oc + 1) * 128, t0:t0 + TW],
                in_=ff_acc[:, oc * TW:(oc + 1) * TW], accum_op=acc_op),
                reads=[f"y:{oc}", f"out:{oc}:{tau}"],
                writes=[f"out:{oc}:{tau}", f"ffdma{oc % 2}"], kind="d",
                dgrp=f"oute{oc % 2}")

    outkeys = [f"out:{ic}:{tau}" for ic in range(KC) for tau in range(TAU)]
    sch.barrier_wait("gpsimd", outkeys)
    sch.barrier_wait("sync", outkeys)
    return finish()


_CACHE = {}


def _consts(S):
    DKH = 64
    pos = np.arange(S, dtype=np.float64)
    k = np.arange(DKH) // 2
    ang = pos[:, None] / np.power(10000.0, 2.0 * k / DKH)[None, :]
    cos = np.cos(ang)
    sin = np.sin(ang) * np.where(np.arange(DKH) % 2 == 0, -1.0, 1.0)[None, :]
    cosT = np.ascontiguousarray(np.tile(cos.T, (2, 1)).astype(BF))
    sinT = np.ascontiguousarray(np.tile(sin.T, (2, 1)).astype(BF))
    idn = np.eye(128, dtype=np.float32)
    prm = np.zeros((128, 128), dtype=np.float32)
    prm[np.arange(128), np.arange(128) ^ 1] = 1.0
    mskT = np.ascontiguousarray(
        np.where(np.arange(128)[:, None] > np.arange(128)[None, :], NEG, 0.0).astype(BF))
    ltri = np.ascontiguousarray(np.triu(np.ones((128, 128), dtype=np.float32), k=1))
    selm = np.zeros((E, E * 128), dtype=np.float32)
    for ee in range(E):
        selm[ee, ee * 128:(ee + 1) * 128] = 1.0
    return dict(cosT=cosT, sinT=sinT, idn=idn, prm=np.ascontiguousarray(prm.astype(BF)),
                mskT=mskT, ltri=ltri, selm=np.ascontiguousarray(selm.astype(BF)))


def prep_inputs(x, wq, wk, wv, wo, n1, n2, w1, w2, w3, wg, We, be, S):
    cst = _consts(S)
    f32 = np.float32

    def T(w):
        return np.ascontiguousarray(np.asarray(w, dtype=f32).T)

    def blocks(wT, n):
        return np.ascontiguousarray(wT.reshape(D, n, wT.shape[1] // n).transpose(1, 0, 2))

    wq_b = blocks(T(wq), KC)
    wk_b = blocks(T(wk), KC)
    wv_b = blocks(T(wv), KC)
    wo_b = blocks(T(wo), 2)
    w1_b = blocks(T(w1), FC)
    w3_b = blocks(T(w3), FC)
    w2_b = np.ascontiguousarray(T(w2).reshape(FC, 128, D))
    WeT = np.asarray(We, dtype=f32).transpose(0, 2, 1)
    we_b = np.ascontiguousarray(WeT.reshape(E, D, 2, 512).transpose(0, 2, 1, 3))
    n1c = np.ascontiguousarray(np.asarray(n1, dtype=f32).reshape(KC, 128).T)
    n2c = np.ascontiguousarray(np.asarray(n2, dtype=f32).reshape(KC, 128).T)
    base = dict(wq_b=wq_b, wk_b=wk_b, wv_b=wv_b, wo_b=wo_b, w1_b=w1_b,
                w3_b=w3_b, w2_b=w2_b, we_b=we_b,
                wg=np.ascontiguousarray(np.asarray(wg, dtype=f32)),
                be=np.ascontiguousarray(np.asarray(be, dtype=f32)),
                n1c=n1c, n2c=n2c, **cst)
    xa = np.asarray(x, dtype=f32)
    in_maps = []
    for b in range(xa.shape[0]):
        m = dict(base)
        m["x_tm"] = np.ascontiguousarray(xa[b])
        m["x_fm"] = np.ascontiguousarray(xa[b].T)
        m["selc"] = (np.arange(E) < b).astype(f32).reshape(E, 1)
        in_maps.append(m)
    return in_maps


def kernel(L0_attn_x, L0_attn_wq, L0_attn_wk, L0_attn_wv, L0_attn_wo,
           L0_attn_norm1_w, L0_attn_norm2_w, L0_attn_w1, L0_attn_w2,
           L0_attn_w3, L0_ffn_wg, L0_ffn_W, L0_ffn_b):
    import os
    upto = int(os.environ.get("K_UPTO", "9"))
    S = np.asarray(L0_attn_x).shape[1]
    key = (S, upto)
    if key not in _CACHE:
        _CACHE[key] = build_program(S, upto)
    nc = _CACHE[key]
    in_maps = prep_inputs(L0_attn_x, L0_attn_wq, L0_attn_wk, L0_attn_wv,
                          L0_attn_wo, L0_attn_norm1_w, L0_attn_norm2_w,
                          L0_attn_w1, L0_attn_w2, L0_attn_w3, L0_ffn_wg,
                          L0_ffn_W, L0_ffn_b, S)
    res = run_bass_kernel_spmd(nc, in_maps, core_ids=list(range(8)))
    out = np.stack([np.asarray(res.results[b]["out"]).T for b in range(8)])
    return np.ascontiguousarray(out.astype(np.float32))


# version 09
# speedup vs baseline: 1.3285x; 1.0012x over previous
import contextlib
import numpy as np
import ml_dtypes
import concourse.bass as bass
import concourse.mybir as mybir
from concourse.bass_utils import run_bass_kernel_spmd

F32 = mybir.dt.float32
BF16 = mybir.dt.bfloat16
AF = mybir.ActivationFunctionType
ALU = mybir.AluOpType
AX = mybir.AxisListType.X

D = 1024
KC = 8
DFF = 4096
FC = 32
E = 8
EPS = 1e-5
NEG = -1e9

BF = ml_dtypes.bfloat16


class Sched:
    ENGS = ("sync", "scalar", "vector", "tensor", "gpsimd")

    def __init__(self, nc, es):
        self.nc = nc
        self.es = es
        self.seq = {e: 0 for e in self.ENGS}
        self.cseq = 0
        self.sem = {e: es.enter_context(nc.semaphore(f"is_{e}")) for e in self.ENGS}
        self.csem = es.enter_context(nc.semaphore("cs"))
        self.dgrp = {}
        self.last_w = {}
        self.readers = {}
        self.watermark = {e: {} for e in self.ENGS}
        self.eng_ops = {e: [] for e in self.ENGS}

    def _group(self, name, queue):
        g = self.dgrp.get(name)
        if g is None:
            g = [self.es.enter_context(self.nc.semaphore(f"dg_{name}")), 0, queue]
            self.dgrp[name] = g
        assert g[2] == queue, f"dma group {name} used from {queue} and {g[2]}"
        return g

    def _need(self, eng, tok, waits, force=False, dma_queue=None):
        if tok is None:
            return
        kind = tok[0]
        if kind == "i":
            _, peng, val = tok
            if peng == eng and eng == "tensor" and not force:
                return
            sem, target, wkey = self.sem[peng], val, ("i", peng)
        elif kind == "d":
            _, grp, val = tok
            g = self.dgrp[grp]
            if dma_queue is not None and g[2] == dma_queue:
                return
            sem, target, wkey = g[0], 16 * val, ("d", grp)
        else:
            _, val = tok
            sem, target, wkey = self.csem, val, ("c",)
        wm = self.watermark[eng]
        if wm.get(wkey, 0) >= target:
            return
        wm[wkey] = target
        waits.append((sem, target))

    def op(self, eng, fn, reads=(), writes=(), kind="i", sreads=(), dgrp=None):
        waits = []
        dq = eng if kind == "d" else None
        for r in sreads:
            self._need(eng, self.last_w.get(r), waits, force=True, dma_queue=dq)
        for r in reads:
            self._need(eng, self.last_w.get(r), waits, dma_queue=dq)
        for w in writes:
            self._need(eng, self.last_w.get(w), waits, dma_queue=dq)
            for t in self.readers.get(w, ()):
                self._need(eng, t, waits, dma_queue=dq)
        if kind == "d":
            g = self._group(dgrp, eng)
            g[1] += 1
            tok = ("d", dgrp, g[1])
            inc = (g[0], 16)
        elif kind == "c":
            self.cseq += 1
            tok = ("c", self.cseq)
            inc = (self.csem, 1)
        else:
            self.seq[eng] += 1
            tok = ("i", eng, self.seq[eng])
            inc = (self.sem[eng], 1)
        self.eng_ops[eng].append((fn, waits, inc))
        for r in list(reads) + list(sreads):
            self.readers.setdefault(r, []).append(tok)
        for w in writes:
            self.last_w[w] = tok
            self.readers[w] = []
        return tok

    def barrier_wait(self, eng, keys):
        waits = []
        for k in keys:
            self._need(eng, self.last_w.get(k), waits)
            for t in self.readers.get(k, ()):
                self._need(eng, t, waits)
        if waits:
            self.eng_ops[eng].append((None, waits, None))

    def replay(self, block):
        for ename in self.ENGS:
            ops = self.eng_ops[ename]

            def make(op_list):
                def run(eng):
                    for fn, waits, inc in op_list:
                        for (s, tgt) in waits:
                            eng.wait_ge(s, tgt)
                        if fn is not None:
                            fn(eng).then_inc(*inc)
                return run

            getattr(block, ename)(make(ops))


def build_program(S, upto=9):
    assert S % 256 == 0
    TW = min(512, S)
    TAU = S // TW
    NT = S // 128
    GPT = TW // 128
    CAPTH = float(S) - 0.5

    nc = bass.Bass("TRN2", target_bir_lowering=False)

    h_xtm = nc.dram_tensor("x_tm", [S, D], F32, kind="ExternalInput")
    h_xfm = nc.dram_tensor("x_fm", [D, S], F32, kind="ExternalInput")
    h_wq = nc.dram_tensor("wq_b", [KC, D, 128], F32, kind="ExternalInput")
    h_wk = nc.dram_tensor("wk_b", [KC, D, 128], F32, kind="ExternalInput")
    h_wv = nc.dram_tensor("wv_b", [KC, D, 128], F32, kind="ExternalInput")
    h_wo = nc.dram_tensor("wo_b", [2, D, 512], F32, kind="ExternalInput")
    h_w1 = nc.dram_tensor("w1_b", [FC, D, 128], F32, kind="ExternalInput")
    h_w3 = nc.dram_tensor("w3_b", [FC, D, 128], F32, kind="ExternalInput")
    h_w2 = nc.dram_tensor("w2_b", [FC, 128, D], F32, kind="ExternalInput")
    h_we = nc.dram_tensor("we_b", [E, 2, D, 512], F32, kind="ExternalInput")
    h_wg = nc.dram_tensor("wg", [D, E], F32, kind="ExternalInput")
    h_be = nc.dram_tensor("be", [E, D], F32, kind="ExternalInput")
    h_n1 = nc.dram_tensor("n1c", [128, KC], F32, kind="ExternalInput")
    h_n2 = nc.dram_tensor("n2c", [128, KC], F32, kind="ExternalInput")
    h_cos = nc.dram_tensor("cosT", [128, S], BF16, kind="ExternalInput")
    h_sin = nc.dram_tensor("sinT", [128, S], BF16, kind="ExternalInput")
    h_idn = nc.dram_tensor("idn", [128, 128], F32, kind="ExternalInput")
    h_prm = nc.dram_tensor("prm", [128, 128], BF16, kind="ExternalInput")
    h_mskT = nc.dram_tensor("mskT", [128, 128], BF16, kind="ExternalInput")
    h_ltri = nc.dram_tensor("ltri", [128, 128], F32, kind="ExternalInput")
    h_sel = nc.dram_tensor("selm", [E, E * 128], BF16, kind="ExternalInput")
    h_selc = nc.dram_tensor("selc", [E, 1], F32, kind="ExternalInput")
    h_out = nc.dram_tensor("out", [D, S], F32, kind="ExternalOutput")
    d_ccin = nc.dram_tensor("cc_in", [1, 512], F32, kind="Internal")
    d_ccout = nc.dram_tensor("cc_out", [8, 512], F32, kind="Internal")

    es = contextlib.ExitStack()

    def sb(name, p, f, dt):
        return es.enter_context(nc.sbuf_tensor(name, [p, f], dt))

    t_xn = sb("xn", 128, KC * S, BF16)
    t_A = sb("uA", 128, KC * S, BF16)
    t_B = sb("uB", 128, KC * S, BF16)
    t_C = sb("uC", 128, NT * 130, BF16)
    t_q = sb("qbuf", 128, S, BF16)
    t_k = sb("kbuf", 128, S, BF16)
    t_xne = sb("xne", 128, max(KC * TW, S), BF16)
    t_wr = sb("wring", 128, 2 * 4096, BF16)
    t_cos = sb("cosb", 128, S, BF16)
    t_sin = sb("sinb", 128, S, BF16)
    t_f32 = sb("f32a", 128, max(KC * TW + 2 * TW, 3 * 1024), F32)
    t_f32b = sb("f32b", 128, 3 * TW, F32)
    t_pg = sb("pgr", 128, 4 * TW, BF16)
    t_w2g = sb("w2g", 128, 4 * D, BF16)
    t_rope = sb("ropes", 128, 2 * TW, BF16)
    t_pt = sb("ptb", 128, 1024, BF16)
    t_sil = sb("silb", 128, 2 * TW, BF16)
    t_ev = sb("evac", 128, 2 * TW, F32)
    t_gb = sb("gbrow", 64, 512, BF16)
    t_idn = sb("idnb", 128, 128, F32)
    t_prm = sb("prmb", 128, 128, BF16)
    t_mskT = sb("mskTb", 128, 128, BF16)
    t_ltri = sb("ltrib", 128, 128, F32)
    t_sel = sb("selb", E, E * 128, BF16)
    t_n1 = sb("n1b", 128, KC, F32)
    t_n2 = sb("n2b", 128, KC, F32)
    t_wg = sb("wgb", 128, KC * E, F32)
    t_be = sb("beb", E, D, BF16)
    t_stats = sb("statc", 128, 5 * NT, F32)
    t_statT = sb("statT", 3 * NT, 128, F32)
    t_rows = sb("rows", 1, max(TW, 512), F32)
    t_mask = sb("maskal", 128, NT * E, F32)
    t_gate = sb("gateal", 128, NT, F32)
    t_kept = sb("keptal", 128, NT * E, F32)
    t_gall = sb("gall", 128, NT * E, F32)
    t_gt = sb("gtb", E, S, BF16)
    t_offs = sb("offs", 1, NT * E + E, F32)
    t_allc = sb("allc", E, E, F32)
    t_selc = sb("selcb", E, 1, F32)
    t_ones = sb("onesf", 128, 1, F32)
    t_onesb = sb("onesbf", 128, 1, BF16)
    t_oner = sb("onerf", 1, 128, F32)
    t_onerb = sb("onerbf", 1, 128, BF16)
    t_nB = sb("negB", 128, KC, F32)
    t_sc = sb("smallc", 128, max(2 * NT, 20), F32)
    t_eps = sb("epsc", 128, 1, F32)
    nc.const_aps.register(F32, EPS, t_eps[:])
    t_sr = sb("smallr", 1, 32, F32)
    t_rec = sb("recrow", 1, 512, BF16)

    ps_st = es.enter_context(nc.psum_tensor("ps_st", [128, 1024], F32))
    ps_ctx = es.enter_context(nc.psum_tensor("ps_ctx", [128, 2048], F32))
    ps_a = es.enter_context(nc.psum_tensor("ps_a", [128, 512], F32))
    ps_b = es.enter_context(nc.psum_tensor("ps_b", [128, 512], F32))

    sch = Sched(nc, es)
    OP = sch.op

    def finish():
        with nc.allow_low_precision(reason="bf16 by design"):
            with nc.Block("main") as block:
                sch.replay(block)
        es.close()
        return nc

    def dbg_finish(stores):
        keys = []
        for idx, (o, i_, rd) in enumerate(stores):
            k = f"dbgout{idx}"
            OP("gpsimd", lambda e, o=o, i_=i_: e.dma_start(out=o, in_=i_),
               reads=rd, writes=[k], kind="d", dgrp="dbg")
            keys.append(k)
        sch.barrier_wait("gpsimd", keys)
        return finish()

    for hh, tt, key in ((h_idn, t_idn, "idn"), (h_prm, t_prm, "prm"),
                        (h_mskT, t_mskT, "mskT"), (h_ltri, t_ltri, "ltri"),
                        (h_sel, t_sel, "sel"), (h_n1, t_n1, "n1"),
                        (h_n2, t_n2, "n2"), (h_cos, t_cos, "cos"),
                        (h_sin, t_sin, "sin"), (h_selc, t_selc, "selc")):
        OP("sync", lambda e, hh=hh, tt=tt: e.dma_start(out=tt[:], in_=hh[:]),
           writes=[key], kind="d", dgrp=key)
    OP("sync", lambda e: e.dma_start(
        out=t_wg.ap().rearrange("p (kc e) -> p kc e", e=E),
        in_=h_wg.rearrange("(kc p) e -> p kc e", p=128)), writes=["wg"], kind="d", dgrp="wg")
    OP("gpsimd", lambda e: e.dma_start(out=t_be[:], in_=h_be[:]), writes=["be"], kind="d", dgrp="be")
    OP("vector", lambda e: e.memset(t_ones[:], 1.0), writes=["ones"])
    OP("vector", lambda e: e.memset(t_onesb[:], 1.0), writes=["onesb"])
    OP("vector", lambda e: e.memset(t_oner[:], 1.0), writes=["oner"])
    OP("vector", lambda e: e.memset(t_onerb[:], 1.0), writes=["onerb"])
    OP("vector", lambda e: e.memset(t_rows[0:1, 0:512], 0.0), writes=["cpad", "rows"])
    OP("vector", lambda e: e.memset(t_eps[:], EPS), writes=["epsc"])

    xtm = (t_f32[:, 0:1024], t_f32[:, 1024:2048])
    sqdump = t_f32[:, 2048:3072]
    C_S1, C_S2, C_MEAN, C_RSTD, C_RSTD1 = 0, NT, 2 * NT, 3 * NT, 4 * NT
    for i in range(NT):
        xb, key = xtm[i % 2], f"xtm{i % 2}"
        OP("sync", lambda e, xb=xb, i=i: e.dma_start(
            out=xb[:], in_=h_xtm[i * 128:(i + 1) * 128, :]), writes=[key, "f32a"],
            kind="d", dgrp=key)
        OP("vector", lambda e, xb=xb, i=i: e.reduce_sum(
            t_stats[:, C_S1 + i:C_S1 + i + 1], xb[:], axis=AX),
            reads=[key], writes=["stats"])
        OP("scalar", lambda e, xb=xb, i=i: e.activation(
            sqdump[:], xb[:], AF.Square, accum_out=t_stats[:, C_S2 + i:C_S2 + i + 1]),
            reads=[key], writes=["stats", "f32a"])
    S1 = t_stats[:, C_S1:C_S1 + NT]
    S2 = t_stats[:, C_S2:C_S2 + NT]
    MEAN = t_stats[:, C_MEAN:C_MEAN + NT]
    RSTD = t_stats[:, C_RSTD:C_RSTD + NT]
    RSTD1 = t_stats[:, C_RSTD1:C_RSTD1 + NT]
    TV = t_sc[:, 0:NT]
    SD = t_sc[:, NT:2 * NT]
    OP("vector", lambda e: e.tensor_scalar_mul(MEAN, S1, 1.0 / D), reads=["stats"], writes=["stats"])
    OP("scalar", lambda e: e.activation(TV, MEAN, AF.Square), reads=["stats"], writes=["tsc"])
    OP("vector", lambda e: e.scalar_tensor_tensor(TV, S2, 1.0 / D, TV, ALU.mult, ALU.subtract),
       reads=["stats", "tsc"], writes=["tsc"])
    OP("scalar", lambda e: e.activation(SD, TV, AF.Sqrt, bias=EPS),
       reads=["tsc", "epsc"], writes=["tsc"])
    OP("vector", lambda e: e.reciprocal(RSTD, SD), reads=["tsc"], writes=["stats"])
    OP("scalar", lambda e: e.activation(RSTD1, RSTD, AF.Square), reads=["stats"], writes=["stats"])
    OP("vector", lambda e: e.tensor_mul(RSTD1, RSTD1, TV), reads=["stats", "tsc"], writes=["stats"])
    OP("scalar", lambda e: e.activation(RSTD1, RSTD1, AF.Sqrt, bias=EPS),
       reads=["stats", "epsc"], writes=["stats"])
    OP("vector", lambda e: e.reciprocal(RSTD1, RSTD1), reads=["stats"], writes=["stats"])
    OP("tensor", lambda e: e.transpose(ps_a[0:3 * NT, 0:128],
                                       t_stats[:, C_MEAN:C_MEAN + 3 * NT], t_idn[:]),
       reads=["stats", "idn"], writes=["ps:A"])
    OP("scalar", lambda e: e.copy(t_statT[:], ps_a[0:3 * NT, 0:128]),
       reads=["ps:A"], writes=["statT"])

    xn32 = t_f32[:, 0:KC * TW]
    xping = (t_f32[:, KC * TW:KC * TW + TW], t_f32[:, KC * TW + TW:KC * TW + 2 * TW])
    mean_b = t_f32b[:, 0:TW]
    rstd_b = t_f32b[:, TW:2 * TW]
    rstd1_b = t_f32b[:, 2 * TW:3 * TW]
    rstd2_b = t_f32b[:, 0:TW]

    for tau in range(TAU):
        t0 = tau * TW
        for si, dst, bk in ((0, mean_b, "bc0"), (1, rstd_b, "bc1"), (2, rstd1_b, "bc2")):
            g0 = tau * GPT
            OP("sync", lambda e, si=si, g0=g0: e.dma_start(
                out=t_rows[0:1, 0:TW].rearrange("o (a b) -> o a b", b=128),
                in_=t_statT[si * NT + g0:si * NT + g0 + GPT, :]),
                reads=["statT"], writes=["rows"], kind="d", dgrp="rows")
            OP("tensor", lambda e, si=si: e.matmul(
                ps_b[:, 0:TW], t_oner[:], t_rows[0:1, 0:TW],
                start=True, stop=True),
                reads=["rows", "oner"], writes=["ps:B"])
            OP("scalar", lambda e, dst=dst: e.copy(dst, ps_b[:, 0:TW]),
               reads=["ps:B"], writes=[bk])
        for kc in range(KC):
            xp, xk = xping[kc % 2], f"xp{kc % 2}"
            OP("sync", lambda e, xp=xp, kc=kc, t0=t0: e.dma_start(
                out=xp[:], in_=h_xfm[kc * 128:(kc + 1) * 128, t0:t0 + TW]),
                writes=[xk, "f32a"], kind="d", dgrp=xk)
            xnk = xn32[:, kc * TW:(kc + 1) * TW]
            OP("vector", lambda e, xp=xp, xnk=xnk: e.tensor_sub(xnk, xp[:], mean_b),
               reads=[xk, "bc0"], writes=[f"xn32:{kc}", "f32a"])
            OP("vector", lambda e, xnk=xnk: e.tensor_mul(xnk, xnk, rstd_b),
               reads=[f"xn32:{kc}", "bc1"], writes=[f"xn32:{kc}"])
            OP("scalar", lambda e, xnk=xnk, kc=kc, t0=t0: e.copy(
                t_xn[:, kc * S + t0:kc * S + t0 + TW], xnk),
                reads=[f"xn32:{kc}"], writes=[f"xn:{kc}:{tau}"])
            OP("vector", lambda e, xnk=xnk, kc=kc, t0=t0: e.scalar_tensor_tensor(
                t_A[:, kc * S + t0:kc * S + t0 + TW], xnk, t_n1[:, kc:kc + 1],
                rstd1_b, ALU.mult, ALU.mult),
                reads=[f"xn32:{kc}", "bc2"], sreads=["n1"], writes=[f"A:{kc}:{tau}"])
        for g in range(GPT):
            gg = tau * GPT + g
            for kc in range(KC):
                OP("tensor", lambda e, kc=kc, g=g: e.matmul(
                    ps_a[:, 0:E],
                    xn32[:, kc * TW + g * 128:kc * TW + g * 128 + 128],
                    t_wg[:, kc * E:(kc + 1) * E],
                    start=(kc == 0), stop=(kc == KC - 1)),
                    reads=[f"xn32:{kc}", "wg"], writes=["ps:A"])
            m8, nm8, sume = t_sc[:, 0:1], t_sc[:, 1:2], t_sc[:, 2:3]
            lgs = t_sc[:, 4:12]
            dmp = t_sc[:, 12:20]
            OP("vector", lambda e, lgs=lgs: e.tensor_copy(lgs, ps_a[:, 0:E]),
               reads=["ps:A"], writes=["lgs", "tsc"])
            OP("vector", lambda e, lgs=lgs: e.reduce_max(m8, lgs, axis=AX),
               reads=["lgs"], writes=["m8", "tsc"])
            OP("vector", lambda e, gg=gg, lgs=lgs: e.tensor_single_scalar(
                t_mask[:, gg * E:(gg + 1) * E], lgs, m8, ALU.is_ge),
                reads=["lgs"], sreads=["m8"], writes=["maskal"])
            OP("vector", lambda e: e.tensor_scalar_mul(nm8, m8, -1.0),
               reads=["m8"], writes=["nm8", "tsc"])
            OP("scalar", lambda e, dmp=dmp, lgs=lgs: e.activation(
                dmp, lgs, AF.Exp, bias=nm8, accum_out=sume),
                reads=["lgs"], sreads=["nm8"], writes=["sume", "tsc"])
            OP("vector", lambda e, gg=gg: e.reciprocal(t_gate[:, gg:gg + 1], sume),
               reads=["sume"], writes=["gateal"])
            if upto == 2:
                OP("vector", lambda e, gg=gg, lgs=lgs: e.tensor_copy(
                    t_kept[:, gg * E:(gg + 1) * E], lgs),
                    reads=["lgs"], writes=["keptal"])
                OP("vector", lambda e, gg=gg: e.tensor_copy(
                    t_gall[:, gg * E:gg * E + 1], m8),
                    sreads=["m8"], writes=["gall"])

    if upto == 1:
        return dbg_finish(
            [(h_out[kc * 128:(kc + 1) * 128, :], t_xn[:, kc * S:(kc + 1) * S],
              [f"xn:{kc}:{tt}" for tt in range(TAU)]) for kc in range(KC)])

    for g in range(NT):
        OP("tensor", lambda e, g=g: e.matmul(
            ps_b[0:1, g * E:(g + 1) * E], t_ones[:], t_mask[:, g * E:(g + 1) * E],
            start=(g == 0), stop=(g == NT - 1)),
            reads=["maskal", "ones"], writes=["ps:B"])
    OP("vector", lambda e: e.tensor_copy(
        t_rows[0:1, 128:128 + NT * E], ps_b[0:1, 0:NT * E]),
        reads=["ps:B", "cpad"], writes=["cpad", "rows", "cnts"])
    OP("vector", lambda e: e.memset(t_offs[0:1, 0:E], 0.0), writes=["offr"])
    for g in range(1, NT):
        OP("vector", lambda e, g=g: e.tensor_add(
            t_offs[0:1, g * E:(g + 1) * E], t_offs[0:1, (g - 1) * E:g * E],
            t_rows[0:1, 128 + (g - 1) * E:128 + g * E]),
            reads=["cnts", "offr"], writes=["offr"])
    OP("vector", lambda e: e.tensor_add(
        t_rows[0:1, 0:E], t_offs[0:1, (NT - 1) * E:NT * E],
        t_rows[0:1, 128 + (NT - 1) * E:128 + NT * E]),
        reads=["cnts", "offr", "cpad"], writes=["cpad", "rows"])
    OP("sync", lambda e: e.dma_start(out=d_ccin[:], in_=t_rows[0:1, 0:512]),
       reads=["cpad", "rows", "cnts"], writes=["ccin"], kind="d", dgrp="ccin")
    OP("gpsimd", lambda e: e.collective_compute(
        "AllGather", ALU.bypass, replica_groups=[list(range(8))],
        ins=[d_ccin[:]], outs=[d_ccout[:]]),
        reads=["ccin"], writes=["ccout"], kind="c")

    def emit_offsets():
        OP("gpsimd", lambda e: e.dma_start(out=t_allc[:], in_=d_ccout[:, 0:E]),
           reads=["ccout"], writes=["allc"], kind="d", dgrp="allc")
        OP("tensor", lambda e: e.matmul(ps_b[0:1, 0:E], t_selc[:], t_allc[:],
                                        start=True, stop=True),
           reads=["allc", "selc"], writes=["ps:B"])
        OP("scalar", lambda e: e.copy(t_offs[0:1, NT * E:NT * E + E],
                                      ps_b[0:1, 0:E]),
           reads=["ps:B"], writes=["coff"])
        for g in range(NT):
            OP("vector", lambda e, g=g: e.tensor_add(
                t_offs[0:1, g * E:(g + 1) * E], t_offs[0:1, g * E:(g + 1) * E],
                t_offs[0:1, NT * E:NT * E + E]),
                reads=["offr", "coff"], writes=["offr"])

    if upto == 2:
        emit_offsets()
        return dbg_finish([
            (h_out[0:128, 0:NT * E], t_mask[:], ["maskal"]),
            (h_out[128:256, 0:NT], t_gate[:], ["gateal"]),
            (h_out[256:257, 0:NT * E], t_offs[0:1, 0:NT * E], ["offr"]),
            (h_out[384:512, 0:NT * E], t_kept[:], ["keptal"]),
            (h_out[512:640, 0:NT * E], t_gall[:], ["gall"]),
        ])

    def wr(slot, off, n):
        return t_wr[:, slot * 4096 + off:slot * 4096 + off + n]

    WRALL = lambda s: [f"wr{s}a", f"wr{s}b", f"wr{s}c", f"wr{s}d"]

    pctr = {"ab": 0, "st": 0}

    def nextab():
        pctr["ab"] += 1
        return (ps_a, "ps:A") if pctr["ab"] % 2 == 0 else (ps_b, "ps:B")

    def nextst():
        pctr["st"] += 1
        b = pctr["st"] % 2
        return b, ps_st[:, b * 512:(b + 1) * 512], f"ps:st{b}"

    def proj_rope(c, reg, woff, hw, dst, dkey, scale):
        slot = c % 2
        wsb = wr(slot, woff, KC * 128)
        wkey = f"wr{slot}{reg}"
        OP("gpsimd", lambda e, wsb=wsb: e.dma_start(
            out=wsb.rearrange("p (kc f) -> p kc f", f=128),
            in_=hw[c].rearrange("(kc p) f -> p kc f", p=128)),
            writes=[wkey], kind="d", dgrp=wkey)
        for tau in range(TAU):
            t0 = tau * TW
            pse, pk = nextab()
            for kc in range(KC):
                OP("tensor", lambda e, kc=kc, t0=t0, wsb=wsb, pse=pse: e.matmul(
                    pse[:, 0:TW], wsb[:, kc * 128:(kc + 1) * 128],
                    t_A[:, kc * S + t0:kc * S + t0 + TW],
                    start=(kc == 0), stop=(kc == KC - 1)),
                    reads=[wkey, f"A:{kc}:{tau}"], writes=[pk])
            qr, qk = t_pg[:, (tau % 2) * TW:(tau % 2 + 1) * TW], f"qraw{tau % 2}"
            OP("scalar", lambda e, qr=qr, pse=pse: e.copy(qr, pse[:, 0:TW]),
               reads=[pk], writes=[qk])
            sb_, sps, spk = nextst()
            OP("tensor", lambda e, qr=qr, sps=sps: e.matmul(
                sps[:, 0:TW], t_prm[:], qr, start=True, stop=True),
                reads=[qk, "prm"], writes=[spk])
            tt1 = t_rope[:, (tau % 2) * TW:(tau % 2 + 1) * TW]
            tk1 = f"tt1{tau % 2}"
            OP("vector", lambda e, qr=qr, t0=t0, tt1=tt1: e.scalar_tensor_tensor(
                tt1, qr, scale, t_cos[:, t0:t0 + TW], ALU.mult, ALU.mult),
                reads=[qk, "cos"], writes=[tk1])
            OP("vector", lambda e, t0=t0, dst=dst, sps=sps, tt1=tt1: e.scalar_tensor_tensor(
                dst[:, t0:t0 + TW], sps[:, 0:TW], scale, t_sin[:, t0:t0 + TW],
                ALU.mult, ALU.mult),
                reads=[spk, "sin"], writes=["tt2s"])
            OP("vector", lambda e, t0=t0, dst=dst, tt1=tt1: e.tensor_add(
                dst[:, t0:t0 + TW], dst[:, t0:t0 + TW], tt1),
                reads=["tt2s", tk1], writes=[dkey])

    def build_V(c):
        slot = c % 2
        wsb = wr(slot, 2048, KC * 128)
        wkey = f"wr{slot}c"
        OP("gpsimd", lambda e, c=c, wsb=wsb: e.dma_start(
            out=wsb.rearrange("p (kc f) -> p kc f", f=128),
            in_=h_wv[c].rearrange("(kc p) f -> p kc f", p=128)),
            writes=[wkey], kind="d", dgrp=wkey)
        if c == 0:
            OP("vector", lambda e: e.memset(
                t_C[:, 0:NT * 130].rearrange("p (kt h z) -> p kt h z", h=2, z=65)[:, :, :, 64],
                1.0), writes=["C"])
        for g4 in range((NT + 3) // 4):
            tts = list(range(4 * g4, min(4 * g4 + 4, NT)))
            pse, pk = nextab()
            for i, tt in enumerate(tts):
                for kc in range(KC):
                    OP("tensor", lambda e, tt=tt, i=i, kc=kc, wsb=wsb, pse=pse: e.matmul(
                        pse[:, i * 128:(i + 1) * 128],
                        t_A[:, kc * S + tt * 128:kc * S + (tt + 1) * 128],
                        wsb[:, kc * 128:(kc + 1) * 128],
                        start=(kc == 0 and i == 0), stop=(kc == KC - 1)),
                        reads=[wkey, f"A:{kc}:{(tt * 128) // TW}"], writes=[pk])
            n = len(tts)
            OP("vector", lambda e, tts=tts, n=n, c=c, pse=pse: e.tensor_copy(
                t_C[:, 0:NT * 130].rearrange("p (kt h z) -> p kt h z", h=2, z=65)[
                    :, tts[0]:tts[0] + n, :, 0:64],
                pse.ap().rearrange("p (i h z) -> p i h z", h=2, z=64)[:, 0:n, :, :]),
                reads=[pk], writes=["C"])

    def bound_B(c):
        sqb = t_xne[:, 0:S]
        for wi, (which, buf, bk) in enumerate((("q", t_q, "qrope"), ("k", t_k, "krope"))):
            OP("scalar", lambda e, buf=buf: e.activation(sqb, buf[:], AF.Square),
               reads=[bk], writes=["sqb", "xnereg"])
            for tau in range(TAU):
                OP("tensor", lambda e, tau=tau: e.matmul(
                    ps_b[0:1, 0:TW], t_onesb[:], sqb[:, tau * TW:(tau + 1) * TW],
                    start=True, stop=True),
                    reads=["sqb", "xnereg", "onesb"], writes=["ps:B"])
                OP("vector", lambda e, tau=tau, wi=wi: e.reduce_max(
                    t_sr[0:1, wi * 8 + tau:wi * 8 + tau + 1], ps_b[0:1, 0:TW], axis=AX),
                    reads=["ps:B"], writes=[f"nm{wi}", "tsr"])
        OP("vector", lambda e: e.reduce_max(t_sr[0:1, 16:17], t_sr[0:1, 0:TAU], axis=AX),
           sreads=["nm0"], writes=["tsr"])
        OP("vector", lambda e: e.reduce_max(t_sr[0:1, 17:18], t_sr[0:1, 8:8 + TAU], axis=AX),
           sreads=["nm1"], writes=["tsr"])
        OP("vector", lambda e: e.tensor_mul(t_sr[0:1, 18:19], t_sr[0:1, 16:17], t_sr[0:1, 17:18]),
           sreads=["tsr"], writes=["tsr"])
        OP("scalar", lambda e: e.activation(t_sr[0:1, 19:20], t_sr[0:1, 18:19], AF.Sqrt,
                                            scale=1.05),
           reads=["tsr"], writes=["bb"])
        OP("tensor", lambda e: e.matmul(ps_b[:, 0:1], t_oner[:], t_sr[0:1, 19:20],
                                        start=True, stop=True),
           reads=["bb", "oner"], writes=["ps:B"])
        OP("scalar", lambda e, c=c: e.activation(t_nB[:, c:c + 1], ps_b[:, 0:1],
                                                 AF.Copy, scale=-1.0),
           reads=["ps:B"], writes=["negB"])

    def attention_head(c, half):
        r0 = 64 * half
        h = 2 * c + half

        def emit_ctx(kt, q0, off, batch, bank):
            for qi in batch:
                j = qi - q0
                OP("tensor", lambda e, kt=kt, qi=qi, j=j, bank=bank, h=h: e.matmul(
                    ps_ctx[0:65, qi * 128:(qi + 1) * 128],
                    t_C[:, kt * 130 + 65 * (h % 2):kt * 130 + 65 * (h % 2) + 65],
                    t_pt[:, bank * 512 + j * 128:bank * 512 + (j + 1) * 128],
                    start=(kt == 0 and qi % 4 == 0), stop=(kt == qi)),
                    reads=[f"pt{bank}", "C"], writes=[f"ps:ctx{qi // 4}"])

        pend = None
        for q0 in range(0, NT, 4):
            ng = min(4, NT - q0)
            for kt in range(q0 + ng):
                off = max(0, kt - q0)
                batch = list(range(q0 + off, q0 + ng))
                bank, sps, stk = nextst()
                OP("tensor", lambda e, kt=kt, q0=q0, off=off, ng=ng, sps=sps,
                   r0=r0: e.matmul(
                    sps[:, off * 128:ng * 128],
                    t_k[r0:r0 + 64, kt * 128:(kt + 1) * 128],
                    t_q[r0:r0 + 64, (q0 + off) * 128:(q0 + ng) * 128],
                    start=True, stop=True),
                    reads=["krope", "qrope"], writes=[stk])
                if kt >= q0:
                    OP("vector", lambda e, off=off, sps=sps: e.tensor_add(
                        sps[:, off * 128:(off + 1) * 128],
                        sps[:, off * 128:(off + 1) * 128],
                        t_mskT[:]),
                        reads=[stk, "mskT"], writes=[stk])
                OP("scalar", lambda e, off=off, ng=ng, bank=bank, sps=sps,
                   c=c: e.activation(
                    t_pt[:, bank * 512 + off * 128:bank * 512 + ng * 128],
                    sps[:, off * 128:ng * 128],
                    AF.Exp, bias=t_nB[:, c:c + 1]),
                    reads=[stk], sreads=["negB"], writes=[f"pt{bank}"])
                if pend is not None:
                    emit_ctx(*pend)
                pend = (kt, q0, off, batch, bank)
            emit_ctx(*pend)
            pend = None

    def attn_evac(c, half):
        kc = c
        for q0 in range(0, NT, 4):
            n = min(4, NT - q0)
            OP("scalar", lambda e, q0=q0, n=n: e.activation(
                t_rows[0:1, 0:n * 128], ps_ctx[64:65, q0 * 128:(q0 + n) * 128],
                AF.Ln),
                reads=[f"ps:ctx{q0 // 4}"] + (
                    [f"ps:ctx{(q0 + n - 1) // 4}"] if (q0 + n - 1) // 4 != q0 // 4 else []),
                writes=["rows"])
            OP("scalar", lambda e, n=n: e.activation(
                t_rec[0:1, 0:n * 128], t_rows[0:1, 0:n * 128], AF.Exp, scale=-1.0),
                reads=["rows"], writes=["rec"])
            bank, sps, stk = nextst()
            OP("tensor", lambda e, n=n, sps=sps: e.matmul(
                sps[0:64, 0:n * 128], t_onerb[0:1, 0:64],
                t_rec[0:1, 0:n * 128], start=True, stop=True),
                reads=["rec", "onerb"], writes=[stk])
            OP("scalar", lambda e, n=n, sps=sps: e.copy(t_gb[:, 0:n * 128],
                                                        sps[0:64, 0:n * 128]),
               reads=[stk], writes=["gbs"])
            tau = (q0 * 128) // TW
            bkeys = sorted({f"B:{kc}:{((q0 + i) * 128) // TW}" for i in range(n)})
            OP("vector", lambda e, q0=q0, n=n, half=half, kc=kc: e.tensor_mul(
                t_B[:, kc * S + q0 * 128:kc * S + (q0 + n) * 128][64 * half:64 * half + 64, :],
                ps_ctx[0:64, q0 * 128:(q0 + n) * 128],
                t_gb[:, 0:n * 128]),
                reads=[f"ps:ctx{q0 // 4}", "gbs"], writes=bkeys)

    for c in range(KC):
        proj_rope(c, "a", 0, h_wq, t_q, "qrope", 0.125)
        proj_rope(c, "b", 1024, h_wk, t_k, "krope", 1.0)
        build_V(c)
        bound_B(c)
        if c > 0:
            attn_evac(c - 1, 1)
        attention_head(c, 0)
        attn_evac(c, 0)
        attention_head(c, 1)
    attn_evac(KC - 1, 1)

    if upto == 3:
        return dbg_finish(
            [(h_out[kc * 128:(kc + 1) * 128, :], t_B[:, kc * S:(kc + 1) * S],
              [f"B:{kc}:{tt}" for tt in range(TAU)]) for kc in range(KC)])

    y_sl = t_f32[:, 0:KC * TW]
    for half in range(2):
        OP("gpsimd", lambda e, half=half: e.dma_start(
            out=wr(half, 0, 4096).rearrange("p (kc f) -> p kc f", f=512),
            in_=h_wo[half].rearrange("(kc p) f -> p kc f", p=128)),
            writes=[f"wr{half}a", f"wr{half}b", f"wr{half}c"], kind="d",
            dgrp=f"wr{half}a")
    for tau in range(TAU):
        t0 = tau * TW
        for ic in range(KC):
            half, iw = ic // 4, ic % 4
            pse, pk = (ps_a, "ps:A") if ic % 2 == 0 else (ps_b, "ps:B")
            for kc in range(KC):
                OP("tensor", lambda e, half=half, iw=iw, kc=kc, t0=t0, pse=pse: e.matmul(
                    pse[:, 0:TW],
                    wr(half, kc * 512 + iw * 128, 128),
                    t_B[:, kc * S + t0:kc * S + t0 + TW],
                    start=(kc == 0), stop=(kc == KC - 1)),
                    reads=[f"wr{half}a", f"B:{kc}:{tau}"], writes=[pk])
            OP("vector", lambda e, ic=ic, t0=t0, pse=pse, pk=pk: e.tensor_add(
                y_sl[:, ic * TW:(ic + 1) * TW], pse[:, 0:TW],
                t_xn[:, ic * S + t0:ic * S + t0 + TW]),
                reads=[pk, f"xn:{ic}:{tau}"], writes=[f"y:{ic}", "f32a"])
        for ic in range(KC):
            si = t_sil[:, 0:TW]
            OP("scalar", lambda e, ic=ic, si=si: e.activation(
                si, y_sl[:, ic * TW:(ic + 1) * TW], AF.Square),
                reads=[f"y:{ic}"], writes=["ysq", "silreg"])
            OP("tensor", lambda e, ic=ic, si=si: e.matmul(
                ps_st[0:1, 0:TW], t_onesb[:], si,
                start=(ic == 0), stop=(ic == KC - 1)),
                reads=["ysq", "silreg", "onesb"], writes=["ps:st0"])
        rrow = t_rows[0:1, 0:TW]
        OP("scalar", lambda e: e.activation(rrow, ps_st[0:1, 0:TW], AF.Sqrt,
                                            scale=1.0 / D, bias=EPS),
           reads=["ps:st0", "epsc"], writes=["r2row", "rows"])
        OP("vector", lambda e: e.reciprocal(rrow, rrow), reads=["r2row"], writes=["r2row"])
        OP("tensor", lambda e: e.matmul(ps_st[:, 0:TW], t_oner[:], rrow,
                                        start=True, stop=True),
           reads=["r2row", "oner"], writes=["ps:st0"])
        OP("scalar", lambda e: e.copy(rstd2_b, ps_st[:, 0:TW]),
           reads=["ps:st0"], writes=["bc3", "bc0"])
        for ic in range(KC):
            OP("vector", lambda e, ic=ic, t0=t0: e.scalar_tensor_tensor(
                t_A[:, ic * S + t0:ic * S + t0 + TW],
                y_sl[:, ic * TW:(ic + 1) * TW], t_n2[:, ic:ic + 1], rstd2_b,
                ALU.mult, ALU.mult),
                reads=[f"y:{ic}", "bc3"], sreads=["n2"], writes=[f"A:{ic}:{tau}"])
            xp, xk = xping[ic % 2], f"xp{ic % 2}"
            OP("sync", lambda e, ic=ic, t0=t0, xp=xp: e.dma_start(
                out=xp[:], in_=h_xfm[ic * 128:(ic + 1) * 128, t0:t0 + TW]),
                writes=[xk, "f32a"], kind="d", dgrp=xk)
            ev, ek = t_ev[:, (ic % 2) * TW:(ic % 2 + 1) * TW], f"ev{ic % 2}"
            OP("vector", lambda e, ic=ic, xp=xp, ev=ev: e.tensor_add(
                ev, y_sl[:, ic * TW:(ic + 1) * TW], xp[:]),
                reads=[f"y:{ic}", xk], writes=[ek])
            OP("gpsimd", lambda e, ic=ic, t0=t0, ev=ev: e.dma_start(
                out=h_out[ic * 128:(ic + 1) * 128, t0:t0 + TW], in_=ev),
                reads=[ek], writes=[f"out:{ic}:{tau}"], kind="d",
                dgrp=f"oute{ic % 2}")

    if upto == 4:
        outkeys = [f"out:{ic}:{tt}" for ic in range(KC) for tt in range(TAU)]
        sch.barrier_wait("sync", outkeys)
        return finish()

    emit_offsets()

    for g in range(NT):
        OP("tensor", lambda e, g=g: e.matmul(
            ps_a[:, g * E:(g + 1) * E], t_ltri[:], t_mask[:, g * E:(g + 1) * E],
            start=(g == 0), stop=False),
            reads=["ltri", "maskal"], writes=["ps:A"])
    OP("tensor", lambda e: e.matmul(
        ps_a[:, 0:NT * E], t_oner[:], t_offs[0:1, 0:NT * E], start=False, stop=True),
        reads=["oner", "offr"], writes=["ps:A"])
    OP("vector", lambda e: e.tensor_single_scalar(
        t_kept[:], ps_a[:, 0:NT * E], CAPTH, ALU.is_lt),
        reads=["ps:A"], writes=["keptal"])
    for g in range(NT):
        OP("vector", lambda e, g=g: e.scalar_tensor_tensor(
            t_gall[:, g * E:(g + 1) * E], t_mask[:, g * E:(g + 1) * E],
            t_gate[:, g:g + 1], t_kept[:, g * E:(g + 1) * E], ALU.mult, ALU.mult),
            reads=["maskal", "keptal"], sreads=["gateal"], writes=["gall"])
    for g in range(NT):
        OP("tensor", lambda e, g=g: e.transpose(
            ps_b[0:E, 0:128], t_gall[:, g * E:(g + 1) * E], t_idn[:]),
            reads=["gall", "idn"], writes=["ps:B"])
        OP("scalar", lambda e, g=g: e.copy(t_gt[:, g * 128:(g + 1) * 128],
                                           ps_b[0:E, 0:128]),
           reads=["ps:B"], writes=["gt"])
    if upto == 41:
        return dbg_finish([
            (h_out[0:128, 0:NT * E], t_kept[:], ["keptal"]),
            (h_out[128:256, 0:NT * E], t_gall[:], ["gall"]),
            (h_out[256:256 + E, 0:S], t_gt[:], ["gt"]),
        ])
    for ee in range(E):
        for tau in range(TAU):
            OP("tensor", lambda e, ee=ee, tau=tau: e.matmul(
                ps_b[:, 0:TW], t_sel[:, ee * 128:ee * 128 + 128],
                t_gt[:, tau * TW:(tau + 1) * TW], start=True, stop=True),
                reads=["sel", "gt"], writes=["ps:B"])
            OP("scalar", lambda e, ee=ee, tau=tau: e.copy(
                t_B[:, ee * S + tau * TW:ee * S + (tau + 1) * TW], ps_b[:, 0:TW]),
                reads=["ps:B"], writes=[f"B:{ee}:{tau}"])
    if upto == 43:
        return dbg_finish(
            [(h_out[ee * 128:ee * 128 + 128, 0:S], t_B[:, ee * S:(ee + 1) * S],
              [f"B:{ee}:{tt}" for tt in range(TAU)]) for ee in range(E)])

    def moe_psum(ic):
        if ic < 4:
            return ps_ctx[:, ic * 512:(ic + 1) * 512], f"ps:ctx{ic}"
        if ic < 6:
            return ps_st[:, (ic - 4) * 512:(ic - 3) * 512], f"ps:st{ic - 4}"
        return (ps_a, "ps:A") if ic == 6 else (ps_b, "ps:B")

    EMM = {45: 0, 46: 0, 47: 1, 48: 2, 49: 4}.get(upto, E)
    wjobs = [(tau, ee, hh) for tau in range(TAU) for ee in range(E)
             for hh in range(2)]

    def wslot3(j):
        s = j % 3
        if s < 2:
            return wr(s, 0, 4096), [f"wr{s}a", f"wr{s}b", f"wr{s}c"]
        return t_w2g[:, 0:4096], ["w2g0", "w2g1", "w2g2", "w2g3"]

    def emit_wdma(j):
        tau_, ee_, hh_ = wjobs[j]
        wbuf, wks = wslot3(j)
        OP("gpsimd", lambda e, ee=ee_, hh=hh_, wbuf=wbuf: e.dma_start(
            out=wbuf.rearrange("p (kc f) -> p kc f", f=512),
            in_=h_we[ee, hh].rearrange("(kc p) f -> p kc f", p=128)),
            writes=wks, kind="d", dgrp=wks[0])

    if upto != 45:
        emit_wdma(0)
        emit_wdma(1)
        emit_wdma(2)
    ji = 0
    for tau in range(TAU):
        t0 = tau * TW
        for ee in range(E):
            for kc in range(KC):
                OP("vector", lambda e, ee=ee, kc=kc, t0=t0, tau=tau: e.tensor_mul(
                    t_xne[:, kc * TW:(kc + 1) * TW],
                    t_xn[:, kc * S + t0:kc * S + t0 + TW],
                    t_B[:, ee * S + t0:ee * S + t0 + TW]),
                    reads=[f"xn:{kc}:{tau}", f"B:{ee}:{tau}"],
                    writes=[f"xne:{kc}", "xnereg"])
            for hh in range(2):
                j = ji
                ji += 1
                wbuf, wks = wslot3(j)
                if ee < EMM:
                    for kc in range(KC):
                        for iw in range(4):
                            ic = hh * 4 + iw
                            pse, pk = moe_psum(ic)
                            OP("tensor", lambda e, kc=kc, iw=iw, pse=pse, wbuf=wbuf,
                               ee=ee, hh=hh: e.matmul(
                                pse[:, 0:TW],
                                wbuf[:, kc * 512 + iw * 128:kc * 512 + (iw + 1) * 128],
                                t_xne[:, kc * TW:(kc + 1) * TW],
                                start=(ee == 0 and kc == 0), stop=False),
                                reads=[wks[0], f"xne:{kc}"], writes=[pk])
                if upto != 45 and j + 3 < len(wjobs):
                    emit_wdma(j + 3)
        if upto in (45, 46):
            continue
        for ic in range(KC):
            pse, pk = moe_psum(ic)
            OP("tensor", lambda e, ic=ic, tau=tau, pse=pse: e.matmul(
                pse[:, 0:TW], t_be[:, ic * 128:(ic + 1) * 128],
                t_gt[:, tau * TW:(tau + 1) * TW], start=False, stop=True),
                reads=["be", "gt"], writes=[pk])
            ev, ek = t_ev[:, (ic % 2) * TW:(ic % 2 + 1) * TW], f"ev{ic % 2}"
            OP("scalar", lambda e, pse=pse, ev=ev: e.copy(ev, pse[:, 0:TW]),
               reads=[pk], writes=[ek])
            if upto == 44:
                OP("gpsimd", lambda e, ic=ic, t0=t0, ev=ev: e.dma_start(
                    out=h_out[ic * 128:(ic + 1) * 128, t0:t0 + TW], in_=ev),
                    reads=[ek, f"out:{ic}:{tau}"], writes=[f"out:{ic}:{tau}"],
                    kind="d", dgrp=f"oute{ic % 2}")
            else:
                OP("gpsimd", lambda e, ic=ic, t0=t0, ev=ev: e.dma_start(
                    out=h_out[ic * 128:(ic + 1) * 128, t0:t0 + TW], in_=ev,
                    accum_op=ALU.add),
                    reads=[ek, f"out:{ic}:{tau}"], writes=[f"out:{ic}:{tau}"],
                    kind="d", dgrp=f"oute{ic % 2}")

    if upto in (45, 46):
        return dbg_finish([(h_out[0:128, 0:TW], t_xne[:, 0:TW], ["xne:0"])])
    if upto in (5, 44, 47, 48, 49):
        outkeys = [f"out:{ic}:{tt}" for ic in range(KC) for tt in range(TAU)]
        sch.barrier_wait("sync", outkeys)
        sch.barrier_wait("gpsimd", outkeys)
        return finish()

    FG = 4
    ff_acc = t_f32[:, 0:KC * TW]
    for tau in range(TAU):
        t0 = tau * TW
        for fc in range(FC):
            slot = fc % 2
            OP("gpsimd", lambda e, fc=fc, slot=slot: e.dma_start(
                out=wr(slot, 0, KC * 128).rearrange("p (kc f) -> p kc f", f=128),
                in_=h_w1[fc].rearrange("(kc p) f -> p kc f", p=128)),
                writes=[f"wr{slot}a"], kind="d", dgrp=f"wr{slot}a")
            OP("gpsimd", lambda e, fc=fc, slot=slot: e.dma_start(
                out=wr(slot, 1024, KC * 128).rearrange("p (kc f) -> p kc f", f=128),
                in_=h_w3[fc].rearrange("(kc p) f -> p kc f", p=128)),
                writes=[f"wr{slot}b"], kind="d", dgrp=f"wr{slot}b")
            p1, p1k = (ps_a, "ps:A") if slot == 0 else (ps_st[:, 0:512], "ps:st0")
            p3, p3k = (ps_b, "ps:B") if slot == 0 else (ps_st[:, 512:1024], "ps:st1")
            for kc in range(KC):
                OP("tensor", lambda e, kc=kc, t0=t0, slot=slot, p1=p1: e.matmul(
                    p1[:, 0:TW], wr(slot, kc * 128, 128),
                    t_A[:, kc * S + t0:kc * S + t0 + TW],
                    start=(kc == 0), stop=(kc == KC - 1)),
                    reads=[f"wr{slot}a", f"A:{kc}:{tau}"], writes=[p1k])
            for kc in range(KC):
                OP("tensor", lambda e, kc=kc, t0=t0, slot=slot, p3=p3: e.matmul(
                    p3[:, 0:TW], wr(slot, 1024 + kc * 128, 128),
                    t_A[:, kc * S + t0:kc * S + t0 + TW],
                    start=(kc == 0), stop=(kc == KC - 1)),
                    reads=[f"wr{slot}b", f"A:{kc}:{tau}"], writes=[p3k])
            si, sk = t_sil[:, slot * TW:(slot + 1) * TW], f"sil{slot}"
            OP("scalar", lambda e, si=si, p1=p1: e.activation(si, p1[:, 0:TW], AF.Silu),
               reads=[p1k], writes=[sk, "silreg"])
            j = fc % FG
            pgt, pgk = t_pg[:, j * TW:(j + 1) * TW], f"pg{j}"
            OP("vector", lambda e, si=si, p3=p3, pgt=pgt: e.tensor_mul(
                pgt, si, p3[:, 0:TW]),
                reads=[sk, p3k], writes=[pgk])
            OP("gpsimd", lambda e, fc=fc, j=j: e.dma_start(
                out=t_w2g[:, j * D:(j + 1) * D], in_=h_w2[fc]),
                writes=[f"w2g{j}"], kind="d", dgrp=f"w2g{j}")
            if j == FG - 1:
                fg0 = fc - FG + 1
                for oc in range(KC):
                    fps = ps_ctx[:, (oc % 2) * 512:(oc % 2) * 512 + TW]
                    fpk = f"ps:ctx{oc % 2}"
                    for jj in range(FG):
                        OP("tensor", lambda e, jj=jj, oc=oc, fps=fps: e.matmul(
                            fps, t_w2g[:, jj * D + oc * 128:jj * D + (oc + 1) * 128],
                            t_pg[:, jj * TW:(jj + 1) * TW],
                            start=(jj == 0), stop=(jj == FG - 1)),
                            reads=[f"w2g{jj}", f"pg{jj}"], writes=[fpk])
                    if fg0 == 0:
                        OP("vector", lambda e, oc=oc, fps=fps: e.tensor_copy(
                            ff_acc[:, oc * TW:(oc + 1) * TW], fps),
                            reads=[fpk], writes=[f"y:{oc}"])
                    else:
                        OP("vector", lambda e, oc=oc, fps=fps: e.tensor_add(
                            ff_acc[:, oc * TW:(oc + 1) * TW],
                            ff_acc[:, oc * TW:(oc + 1) * TW], fps),
                            reads=[fpk, f"y:{oc}"], writes=[f"y:{oc}"])
        for oc in range(KC):
            acc_op = {61: ALU.bypass}.get(upto, ALU.add)
            OP("gpsimd", lambda e, oc=oc, t0=t0, acc_op=acc_op: e.dma_start(
                out=h_out[oc * 128:(oc + 1) * 128, t0:t0 + TW],
                in_=ff_acc[:, oc * TW:(oc + 1) * TW], accum_op=acc_op),
                reads=[f"y:{oc}", f"out:{oc}:{tau}"],
                writes=[f"out:{oc}:{tau}", f"ffdma{oc % 2}"], kind="d",
                dgrp=f"oute{oc % 2}")

    outkeys = [f"out:{ic}:{tau}" for ic in range(KC) for tau in range(TAU)]
    sch.barrier_wait("gpsimd", outkeys)
    sch.barrier_wait("sync", outkeys)
    return finish()


_CACHE = {}


def _consts(S):
    DKH = 64
    pos = np.arange(S, dtype=np.float64)
    k = np.arange(DKH) // 2
    ang = pos[:, None] / np.power(10000.0, 2.0 * k / DKH)[None, :]
    cos = np.cos(ang)
    sin = np.sin(ang) * np.where(np.arange(DKH) % 2 == 0, -1.0, 1.0)[None, :]
    cosT = np.ascontiguousarray(np.tile(cos.T, (2, 1)).astype(BF))
    sinT = np.ascontiguousarray(np.tile(sin.T, (2, 1)).astype(BF))
    idn = np.eye(128, dtype=np.float32)
    prm = np.zeros((128, 128), dtype=np.float32)
    prm[np.arange(128), np.arange(128) ^ 1] = 1.0
    mskT = np.ascontiguousarray(
        np.where(np.arange(128)[:, None] > np.arange(128)[None, :], NEG, 0.0).astype(BF))
    ltri = np.ascontiguousarray(np.triu(np.ones((128, 128), dtype=np.float32), k=1))
    selm = np.zeros((E, E * 128), dtype=np.float32)
    for ee in range(E):
        selm[ee, ee * 128:(ee + 1) * 128] = 1.0
    return dict(cosT=cosT, sinT=sinT, idn=idn, prm=np.ascontiguousarray(prm.astype(BF)),
                mskT=mskT, ltri=ltri, selm=np.ascontiguousarray(selm.astype(BF)))


def prep_inputs(x, wq, wk, wv, wo, n1, n2, w1, w2, w3, wg, We, be, S):
    cst = _consts(S)
    f32 = np.float32

    def T(w):
        return np.ascontiguousarray(np.asarray(w, dtype=f32).T)

    def blocks(wT, n):
        return np.ascontiguousarray(wT.reshape(D, n, wT.shape[1] // n).transpose(1, 0, 2))

    wq_b = blocks(T(wq), KC)
    wk_b = blocks(T(wk), KC)
    wv_b = blocks(T(wv), KC)
    wo_b = blocks(T(wo), 2)
    w1_b = blocks(T(w1), FC)
    w3_b = blocks(T(w3), FC)
    w2_b = np.ascontiguousarray(T(w2).reshape(FC, 128, D))
    WeT = np.asarray(We, dtype=f32).transpose(0, 2, 1)
    we_b = np.ascontiguousarray(WeT.reshape(E, D, 2, 512).transpose(0, 2, 1, 3))
    n1c = np.ascontiguousarray(np.asarray(n1, dtype=f32).reshape(KC, 128).T)
    n2c = np.ascontiguousarray(np.asarray(n2, dtype=f32).reshape(KC, 128).T)
    base = dict(wq_b=wq_b, wk_b=wk_b, wv_b=wv_b, wo_b=wo_b, w1_b=w1_b,
                w3_b=w3_b, w2_b=w2_b, we_b=we_b,
                wg=np.ascontiguousarray(np.asarray(wg, dtype=f32)),
                be=np.ascontiguousarray(np.asarray(be, dtype=f32)),
                n1c=n1c, n2c=n2c, **cst)
    xa = np.asarray(x, dtype=f32)
    in_maps = []
    for b in range(xa.shape[0]):
        m = dict(base)
        m["x_tm"] = np.ascontiguousarray(xa[b])
        m["x_fm"] = np.ascontiguousarray(xa[b].T)
        m["selc"] = (np.arange(E) < b).astype(f32).reshape(E, 1)
        in_maps.append(m)
    return in_maps


def kernel(L0_attn_x, L0_attn_wq, L0_attn_wk, L0_attn_wv, L0_attn_wo,
           L0_attn_norm1_w, L0_attn_norm2_w, L0_attn_w1, L0_attn_w2,
           L0_attn_w3, L0_ffn_wg, L0_ffn_W, L0_ffn_b):
    import os
    upto = int(os.environ.get("K_UPTO", "9"))
    S = np.asarray(L0_attn_x).shape[1]
    key = (S, upto)
    if key not in _CACHE:
        _CACHE[key] = build_program(S, upto)
    nc = _CACHE[key]
    in_maps = prep_inputs(L0_attn_x, L0_attn_wq, L0_attn_wk, L0_attn_wv,
                          L0_attn_wo, L0_attn_norm1_w, L0_attn_norm2_w,
                          L0_attn_w1, L0_attn_w2, L0_attn_w3, L0_ffn_wg,
                          L0_ffn_W, L0_ffn_b, S)
    res = run_bass_kernel_spmd(nc, in_maps, core_ids=list(range(8)))
    out = np.stack([np.asarray(res.results[b]["out"]).T for b in range(8)])
    return np.ascontiguousarray(out.astype(np.float32))


# version 10
# speedup vs baseline: 1.3481x; 1.0148x over previous
import contextlib
import numpy as np
import ml_dtypes
import concourse.bass as bass
import concourse.mybir as mybir
from concourse.bass_utils import run_bass_kernel_spmd

F32 = mybir.dt.float32
BF16 = mybir.dt.bfloat16
AF = mybir.ActivationFunctionType
ALU = mybir.AluOpType
AX = mybir.AxisListType.X

D = 1024
KC = 8
DFF = 4096
FC = 32
E = 8
EPS = 1e-5
NEG = -1e9

BF = ml_dtypes.bfloat16


class Sched:
    ENGS = ("sync", "scalar", "vector", "tensor", "gpsimd")

    def __init__(self, nc, es):
        self.nc = nc
        self.es = es
        self.seq = {e: 0 for e in self.ENGS}
        self.cseq = 0
        self.sem = {e: es.enter_context(nc.semaphore(f"is_{e}")) for e in self.ENGS}
        self.csem = es.enter_context(nc.semaphore("cs"))
        self.dgrp = {}
        self.last_w = {}
        self.readers = {}
        self.watermark = {e: {} for e in self.ENGS}
        self.eng_ops = {e: [] for e in self.ENGS}

    def _group(self, name, queue):
        g = self.dgrp.get(name)
        if g is None:
            g = [self.es.enter_context(self.nc.semaphore(f"dg_{name}")), 0, queue]
            self.dgrp[name] = g
        assert g[2] == queue, f"dma group {name} used from {queue} and {g[2]}"
        return g

    def _need(self, eng, tok, waits, force=False, dma_queue=None):
        if tok is None:
            return
        kind = tok[0]
        if kind == "i":
            _, peng, val = tok
            if peng == eng and eng == "tensor" and not force:
                return
            sem, target, wkey = self.sem[peng], val, ("i", peng)
        elif kind == "d":
            _, grp, val = tok
            g = self.dgrp[grp]
            if dma_queue is not None and g[2] == dma_queue:
                return
            sem, target, wkey = g[0], 16 * val, ("d", grp)
        else:
            _, val = tok
            sem, target, wkey = self.csem, val, ("c",)
        wm = self.watermark[eng]
        if wm.get(wkey, 0) >= target:
            return
        wm[wkey] = target
        waits.append((sem, target))

    def op(self, eng, fn, reads=(), writes=(), kind="i", sreads=(), dgrp=None):
        waits = []
        dq = eng if kind == "d" else None
        for r in sreads:
            self._need(eng, self.last_w.get(r), waits, force=True, dma_queue=dq)
        for r in reads:
            self._need(eng, self.last_w.get(r), waits, dma_queue=dq)
        for w in writes:
            self._need(eng, self.last_w.get(w), waits, dma_queue=dq)
            for t in self.readers.get(w, ()):
                self._need(eng, t, waits, dma_queue=dq)
        if kind == "d":
            g = self._group(dgrp, eng)
            g[1] += 1
            tok = ("d", dgrp, g[1])
            inc = (g[0], 16)
        elif kind == "c":
            self.cseq += 1
            tok = ("c", self.cseq)
            inc = (self.csem, 1)
        else:
            self.seq[eng] += 1
            tok = ("i", eng, self.seq[eng])
            inc = (self.sem[eng], 1)
        self.eng_ops[eng].append((fn, waits, inc))
        for r in list(reads) + list(sreads):
            self.readers.setdefault(r, []).append(tok)
        for w in writes:
            self.last_w[w] = tok
            self.readers[w] = []
        return tok

    def barrier_wait(self, eng, keys):
        waits = []
        for k in keys:
            self._need(eng, self.last_w.get(k), waits)
            for t in self.readers.get(k, ()):
                self._need(eng, t, waits)
        if waits:
            self.eng_ops[eng].append((None, waits, None))

    def replay(self, block):
        for ename in self.ENGS:
            ops = self.eng_ops[ename]

            def make(op_list):
                def run(eng):
                    for fn, waits, inc in op_list:
                        for (s, tgt) in waits:
                            eng.wait_ge(s, tgt)
                        if fn is not None:
                            fn(eng).then_inc(*inc)
                return run

            getattr(block, ename)(make(ops))


def build_program(S, upto=9):
    assert S % 256 == 0
    TW = min(512, S)
    TAU = S // TW
    NT = S // 128
    GPT = TW // 128
    CAPTH = float(S) - 0.5

    nc = bass.Bass("TRN2", target_bir_lowering=False)

    h_xtm = nc.dram_tensor("x_tm", [S, D], F32, kind="ExternalInput")
    h_xfm = nc.dram_tensor("x_fm", [D, S], F32, kind="ExternalInput")
    h_wq = nc.dram_tensor("wq_b", [KC, D, 128], F32, kind="ExternalInput")
    h_wk = nc.dram_tensor("wk_b", [KC, D, 128], F32, kind="ExternalInput")
    h_wv = nc.dram_tensor("wv_b", [KC, D, 128], F32, kind="ExternalInput")
    h_wo = nc.dram_tensor("wo_b", [2, D, 512], F32, kind="ExternalInput")
    h_w1 = nc.dram_tensor("w1_b", [FC, D, 128], F32, kind="ExternalInput")
    h_w3 = nc.dram_tensor("w3_b", [FC, D, 128], F32, kind="ExternalInput")
    h_w2 = nc.dram_tensor("w2_b", [FC, 128, D], F32, kind="ExternalInput")
    h_we = nc.dram_tensor("we_b", [E, 2, D, 512], F32, kind="ExternalInput")
    h_wg = nc.dram_tensor("wg", [D, E], F32, kind="ExternalInput")
    h_be = nc.dram_tensor("be", [E, D], F32, kind="ExternalInput")
    h_n1 = nc.dram_tensor("n1c", [128, KC], F32, kind="ExternalInput")
    h_n2 = nc.dram_tensor("n2c", [128, KC], F32, kind="ExternalInput")
    h_cos = nc.dram_tensor("cosT", [128, S], BF16, kind="ExternalInput")
    h_sin = nc.dram_tensor("sinT", [128, S], BF16, kind="ExternalInput")
    h_idn = nc.dram_tensor("idn", [128, 128], F32, kind="ExternalInput")
    h_prm = nc.dram_tensor("prm", [128, 128], BF16, kind="ExternalInput")
    h_mskT = nc.dram_tensor("mskT", [128, 128], BF16, kind="ExternalInput")
    h_ltri = nc.dram_tensor("ltri", [128, 128], F32, kind="ExternalInput")
    h_sel = nc.dram_tensor("selm", [E, E * 128], BF16, kind="ExternalInput")
    h_selc = nc.dram_tensor("selc", [E, 1], F32, kind="ExternalInput")
    h_out = nc.dram_tensor("out", [D, S], F32, kind="ExternalOutput")
    d_ccin = nc.dram_tensor("cc_in", [1, 512], F32, kind="Internal")
    d_ccout = nc.dram_tensor("cc_out", [8, 512], F32, kind="Internal")

    es = contextlib.ExitStack()

    def sb(name, p, f, dt):
        return es.enter_context(nc.sbuf_tensor(name, [p, f], dt))

    t_xn = sb("xn", 128, KC * S, BF16)
    t_A = sb("uA", 128, KC * S, BF16)
    t_B = sb("uB", 128, KC * S, BF16)
    t_C = sb("uC", 128, NT * 130, BF16)
    t_q = sb("qbuf", 128, S, BF16)
    t_k = sb("kbuf", 128, S, BF16)
    t_xne = sb("xne", 128, max(KC * TW, S), BF16)
    t_wr = sb("wring", 128, 2 * 4096, BF16)
    t_cos = sb("cosb", 128, S, BF16)
    t_sin = sb("sinb", 128, S, BF16)
    t_f32 = sb("f32a", 128, max(KC * TW + 2 * TW, 3 * 1024), F32)
    t_f32b = sb("f32b", 128, 3 * TW, F32)
    t_pg = sb("pgr", 128, 4 * TW, BF16)
    t_w2g = sb("w2g", 128, 4 * D, BF16)
    t_rope = sb("ropes", 128, 2 * TW, BF16)
    t_pt = sb("ptb", 128, 1024, BF16)
    t_sil = sb("silb", 128, 2 * TW, BF16)
    t_ev = sb("evac", 128, 2 * TW, F32)
    t_gb = sb("gbrow", 64, 512, BF16)
    t_idn = sb("idnb", 128, 128, F32)
    t_prm = sb("prmb", 128, 128, BF16)
    t_mskT = sb("mskTb", 128, 128, BF16)
    t_ltri = sb("ltrib", 128, 128, F32)
    t_sel = sb("selb", E, E * 128, BF16)
    t_n1 = sb("n1b", 128, KC, F32)
    t_n2 = sb("n2b", 128, KC, F32)
    t_wg = sb("wgb", 128, KC * E, F32)
    t_be = sb("beb", E, D, BF16)
    t_stats = sb("statc", 128, 5 * NT, F32)
    t_statT = sb("statT", 3 * NT, 128, F32)
    t_rows = sb("rows", 1, max(TW, 512), F32)
    t_mask = sb("maskal", 128, NT * E, F32)
    t_gate = sb("gateal", 128, NT, F32)
    t_kept = sb("keptal", 128, NT * E, F32)
    t_gall = sb("gall", 128, NT * E, F32)
    t_gt = sb("gtb", E, S, BF16)
    t_offs = sb("offs", 1, NT * E + E, F32)
    t_allc = sb("allc", E, E, F32)
    t_selc = sb("selcb", E, 1, F32)
    t_ones = sb("onesf", 128, 1, F32)
    t_onesb = sb("onesbf", 128, 1, BF16)
    t_oner = sb("onerf", 1, 128, F32)
    t_onerb = sb("onerbf", 1, 128, BF16)
    t_nB = sb("negB", 128, KC, F32)
    t_sc = sb("smallc", 128, max(2 * NT, 20), F32)
    t_eps = sb("epsc", 128, 1, F32)
    nc.const_aps.register(F32, EPS, t_eps[:])
    t_sr = sb("smallr", 1, 32, F32)
    t_rec = sb("recrow", 1, 512, BF16)

    ps_st = es.enter_context(nc.psum_tensor("ps_st", [128, 1024], F32))
    ps_ctx = es.enter_context(nc.psum_tensor("ps_ctx", [128, 2048], F32))
    ps_a = es.enter_context(nc.psum_tensor("ps_a", [128, 512], F32))
    ps_b = es.enter_context(nc.psum_tensor("ps_b", [128, 512], F32))

    sch = Sched(nc, es)
    OP = sch.op

    def finish():
        with nc.allow_low_precision(reason="bf16 by design"):
            with nc.Block("main") as block:
                sch.replay(block)
        es.close()
        return nc

    def dbg_finish(stores):
        keys = []
        for idx, (o, i_, rd) in enumerate(stores):
            k = f"dbgout{idx}"
            OP("gpsimd", lambda e, o=o, i_=i_: e.dma_start(out=o, in_=i_),
               reads=rd, writes=[k], kind="d", dgrp="dbg")
            keys.append(k)
        sch.barrier_wait("gpsimd", keys)
        return finish()

    for hh, tt, key in ((h_idn, t_idn, "idn"), (h_prm, t_prm, "prm"),
                        (h_mskT, t_mskT, "mskT"), (h_ltri, t_ltri, "ltri"),
                        (h_sel, t_sel, "sel"), (h_n1, t_n1, "n1"),
                        (h_n2, t_n2, "n2"), (h_cos, t_cos, "cos"),
                        (h_sin, t_sin, "sin"), (h_selc, t_selc, "selc")):
        OP("sync", lambda e, hh=hh, tt=tt: e.dma_start(out=tt[:], in_=hh[:]),
           writes=[key], kind="d", dgrp=key)
    OP("sync", lambda e: e.dma_start(
        out=t_wg.ap().rearrange("p (kc e) -> p kc e", e=E),
        in_=h_wg.rearrange("(kc p) e -> p kc e", p=128)), writes=["wg"], kind="d", dgrp="wg")
    OP("gpsimd", lambda e: e.dma_start(out=t_be[:], in_=h_be[:]), writes=["be"], kind="d", dgrp="be")
    OP("vector", lambda e: e.memset(t_ones[:], 1.0), writes=["ones"])
    OP("vector", lambda e: e.memset(t_onesb[:], 1.0), writes=["onesb"])
    OP("vector", lambda e: e.memset(t_oner[:], 1.0), writes=["oner"])
    OP("vector", lambda e: e.memset(t_onerb[:], 1.0), writes=["onerb"])
    OP("vector", lambda e: e.memset(t_rows[0:1, 0:512], 0.0), writes=["cpad", "rows"])
    OP("vector", lambda e: e.memset(t_eps[:], EPS), writes=["epsc"])

    xtm = (t_f32[:, 0:1024], t_f32[:, 1024:2048])
    sqdump = t_f32[:, 2048:3072]
    C_S1, C_S2, C_MEAN, C_RSTD, C_RSTD1 = 0, NT, 2 * NT, 3 * NT, 4 * NT
    for i in range(NT):
        xb, key = xtm[i % 2], f"xtm{i % 2}"
        OP("sync", lambda e, xb=xb, i=i: e.dma_start(
            out=xb[:], in_=h_xtm[i * 128:(i + 1) * 128, :]), writes=[key, "f32a"],
            kind="d", dgrp=key)
        OP("vector", lambda e, xb=xb, i=i: e.reduce_sum(
            t_stats[:, C_S1 + i:C_S1 + i + 1], xb[:], axis=AX),
            reads=[key], writes=["stats"])
        OP("scalar", lambda e, xb=xb, i=i: e.activation(
            sqdump[:], xb[:], AF.Square, accum_out=t_stats[:, C_S2 + i:C_S2 + i + 1]),
            reads=[key], writes=["stats", "f32a"])
    S1 = t_stats[:, C_S1:C_S1 + NT]
    S2 = t_stats[:, C_S2:C_S2 + NT]
    MEAN = t_stats[:, C_MEAN:C_MEAN + NT]
    RSTD = t_stats[:, C_RSTD:C_RSTD + NT]
    RSTD1 = t_stats[:, C_RSTD1:C_RSTD1 + NT]
    TV = t_sc[:, 0:NT]
    SD = t_sc[:, NT:2 * NT]
    OP("vector", lambda e: e.tensor_scalar_mul(MEAN, S1, 1.0 / D), reads=["stats"], writes=["stats"])
    OP("scalar", lambda e: e.activation(TV, MEAN, AF.Square), reads=["stats"], writes=["tsc"])
    OP("vector", lambda e: e.scalar_tensor_tensor(TV, S2, 1.0 / D, TV, ALU.mult, ALU.subtract),
       reads=["stats", "tsc"], writes=["tsc"])
    OP("scalar", lambda e: e.activation(SD, TV, AF.Sqrt, bias=EPS),
       reads=["tsc", "epsc"], writes=["tsc"])
    OP("vector", lambda e: e.reciprocal(RSTD, SD), reads=["tsc"], writes=["stats"])
    OP("scalar", lambda e: e.activation(RSTD1, RSTD, AF.Square), reads=["stats"], writes=["stats"])
    OP("vector", lambda e: e.tensor_mul(RSTD1, RSTD1, TV), reads=["stats", "tsc"], writes=["stats"])
    OP("scalar", lambda e: e.activation(RSTD1, RSTD1, AF.Sqrt, bias=EPS),
       reads=["stats", "epsc"], writes=["stats"])
    OP("vector", lambda e: e.reciprocal(RSTD1, RSTD1), reads=["stats"], writes=["stats"])
    OP("tensor", lambda e: e.transpose(ps_a[0:3 * NT, 0:128],
                                       t_stats[:, C_MEAN:C_MEAN + 3 * NT], t_idn[:]),
       reads=["stats", "idn"], writes=["ps:A"])
    OP("scalar", lambda e: e.copy(t_statT[:], ps_a[0:3 * NT, 0:128]),
       reads=["ps:A"], writes=["statT"])

    xn32 = t_f32[:, 0:KC * TW]
    xping = (t_f32[:, KC * TW:KC * TW + TW], t_f32[:, KC * TW + TW:KC * TW + 2 * TW])
    mean_b = t_f32b[:, 0:TW]
    rstd_b = t_f32b[:, TW:2 * TW]
    rstd1_b = t_f32b[:, 2 * TW:3 * TW]
    rstd2_b = t_f32b[:, 0:TW]

    for tau in range(TAU):
        t0 = tau * TW
        for si, dst, bk in ((0, mean_b, "bc0"), (1, rstd_b, "bc1"), (2, rstd1_b, "bc2")):
            g0 = tau * GPT
            OP("sync", lambda e, si=si, g0=g0: e.dma_start(
                out=t_rows[0:1, 0:TW].rearrange("o (a b) -> o a b", b=128),
                in_=t_statT[si * NT + g0:si * NT + g0 + GPT, :]),
                reads=["statT"], writes=["rows"], kind="d", dgrp="rows")
            OP("tensor", lambda e, si=si: e.matmul(
                ps_b[:, 0:TW], t_oner[:], t_rows[0:1, 0:TW],
                start=True, stop=True, skip_group_check=True),
                reads=["rows", "oner"], writes=["ps:B"])
            OP("scalar", lambda e, dst=dst: e.copy(dst, ps_b[:, 0:TW]),
               reads=["ps:B"], writes=[bk])
        for kc in range(KC):
            xp, xk = xping[kc % 2], f"xp{kc % 2}"
            OP("sync", lambda e, xp=xp, kc=kc, t0=t0: e.dma_start(
                out=xp[:], in_=h_xfm[kc * 128:(kc + 1) * 128, t0:t0 + TW]),
                writes=[xk, "f32a"], kind="d", dgrp=xk)
            xnk = xn32[:, kc * TW:(kc + 1) * TW]
            OP("vector", lambda e, xp=xp, xnk=xnk: e.tensor_sub(xnk, xp[:], mean_b),
               reads=[xk, "bc0"], writes=[f"xn32:{kc}", "f32a"])
            OP("vector", lambda e, xnk=xnk: e.tensor_mul(xnk, xnk, rstd_b),
               reads=[f"xn32:{kc}", "bc1"], writes=[f"xn32:{kc}"])
            OP("scalar", lambda e, xnk=xnk, kc=kc, t0=t0: e.copy(
                t_xn[:, kc * S + t0:kc * S + t0 + TW], xnk),
                reads=[f"xn32:{kc}"], writes=[f"xn:{kc}:{tau}"])
            OP("vector", lambda e, xnk=xnk, kc=kc, t0=t0: e.scalar_tensor_tensor(
                t_A[:, kc * S + t0:kc * S + t0 + TW], xnk, t_n1[:, kc:kc + 1],
                rstd1_b, ALU.mult, ALU.mult),
                reads=[f"xn32:{kc}", "bc2"], sreads=["n1"], writes=[f"A:{kc}:{tau}"])
        for g in range(GPT):
            gg = tau * GPT + g
            for kc in range(KC):
                OP("tensor", lambda e, kc=kc, g=g: e.matmul(
                    ps_a[:, 0:E],
                    xn32[:, kc * TW + g * 128:kc * TW + g * 128 + 128],
                    t_wg[:, kc * E:(kc + 1) * E],
                    start=(kc == 0), stop=(kc == KC - 1), skip_group_check=True),
                    reads=[f"xn32:{kc}", "wg"], writes=["ps:A"])
            m8, nm8, sume = t_sc[:, 0:1], t_sc[:, 1:2], t_sc[:, 2:3]
            lgs = t_sc[:, 4:12]
            dmp = t_sc[:, 12:20]
            OP("vector", lambda e, lgs=lgs: e.tensor_copy(lgs, ps_a[:, 0:E]),
               reads=["ps:A"], writes=["lgs", "tsc"])
            OP("vector", lambda e, lgs=lgs: e.reduce_max(m8, lgs, axis=AX),
               reads=["lgs"], writes=["m8", "tsc"])
            OP("vector", lambda e, gg=gg, lgs=lgs: e.tensor_single_scalar(
                t_mask[:, gg * E:(gg + 1) * E], lgs, m8, ALU.is_ge),
                reads=["lgs"], sreads=["m8"], writes=["maskal"])
            OP("vector", lambda e: e.tensor_scalar_mul(nm8, m8, -1.0),
               reads=["m8"], writes=["nm8", "tsc"])
            OP("scalar", lambda e, dmp=dmp, lgs=lgs: e.activation(
                dmp, lgs, AF.Exp, bias=nm8, accum_out=sume),
                reads=["lgs"], sreads=["nm8"], writes=["sume", "tsc"])
            OP("vector", lambda e, gg=gg: e.reciprocal(t_gate[:, gg:gg + 1], sume),
               reads=["sume"], writes=["gateal"])
            if upto == 2:
                OP("vector", lambda e, gg=gg, lgs=lgs: e.tensor_copy(
                    t_kept[:, gg * E:(gg + 1) * E], lgs),
                    reads=["lgs"], writes=["keptal"])
                OP("vector", lambda e, gg=gg: e.tensor_copy(
                    t_gall[:, gg * E:gg * E + 1], m8),
                    sreads=["m8"], writes=["gall"])

    if upto == 1:
        return dbg_finish(
            [(h_out[kc * 128:(kc + 1) * 128, :], t_xn[:, kc * S:(kc + 1) * S],
              [f"xn:{kc}:{tt}" for tt in range(TAU)]) for kc in range(KC)])

    for g in range(NT):
        OP("tensor", lambda e, g=g: e.matmul(
            ps_b[0:1, g * E:(g + 1) * E], t_ones[:], t_mask[:, g * E:(g + 1) * E],
            start=(g == 0), stop=(g == NT - 1), skip_group_check=True),
            reads=["maskal", "ones"], writes=["ps:B"])
    OP("vector", lambda e: e.tensor_copy(
        t_rows[0:1, 128:128 + NT * E], ps_b[0:1, 0:NT * E]),
        reads=["ps:B", "cpad"], writes=["cpad", "rows", "cnts"])
    OP("vector", lambda e: e.memset(t_offs[0:1, 0:E], 0.0), writes=["offr"])
    for g in range(1, NT):
        OP("vector", lambda e, g=g: e.tensor_add(
            t_offs[0:1, g * E:(g + 1) * E], t_offs[0:1, (g - 1) * E:g * E],
            t_rows[0:1, 128 + (g - 1) * E:128 + g * E]),
            reads=["cnts", "offr"], writes=["offr"])
    OP("vector", lambda e: e.tensor_add(
        t_rows[0:1, 0:E], t_offs[0:1, (NT - 1) * E:NT * E],
        t_rows[0:1, 128 + (NT - 1) * E:128 + NT * E]),
        reads=["cnts", "offr", "cpad"], writes=["cpad", "rows"])
    OP("sync", lambda e: e.dma_start(out=d_ccin[:], in_=t_rows[0:1, 0:512]),
       reads=["cpad", "rows", "cnts"], writes=["ccin"], kind="d", dgrp="ccin")
    OP("gpsimd", lambda e: e.collective_compute(
        "AllGather", ALU.bypass, replica_groups=[list(range(8))],
        ins=[d_ccin[:]], outs=[d_ccout[:]]),
        reads=["ccin"], writes=["ccout"], kind="c")

    def emit_offsets():
        OP("gpsimd", lambda e: e.dma_start(out=t_allc[:], in_=d_ccout[:, 0:E]),
           reads=["ccout"], writes=["allc"], kind="d", dgrp="allc")
        OP("tensor", lambda e: e.matmul(ps_b[0:1, 0:E], t_selc[:], t_allc[:],
                                        start=True, stop=True, skip_group_check=True),
           reads=["allc", "selc"], writes=["ps:B"])
        OP("scalar", lambda e: e.copy(t_offs[0:1, NT * E:NT * E + E],
                                      ps_b[0:1, 0:E]),
           reads=["ps:B"], writes=["coff"])
        for g in range(NT):
            OP("vector", lambda e, g=g: e.tensor_add(
                t_offs[0:1, g * E:(g + 1) * E], t_offs[0:1, g * E:(g + 1) * E],
                t_offs[0:1, NT * E:NT * E + E]),
                reads=["offr", "coff"], writes=["offr"])

    if upto == 2:
        emit_offsets()
        return dbg_finish([
            (h_out[0:128, 0:NT * E], t_mask[:], ["maskal"]),
            (h_out[128:256, 0:NT], t_gate[:], ["gateal"]),
            (h_out[256:257, 0:NT * E], t_offs[0:1, 0:NT * E], ["offr"]),
            (h_out[384:512, 0:NT * E], t_kept[:], ["keptal"]),
            (h_out[512:640, 0:NT * E], t_gall[:], ["gall"]),
        ])

    def wr(slot, off, n):
        return t_wr[:, slot * 4096 + off:slot * 4096 + off + n]

    WRALL = lambda s: [f"wr{s}a", f"wr{s}b", f"wr{s}c", f"wr{s}d"]

    pctr = {"ab": 0, "st": 0}

    def nextab():
        pctr["ab"] += 1
        return (ps_a, "ps:A") if pctr["ab"] % 2 == 0 else (ps_b, "ps:B")

    def nextst():
        pctr["st"] += 1
        b = pctr["st"] % 2
        return b, ps_st[:, b * 512:(b + 1) * 512], f"ps:st{b}"

    def proj_rope(c, reg, woff, hw, dst, dkey, scale):
        slot = c % 2
        wsb = wr(slot, woff, KC * 128)
        wkey = f"wr{slot}{reg}"
        OP("gpsimd", lambda e, wsb=wsb: e.dma_start(
            out=wsb.rearrange("p (kc f) -> p kc f", f=128),
            in_=hw[c].rearrange("(kc p) f -> p kc f", p=128)),
            writes=[wkey], kind="d", dgrp=wkey)
        for tau in range(TAU):
            t0 = tau * TW
            pse, pk = nextab()
            for kc in range(KC):
                OP("tensor", lambda e, kc=kc, t0=t0, wsb=wsb, pse=pse: e.matmul(
                    pse[:, 0:TW], wsb[:, kc * 128:(kc + 1) * 128],
                    t_A[:, kc * S + t0:kc * S + t0 + TW],
                    start=(kc == 0), stop=(kc == KC - 1), skip_group_check=True),
                    reads=[wkey, f"A:{kc}:{tau}"], writes=[pk])
            qr, qk = t_pg[:, (tau % 2) * TW:(tau % 2 + 1) * TW], f"qraw{tau % 2}"
            OP("scalar", lambda e, qr=qr, pse=pse: e.copy(qr, pse[:, 0:TW]),
               reads=[pk], writes=[qk])
            sb_, sps, spk = nextst()
            OP("tensor", lambda e, qr=qr, sps=sps: e.matmul(
                sps[:, 0:TW], t_prm[:], qr, start=True, stop=True, skip_group_check=True),
                reads=[qk, "prm"], writes=[spk])
            tt1 = t_rope[:, (tau % 2) * TW:(tau % 2 + 1) * TW]
            tk1 = f"tt1{tau % 2}"
            OP("vector", lambda e, qr=qr, t0=t0, tt1=tt1: e.scalar_tensor_tensor(
                tt1, qr, scale, t_cos[:, t0:t0 + TW], ALU.mult, ALU.mult),
                reads=[qk, "cos"], writes=[tk1])
            OP("vector", lambda e, t0=t0, dst=dst, sps=sps, tt1=tt1: e.scalar_tensor_tensor(
                dst[:, t0:t0 + TW], sps[:, 0:TW], scale, t_sin[:, t0:t0 + TW],
                ALU.mult, ALU.mult),
                reads=[spk, "sin"], writes=["tt2s"])
            OP("vector", lambda e, t0=t0, dst=dst, tt1=tt1: e.tensor_add(
                dst[:, t0:t0 + TW], dst[:, t0:t0 + TW], tt1),
                reads=["tt2s", tk1], writes=[dkey])

    def build_V(c):
        slot = c % 2
        wsb = wr(slot, 2048, KC * 128)
        wkey = f"wr{slot}c"
        OP("gpsimd", lambda e, c=c, wsb=wsb: e.dma_start(
            out=wsb.rearrange("p (kc f) -> p kc f", f=128),
            in_=h_wv[c].rearrange("(kc p) f -> p kc f", p=128)),
            writes=[wkey], kind="d", dgrp=wkey)
        if c == 0:
            OP("vector", lambda e: e.memset(
                t_C[:, 0:NT * 130].rearrange("p (kt h z) -> p kt h z", h=2, z=65)[:, :, :, 64],
                1.0), writes=["C"])
        for g4 in range((NT + 3) // 4):
            tts = list(range(4 * g4, min(4 * g4 + 4, NT)))
            pse, pk = nextab()
            for i, tt in enumerate(tts):
                for kc in range(KC):
                    OP("tensor", lambda e, tt=tt, i=i, kc=kc, wsb=wsb, pse=pse: e.matmul(
                        pse[:, i * 128:(i + 1) * 128],
                        t_A[:, kc * S + tt * 128:kc * S + (tt + 1) * 128],
                        wsb[:, kc * 128:(kc + 1) * 128],
                        start=(kc == 0 and i == 0), stop=(kc == KC - 1), skip_group_check=True),
                        reads=[wkey, f"A:{kc}:{(tt * 128) // TW}"], writes=[pk])
            n = len(tts)
            OP("vector", lambda e, tts=tts, n=n, c=c, pse=pse: e.tensor_copy(
                t_C[:, 0:NT * 130].rearrange("p (kt h z) -> p kt h z", h=2, z=65)[
                    :, tts[0]:tts[0] + n, :, 0:64],
                pse.ap().rearrange("p (i h z) -> p i h z", h=2, z=64)[:, 0:n, :, :]),
                reads=[pk], writes=["C"])

    def bound_B(c):
        sqb = t_xne[:, 0:S]
        for wi, (which, buf, bk) in enumerate((("q", t_q, "qrope"), ("k", t_k, "krope"))):
            OP("scalar", lambda e, buf=buf: e.activation(sqb, buf[:], AF.Square),
               reads=[bk], writes=["sqb", "xnereg"])
            for tau in range(TAU):
                OP("tensor", lambda e, tau=tau: e.matmul(
                    ps_b[0:1, 0:TW], t_onesb[:], sqb[:, tau * TW:(tau + 1) * TW],
                    start=True, stop=True, skip_group_check=True),
                    reads=["sqb", "xnereg", "onesb"], writes=["ps:B"])
                OP("vector", lambda e, tau=tau, wi=wi: e.reduce_max(
                    t_sr[0:1, wi * 8 + tau:wi * 8 + tau + 1], ps_b[0:1, 0:TW], axis=AX),
                    reads=["ps:B"], writes=[f"nm{wi}", "tsr"])
        OP("vector", lambda e: e.reduce_max(t_sr[0:1, 16:17], t_sr[0:1, 0:TAU], axis=AX),
           sreads=["nm0"], writes=["tsr"])
        OP("vector", lambda e: e.reduce_max(t_sr[0:1, 17:18], t_sr[0:1, 8:8 + TAU], axis=AX),
           sreads=["nm1"], writes=["tsr"])
        OP("vector", lambda e: e.tensor_mul(t_sr[0:1, 18:19], t_sr[0:1, 16:17], t_sr[0:1, 17:18]),
           sreads=["tsr"], writes=["tsr"])
        OP("scalar", lambda e: e.activation(t_sr[0:1, 19:20], t_sr[0:1, 18:19], AF.Sqrt,
                                            scale=1.05),
           reads=["tsr"], writes=["bb"])
        OP("tensor", lambda e: e.matmul(ps_b[:, 0:1], t_oner[:], t_sr[0:1, 19:20],
                                        start=True, stop=True, skip_group_check=True),
           reads=["bb", "oner"], writes=["ps:B"])
        OP("scalar", lambda e, c=c: e.activation(t_nB[:, c:c + 1], ps_b[:, 0:1],
                                                 AF.Copy, scale=-1.0),
           reads=["ps:B"], writes=["negB"])

    def attention_head(c, half):
        r0 = 64 * half
        h = 2 * c + half

        def emit_ctx(kt, q0, off, batch, bank):
            for qi in batch:
                j = qi - q0
                OP("tensor", lambda e, kt=kt, qi=qi, j=j, bank=bank, h=h: e.matmul(
                    ps_ctx[0:65, qi * 128:(qi + 1) * 128],
                    t_C[:, kt * 130 + 65 * (h % 2):kt * 130 + 65 * (h % 2) + 65],
                    t_pt[:, bank * 512 + j * 128:bank * 512 + (j + 1) * 128],
                    start=(kt == 0 and qi % 4 == 0), stop=(kt == qi), skip_group_check=True),
                    reads=[f"pt{bank}", "C"], writes=[f"ps:ctx{qi // 4}"])

        pend = None
        for q0 in range(0, NT, 4):
            ng = min(4, NT - q0)
            for kt in range(q0 + ng):
                off = max(0, kt - q0)
                batch = list(range(q0 + off, q0 + ng))
                bank, sps, stk = nextst()
                OP("tensor", lambda e, kt=kt, q0=q0, off=off, ng=ng, sps=sps,
                   r0=r0: e.matmul(
                    sps[:, off * 128:ng * 128],
                    t_k[r0:r0 + 64, kt * 128:(kt + 1) * 128],
                    t_q[r0:r0 + 64, (q0 + off) * 128:(q0 + ng) * 128],
                    start=True, stop=True, skip_group_check=True),
                    reads=["krope", "qrope"], writes=[stk])
                if kt >= q0:
                    OP("vector", lambda e, off=off, sps=sps: e.tensor_add(
                        sps[:, off * 128:(off + 1) * 128],
                        sps[:, off * 128:(off + 1) * 128],
                        t_mskT[:]),
                        reads=[stk, "mskT"], writes=[stk])
                OP("scalar", lambda e, off=off, ng=ng, bank=bank, sps=sps,
                   c=c: e.activation(
                    t_pt[:, bank * 512 + off * 128:bank * 512 + ng * 128],
                    sps[:, off * 128:ng * 128],
                    AF.Exp, bias=t_nB[:, c:c + 1]),
                    reads=[stk], sreads=["negB"], writes=[f"pt{bank}"])
                if pend is not None:
                    emit_ctx(*pend)
                pend = (kt, q0, off, batch, bank)
            emit_ctx(*pend)
            pend = None

    def attn_evac(c, half):
        kc = c
        for q0 in range(0, NT, 4):
            n = min(4, NT - q0)
            OP("scalar", lambda e, q0=q0, n=n: e.activation(
                t_rows[0:1, 0:n * 128], ps_ctx[64:65, q0 * 128:(q0 + n) * 128],
                AF.Ln),
                reads=[f"ps:ctx{q0 // 4}"] + (
                    [f"ps:ctx{(q0 + n - 1) // 4}"] if (q0 + n - 1) // 4 != q0 // 4 else []),
                writes=["rows"])
            OP("scalar", lambda e, n=n: e.activation(
                t_rec[0:1, 0:n * 128], t_rows[0:1, 0:n * 128], AF.Exp, scale=-1.0),
                reads=["rows"], writes=["rec"])
            bank, sps, stk = nextst()
            OP("tensor", lambda e, n=n, sps=sps: e.matmul(
                sps[0:64, 0:n * 128], t_onerb[0:1, 0:64],
                t_rec[0:1, 0:n * 128], start=True, stop=True, skip_group_check=True),
                reads=["rec", "onerb"], writes=[stk])
            OP("scalar", lambda e, n=n, sps=sps: e.copy(t_gb[:, 0:n * 128],
                                                        sps[0:64, 0:n * 128]),
               reads=[stk], writes=["gbs"])
            tau = (q0 * 128) // TW
            bkeys = sorted({f"B:{kc}:{((q0 + i) * 128) // TW}" for i in range(n)})
            OP("vector", lambda e, q0=q0, n=n, half=half, kc=kc: e.tensor_mul(
                t_B[:, kc * S + q0 * 128:kc * S + (q0 + n) * 128][64 * half:64 * half + 64, :],
                ps_ctx[0:64, q0 * 128:(q0 + n) * 128],
                t_gb[:, 0:n * 128]),
                reads=[f"ps:ctx{q0 // 4}", "gbs"], writes=bkeys)

    for c in range(KC):
        proj_rope(c, "a", 0, h_wq, t_q, "qrope", 0.125)
        proj_rope(c, "b", 1024, h_wk, t_k, "krope", 1.0)
        build_V(c)
        bound_B(c)
        if c > 0:
            attn_evac(c - 1, 1)
        attention_head(c, 0)
        attn_evac(c, 0)
        attention_head(c, 1)
    attn_evac(KC - 1, 1)

    if upto == 3:
        return dbg_finish(
            [(h_out[kc * 128:(kc + 1) * 128, :], t_B[:, kc * S:(kc + 1) * S],
              [f"B:{kc}:{tt}" for tt in range(TAU)]) for kc in range(KC)])

    y_sl = t_f32[:, 0:KC * TW]
    for half in range(2):
        OP("gpsimd", lambda e, half=half: e.dma_start(
            out=wr(half, 0, 4096).rearrange("p (kc f) -> p kc f", f=512),
            in_=h_wo[half].rearrange("(kc p) f -> p kc f", p=128)),
            writes=[f"wr{half}a", f"wr{half}b", f"wr{half}c"], kind="d",
            dgrp=f"wr{half}a")
    for tau in range(TAU):
        t0 = tau * TW
        for ic in range(KC):
            half, iw = ic // 4, ic % 4
            pse, pk = (ps_a, "ps:A") if ic % 2 == 0 else (ps_b, "ps:B")
            for kc in range(KC):
                OP("tensor", lambda e, half=half, iw=iw, kc=kc, t0=t0, pse=pse: e.matmul(
                    pse[:, 0:TW],
                    wr(half, kc * 512 + iw * 128, 128),
                    t_B[:, kc * S + t0:kc * S + t0 + TW],
                    start=(kc == 0), stop=(kc == KC - 1), skip_group_check=True),
                    reads=[f"wr{half}a", f"B:{kc}:{tau}"], writes=[pk])
            OP("vector", lambda e, ic=ic, t0=t0, pse=pse, pk=pk: e.tensor_add(
                y_sl[:, ic * TW:(ic + 1) * TW], pse[:, 0:TW],
                t_xn[:, ic * S + t0:ic * S + t0 + TW]),
                reads=[pk, f"xn:{ic}:{tau}"], writes=[f"y:{ic}", "f32a"])
        for ic in range(KC):
            si = t_sil[:, 0:TW]
            OP("scalar", lambda e, ic=ic, si=si: e.activation(
                si, y_sl[:, ic * TW:(ic + 1) * TW], AF.Square),
                reads=[f"y:{ic}"], writes=["ysq", "silreg"])
            OP("tensor", lambda e, ic=ic, si=si: e.matmul(
                ps_st[0:1, 0:TW], t_onesb[:], si,
                start=(ic == 0), stop=(ic == KC - 1), skip_group_check=True),
                reads=["ysq", "silreg", "onesb"], writes=["ps:st0"])
        rrow = t_rows[0:1, 0:TW]
        OP("scalar", lambda e: e.activation(rrow, ps_st[0:1, 0:TW], AF.Sqrt,
                                            scale=1.0 / D, bias=EPS),
           reads=["ps:st0", "epsc"], writes=["r2row", "rows"])
        OP("vector", lambda e: e.reciprocal(rrow, rrow), reads=["r2row"], writes=["r2row"])
        OP("tensor", lambda e: e.matmul(ps_st[:, 0:TW], t_oner[:], rrow,
                                        start=True, stop=True, skip_group_check=True),
           reads=["r2row", "oner"], writes=["ps:st0"])
        OP("scalar", lambda e: e.copy(rstd2_b, ps_st[:, 0:TW]),
           reads=["ps:st0"], writes=["bc3", "bc0"])
        for ic in range(KC):
            OP("vector", lambda e, ic=ic, t0=t0: e.scalar_tensor_tensor(
                t_A[:, ic * S + t0:ic * S + t0 + TW],
                y_sl[:, ic * TW:(ic + 1) * TW], t_n2[:, ic:ic + 1], rstd2_b,
                ALU.mult, ALU.mult),
                reads=[f"y:{ic}", "bc3"], sreads=["n2"], writes=[f"A:{ic}:{tau}"])
            xp, xk = xping[ic % 2], f"xp{ic % 2}"
            OP("sync", lambda e, ic=ic, t0=t0, xp=xp: e.dma_start(
                out=xp[:], in_=h_xfm[ic * 128:(ic + 1) * 128, t0:t0 + TW]),
                writes=[xk, "f32a"], kind="d", dgrp=xk)
            ev, ek = t_ev[:, (ic % 2) * TW:(ic % 2 + 1) * TW], f"ev{ic % 2}"
            OP("vector", lambda e, ic=ic, xp=xp, ev=ev: e.tensor_add(
                ev, y_sl[:, ic * TW:(ic + 1) * TW], xp[:]),
                reads=[f"y:{ic}", xk], writes=[ek])
            OP("gpsimd", lambda e, ic=ic, t0=t0, ev=ev: e.dma_start(
                out=h_out[ic * 128:(ic + 1) * 128, t0:t0 + TW], in_=ev),
                reads=[ek], writes=[f"out:{ic}:{tau}"], kind="d",
                dgrp=f"oute{ic % 2}")

    if upto == 4:
        outkeys = [f"out:{ic}:{tt}" for ic in range(KC) for tt in range(TAU)]
        sch.barrier_wait("sync", outkeys)
        return finish()

    emit_offsets()

    for g in range(NT):
        OP("tensor", lambda e, g=g: e.matmul(
            ps_a[:, g * E:(g + 1) * E], t_ltri[:], t_mask[:, g * E:(g + 1) * E],
            start=(g == 0), stop=False, skip_group_check=True),
            reads=["ltri", "maskal"], writes=["ps:A"])
    OP("tensor", lambda e: e.matmul(
        ps_a[:, 0:NT * E], t_oner[:], t_offs[0:1, 0:NT * E], start=False, stop=True, skip_group_check=True),
        reads=["oner", "offr"], writes=["ps:A"])
    OP("vector", lambda e: e.tensor_single_scalar(
        t_kept[:], ps_a[:, 0:NT * E], CAPTH, ALU.is_lt),
        reads=["ps:A"], writes=["keptal"])
    for g in range(NT):
        OP("vector", lambda e, g=g: e.scalar_tensor_tensor(
            t_gall[:, g * E:(g + 1) * E], t_mask[:, g * E:(g + 1) * E],
            t_gate[:, g:g + 1], t_kept[:, g * E:(g + 1) * E], ALU.mult, ALU.mult),
            reads=["maskal", "keptal"], sreads=["gateal"], writes=["gall"])
    for g in range(NT):
        OP("tensor", lambda e, g=g: e.transpose(
            ps_b[0:E, 0:128], t_gall[:, g * E:(g + 1) * E], t_idn[:]),
            reads=["gall", "idn"], writes=["ps:B"])
        OP("scalar", lambda e, g=g: e.copy(t_gt[:, g * 128:(g + 1) * 128],
                                           ps_b[0:E, 0:128]),
           reads=["ps:B"], writes=["gt"])
    if upto == 41:
        return dbg_finish([
            (h_out[0:128, 0:NT * E], t_kept[:], ["keptal"]),
            (h_out[128:256, 0:NT * E], t_gall[:], ["gall"]),
            (h_out[256:256 + E, 0:S], t_gt[:], ["gt"]),
        ])
    for ee in range(E):
        for tau in range(TAU):
            OP("tensor", lambda e, ee=ee, tau=tau: e.matmul(
                ps_b[:, 0:TW], t_sel[:, ee * 128:ee * 128 + 128],
                t_gt[:, tau * TW:(tau + 1) * TW], start=True, stop=True, skip_group_check=True),
                reads=["sel", "gt"], writes=["ps:B"])
            OP("scalar", lambda e, ee=ee, tau=tau: e.copy(
                t_B[:, ee * S + tau * TW:ee * S + (tau + 1) * TW], ps_b[:, 0:TW]),
                reads=["ps:B"], writes=[f"B:{ee}:{tau}"])
    if upto == 43:
        return dbg_finish(
            [(h_out[ee * 128:ee * 128 + 128, 0:S], t_B[:, ee * S:(ee + 1) * S],
              [f"B:{ee}:{tt}" for tt in range(TAU)]) for ee in range(E)])

    def moe_psum(ic):
        if ic < 4:
            return ps_ctx[:, ic * 512:(ic + 1) * 512], f"ps:ctx{ic}"
        if ic < 6:
            return ps_st[:, (ic - 4) * 512:(ic - 3) * 512], f"ps:st{ic - 4}"
        return (ps_a, "ps:A") if ic == 6 else (ps_b, "ps:B")

    EMM = {45: 0, 46: 0, 47: 1, 48: 2, 49: 4}.get(upto, E)
    wjobs = [(tau, ee, hh) for tau in range(TAU) for ee in range(E)
             for hh in range(2)]

    def wslot3(j):
        s = j % 3
        if s < 2:
            return wr(s, 0, 4096), [f"wr{s}a", f"wr{s}b", f"wr{s}c"]
        return t_w2g[:, 0:4096], ["w2g0", "w2g1", "w2g2", "w2g3"]

    def emit_wdma(j):
        tau_, ee_, hh_ = wjobs[j]
        wbuf, wks = wslot3(j)
        OP("gpsimd", lambda e, ee=ee_, hh=hh_, wbuf=wbuf: e.dma_start(
            out=wbuf.rearrange("p (kc f) -> p kc f", f=512),
            in_=h_we[ee, hh].rearrange("(kc p) f -> p kc f", p=128)),
            writes=wks, kind="d", dgrp=wks[0])

    if upto != 45:
        emit_wdma(0)
        emit_wdma(1)
        emit_wdma(2)
    ji = 0
    for tau in range(TAU):
        t0 = tau * TW
        for ee in range(E):
            for kc in range(KC):
                OP("vector", lambda e, ee=ee, kc=kc, t0=t0, tau=tau: e.tensor_mul(
                    t_xne[:, kc * TW:(kc + 1) * TW],
                    t_xn[:, kc * S + t0:kc * S + t0 + TW],
                    t_B[:, ee * S + t0:ee * S + t0 + TW]),
                    reads=[f"xn:{kc}:{tau}", f"B:{ee}:{tau}"],
                    writes=[f"xne:{kc}", "xnereg"])
            for hh in range(2):
                j = ji
                ji += 1
                wbuf, wks = wslot3(j)
                if ee < EMM:
                    for kc in range(KC):
                        for iw in range(4):
                            ic = hh * 4 + iw
                            pse, pk = moe_psum(ic)
                            OP("tensor", lambda e, kc=kc, iw=iw, pse=pse, wbuf=wbuf,
                               ee=ee, hh=hh: e.matmul(
                                pse[:, 0:TW],
                                wbuf[:, kc * 512 + iw * 128:kc * 512 + (iw + 1) * 128],
                                t_xne[:, kc * TW:(kc + 1) * TW],
                                start=(ee == 0 and kc == 0), stop=False, skip_group_check=True),
                                reads=[wks[0], f"xne:{kc}"], writes=[pk])
                if upto != 45 and j + 3 < len(wjobs):
                    emit_wdma(j + 3)
        if upto in (45, 46):
            continue
        for ic in range(KC):
            pse, pk = moe_psum(ic)
            OP("tensor", lambda e, ic=ic, tau=tau, pse=pse: e.matmul(
                pse[:, 0:TW], t_be[:, ic * 128:(ic + 1) * 128],
                t_gt[:, tau * TW:(tau + 1) * TW], start=False, stop=True, skip_group_check=True),
                reads=["be", "gt"], writes=[pk])
            ev, ek = t_ev[:, (ic % 2) * TW:(ic % 2 + 1) * TW], f"ev{ic % 2}"
            OP("scalar", lambda e, pse=pse, ev=ev: e.copy(ev, pse[:, 0:TW]),
               reads=[pk], writes=[ek])
            if upto == 44:
                OP("gpsimd", lambda e, ic=ic, t0=t0, ev=ev: e.dma_start(
                    out=h_out[ic * 128:(ic + 1) * 128, t0:t0 + TW], in_=ev),
                    reads=[ek, f"out:{ic}:{tau}"], writes=[f"out:{ic}:{tau}"],
                    kind="d", dgrp=f"oute{ic % 2}")
            else:
                OP("gpsimd", lambda e, ic=ic, t0=t0, ev=ev: e.dma_start(
                    out=h_out[ic * 128:(ic + 1) * 128, t0:t0 + TW], in_=ev,
                    accum_op=ALU.add),
                    reads=[ek, f"out:{ic}:{tau}"], writes=[f"out:{ic}:{tau}"],
                    kind="d", dgrp=f"oute{ic % 2}")

    if upto in (45, 46):
        return dbg_finish([(h_out[0:128, 0:TW], t_xne[:, 0:TW], ["xne:0"])])
    if upto in (5, 44, 47, 48, 49):
        outkeys = [f"out:{ic}:{tt}" for ic in range(KC) for tt in range(TAU)]
        sch.barrier_wait("sync", outkeys)
        sch.barrier_wait("gpsimd", outkeys)
        return finish()

    FG = 4
    ff_acc = t_f32[:, 0:KC * TW]
    for tau in range(TAU):
        t0 = tau * TW
        for fc in range(FC):
            slot = fc % 2
            OP("gpsimd", lambda e, fc=fc, slot=slot: e.dma_start(
                out=wr(slot, 0, KC * 128).rearrange("p (kc f) -> p kc f", f=128),
                in_=h_w1[fc].rearrange("(kc p) f -> p kc f", p=128)),
                writes=[f"wr{slot}a"], kind="d", dgrp=f"wr{slot}a")
            OP("gpsimd", lambda e, fc=fc, slot=slot: e.dma_start(
                out=wr(slot, 1024, KC * 128).rearrange("p (kc f) -> p kc f", f=128),
                in_=h_w3[fc].rearrange("(kc p) f -> p kc f", p=128)),
                writes=[f"wr{slot}b"], kind="d", dgrp=f"wr{slot}b")
            p1, p1k = (ps_a, "ps:A") if slot == 0 else (ps_st[:, 0:512], "ps:st0")
            p3, p3k = (ps_b, "ps:B") if slot == 0 else (ps_st[:, 512:1024], "ps:st1")
            for kc in range(KC):
                OP("tensor", lambda e, kc=kc, t0=t0, slot=slot, p1=p1: e.matmul(
                    p1[:, 0:TW], wr(slot, kc * 128, 128),
                    t_A[:, kc * S + t0:kc * S + t0 + TW],
                    start=(kc == 0), stop=(kc == KC - 1), skip_group_check=True),
                    reads=[f"wr{slot}a", f"A:{kc}:{tau}"], writes=[p1k])
            for kc in range(KC):
                OP("tensor", lambda e, kc=kc, t0=t0, slot=slot, p3=p3: e.matmul(
                    p3[:, 0:TW], wr(slot, 1024 + kc * 128, 128),
                    t_A[:, kc * S + t0:kc * S + t0 + TW],
                    start=(kc == 0), stop=(kc == KC - 1), skip_group_check=True),
                    reads=[f"wr{slot}b", f"A:{kc}:{tau}"], writes=[p3k])
            si, sk = t_sil[:, slot * TW:(slot + 1) * TW], f"sil{slot}"
            OP("scalar", lambda e, si=si, p1=p1: e.activation(si, p1[:, 0:TW], AF.Silu),
               reads=[p1k], writes=[sk, "silreg"])
            j = fc % FG
            pgt, pgk = t_pg[:, j * TW:(j + 1) * TW], f"pg{j}"
            OP("vector", lambda e, si=si, p3=p3, pgt=pgt: e.tensor_mul(
                pgt, si, p3[:, 0:TW]),
                reads=[sk, p3k], writes=[pgk])
            OP("gpsimd", lambda e, fc=fc, j=j: e.dma_start(
                out=t_w2g[:, j * D:(j + 1) * D], in_=h_w2[fc]),
                writes=[f"w2g{j}"], kind="d", dgrp=f"w2g{j}")
            if j == FG - 1:
                fg0 = fc - FG + 1
                for oc in range(KC):
                    fps = ps_ctx[:, (oc % 2) * 512:(oc % 2) * 512 + TW]
                    fpk = f"ps:ctx{oc % 2}"
                    for jj in range(FG):
                        OP("tensor", lambda e, jj=jj, oc=oc, fps=fps: e.matmul(
                            fps, t_w2g[:, jj * D + oc * 128:jj * D + (oc + 1) * 128],
                            t_pg[:, jj * TW:(jj + 1) * TW],
                            start=(jj == 0), stop=(jj == FG - 1), skip_group_check=True),
                            reads=[f"w2g{jj}", f"pg{jj}"], writes=[fpk])
                    if fg0 == 0:
                        OP("vector", lambda e, oc=oc, fps=fps: e.tensor_copy(
                            ff_acc[:, oc * TW:(oc + 1) * TW], fps),
                            reads=[fpk], writes=[f"y:{oc}"])
                    else:
                        OP("vector", lambda e, oc=oc, fps=fps: e.tensor_add(
                            ff_acc[:, oc * TW:(oc + 1) * TW],
                            ff_acc[:, oc * TW:(oc + 1) * TW], fps),
                            reads=[fpk, f"y:{oc}"], writes=[f"y:{oc}"])
        for oc in range(KC):
            acc_op = {61: ALU.bypass}.get(upto, ALU.add)
            OP("gpsimd", lambda e, oc=oc, t0=t0, acc_op=acc_op: e.dma_start(
                out=h_out[oc * 128:(oc + 1) * 128, t0:t0 + TW],
                in_=ff_acc[:, oc * TW:(oc + 1) * TW], accum_op=acc_op),
                reads=[f"y:{oc}", f"out:{oc}:{tau}"],
                writes=[f"out:{oc}:{tau}", f"ffdma{oc % 2}"], kind="d",
                dgrp=f"oute{oc % 2}")

    outkeys = [f"out:{ic}:{tau}" for ic in range(KC) for tau in range(TAU)]
    sch.barrier_wait("gpsimd", outkeys)
    sch.barrier_wait("sync", outkeys)
    return finish()


_CACHE = {}


def _consts(S):
    DKH = 64
    pos = np.arange(S, dtype=np.float64)
    k = np.arange(DKH) // 2
    ang = pos[:, None] / np.power(10000.0, 2.0 * k / DKH)[None, :]
    cos = np.cos(ang)
    sin = np.sin(ang) * np.where(np.arange(DKH) % 2 == 0, -1.0, 1.0)[None, :]
    cosT = np.ascontiguousarray(np.tile(cos.T, (2, 1)).astype(BF))
    sinT = np.ascontiguousarray(np.tile(sin.T, (2, 1)).astype(BF))
    idn = np.eye(128, dtype=np.float32)
    prm = np.zeros((128, 128), dtype=np.float32)
    prm[np.arange(128), np.arange(128) ^ 1] = 1.0
    mskT = np.ascontiguousarray(
        np.where(np.arange(128)[:, None] > np.arange(128)[None, :], NEG, 0.0).astype(BF))
    ltri = np.ascontiguousarray(np.triu(np.ones((128, 128), dtype=np.float32), k=1))
    selm = np.zeros((E, E * 128), dtype=np.float32)
    for ee in range(E):
        selm[ee, ee * 128:(ee + 1) * 128] = 1.0
    return dict(cosT=cosT, sinT=sinT, idn=idn, prm=np.ascontiguousarray(prm.astype(BF)),
                mskT=mskT, ltri=ltri, selm=np.ascontiguousarray(selm.astype(BF)))


def prep_inputs(x, wq, wk, wv, wo, n1, n2, w1, w2, w3, wg, We, be, S):
    cst = _consts(S)
    f32 = np.float32

    def T(w):
        return np.ascontiguousarray(np.asarray(w, dtype=f32).T)

    def blocks(wT, n):
        return np.ascontiguousarray(wT.reshape(D, n, wT.shape[1] // n).transpose(1, 0, 2))

    wq_b = blocks(T(wq), KC)
    wk_b = blocks(T(wk), KC)
    wv_b = blocks(T(wv), KC)
    wo_b = blocks(T(wo), 2)
    w1_b = blocks(T(w1), FC)
    w3_b = blocks(T(w3), FC)
    w2_b = np.ascontiguousarray(T(w2).reshape(FC, 128, D))
    WeT = np.asarray(We, dtype=f32).transpose(0, 2, 1)
    we_b = np.ascontiguousarray(WeT.reshape(E, D, 2, 512).transpose(0, 2, 1, 3))
    n1c = np.ascontiguousarray(np.asarray(n1, dtype=f32).reshape(KC, 128).T)
    n2c = np.ascontiguousarray(np.asarray(n2, dtype=f32).reshape(KC, 128).T)
    base = dict(wq_b=wq_b, wk_b=wk_b, wv_b=wv_b, wo_b=wo_b, w1_b=w1_b,
                w3_b=w3_b, w2_b=w2_b, we_b=we_b,
                wg=np.ascontiguousarray(np.asarray(wg, dtype=f32)),
                be=np.ascontiguousarray(np.asarray(be, dtype=f32)),
                n1c=n1c, n2c=n2c, **cst)
    xa = np.asarray(x, dtype=f32)
    in_maps = []
    for b in range(xa.shape[0]):
        m = dict(base)
        m["x_tm"] = np.ascontiguousarray(xa[b])
        m["x_fm"] = np.ascontiguousarray(xa[b].T)
        m["selc"] = (np.arange(E) < b).astype(f32).reshape(E, 1)
        in_maps.append(m)
    return in_maps


def kernel(L0_attn_x, L0_attn_wq, L0_attn_wk, L0_attn_wv, L0_attn_wo,
           L0_attn_norm1_w, L0_attn_norm2_w, L0_attn_w1, L0_attn_w2,
           L0_attn_w3, L0_ffn_wg, L0_ffn_W, L0_ffn_b):
    import os
    upto = int(os.environ.get("K_UPTO", "9"))
    S = np.asarray(L0_attn_x).shape[1]
    key = (S, upto)
    if key not in _CACHE:
        _CACHE[key] = build_program(S, upto)
    nc = _CACHE[key]
    in_maps = prep_inputs(L0_attn_x, L0_attn_wq, L0_attn_wk, L0_attn_wv,
                          L0_attn_wo, L0_attn_norm1_w, L0_attn_norm2_w,
                          L0_attn_w1, L0_attn_w2, L0_attn_w3, L0_ffn_wg,
                          L0_ffn_W, L0_ffn_b, S)
    res = run_bass_kernel_spmd(nc, in_maps, core_ids=list(range(8)))
    out = np.stack([np.asarray(res.results[b]["out"]).T for b in range(8)])
    return np.ascontiguousarray(out.astype(np.float32))
